# version 01
# baseline (speedup 1.0000x reference)
import contextlib
import numpy as np
import ml_dtypes

import concourse.bass as bass
import concourse.mybir as mybir
from concourse.bass_utils import run_bass_kernel_spmd

F32 = mybir.dt.float32
BF16 = mybir.dt.bfloat16
AF = mybir.ActivationFunctionType
ALU = mybir.AluOpType
AX = mybir.AxisListType
ET = mybir.EngineType

MAGIC = float(1.5 * 2 ** 23)
SQ5 = 0.7071067811865476
QS = 0.35355339059327373

T_SEQ = 4096
D = 1024
C = 128
NCH = T_SEQ // C
H0, F0, V0 = 16, 16, 64
H1, D1 = 4, 64

def colh(h):
    return 65 * (h % 7) + 512 * (h // 7)

S_W = 1536


class KB:

    def __init__(self, nchunks=NCH, layers=(0, 1), dbg=False):
        self.nchunks = nchunks
        self.layers = layers
        self.dbg = dbg
        self.cur_chunk = -1
        self.dbg_tensors = {}
        self.nc = bass.Bass("TRN2", target_bir_lowering=False)
        self.stack = contextlib.ExitStack()
        self.ld = 0
        self.wld = 0
        self.st = 0

    def bar(self, pool=False):
        self.nc.all_engine_barrier()

    def alloc(self):
        nc = self.nc
        sb = nc.alloc_sbuf_tensor
        ps = nc.alloc_psum_tensor

        self.i_x = nc.dram_tensor("x", [T_SEQ, D], F32, kind="ExternalInput").ap()
        self.i_wq0 = nc.dram_tensor("wq0", [256, 1024], F32, kind="ExternalInput").ap()
        self.i_wk0 = nc.dram_tensor("wk0", [256, 1024], F32, kind="ExternalInput").ap()
        self.i_wv0 = nc.dram_tensor("wv0", [1024, 1024], F32, kind="ExternalInput").ap()
        self.i_wo0 = nc.dram_tensor("wo0", [1024, 1024], F32, kind="ExternalInput").ap()
        self.i_ker = nc.dram_tensor("ker", [1024, 1024], F32, kind="ExternalInput").ap()
        self.i_bias = nc.dram_tensor("bias", [1, 1024], F32, kind="ExternalInput").ap()
        self.i_a = nc.dram_tensor("la", [1024, 16], F32, kind="ExternalInput").ap()
        self.i_b = nc.dram_tensor("lb", [16, 1024], F32, kind="ExternalInput").ap()
        self.i_wq1 = nc.dram_tensor("wq1", [256, 1024], F32, kind="ExternalInput").ap()
        self.i_wk1 = nc.dram_tensor("wk1", [256, 1024], F32, kind="ExternalInput").ap()
        self.i_wv1 = nc.dram_tensor("wv1", [256, 1024], F32, kind="ExternalInput").ap()
        self.i_wo1 = nc.dram_tensor("wo1", [1024, 256], F32, kind="ExternalInput").ap()
        self.i_w1 = nc.dram_tensor("w1", [1024, 1024], F32, kind="ExternalInput").ap()
        self.i_idn = nc.dram_tensor("idn", [128, 128], F32, kind="ExternalInput").ap()
        self.i_mask = nc.dram_tensor("maskT", [128, 128], F32, kind="ExternalInput").ap()
        self.o_out = nc.dram_tensor("out", [T_SEQ, D], F32, kind="ExternalOutput").ap()

        self.wqT = sb("wqT", [128, 2048], BF16)
        self.wkT = sb("wkT", [128, 2048], BF16)
        self.wvT = sb("wvT", [128, 8192], BF16)
        self.woT = sb("woT", [128, 8192], BF16)
        self.kerq = sb("kerq", [128, 8192], BF16)
        self.a_sb = sb("a_sb", [128, 128], BF16)
        self.b_sb = sb("b_sb", [16, 1024], BF16)
        self.bias_bf = sb("bias_bf", [1, 1024], BF16)
        self.wq1T = sb("wq1T", [128, 2048], BF16)
        self.wk1T = sb("wk1T", [128, 2048], BF16)
        self.wv1T = sb("wv1T", [128, 2048], BF16)
        self.wo1T = sb("wo1T", [128, 2048], BF16)
        self.w1T = sb("w1T", [128, 8192], BF16)

        self.idn_f = sb("idn_f", [128, 128], F32)
        self.idn_bf = sb("idn_bf", [128, 128], BF16)
        self.maskT = sb("maskT_sb", [128, 128], F32)
        self.ones_bf = sb("ones_bf", [1, 128], BF16)
        self.ones_f = sb("ones_f", [1, 128], F32)
        self.consts = sb("consts", [128, 8], F32)

        self.x_sb = [sb("x_sb0", [128, D], F32), sb("x_sb1", [128, D], F32)]
        self.sq = sb("sq_scr", [128, 2048], F32)
        self.xn = sb("xn_bf", [128, D], BF16)
        self.xnT = sb("xnT_bf", [128, D], BF16)
        self.qpad = sb("qpad", [128, 2048], BF16)
        self.kpad = sb("kpad", [128, 2048], BF16)
        self.q_s = sb("q_s", [128, 256], BF16)
        self.k_s = sb("k_s", [128, 256], BF16)
        self.qkT = sb("qkT", [128, 4096], BF16)
        self.v_aug = sb("v_aug", [128, H0 * 65], BF16)
        self.uT = sb("uT_bf", [16, 128], BF16)
        self.qf = sb("qf", [128, 4096], BF16)
        self.kf = sb("kf", [128, 4096], BF16)
        self.qfT = sb("qfT", [128, 4096], BF16)
        self.AT = sb("AT", [128, 2048], BF16)
        self.y_bf = sb("y_bf", [128, 1024], BF16)
        self.yT = sb("yT", [128, 1024], BF16)
        self.m_col = sb("m_col", [128, 1], F32)
        self.s2_col = sb("s2_col", [128, 1], F32)
        self.t_col = sb("t_col", [128, 1], F32)
        self.v_col = sb("v_col", [128, 1], F32)
        self.r_col = sb("r_col", [128, 1], F32)
        self.dent = sb("dent", [128, 16], F32)
        self.z16 = sb("z16", [128, 16], F32)

        self.S_low = sb("S_low", [17, S_W], F32)
        self.S_low_bf = sb("S_low_bf", [17, S_W], BF16)
        self.S_q = [sb("S_q0", [128, S_W], F32), sb("S_q1", [128, S_W], F32)]
        self.S_q_bf = [sb("S_q0_bf", [128, S_W], BF16), sb("S_q1_bf", [128, S_W], BF16)]

        self.p_qk = sb("p_qk", [128, 512], F32)
        self.nrm = sb("nrm", [128, 8], F32)
        self.rn = sb("rn", [128, 8], F32)
        self.q1pad = sb("q1pad", [128, 512], BF16)
        self.k1pad = sb("k1pad", [128, 512], BF16)
        self.qk1T = sb("qk1T", [128, 1024], BF16)
        self.v1_aug = sb("v1_aug", [128, H1 * 65], BF16)
        self.A1T = sb("A1T", [128, 512], BF16)
        self.y1_bf = sb("y1_bf", [128, 256], BF16)
        self.y1T = sb("y1T", [128, 256], BF16)
        self.dent4 = sb("dent4", [128, 4], F32)
        self.z4 = sb("z4", [128, 4], F32)
        self.S1 = sb("S1", [64, H1 * 65], F32)
        self.S1_bf = sb("S1_bf", [64, H1 * 65], BF16)

        self.stage = self.x_sb[0]
        self.stage_bf = self.y_bf
        self.amax8 = sb("amax8", [128, 8], F32)
        self.red_row = sb("red_row", [1, 128], F32)
        self.scal2 = sb("scal2", [1, 2], F32)
        self.rs_cols = sb("rs_cols", [128, 2], F32)

        self.tp_ps = ps("tp_ps", [128, 1024], BF16)
        self.out_ps = ps("out_ps", [128, 1024], F32)
        self.ws_ps = ps("ws_ps", [128, 2048], F32)
        self.sm_ps = ps("sm_ps", [128, 512], F32)

        self.sem_ld = self.stack.enter_context(nc.semaphore("s_ld"))
        self.sem_wld = self.stack.enter_context(nc.semaphore("s_wld"))
        self.sem_st = self.stack.enter_context(nc.semaphore("s_st"))
        self.sem_rdy = self.stack.enter_context(nc.semaphore("s_rdy"))

    def dump(self, name, ap):
        if not self.dbg or self.cur_chunk > 0:
            return
        nc = self.nc
        if name not in self.dbg_tensors:
            sh = list(ap.shape)
            self.dbg_tensors[name] = nc.dram_tensor(
                "dbg_" + name, sh, F32, kind="ExternalOutput").ap()
        self.bar(pool=True)
        nc.gpsimd.dma_start(self.dbg_tensors[name], ap).then_inc(self.sem_wld, 16)
        self.wld += 16
        nc.gpsimd.wait_ge(self.sem_wld, self.wld)
        self.bar(pool=True)

    def wload(self, dst_ap, src_ap):
        nc = self.nc
        nc.sync.dma_start(dst_ap, src_ap).then_inc(self.sem_wld, 16)
        self.wld += 16
        nc.vector.wait_ge(self.sem_wld, self.wld)
        nc.scalar.wait_ge(self.sem_wld, self.wld)
        self.bar()

    def mm(self, out_ap, lhsT, rhs, start, stop, tile_position=None):
        self.nc.tensor.matmul(out_ap, lhsT, rhs, start=start, stop=stop,
                              tile_position=tile_position)

    def transpose_blocks(self, src_bf, blocks, dst_view):
        nc = self.nc
        for (sc, tc) in blocks:
            nc.tensor.transpose(self.tp_ps[:, tc:tc + 128],
                                src_bf[:, sc:sc + 128], self.idn_bf[:, :])
        self.bar()

    def prep_wT(self, w_ap, R, Dc, dst):
        nc = self.nc
        nblk = Dc // 128
        for g in range(R // 128):
            self.wload(self.stage[:, 0:Dc], w_ap[g * 128:(g + 1) * 128, :])
            nc.vector.tensor_copy(self.stage_bf[:, 0:Dc], self.stage[:, 0:Dc])
            self.bar()
            for j in range(nblk):
                nc.tensor.transpose(self.tp_ps[:, j * 128:(j + 1) * 128],
                                    self.stage_bf[:, j * 128:(j + 1) * 128],
                                    self.idn_bf[:, :])
            self.bar()
            src = self.tp_ps[:, 0:nblk * 128].rearrange("p (j r) -> p j r", j=nblk)
            dstv = dst[:, :].rearrange("p (j r) -> p j r", j=nblk)[:, :, g * 128:(g + 1) * 128]
            nc.scalar.activation(out=dstv, in_=src, func=AF.Copy)
            self.bar()

    def quant_scale_pass(self, w_ap, mode):
        nc = self.nc
        for g in range(8):
            self.wload(self.stage[:, :], w_ap[g * 128:(g + 1) * 128, :])
            op = ALU.max if mode == "int4" else ALU.add
            nc.vector.tensor_reduce(out=self.amax8[:, g:g + 1], in_=self.stage[:, :],
                                    axis=AX.X, op=op, apply_absolute_value=True)
            self.bar()
        op = ALU.max if mode == "int4" else ALU.add
        nc.vector.tensor_reduce(out=self.m_col[:, :], in_=self.amax8[:, :],
                                axis=AX.X, op=op)
        self.bar()
        nc.tensor.transpose(self.sm_ps[0:1, 0:128], self.m_col[:, :], self.idn_f[:, :])
        self.bar()
        nc.vector.tensor_copy(self.red_row[:, :], self.sm_ps[0:1, 0:128])
        self.bar()
        nc.vector.tensor_reduce(out=self.scal2[0:1, 0:1], in_=self.red_row[:, :],
                                axis=AX.X, op=op)
        self.bar()
        if mode == "int4":
            nc.vector.tensor_scalar(out=self.scal2[0:1, 0:1], in0=self.scal2[0:1, 0:1],
                                    scalar1=float(1.0 / 7.0), scalar2=None, op0=ALU.mult)
        else:
            nc.vector.tensor_scalar(out=self.scal2[0:1, 0:1], in0=self.scal2[0:1, 0:1],
                                    scalar1=float(1.0 / 1048576.0), scalar2=float(1e-5),
                                    op0=ALU.mult, op1=ALU.max)
        self.bar()
        self.bar()
        nc.vector.reciprocal(self.scal2[0:1, 1:2], self.scal2[0:1, 0:1])
        self.bar()
        self.mm(self.sm_ps[:, 0:2], self.ones_f[:, :], self.scal2[0:1, 0:2],
                start=True, stop=True)
        self.bar()
        nc.vector.tensor_copy(self.rs_cols[:, :], self.sm_ps[:, 0:2])
        self.bar()

    def quant_tile(self, dst_bf_ap, qmin, qmax):
        nc = self.nc
        s = self.stage[:, :]
        nc.vector.tensor_scalar(out=s, in0=s, scalar1=self.rs_cols[:, 1:2],
                                scalar2=None, op0=ALU.mult)
        self.bar()
        nc.vector.tensor_scalar(out=s, in0=s, scalar1=MAGIC, scalar2=MAGIC,
                                op0=ALU.add, op1=ALU.subtract)
        self.bar()
        nc.vector.tensor_scalar(out=s, in0=s, scalar1=float(qmin), scalar2=float(qmax),
                                op0=ALU.max, op1=ALU.min)
        self.bar()
        nc.vector.tensor_scalar(out=dst_bf_ap, in0=s, scalar1=self.rs_cols[:, 0:1],
                                scalar2=None, op0=ALU.mult)
        self.bar()

    def prep(self):
        nc = self.nc
        self.wload(self.idn_f[:, :], self.i_idn)
        self.wload(self.maskT[:, :], self.i_mask)
        nc.vector.tensor_copy(self.idn_bf[:, :], self.idn_f[:, :])
        nc.vector.memset(self.ones_bf[:, :], 1.0)
        nc.vector.memset(self.ones_f[:, :], 1.0)
        for col, val in ((0, 1e-5), (1, -0.5), (2, SQ5), (3, 0.5), (4, QS),
                         (5, 1.0 / 1024.0)):
            nc.vector.memset(self.consts[:, col:col + 1], float(val))
        nc.vector.memset(self.ws_ps[:, :], 0.0)
        nc.vector.memset(self.out_ps[:, :], 0.0)
        nc.vector.memset(self.sm_ps[:, :], 0.0)
        nc.vector.memset(self.qpad[:, :], 0.0)
        nc.vector.memset(self.kpad[:, :], 0.0)
        nc.vector.memset(self.v_aug[:, :], 0.0)
        nc.vector.memset(self.v1_aug[:, :], 0.0)
        nc.vector.memset(self.q1pad[:, :], 0.0)
        nc.vector.memset(self.k1pad[:, :], 0.0)
        for t in (self.S_low, self.S_q[0], self.S_q[1], self.S1):
            nc.vector.memset(t[:, :], 0.0)
        for t in (self.S_low_bf, self.S_q_bf[0], self.S_q_bf[1], self.S1_bf):
            nc.vector.memset(t[:, :], 0.0)
        self.bar()
        nc.vector.memset(
            self.qpad[:, :].rearrange("p (h c) -> p h c", c=128)[:, :, 16:17], 1.0)
        nc.vector.memset(
            self.kpad[:, :].rearrange("p (h c) -> p h c", c=128)[:, :, 16:17], 1.0)
        nc.vector.memset(
            self.v_aug[:, :].rearrange("p (h c) -> p h c", c=65)[:, :, 64:65], 1.0)
        nc.vector.memset(
            self.v1_aug[:, :].rearrange("p (h c) -> p h c", c=65)[:, :, 64:65], 1.0)
        self.bar()

        self.prep_wT(self.i_wq0, 256, 1024, self.wqT)
        self.prep_wT(self.i_wk0, 256, 1024, self.wkT)
        self.prep_wT(self.i_wv0, 1024, 1024, self.wvT)
        self.prep_wT(self.i_wo0, 1024, 1024, self.woT)
        self.prep_wT(self.i_wq1, 256, 1024, self.wq1T)
        self.prep_wT(self.i_wk1, 256, 1024, self.wk1T)
        self.prep_wT(self.i_wv1, 256, 1024, self.wv1T)
        self.prep_wT(self.i_wo1, 1024, 256, self.wo1T)

        a_src = self.i_a.rearrange("(g p) r -> p g r", p=128)
        self.wload(self.stage[:, 0:128].rearrange("p (g r) -> p g r", g=8), a_src)
        nc.vector.tensor_copy(self.a_sb[:, :], self.stage[:, 0:128])
        self.wload(self.stage[0:16, 0:1024], self.i_b)
        nc.vector.tensor_copy(self.b_sb[:, :], self.stage[0:16, 0:1024])
        self.wload(self.stage[0:1, 0:1024], self.i_bias)
        nc.vector.tensor_copy(self.bias_bf[:, :], self.stage[0:1, 0:1024])
        self.bar()

        self.quant_scale_pass(self.i_ker, "int4")
        for g in range(8):
            self.wload(self.stage[:, :], self.i_ker[g * 128:(g + 1) * 128, :])
            self.quant_tile(self.kerq[:, g * 1024:(g + 1) * 1024], -8.0, 7.0)

        self.quant_scale_pass(self.i_w1, "tern")
        for g in range(8):
            self.wload(self.stage[:, :], self.i_w1[g * 128:(g + 1) * 128, :])
            self.quant_tile(self.stage_bf[:, :], -1.0, 1.0)
            for j in range(8):
                nc.tensor.transpose(self.tp_ps[:, j * 128:(j + 1) * 128],
                                    self.stage_bf[:, j * 128:(j + 1) * 128],
                                    self.idn_bf[:, :])
            self.bar()
            src = self.tp_ps[:, :].rearrange("p (j r) -> p j r", j=8)
            dstv = self.w1T[:, :].rearrange("p (j r) -> p j r", j=8)[:, :, g * 128:(g + 1) * 128]
            nc.scalar.activation(out=dstv, in_=src, func=AF.Copy)
            self.bar()
        self.cur_chunk = 0
        self.dump("kerq0", self.kerq[:, 0:1024])
        self.dump("wqT0", self.wqT[:, 0:256])
        self.dump("wvT0", self.wvT[:, 0:1024])
        self.dump("w1T0", self.w1T[:, 0:1024])
        self.dump("wo1T0", self.wo1T[:, 0:1024])
        self.dump("a_sb", self.a_sb[:, :])
        self.dump("b_sb", self.b_sb[:, :])
        self.dump("bias_bf", self.bias_bf[:, :])
        self.dump("idn_bf", self.idn_bf[:, :])
        self.cur_chunk = -1

    def layernorm(self, x_ap):
        nc = self.nc
        nc.vector.tensor_reduce(out=self.m_col[:, :], in_=x_ap, axis=AX.X, op=ALU.add)
        nc.scalar.activation(out=self.sq[:, 0:1024], in_=x_ap, func=AF.Square,
                             accum_out=self.s2_col[:, :])
        self.bar()
        nc.vector.tensor_scalar(out=self.m_col[:, :], in0=self.m_col[:, :],
                                scalar1=float(1.0 / 1024.0), scalar2=None, op0=ALU.mult)
        self.bar()
        nc.vector.tensor_tensor(out=self.t_col[:, :], in0=self.m_col[:, :],
                                in1=self.m_col[:, :], op=ALU.mult)
        self.bar()
        nc.vector.scalar_tensor_tensor(out=self.v_col[:, :], in0=self.s2_col[:, :],
                                       scalar=float(1.0 / 1024.0), in1=self.t_col[:, :],
                                       op0=ALU.mult, op1=ALU.subtract)
        self.bar()
        nc.scalar.activation(out=self.r_col[:, :], in_=self.v_col[:, :], func=AF.Ln,
                             bias=self.consts[:, 0:1])
        self.bar()
        nc.scalar.activation(out=self.r_col[:, :], in_=self.r_col[:, :], func=AF.Exp,
                             scale=self.consts[:, 1:2])
        self.bar()
        nc.vector.tensor_scalar(out=self.xn[:, :], in0=x_ap,
                                scalar1=self.m_col[:, 0:1], scalar2=self.r_col[:, 0:1],
                                op0=ALU.subtract, op1=ALU.mult)
        self.bar()

    def xn_transpose(self):
        nc = self.nc
        for j in range(8):
            nc.tensor.transpose(self.tp_ps[:, j * 128:(j + 1) * 128],
                                self.xn[:, j * 128:(j + 1) * 128], self.idn_bf[:, :])
        self.bar()
        nc.scalar.activation(out=self.xnT[:, :], in_=self.tp_ps[:, :], func=AF.Copy)
        self.bar()

    def layer0(self, x_ap):
        nc = self.nc
        self.dump("l0_xin", x_ap)
        self.layernorm(x_ap)
        self.dump("ln_m", self.m_col[:, :])
        self.dump("ln_s2", self.s2_col[:, :])
        self.dump("ln_v", self.v_col[:, :])
        self.dump("ln_r", self.r_col[:, :])
        self.dump("l0_xn", self.xn[:, :])
        self.xn_transpose()
        self.dump("l0_xnT", self.xnT[:, :])

        for g in range(8):
            lhs = self.xnT[:, g * 128:(g + 1) * 128]
            self.mm(self.ws_ps[:, 0:256], lhs, self.wqT[:, g * 256:(g + 1) * 256],
                    start=(g == 0), stop=False)
            self.mm(self.ws_ps[:, 256:512], lhs, self.wkT[:, g * 256:(g + 1) * 256],
                    start=False, stop=(g == 7))
            self.mm(self.ws_ps[:, 512:1024], lhs,
                    self.wvT[:, g * 1024:g * 1024 + 512], start=(g == 0), stop=(g == 7))
            self.mm(self.ws_ps[:, 1024:1536], lhs,
                    self.wvT[:, g * 1024 + 512:(g + 1) * 1024],
                    start=(g == 0), stop=(g == 7))
        self.mm(self.out_ps[:, 0:512], self.ones_bf[:, :], self.bias_bf[:, 0:512],
                start=True, stop=False)
        self.mm(self.out_ps[:, 512:1024], self.ones_bf[:, :], self.bias_bf[:, 512:1024],
                start=True, stop=False)
        for g in range(8):
            lhs = self.xnT[:, g * 128:(g + 1) * 128]
            self.mm(self.out_ps[:, 0:512], lhs, self.kerq[:, g * 1024:g * 1024 + 512],
                    start=False, stop=False)
            self.mm(self.out_ps[:, 512:1024], lhs,
                    self.kerq[:, g * 1024 + 512:(g + 1) * 1024], start=False, stop=False)
        for g in range(8):
            self.mm(self.sm_ps[0:16, 0:128], self.a_sb[:, g * 16:(g + 1) * 16],
                    self.xnT[:, g * 128:(g + 1) * 128], start=(g == 0), stop=(g == 7))
        self.bar()

        qv = self.ws_ps[:, 0:256].rearrange("p (h f) -> p h f", h=16)
        kv = self.ws_ps[:, 256:512].rearrange("p (h f) -> p h f", h=16)
        qpadv = self.qpad[:, :].rearrange("p (h c) -> p h c", c=128)[:, :, 0:16]
        kpadv = self.kpad[:, :].rearrange("p (h c) -> p h c", c=128)[:, :, 0:16]
        nc.scalar.activation(out=qpadv, in_=qv, func=AF.Copy, scale=self.consts[:, 3:4])
        nc.scalar.activation(out=kpadv, in_=kv, func=AF.Copy, scale=self.consts[:, 3:4])
        nc.scalar.activation(out=self.q_s[:, :].rearrange("p (h f) -> p h f", h=16),
                             in_=qv, func=AF.Copy, scale=self.consts[:, 4:5])
        nc.scalar.activation(out=self.k_s[:, :].rearrange("p (h f) -> p h f", h=16),
                             in_=kv, func=AF.Copy, scale=self.consts[:, 4:5])
        vaugv = self.v_aug[:, :].rearrange("p (h c) -> p h c", c=65)[:, :, 0:64]
        nc.scalar.activation(out=vaugv,
                             in_=self.ws_ps[:, 512:1536].rearrange("p (h e) -> p h e", h=16),
                             func=AF.Copy)
        nc.vector.tensor_copy(self.uT[:, :], self.sm_ps[0:16, 0:128])
        self.bar()
        self.dump("l0_qpad", self.qpad[:, :])
        self.dump("l0_q_s", self.q_s[:, :])
        self.dump("l0_v_aug", self.v_aug[:, :])
        self.dump("l0_uT", self.uT[:, :])

        self.mm(self.out_ps[:, 0:512], self.uT[0:16, :], self.b_sb[:, 0:512],
                start=False, stop=False)
        self.mm(self.out_ps[:, 512:1024], self.uT[0:16, :], self.b_sb[:, 512:1024],
                start=False, stop=False)

        qs_v = self.q_s[:, :].rearrange("p (h f) -> p h f", h=16)
        ks_v = self.k_s[:, :].rearrange("p (h f) -> p h f", h=16)
        qf_v = self.qf[:, :].rearrange("p (h i j) -> p h i j", h=16, i=16)
        kf_v = self.kf[:, :].rearrange("p (h i j) -> p h i j", h=16, i=16)
        nc.vector.tensor_tensor(
            out=qf_v,
            in0=qs_v.unsqueeze(3).broadcast_to([128, 16, 16, 16]),
            in1=qpadv.unsqueeze(2).broadcast_to([128, 16, 16, 16]), op=ALU.mult)
        nc.gpsimd.tensor_tensor(
            out=kf_v,
            in0=ks_v.unsqueeze(3).broadcast_to([128, 16, 16, 16]),
            in1=kpadv.unsqueeze(2).broadcast_to([128, 16, 16, 16]), op=ALU.mult)
        self.bar(pool=True)
        self.dump("l0_qf", self.qf[:, :])
        self.dump("l0_kf", self.kf[:, :])

        for r in range(4):
            for j in range(8):
                b = r * 8 + j
                srct = self.qpad if b < 16 else self.kpad
                sc = (b % 16) * 128
                nc.tensor.transpose(self.tp_ps[:, j * 128:(j + 1) * 128],
                                    srct[:, sc:sc + 128], self.idn_bf[:, :])
            self.bar()
            nc.scalar.activation(out=self.qkT[:, r * 1024:(r + 1) * 1024],
                                 in_=self.tp_ps[:, :], func=AF.Copy)
            self.bar()
        for r in range(4):
            for j in range(8):
                b = r * 8 + j
                nc.tensor.transpose(self.tp_ps[:, j * 128:(j + 1) * 128],
                                    self.qf[:, b * 128:(b + 1) * 128], self.idn_bf[:, :])
            self.bar()
            nc.scalar.activation(out=self.qfT[:, r * 1024:(r + 1) * 1024],
                                 in_=self.tp_ps[:, :], func=AF.Copy)
            self.bar()
        self.dump("l0_qkT", self.qkT[:, :])
        self.dump("l0_qfT", self.qfT[:, :])

        for h in range(16):
            lhs = self.qkT[0:16, 2048 + h * 128:2048 + (h + 1) * 128]
            rhs = self.qkT[0:16, h * 128:(h + 1) * 128]
            self.mm(self.ws_ps[:, h * 128:(h + 1) * 128], lhs, rhs,
                    start=(h % 4 == 0), stop=(h % 4 == 3))
        self.bar()

        nc.scalar.activation(out=self.sq[:, 0:2048], in_=self.ws_ps[:, :],
                             func=AF.Square, scale=self.consts[:, 2:3],
                             bias=self.consts[:, 2:3])
        self.bar()
        nc.vector.scalar_tensor_tensor(
            out=self.AT[:, :].rearrange("p (h t) -> p h t", h=16),
            in0=self.sq[:, 0:2048].rearrange("p (h t) -> p h t", h=16),
            scalar=0.5,
            in1=self.maskT[:, :].unsqueeze(1).broadcast_to([128, 16, 128]),
            op0=ALU.add, op1=ALU.mult)
        self.bar()
        self.dump("l0_Asq", self.sq[:, 0:2048])
        self.dump("l0_AT", self.AT[:, :])

        for h in range(16):
            yc = h * 128
            self.mm(self.ws_ps[:, yc:yc + 65],
                    self.AT[:, h * 128:(h + 1) * 128],
                    self.v_aug[:, h * 65:(h + 1) * 65],
                    start=(h % 4 == 0), stop=False)
            self.mm(self.ws_ps[:, yc:yc + 65],
                    self.qkT[0:17, h * 128:(h + 1) * 128],
                    self.S_low_bf[0:17, colh(h):colh(h) + 65],
                    start=False, stop=False)
            for j in range(2):
                self.mm(self.ws_ps[:, yc:yc + 65],
                        self.qfT[:, (2 * h + j) * 128:(2 * h + j + 1) * 128],
                        self.S_q_bf[j][:, colh(h):colh(h) + 65],
                        start=False, stop=(h % 4 == 3 and j == 1))
        self.bar()

        denv = self.ws_ps[:, :].rearrange("p (h c) -> p h c", c=128)[:, :, 64:65]
        nc.vector.tensor_scalar(out=self.dent[:, :], in0=denv.squeeze(2),
                                scalar1=float(1e-12), scalar2=None, op0=ALU.add)
        self.bar()
        nc.vector.reciprocal(self.z16[:, :], self.dent[:, :])
        self.bar()
        nc.vector.tensor_tensor(
            out=self.y_bf[:, :].rearrange("p (h e) -> p h e", h=16),
            in0=self.ws_ps[:, :].rearrange("p (h c) -> p h c", c=128)[:, :, 0:64],
            in1=self.z16[:, :].unsqueeze(2).broadcast_to([128, 16, 64]), op=ALU.mult)
        self.bar()
        self.dump("l0_dent", self.dent[:, :])
        self.dump("l0_ybf", self.y_bf[:, :])

        for j in range(8):
            nc.tensor.transpose(self.tp_ps[:, j * 128:(j + 1) * 128],
                                self.y_bf[:, j * 128:(j + 1) * 128], self.idn_bf[:, :])
        self.bar()
        nc.scalar.activation(out=self.yT[:, :], in_=self.tp_ps[:, :], func=AF.Copy)
        self.bar()
        for j in range(8):
            lhs = self.yT[:, j * 128:(j + 1) * 128]
            self.mm(self.out_ps[:, 0:512], lhs, self.woT[:, j * 1024:j * 1024 + 512],
                    start=False, stop=(j == 7))
            self.mm(self.out_ps[:, 512:1024], lhs,
                    self.woT[:, j * 1024 + 512:(j + 1) * 1024],
                    start=False, stop=(j == 7))

        for rnd in range(3):
            for h in range(16):
                c0 = colh(h)
                if rnd == 0:
                    lhs = self.kpad[:, 128 * h:128 * h + 17]
                    out = self.ws_ps[0:17, c0:c0 + 65]
                else:
                    j = rnd - 1
                    lhs = self.kf[:, (2 * h + j) * 128:(2 * h + j + 1) * 128]
                    out = self.ws_ps[:, c0:c0 + 65]
                start = h in (0, 7, 14)
                stop = h in (6, 13, 15)
                self.mm(out, lhs, self.v_aug[:, h * 65:(h + 1) * 65],
                        start=start, stop=stop)
            self.bar()
            S = self.S_low if rnd == 0 else self.S_q[rnd - 1]
            Sbf = self.S_low_bf if rnd == 0 else self.S_q_bf[rnd - 1]
            rows = 17 if rnd == 0 else 128
            nc.vector.tensor_tensor(out=S[0:rows, :], in0=S[0:rows, :],
                                    in1=self.ws_ps[0:rows, 0:S_W], op=ALU.add)
            self.bar()
            nc.scalar.activation(out=Sbf[0:rows, :], in_=S[0:rows, :], func=AF.Copy)
            self.bar()

        self.dump("l0_Slow", self.S_low_bf[:, :])
        self.dump("l0_Sq0", self.S_q_bf[0][:, :])
        self.dump("l0_Sq1", self.S_q_bf[1][:, :])
        nc.vector.tensor_tensor(out=x_ap, in0=x_ap, in1=self.out_ps[:, :], op=ALU.add)
        self.bar()
        self.dump("l0_x", x_ap)

    def layer1(self, x_ap):
        nc = self.nc
        self.layernorm(x_ap)
        self.dump("l1_xn", self.xn[:, :])
        self.xn_transpose()

        for g in range(8):
            lhs = self.xnT[:, g * 128:(g + 1) * 128]
            self.mm(self.ws_ps[:, 0:256], lhs, self.wq1T[:, g * 256:(g + 1) * 256],
                    start=(g == 0), stop=False)
            self.mm(self.ws_ps[:, 256:512], lhs, self.wk1T[:, g * 256:(g + 1) * 256],
                    start=False, stop=(g == 7))
            self.mm(self.ws_ps[:, 512:768], lhs, self.wv1T[:, g * 256:(g + 1) * 256],
                    start=(g == 0), stop=(g == 7))
            self.mm(self.out_ps[:, 0:512], lhs, self.w1T[:, g * 1024:g * 1024 + 512],
                    start=(g == 0), stop=False)
            self.mm(self.out_ps[:, 512:1024], lhs,
                    self.w1T[:, g * 1024 + 512:(g + 1) * 1024],
                    start=(g == 0), stop=False)
        self.bar()

        e_qk = self.sq[:, 1024:1536]
        nc.vector.tensor_scalar(out=e_qk, in0=self.ws_ps[:, 0:512],
                                scalar1=0.0, scalar2=None, op0=ALU.min)
        self.bar()
        nc.scalar.activation(out=e_qk, in_=e_qk, func=AF.Exp)
        self.bar()
        nc.vector.scalar_tensor_tensor(out=self.p_qk[:, :], in0=self.ws_ps[:, 0:512],
                                       scalar=0.0, in1=e_qk,
                                       op0=ALU.max, op1=ALU.add)
        nc.scalar.activation(
            out=self.v1_aug[:, :].rearrange("p (h c) -> p h c", c=65)[:, :, 0:64],
            in_=self.ws_ps[:, 512:768].rearrange("p (h e) -> p h e", h=4),
            func=AF.Copy)
        self.bar()
        nc.scalar.activation(out=self.sq[:, 0:512], in_=self.p_qk[:, :], func=AF.Square)
        self.bar()
        nc.vector.tensor_reduce(
            out=self.nrm[:, :].rearrange("p (a b) -> p a b", b=1),
            in_=self.sq[:, 0:512].rearrange("p (a b) -> p a b", a=8),
            axis=AX.X, op=ALU.add)
        self.bar()
        nc.scalar.activation(out=self.rn[:, :], in_=self.nrm[:, :], func=AF.Ln)
        self.bar()
        nc.scalar.activation(out=self.rn[:, :], in_=self.rn[:, :], func=AF.Exp,
                             scale=self.consts[:, 3:4])
        self.bar()
        nc.vector.tensor_scalar(out=self.rn[:, :], in0=self.rn[:, :],
                                scalar1=float(1e-6), scalar2=None, op0=ALU.add)
        self.bar()
        nc.vector.reciprocal(self.rn[:, :], self.rn[:, :])
        self.bar()
        nc.vector.tensor_tensor(
            out=self.q1pad[:, :].rearrange("p (h c) -> p h c", c=128)[:, :, 0:64],
            in0=self.p_qk[:, 0:256].rearrange("p (a b) -> p a b", a=4),
            in1=self.rn[:, 0:4].unsqueeze(2).broadcast_to([128, 4, 64]), op=ALU.mult)
        nc.vector.tensor_tensor(
            out=self.k1pad[:, :].rearrange("p (h c) -> p h c", c=128)[:, :, 0:64],
            in0=self.p_qk[:, 256:512].rearrange("p (a b) -> p a b", a=4),
            in1=self.rn[:, 4:8].unsqueeze(2).broadcast_to([128, 4, 64]), op=ALU.mult)
        self.bar()
        self.dump("l1_pqk", self.p_qk[:, :])
        self.dump("l1_rn", self.rn[:, :])
        self.dump("l1_q1pad", self.q1pad[:, :])
        self.dump("l1_k1pad", self.k1pad[:, :])
        self.dump("l1_v1aug", self.v1_aug[:, :])

        for j in range(4):
            nc.tensor.transpose(self.tp_ps[:, j * 128:(j + 1) * 128],
                                self.q1pad[:, j * 128:(j + 1) * 128], self.idn_bf[:, :])
            nc.tensor.transpose(self.tp_ps[:, 512 + j * 128:512 + (j + 1) * 128],
                                self.k1pad[:, j * 128:(j + 1) * 128], self.idn_bf[:, :])
        self.bar()
        nc.scalar.activation(out=self.qk1T[:, :], in_=self.tp_ps[:, :], func=AF.Copy)
        self.bar()

        for h in range(4):
            lhs = self.qk1T[0:64, 512 + h * 128:512 + (h + 1) * 128]
            rhs = self.qk1T[0:64, h * 128:(h + 1) * 128]
            self.mm(self.sm_ps[:, h * 128:(h + 1) * 128], lhs, rhs,
                    start=(h == 0), stop=(h == 3))
        self.bar()
        nc.vector.tensor_tensor(
            out=self.A1T[:, :].rearrange("p (h t) -> p h t", h=4),
            in0=self.sm_ps[:, :].rearrange("p (h t) -> p h t", h=4),
            in1=self.maskT[:, :].unsqueeze(1).broadcast_to([128, 4, 128]), op=ALU.mult)
        self.bar()
        self.dump("l1_A1T", self.A1T[:, :])

        for h in range(4):
            yc = h * 128
            self.mm(self.sm_ps[:, yc:yc + 65], self.A1T[:, h * 128:(h + 1) * 128],
                    self.v1_aug[:, h * 65:(h + 1) * 65], start=(h == 0), stop=False)
            self.mm(self.sm_ps[:, yc:yc + 65],
                    self.qk1T[0:64, h * 128:(h + 1) * 128],
                    self.S1_bf[0:64, h * 65:(h + 1) * 65],
                    start=False, stop=(h == 3))
        self.bar()
        denv = self.sm_ps[:, :].rearrange("p (h c) -> p h c", c=128)[:, :, 64:65]
        nc.vector.tensor_scalar(out=self.dent4[:, :], in0=denv.squeeze(2),
                                scalar1=float(1e-6), scalar2=None, op0=ALU.add)
        self.bar()
        nc.vector.reciprocal(self.z4[:, :], self.dent4[:, :])
        self.bar()
        nc.vector.tensor_tensor(
            out=self.y1_bf[:, :].rearrange("p (h e) -> p h e", h=4),
            in0=self.sm_ps[:, :].rearrange("p (h c) -> p h c", c=128)[:, :, 0:64],
            in1=self.z4[:, :].unsqueeze(2).broadcast_to([128, 4, 64]), op=ALU.mult)
        self.bar()
        self.dump("l1_y1bf", self.y1_bf[:, :])

        for j in range(2):
            nc.tensor.transpose(self.tp_ps[:, j * 128:(j + 1) * 128],
                                self.y1_bf[:, j * 128:(j + 1) * 128], self.idn_bf[:, :])
        self.bar()
        nc.scalar.activation(out=self.y1T[:, :], in_=self.tp_ps[:, 0:256], func=AF.Copy)
        self.bar()
        for j in range(2):
            lhs = self.y1T[:, j * 128:(j + 1) * 128]
            self.mm(self.out_ps[:, 0:512], lhs, self.wo1T[:, j * 1024:j * 1024 + 512],
                    start=False, stop=(j == 1))
            self.mm(self.out_ps[:, 512:1024], lhs,
                    self.wo1T[:, j * 1024 + 512:(j + 1) * 1024],
                    start=False, stop=(j == 1))

        for h in range(4):
            self.mm(self.sm_ps[0:64, h * 65:(h + 1) * 65],
                    self.k1pad[:, h * 128:h * 128 + 64],
                    self.v1_aug[:, h * 65:(h + 1) * 65],
                    start=(h == 0), stop=(h == 3))
        self.bar()
        nc.vector.tensor_tensor(out=self.S1[:, :], in0=self.S1[:, :],
                                in1=self.sm_ps[0:64, 0:260], op=ALU.add)
        self.bar()
        nc.scalar.activation(out=self.S1_bf[:, :], in_=self.S1[:, :], func=AF.Copy)
        self.bar()
        self.dump("l1_S1", self.S1_bf[:, :])

        nc.vector.tensor_tensor(out=x_ap, in0=x_ap, in1=self.out_ps[:, :], op=ALU.add)
        self.bar()

    def build(self):
        nc = self.nc
        self.alloc()
        self.prep()

        for c in range(min(2, self.nchunks)):
            nc.sync.dma_start(self.x_sb[c % 2][:, :],
                              self.i_x[c * C:(c + 1) * C, :]).then_inc(self.sem_ld, 16)
            self.ld += 16

        for c in range(self.nchunks):
            self.cur_chunk = c
            buf = self.x_sb[c % 2][:, :]
            nc.vector.wait_ge(self.sem_ld, 16 * (c + 1))
            nc.scalar.wait_ge(self.sem_ld, 16 * (c + 1))
            self.bar()
            if 0 in self.layers:
                self.layer0(buf)
            if 1 in self.layers:
                self.layer1(buf)
            nc.vector.nop().then_inc(self.sem_rdy, 1)
            nc.sync.wait_ge(self.sem_rdy, c + 1)
            nc.sync.dma_start(self.o_out[c * C:(c + 1) * C, :], buf).then_inc(
                self.sem_st, 16)
            self.st += 16
            if c + 2 < self.nchunks:
                nc.sync.wait_ge(self.sem_st, 16 * (c + 1))
                nc.sync.dma_start(self.x_sb[c % 2][:, :],
                                  self.i_x[(c + 2) * C:(c + 3) * C, :]).then_inc(
                    self.sem_ld, 16)
                self.ld += 16

        nc.sync.wait_ge(self.sem_st, self.st)
        return nc


def build_program(nchunks=NCH, layers=(0, 1), dbg=False):
    return KB(nchunks=nchunks, layers=layers, dbg=dbg).build()


def make_in_map(inputs, b):
    f32 = np.float32
    return {
        "x": np.ascontiguousarray(np.asarray(inputs["L0_attn_hidden_states"])[b], dtype=f32),
        "wq0": np.ascontiguousarray(np.asarray(inputs["L0_attn_Wq"], dtype=f32)),
        "wk0": np.ascontiguousarray(np.asarray(inputs["L0_attn_Wk"], dtype=f32)),
        "wv0": np.ascontiguousarray(np.asarray(inputs["L0_attn_Wv"], dtype=f32)),
        "wo0": np.ascontiguousarray(np.asarray(inputs["L0_attn_Wo"], dtype=f32)),
        "ker": np.ascontiguousarray(np.asarray(inputs["L0_ffn_kernel"], dtype=f32)),
        "bias": np.ascontiguousarray(np.asarray(inputs["L0_ffn_bias"], dtype=f32)).reshape(1, 1024),
        "la": np.ascontiguousarray(np.asarray(inputs["L0_ffn_a"], dtype=f32)),
        "lb": np.ascontiguousarray(np.asarray(inputs["L0_ffn_b"], dtype=f32)),
        "wq1": np.ascontiguousarray(np.asarray(inputs["L1_attn_Wq"], dtype=f32)),
        "wk1": np.ascontiguousarray(np.asarray(inputs["L1_attn_Wk"], dtype=f32)),
        "wv1": np.ascontiguousarray(np.asarray(inputs["L1_attn_Wv"], dtype=f32)),
        "wo1": np.ascontiguousarray(np.asarray(inputs["L1_attn_Wo"], dtype=f32)),
        "w1": np.ascontiguousarray(np.asarray(inputs["L1_ffn_weight"], dtype=f32)),
        "idn": np.eye(128, dtype=f32),
        "maskT": np.triu(np.ones((128, 128), dtype=f32)),
    }


def kernel(**inputs):
    nc = build_program()
    in_maps = [make_in_map(inputs, b) for b in range(8)]
    res = run_bass_kernel_spmd(nc, in_maps, core_ids=list(range(8)))
    return np.stack([r["out"] for r in res.results], axis=0).astype(np.float32)


# version 02
# speedup vs baseline: 1.3252x; 1.3252x over previous
import contextlib
import numpy as np
import ml_dtypes

import concourse.bass as bass
import concourse.mybir as mybir
from concourse.bass_utils import run_bass_kernel_spmd

F32 = mybir.dt.float32
BF16 = mybir.dt.bfloat16
AF = mybir.ActivationFunctionType
ALU = mybir.AluOpType
AX = mybir.AxisListType
ET = mybir.EngineType

MAGIC = float(1.5 * 2 ** 23)
SQ5 = 0.7071067811865476
QS = 0.35355339059327373

T_SEQ = 4096
D = 1024
C = 128
NCH = T_SEQ // C
H0, F0, V0 = 16, 16, 64
H1, D1 = 4, 64

def colh(h):
    return 65 * (h % 7) + 512 * (h // 7)

S_W = 1536


class KB:

    def __init__(self, nchunks=NCH, layers=(0, 1), dbg=False):
        self.nchunks = nchunks
        self.layers = layers
        self.dbg = dbg
        self.cur_chunk = -1
        self.dbg_tensors = {}
        self.nc = bass.Bass("TRN2", target_bir_lowering=False)
        self.stack = contextlib.ExitStack()
        self.ld = 0
        self.wld = 0
        self.st = 0
        self.n = {"pe": 0, "dve": 0, "act": 0, "pool": 0}
        self.deps = {}
        self.last_wait = {}

    def bar(self, pool=False):
        self.nc.all_engine_barrier()

    def eng_handle(self, eng):
        nc = self.nc
        return {"pe": nc.tensor, "dve": nc.vector, "act": nc.scalar,
                "pool": nc.gpsimd}[eng]

    def reset_deps(self):
        self.deps = {}
        self.last_wait = {}

    def dep_wait(self, eng, reads=(), writes=()):
        h = self.eng_handle(eng)
        need = {}
        for b in reads:
            w = self.deps.get(b, {}).get("w")
            if w and w[0] != eng:
                need[w[0]] = max(need.get(w[0], 0), w[1])
        for b in writes:
            d = self.deps.get(b, {})
            w = d.get("w")
            if w and w[0] != eng:
                need[w[0]] = max(need.get(w[0], 0), w[1])
            for re_, rc in d.get("r", {}).items():
                if re_ != eng:
                    need[re_] = max(need.get(re_, 0), rc)
        for prod, cnt in sorted(need.items()):
            key = (eng, prod)
            if self.last_wait.get(key, 0) < cnt:
                h.wait_ge(self.sems[prod], cnt)
                self.last_wait[key] = cnt

    def dep_done(self, inst, eng, reads=(), writes=()):
        self.n[eng] += 1
        inst.then_inc(self.sems[eng], 1)
        c = self.n[eng]
        for b in writes:
            self.deps[b] = {"w": (eng, c), "r": {}}
        for b in reads:
            d = self.deps.setdefault(b, {"w": None, "r": {}})
            d["r"][eng] = c

    def g(self, eng, reads, writes, emit, drain=False):
        self.dep_wait(eng, reads, writes)
        if drain:
            self.eng_handle(eng).drain()
        last = emit()
        self.dep_done(last, eng, reads, writes)
        return self.n[eng]

    def alloc(self):
        nc = self.nc
        sb = nc.alloc_sbuf_tensor
        ps = nc.alloc_psum_tensor

        self.i_x = nc.dram_tensor("x", [T_SEQ, D], F32, kind="ExternalInput").ap()
        self.i_wq0 = nc.dram_tensor("wq0", [256, 1024], F32, kind="ExternalInput").ap()
        self.i_wk0 = nc.dram_tensor("wk0", [256, 1024], F32, kind="ExternalInput").ap()
        self.i_wv0 = nc.dram_tensor("wv0", [1024, 1024], F32, kind="ExternalInput").ap()
        self.i_wo0 = nc.dram_tensor("wo0", [1024, 1024], F32, kind="ExternalInput").ap()
        self.i_ker = nc.dram_tensor("ker", [1024, 1024], F32, kind="ExternalInput").ap()
        self.i_bias = nc.dram_tensor("bias", [1, 1024], F32, kind="ExternalInput").ap()
        self.i_a = nc.dram_tensor("la", [1024, 16], F32, kind="ExternalInput").ap()
        self.i_b = nc.dram_tensor("lb", [16, 1024], F32, kind="ExternalInput").ap()
        self.i_wq1 = nc.dram_tensor("wq1", [256, 1024], F32, kind="ExternalInput").ap()
        self.i_wk1 = nc.dram_tensor("wk1", [256, 1024], F32, kind="ExternalInput").ap()
        self.i_wv1 = nc.dram_tensor("wv1", [256, 1024], F32, kind="ExternalInput").ap()
        self.i_wo1 = nc.dram_tensor("wo1", [1024, 256], F32, kind="ExternalInput").ap()
        self.i_w1 = nc.dram_tensor("w1", [1024, 1024], F32, kind="ExternalInput").ap()
        self.i_idn = nc.dram_tensor("idn", [128, 128], F32, kind="ExternalInput").ap()
        self.i_mask = nc.dram_tensor("maskT", [128, 128], F32, kind="ExternalInput").ap()
        self.i_mask05 = nc.dram_tensor("mask05", [128, 128], BF16, kind="ExternalInput").ap()
        self.o_out = nc.dram_tensor("out", [T_SEQ, D], F32, kind="ExternalOutput").ap()

        self.wqT = sb("wqT", [128, 2048], BF16)
        self.wkT = sb("wkT", [128, 2048], BF16)
        self.wvT = sb("wvT", [128, 8192], BF16)
        self.woT = sb("woT", [128, 8192], BF16)
        self.kerq = sb("kerq", [128, 8192], BF16)
        self.a_sb = sb("a_sb", [128, 128], BF16)
        self.b_sb = sb("b_sb", [16, 1024], BF16)
        self.bias_bf = sb("bias_bf", [1, 1024], BF16)
        self.wq1T = sb("wq1T", [128, 2048], BF16)
        self.wk1T = sb("wk1T", [128, 2048], BF16)
        self.wv1T = sb("wv1T", [128, 2048], BF16)
        self.wo1T = sb("wo1T", [128, 2048], BF16)
        self.w1T = sb("w1T", [128, 8192], BF16)

        self.idn_f = sb("idn_f", [128, 128], F32)
        self.idn_bf = sb("idn_bf", [128, 128], BF16)
        self.maskT = sb("maskT_sb", [128, 128], F32)
        self.mask05 = sb("mask05_sb", [128, 128], BF16)
        self.ones_bf = sb("ones_bf", [1, 128], BF16)
        self.ones_f = sb("ones_f", [1, 128], F32)
        self.consts = sb("consts", [128, 8], F32)

        self.x_sb = [sb("x_sb0", [128, D], F32), sb("x_sb1", [128, D], F32)]
        self.sq = sb("sq_scr", [128, 2048], F32)
        self.xn = sb("xn_bf", [128, D], BF16)
        self.xnT = sb("xnT_bf", [128, D], BF16)
        self.qpad = sb("qpad", [128, 2048], BF16)
        self.kpad = sb("kpad", [128, 2048], BF16)
        self.q_s = sb("q_s", [128, 256], BF16)
        self.k_s = sb("k_s", [128, 256], BF16)
        self.qkT = sb("qkT", [128, 4096], BF16)
        self.v_aug = sb("v_aug", [128, H0 * 65], BF16)
        self.uT = sb("uT_bf", [16, 128], BF16)
        self.qf = sb("qf", [128, 4096], BF16)
        self.kf = sb("kf", [128, 4096], BF16)
        self.qfT = sb("qfT", [128, 4096], BF16)
        self.AT = sb("AT", [128, 2048], BF16)
        self.y_bf = sb("y_bf", [128, 1024], BF16)
        self.yT = sb("yT", [128, 1024], BF16)
        self.m_col = sb("m_col", [128, 1], F32)
        self.s2_col = sb("s2_col", [128, 1], F32)
        self.t_col = sb("t_col", [128, 1], F32)
        self.v_col = sb("v_col", [128, 1], F32)
        self.r_col = sb("r_col", [128, 1], F32)
        self.dent = sb("dent", [128, 16], F32)
        self.z16 = sb("z16", [128, 16], F32)

        self.S_low = sb("S_low", [17, S_W], F32)
        self.S_low_bf = sb("S_low_bf", [17, S_W], BF16)
        self.S_q = [sb("S_q0", [128, S_W], F32), sb("S_q1", [128, S_W], F32)]
        self.S_q_bf = [sb("S_q0_bf", [128, S_W], BF16), sb("S_q1_bf", [128, S_W], BF16)]

        self.p_qk = sb("p_qk", [128, 512], F32)
        self.nrm = sb("nrm", [128, 8], F32)
        self.rn = sb("rn", [128, 8], F32)
        self.q1pad = sb("q1pad", [128, 512], BF16)
        self.k1pad = sb("k1pad", [128, 512], BF16)
        self.qk1T = sb("qk1T", [128, 1024], BF16)
        self.v1_aug = sb("v1_aug", [128, H1 * 65], BF16)
        self.A1T = sb("A1T", [128, 512], BF16)
        self.y1_bf = sb("y1_bf", [128, 256], BF16)
        self.y1T = sb("y1T", [128, 256], BF16)
        self.dent4 = sb("dent4", [128, 4], F32)
        self.z4 = sb("z4", [128, 4], F32)
        self.S1 = sb("S1", [64, H1 * 65], F32)
        self.S1_bf = sb("S1_bf", [64, H1 * 65], BF16)

        self.stage = self.x_sb[0]
        self.stage_bf = self.y_bf
        self.amax8 = sb("amax8", [128, 8], F32)
        self.red_row = sb("red_row", [1, 128], F32)
        self.scal2 = sb("scal2", [1, 2], F32)
        self.rs_cols = sb("rs_cols", [128, 2], F32)

        self.tp_ps = ps("tp_ps", [128, 1024], BF16)
        self.out_ps = ps("out_ps", [128, 1024], F32)
        self.ws_ps = ps("ws_ps", [128, 2048], F32)
        self.sm_ps = ps("sm_ps", [128, 512], F32)

        self.sem_ld = self.stack.enter_context(nc.semaphore("s_ld"))
        self.sem_wld = self.stack.enter_context(nc.semaphore("s_wld"))
        self.sem_st = self.stack.enter_context(nc.semaphore("s_st"))
        self.sem_rdy = self.stack.enter_context(nc.semaphore("s_rdy"))
        self.sems = {
            "pe": self.stack.enter_context(nc.semaphore("s_pe")),
            "dve": self.stack.enter_context(nc.semaphore("s_dve")),
            "act": self.stack.enter_context(nc.semaphore("s_act")),
            "pool": self.stack.enter_context(nc.semaphore("s_pool")),
        }

    def dump(self, name, ap):
        if not self.dbg or self.cur_chunk > 0:
            return
        nc = self.nc
        if name not in self.dbg_tensors:
            sh = list(ap.shape)
            self.dbg_tensors[name] = nc.dram_tensor(
                "dbg_" + name, sh, F32, kind="ExternalOutput").ap()
        self.bar(pool=True)
        nc.gpsimd.dma_start(self.dbg_tensors[name], ap).then_inc(self.sem_wld, 16)
        self.wld += 16
        nc.gpsimd.wait_ge(self.sem_wld, self.wld)
        self.bar(pool=True)

    def wload(self, dst_ap, src_ap):
        nc = self.nc
        nc.sync.dma_start(dst_ap, src_ap).then_inc(self.sem_wld, 16)
        self.wld += 16
        nc.vector.wait_ge(self.sem_wld, self.wld)
        nc.scalar.wait_ge(self.sem_wld, self.wld)
        self.bar()

    def mm(self, out_ap, lhsT, rhs, start, stop, tile_position=None):
        return self.nc.tensor.matmul(out_ap, lhsT, rhs, start=start, stop=stop,
                                     tile_position=tile_position)

    def transpose_blocks(self, src_bf, blocks, dst_view):
        nc = self.nc
        for (sc, tc) in blocks:
            nc.tensor.transpose(self.tp_ps[:, tc:tc + 128],
                                src_bf[:, sc:sc + 128], self.idn_bf[:, :])
        self.bar()

    def prep_wT(self, w_ap, R, Dc, dst):
        nc = self.nc
        nblk = Dc // 128
        for g in range(R // 128):
            self.wload(self.stage[:, 0:Dc], w_ap[g * 128:(g + 1) * 128, :])
            nc.vector.tensor_copy(self.stage_bf[:, 0:Dc], self.stage[:, 0:Dc])
            self.bar()
            for j in range(nblk):
                nc.tensor.transpose(self.tp_ps[:, j * 128:(j + 1) * 128],
                                    self.stage_bf[:, j * 128:(j + 1) * 128],
                                    self.idn_bf[:, :])
            self.bar()
            src = self.tp_ps[:, 0:nblk * 128].rearrange("p (j r) -> p j r", j=nblk)
            dstv = dst[:, :].rearrange("p (j r) -> p j r", j=nblk)[:, :, g * 128:(g + 1) * 128]
            nc.scalar.activation(out=dstv, in_=src, func=AF.Copy)
            self.bar()

    def quant_scale_pass(self, w_ap, mode):
        nc = self.nc
        for g in range(8):
            self.wload(self.stage[:, :], w_ap[g * 128:(g + 1) * 128, :])
            op = ALU.max if mode == "int4" else ALU.add
            nc.vector.tensor_reduce(out=self.amax8[:, g:g + 1], in_=self.stage[:, :],
                                    axis=AX.X, op=op, apply_absolute_value=True)
            self.bar()
        op = ALU.max if mode == "int4" else ALU.add
        nc.vector.tensor_reduce(out=self.m_col[:, :], in_=self.amax8[:, :],
                                axis=AX.X, op=op)
        self.bar()
        nc.tensor.transpose(self.sm_ps[0:1, 0:128], self.m_col[:, :], self.idn_f[:, :])
        self.bar()
        nc.vector.tensor_copy(self.red_row[:, :], self.sm_ps[0:1, 0:128])
        self.bar()
        nc.vector.tensor_reduce(out=self.scal2[0:1, 0:1], in_=self.red_row[:, :],
                                axis=AX.X, op=op)
        self.bar()
        if mode == "int4":
            nc.vector.tensor_scalar(out=self.scal2[0:1, 0:1], in0=self.scal2[0:1, 0:1],
                                    scalar1=float(1.0 / 7.0), scalar2=None, op0=ALU.mult)
        else:
            nc.vector.tensor_scalar(out=self.scal2[0:1, 0:1], in0=self.scal2[0:1, 0:1],
                                    scalar1=float(1.0 / 1048576.0), scalar2=float(1e-5),
                                    op0=ALU.mult, op1=ALU.max)
        self.bar()
        self.bar()
        nc.vector.reciprocal(self.scal2[0:1, 1:2], self.scal2[0:1, 0:1])
        self.bar()
        self.mm(self.sm_ps[:, 0:2], self.ones_f[:, :], self.scal2[0:1, 0:2],
                start=True, stop=True)
        self.bar()
        nc.vector.tensor_copy(self.rs_cols[:, :], self.sm_ps[:, 0:2])
        self.bar()

    def quant_tile(self, dst_bf_ap, qmin, qmax):
        nc = self.nc
        s = self.stage[:, :]
        nc.vector.tensor_scalar(out=s, in0=s, scalar1=self.rs_cols[:, 1:2],
                                scalar2=None, op0=ALU.mult)
        self.bar()
        nc.vector.tensor_scalar(out=s, in0=s, scalar1=MAGIC, scalar2=MAGIC,
                                op0=ALU.add, op1=ALU.subtract)
        self.bar()
        nc.vector.tensor_scalar(out=s, in0=s, scalar1=float(qmin), scalar2=float(qmax),
                                op0=ALU.max, op1=ALU.min)
        self.bar()
        nc.vector.tensor_scalar(out=dst_bf_ap, in0=s, scalar1=self.rs_cols[:, 0:1],
                                scalar2=None, op0=ALU.mult)
        self.bar()

    def prep(self):
        nc = self.nc
        self.wload(self.idn_f[:, :], self.i_idn)
        self.wload(self.maskT[:, :], self.i_mask)
        self.wload(self.mask05[:, :], self.i_mask05)
        nc.vector.tensor_copy(self.idn_bf[:, :], self.idn_f[:, :])
        nc.vector.memset(self.ones_bf[:, :], 1.0)
        nc.vector.memset(self.ones_f[:, :], 1.0)
        for col, val in ((0, 1e-5), (1, -0.5), (2, SQ5), (3, 0.5), (4, QS),
                         (5, 1.0 / 1024.0)):
            nc.vector.memset(self.consts[:, col:col + 1], float(val))
        nc.vector.memset(self.ws_ps[:, :], 0.0)
        nc.vector.memset(self.out_ps[:, :], 0.0)
        nc.vector.memset(self.sm_ps[:, :], 0.0)
        nc.vector.memset(self.qpad[:, :], 0.0)
        nc.vector.memset(self.kpad[:, :], 0.0)
        nc.vector.memset(self.v_aug[:, :], 0.0)
        nc.vector.memset(self.v1_aug[:, :], 0.0)
        nc.vector.memset(self.q1pad[:, :], 0.0)
        nc.vector.memset(self.k1pad[:, :], 0.0)
        for t in (self.S_low, self.S_q[0], self.S_q[1], self.S1):
            nc.vector.memset(t[:, :], 0.0)
        for t in (self.S_low_bf, self.S_q_bf[0], self.S_q_bf[1], self.S1_bf):
            nc.vector.memset(t[:, :], 0.0)
        self.bar()
        nc.vector.memset(
            self.qpad[:, :].rearrange("p (h c) -> p h c", c=128)[:, :, 16:17], 1.0)
        nc.vector.memset(
            self.kpad[:, :].rearrange("p (h c) -> p h c", c=128)[:, :, 16:17], 1.0)
        nc.vector.memset(
            self.v_aug[:, :].rearrange("p (h c) -> p h c", c=65)[:, :, 64:65], 1.0)
        nc.vector.memset(
            self.v1_aug[:, :].rearrange("p (h c) -> p h c", c=65)[:, :, 64:65], 1.0)
        self.bar()

        self.prep_wT(self.i_wq0, 256, 1024, self.wqT)
        self.prep_wT(self.i_wk0, 256, 1024, self.wkT)
        self.prep_wT(self.i_wv0, 1024, 1024, self.wvT)
        self.prep_wT(self.i_wo0, 1024, 1024, self.woT)
        self.prep_wT(self.i_wq1, 256, 1024, self.wq1T)
        self.prep_wT(self.i_wk1, 256, 1024, self.wk1T)
        self.prep_wT(self.i_wv1, 256, 1024, self.wv1T)
        self.prep_wT(self.i_wo1, 1024, 256, self.wo1T)

        a_src = self.i_a.rearrange("(g p) r -> p g r", p=128)
        self.wload(self.stage[:, 0:128].rearrange("p (g r) -> p g r", g=8), a_src)
        nc.vector.tensor_copy(self.a_sb[:, :], self.stage[:, 0:128])
        self.wload(self.stage[0:16, 0:1024], self.i_b)
        nc.vector.tensor_copy(self.b_sb[:, :], self.stage[0:16, 0:1024])
        self.wload(self.stage[0:1, 0:1024], self.i_bias)
        nc.vector.tensor_copy(self.bias_bf[:, :], self.stage[0:1, 0:1024])
        self.bar()

        self.quant_scale_pass(self.i_ker, "int4")
        for g in range(8):
            self.wload(self.stage[:, :], self.i_ker[g * 128:(g + 1) * 128, :])
            self.quant_tile(self.kerq[:, g * 1024:(g + 1) * 1024], -8.0, 7.0)

        self.quant_scale_pass(self.i_w1, "tern")
        for g in range(8):
            self.wload(self.stage[:, :], self.i_w1[g * 128:(g + 1) * 128, :])
            self.quant_tile(self.stage_bf[:, :], -1.0, 1.0)
            for j in range(8):
                nc.tensor.transpose(self.tp_ps[:, j * 128:(j + 1) * 128],
                                    self.stage_bf[:, j * 128:(j + 1) * 128],
                                    self.idn_bf[:, :])
            self.bar()
            src = self.tp_ps[:, :].rearrange("p (j r) -> p j r", j=8)
            dstv = self.w1T[:, :].rearrange("p (j r) -> p j r", j=8)[:, :, g * 128:(g + 1) * 128]
            nc.scalar.activation(out=dstv, in_=src, func=AF.Copy)
            self.bar()
        self.cur_chunk = 0
        self.dump("kerq0", self.kerq[:, 0:1024])
        self.dump("wqT0", self.wqT[:, 0:256])
        self.dump("wvT0", self.wvT[:, 0:1024])
        self.dump("w1T0", self.w1T[:, 0:1024])
        self.dump("wo1T0", self.wo1T[:, 0:1024])
        self.dump("a_sb", self.a_sb[:, :])
        self.dump("b_sb", self.b_sb[:, :])
        self.dump("bias_bf", self.bias_bf[:, :])
        self.dump("idn_bf", self.idn_bf[:, :])
        self.cur_chunk = -1

    def layernorm(self, x_ap, xbuf):
        nc = self.nc
        C1024 = float(1.0 / 1024.0)
        self.g("dve", [xbuf], ["mcol"], lambda: nc.vector.tensor_reduce(
            out=self.m_col[:, :], in_=x_ap, axis=AX.X, op=ALU.add))
        self.g("act", [xbuf], ["sq", "s2col"], lambda: nc.scalar.activation(
            out=self.sq[:, 0:1024], in_=x_ap, func=AF.Square,
            accum_out=self.s2_col[:, :]))

        def _v0():
            nc.vector.scalar_tensor_tensor(
                out=self.v_col[:, :], in0=self.m_col[:, :], scalar=-C1024,
                in1=self.m_col[:, :], op0=ALU.mult, op1=ALU.mult)
            nc.vector.drain()
            return nc.vector.tensor_scalar(
                out=self.t_col[:, :], in0=self.m_col[:, :], scalar1=C1024,
                scalar2=None, op0=ALU.mult)
        self.g("dve", ["mcol"], ["mcol"], _v0, drain=True)
        self.g("dve", ["mcol", "s2col"], ["mcol"], lambda: nc.vector.tensor_scalar(
            out=self.v_col[:, :], in0=self.v_col[:, :],
            scalar1=self.s2_col[:, 0:1], scalar2=C1024,
            op0=ALU.add, op1=ALU.mult), drain=True)

        def _rstd():
            nc.scalar.activation(out=self.r_col[:, :], in_=self.v_col[:, :],
                                 func=AF.Ln, bias=self.consts[:, 0:1])
            nc.scalar.drain()
            return nc.scalar.activation(out=self.r_col[:, :], in_=self.r_col[:, :],
                                        func=AF.Exp, scale=self.consts[:, 1:2])
        self.g("act", ["mcol"], ["rcol"], _rstd)
        self.g("dve", [xbuf, "mcol", "rcol"], ["xn"], lambda: nc.vector.tensor_scalar(
            out=self.xn[:, :], in0=x_ap, scalar1=self.t_col[:, 0:1],
            scalar2=self.r_col[:, 0:1], op0=ALU.subtract, op1=ALU.mult),
            drain=True)

    def xn_transpose(self):
        nc = self.nc

        def _t():
            for j in range(8):
                last = nc.tensor.transpose(self.tp_ps[:, j * 128:(j + 1) * 128],
                                           self.xn[:, j * 128:(j + 1) * 128],
                                           self.idn_bf[:, :])
            return last
        self.g("pe", ["xn"], ["tp"], _t)
        self.g("act", ["tp"], ["xnT"], lambda: nc.scalar.activation(
            out=self.xnT[:, :], in_=self.tp_ps[:, :], func=AF.Copy))

    def layer0(self, x_ap, xbuf):
        nc = self.nc
        self.layernorm(x_ap, xbuf)
        self.dump("l0_xn", self.xn[:, :])
        self.xn_transpose()
        self.dump("l0_xnT", self.xnT[:, :])

        def _proj():
            for g_ in range(8):
                lhs = self.xnT[:, g_ * 128:(g_ + 1) * 128]
                self.mm(self.ws_ps[:, 0:256], lhs, self.wqT[:, g_ * 256:(g_ + 1) * 256],
                        start=(g_ == 0), stop=False)
                self.mm(self.ws_ps[:, 256:512], lhs, self.wkT[:, g_ * 256:(g_ + 1) * 256],
                        start=False, stop=(g_ == 7))
                self.mm(self.ws_ps[:, 512:1024], lhs,
                        self.wvT[:, g_ * 1024:g_ * 1024 + 512], start=(g_ == 0), stop=(g_ == 7))
                last = self.mm(self.ws_ps[:, 1024:1536], lhs,
                               self.wvT[:, g_ * 1024 + 512:(g_ + 1) * 1024],
                               start=(g_ == 0), stop=(g_ == 7))
            return last
        self.g("pe", ["xnT"], ["ws"], _proj)

        def _ffn():
            self.mm(self.out_ps[:, 0:512], self.ones_bf[:, :], self.bias_bf[:, 0:512],
                    start=True, stop=False)
            self.mm(self.out_ps[:, 512:1024], self.ones_bf[:, :], self.bias_bf[:, 512:1024],
                    start=True, stop=False)
            for g_ in range(8):
                lhs = self.xnT[:, g_ * 128:(g_ + 1) * 128]
                self.mm(self.out_ps[:, 0:512], lhs, self.kerq[:, g_ * 1024:g_ * 1024 + 512],
                        start=False, stop=False)
                last = self.mm(self.out_ps[:, 512:1024], lhs,
                               self.kerq[:, g_ * 1024 + 512:(g_ + 1) * 1024],
                               start=False, stop=False)
            return last
        self.g("pe", ["xnT"], ["out"], _ffn)

        def _ut():
            for g_ in range(8):
                last = self.mm(self.sm_ps[0:16, 0:128], self.a_sb[:, g_ * 16:(g_ + 1) * 16],
                               self.xnT[:, g_ * 128:(g_ + 1) * 128],
                               start=(g_ == 0), stop=(g_ == 7))
            return last
        self.g("pe", ["xnT"], ["sm"], _ut)

        qv = self.ws_ps[:, 0:256].rearrange("p (h f) -> p h f", h=16)
        kv = self.ws_ps[:, 256:512].rearrange("p (h f) -> p h f", h=16)
        qpadv = self.qpad[:, :].rearrange("p (h c) -> p h c", c=128)[:, :, 0:16]
        kpadv = self.kpad[:, :].rearrange("p (h c) -> p h c", c=128)[:, :, 0:16]

        def _evac():
            nc.scalar.activation(out=qpadv, in_=qv, func=AF.Copy, scale=self.consts[:, 3:4])
            nc.scalar.activation(out=kpadv, in_=kv, func=AF.Copy, scale=self.consts[:, 3:4])
            nc.scalar.activation(out=self.q_s[:, :].rearrange("p (h f) -> p h f", h=16),
                                 in_=qv, func=AF.Copy, scale=self.consts[:, 4:5])
            nc.scalar.activation(out=self.k_s[:, :].rearrange("p (h f) -> p h f", h=16),
                                 in_=kv, func=AF.Copy, scale=self.consts[:, 4:5])
            vaugv = self.v_aug[:, :].rearrange("p (h c) -> p h c", c=65)[:, :, 0:64]
            return nc.scalar.activation(
                out=vaugv,
                in_=self.ws_ps[:, 512:1536].rearrange("p (h e) -> p h e", h=16),
                func=AF.Copy)
        self.g("act", ["ws"], ["qpad", "kpad", "q_s", "k_s", "v_aug"], _evac)
        self.g("dve", ["sm"], ["uT"], lambda: nc.vector.tensor_copy(
            self.uT[:, :], self.sm_ps[0:16, 0:128]))
        self.dump("l0_qpad", self.qpad[:, :])
        self.dump("l0_q_s", self.q_s[:, :])
        self.dump("l0_v_aug", self.v_aug[:, :])
        self.dump("l0_uT", self.uT[:, :])

        def _lora():
            self.mm(self.out_ps[:, 0:512], self.uT[0:16, :], self.b_sb[:, 0:512],
                    start=False, stop=False)
            return self.mm(self.out_ps[:, 512:1024], self.uT[0:16, :], self.b_sb[:, 512:1024],
                           start=False, stop=False)
        self.g("pe", ["uT"], ["out"], _lora)

        qs_v = self.q_s[:, :].rearrange("p (h f) -> p h f", h=16)
        ks_v = self.k_s[:, :].rearrange("p (h f) -> p h f", h=16)
        qf_v = self.qf[:, :].rearrange("p (h i j) -> p h i j", h=16, i=16)
        kf_v = self.kf[:, :].rearrange("p (h i j) -> p h i j", h=16, i=16)
        self.g("dve", ["q_s", "qpad"], ["qf"], lambda: nc.vector.tensor_tensor(
            out=qf_v,
            in0=qs_v.unsqueeze(3).broadcast_to([128, 16, 16, 16]),
            in1=qpadv.unsqueeze(2).broadcast_to([128, 16, 16, 16]), op=ALU.mult))
        self.g("pool", ["k_s", "kpad"], ["kf"], lambda: nc.gpsimd.tensor_tensor(
            out=kf_v,
            in0=ks_v.unsqueeze(3).broadcast_to([128, 16, 16, 16]),
            in1=kpadv.unsqueeze(2).broadcast_to([128, 16, 16, 16]), op=ALU.mult))
        self.dump("l0_qf", self.qf[:, :])
        self.dump("l0_kf", self.kf[:, :])

        for r in range(8):
            if r < 2:
                srct, src_name = self.qpad, "qpad"
                sc0 = r * 1024
            elif r < 4:
                srct, src_name = self.kpad, "kpad"
                sc0 = (r - 2) * 1024
            else:
                srct, src_name = self.qf, "qf"
                sc0 = (r - 4) * 1024
            use_ws = (r % 2 == 1)
            tgt = self.ws_ps[:, 0:512].bitcast(BF16) if use_ws else self.tp_ps
            tname = "ws" if use_ws else "tp"
            if r < 4:
                dname, dst, dc0 = "qkT", self.qkT, r * 1024
            else:
                dname, dst, dc0 = "qfT", self.qfT, (r - 4) * 1024

            def _t(srct=srct, sc0=sc0, tgt=tgt):
                for j in range(8):
                    last = nc.tensor.transpose(tgt[:, j * 128:(j + 1) * 128],
                                               srct[:, sc0 + j * 128:sc0 + (j + 1) * 128],
                                               self.idn_bf[:, :])
                return last
            self.g("pe", [src_name], [tname], _t)
            eng = "act" if r % 2 == 0 else "dve"
            h = nc.scalar if eng == "act" else nc.vector

            def _e(h=h, dst=dst, dc0=dc0, tgt=tgt):
                return h.tensor_copy(dst[:, dc0:dc0 + 1024], tgt[:, 0:1024]) \
                    if h is nc.vector else \
                    h.activation(out=dst[:, dc0:dc0 + 1024], in_=tgt[:, 0:1024],
                                 func=AF.Copy)
            self.g(eng, [tname], [dname], _e)
        self.dump("l0_qkT", self.qkT[:, :])
        self.dump("l0_qfT", self.qfT[:, :])

        def _p():
            for h in range(16):
                lhs = self.qkT[0:16, 2048 + h * 128:2048 + (h + 1) * 128]
                rhs = self.qkT[0:16, h * 128:(h + 1) * 128]
                last = self.mm(self.ws_ps[:, h * 128:(h + 1) * 128], lhs, rhs,
                               start=(h % 4 == 0), stop=(h % 4 == 3))
            return last
        self.g("pe", ["qkT"], ["ws"], _p)

        self.g("act", ["ws"], ["sq"], lambda: nc.scalar.activation(
            out=self.sq[:, 0:2048], in_=self.ws_ps[:, :], func=AF.Square,
            scale=self.consts[:, 2:3], bias=self.consts[:, 2:3]))
        self.g("pool", ["sq"], ["AT"], lambda: nc.gpsimd.tensor_tensor(
            out=self.AT[:, :].rearrange("p (h t) -> p h t", h=16),
            in0=self.sq[:, 0:2048].rearrange("p (h t) -> p h t", h=16),
            in1=self.maskT[:, :].unsqueeze(1).broadcast_to([128, 16, 128]),
            op=ALU.mult))
        self.dump("l0_AT", self.AT[:, :])

        def _y():
            for h in range(16):
                yc = h * 128
                self.mm(self.ws_ps[:, yc:yc + 65],
                        self.AT[:, h * 128:(h + 1) * 128],
                        self.v_aug[:, h * 65:(h + 1) * 65],
                        start=(h % 4 == 0), stop=False)
                self.mm(self.ws_ps[:, yc:yc + 65],
                        self.mask05[:, :],
                        self.v_aug[:, h * 65:(h + 1) * 65],
                        start=False, stop=False)
                self.mm(self.ws_ps[:, yc:yc + 65],
                        self.qkT[0:17, h * 128:(h + 1) * 128],
                        self.S_low_bf[0:17, colh(h):colh(h) + 65],
                        start=False, stop=False)
                for j in range(2):
                    last = self.mm(self.ws_ps[:, yc:yc + 65],
                                   self.qfT[:, (2 * h + j) * 128:(2 * h + j + 1) * 128],
                                   self.S_q_bf[j][:, colh(h):colh(h) + 65],
                                   start=False, stop=(h % 4 == 3 and j == 1))
            return last
        self.g("pe", ["AT", "v_aug", "qkT", "qfT", "Slowbf", "Sq0bf", "Sq1bf"],
               ["ws"], _y)

        denv = self.ws_ps[:, :].rearrange("p (h c) -> p h c", c=128)[:, :, 64:65]

        def _z():
            nc.vector.tensor_scalar(out=self.dent[:, :], in0=denv.squeeze(2),
                                    scalar1=float(1e-12), scalar2=None, op0=ALU.add)
            nc.vector.drain()
            return nc.vector.reciprocal(self.z16[:, :], self.dent[:, :])
        self.g("dve", ["ws"], ["z16"], _z)
        self.g("dve", ["ws", "z16"], ["y_bf"], lambda: nc.vector.tensor_tensor(
            out=self.y_bf[:, :].rearrange("p (h e) -> p h e", h=16),
            in0=self.ws_ps[:, :].rearrange("p (h c) -> p h c", c=128)[:, :, 0:64],
            in1=self.z16[:, :].unsqueeze(2).broadcast_to([128, 16, 64]), op=ALU.mult),
            drain=True)
        self.dump("l0_dent", self.dent[:, :])
        self.dump("l0_ybf", self.y_bf[:, :])

        def _yt():
            for j in range(8):
                last = nc.tensor.transpose(self.tp_ps[:, j * 128:(j + 1) * 128],
                                           self.y_bf[:, j * 128:(j + 1) * 128],
                                           self.idn_bf[:, :])
            return last
        self.g("pe", ["y_bf"], ["tp"], _yt)
        self.g("act", ["tp"], ["yT"], lambda: nc.scalar.activation(
            out=self.yT[:, :], in_=self.tp_ps[:, :], func=AF.Copy))

        def _wo():
            for j in range(8):
                lhs = self.yT[:, j * 128:(j + 1) * 128]
                self.mm(self.out_ps[:, 0:512], lhs, self.woT[:, j * 1024:j * 1024 + 512],
                        start=False, stop=(j == 7))
                last = self.mm(self.out_ps[:, 512:1024], lhs,
                               self.woT[:, j * 1024 + 512:(j + 1) * 1024],
                               start=False, stop=(j == 7))
            return last
        self.g("pe", ["yT"], ["out"], _wo)

        self.g("dve", ["out", xbuf], [xbuf], lambda: nc.vector.tensor_tensor(
            out=x_ap, in0=x_ap, in1=self.out_ps[:, :], op=ALU.add))
        self.dump("l0_x", x_ap)

        for rnd in range(3):
            def _ds(rnd=rnd):
                for h in range(16):
                    c0 = colh(h)
                    if rnd == 0:
                        lhs = self.kpad[:, 128 * h:128 * h + 17]
                        out = self.ws_ps[0:17, c0:c0 + 65]
                    else:
                        j = rnd - 1
                        lhs = self.kf[:, (2 * h + j) * 128:(2 * h + j + 1) * 128]
                        out = self.ws_ps[:, c0:c0 + 65]
                    last = self.mm(out, lhs, self.v_aug[:, h * 65:(h + 1) * 65],
                                   start=h in (0, 7, 14), stop=h in (6, 13, 15))
                return last
            src_name = "kpad" if rnd == 0 else "kf"
            self.g("pe", [src_name, "v_aug"], ["ws"], _ds)
            S = self.S_low if rnd == 0 else self.S_q[rnd - 1]
            Sbf = self.S_low_bf if rnd == 0 else self.S_q_bf[rnd - 1]
            sname = ["Slow", "Sq0", "Sq1"][rnd]
            rows = 17 if rnd == 0 else 128
            self.g("dve", ["ws", sname], [sname], lambda S=S, rows=rows:
                   nc.vector.tensor_tensor(out=S[0:rows, :], in0=S[0:rows, :],
                                           in1=self.ws_ps[0:rows, 0:S_W], op=ALU.add))
            self.g("act", [sname], [sname + "bf"], lambda Sbf=Sbf, S=S, rows=rows:
                   nc.scalar.activation(out=Sbf[0:rows, :], in_=S[0:rows, :],
                                        func=AF.Copy))
        self.dump("l0_Slow", self.S_low_bf[:, :])
        self.dump("l0_Sq0", self.S_q_bf[0][:, :])
        self.dump("l0_Sq1", self.S_q_bf[1][:, :])

    def layer1(self, x_ap, xbuf):
        nc = self.nc
        self.layernorm(x_ap, xbuf)
        self.dump("l1_xn", self.xn[:, :])
        self.xn_transpose()

        def _proj():
            for g_ in range(8):
                lhs = self.xnT[:, g_ * 128:(g_ + 1) * 128]
                self.mm(self.ws_ps[:, 0:256], lhs, self.wq1T[:, g_ * 256:(g_ + 1) * 256],
                        start=(g_ == 0), stop=False)
                self.mm(self.ws_ps[:, 256:512], lhs, self.wk1T[:, g_ * 256:(g_ + 1) * 256],
                        start=False, stop=(g_ == 7))
                last = self.mm(self.ws_ps[:, 512:768], lhs,
                               self.wv1T[:, g_ * 256:(g_ + 1) * 256],
                               start=(g_ == 0), stop=(g_ == 7))
            return last
        self.g("pe", ["xnT"], ["ws"], _proj)

        def _ffn1():
            for g_ in range(8):
                lhs = self.xnT[:, g_ * 128:(g_ + 1) * 128]
                self.mm(self.out_ps[:, 0:512], lhs, self.w1T[:, g_ * 1024:g_ * 1024 + 512],
                        start=(g_ == 0), stop=False)
                last = self.mm(self.out_ps[:, 512:1024], lhs,
                               self.w1T[:, g_ * 1024 + 512:(g_ + 1) * 1024],
                               start=(g_ == 0), stop=False)
            return last
        self.g("pe", ["xnT"], ["out"], _ffn1)

        e_qk = self.sq[:, 1024:1536]
        self.g("dve", ["ws"], ["sq"], lambda: nc.vector.tensor_scalar(
            out=e_qk, in0=self.ws_ps[:, 0:512], scalar1=0.0, scalar2=None,
            op0=ALU.min))
        self.g("act", ["sq"], ["sq"], lambda: nc.scalar.activation(
            out=e_qk, in_=e_qk, func=AF.Exp))
        self.g("dve", ["ws", "sq"], ["pqk"], lambda: nc.vector.scalar_tensor_tensor(
            out=self.p_qk[:, :], in0=self.ws_ps[:, 0:512], scalar=0.0, in1=e_qk,
            op0=ALU.max, op1=ALU.add))
        self.g("act", ["ws"], ["v1aug"], lambda: nc.scalar.activation(
            out=self.v1_aug[:, :].rearrange("p (h c) -> p h c", c=65)[:, :, 0:64],
            in_=self.ws_ps[:, 512:768].rearrange("p (h e) -> p h e", h=4),
            func=AF.Copy))
        self.g("act", ["pqk"], ["sq"], lambda: nc.scalar.activation(
            out=self.sq[:, 0:512], in_=self.p_qk[:, :], func=AF.Square))
        self.g("dve", ["sq"], ["nrm"], lambda: nc.vector.tensor_reduce(
            out=self.nrm[:, :].rearrange("p (a b) -> p a b", b=1),
            in_=self.sq[:, 0:512].rearrange("p (a b) -> p a b", a=8),
            axis=AX.X, op=ALU.add))

        def _sqrt():
            nc.scalar.activation(out=self.rn[:, :], in_=self.nrm[:, :], func=AF.Ln)
            nc.scalar.drain()
            return nc.scalar.activation(out=self.rn[:, :], in_=self.rn[:, :],
                                        func=AF.Exp, scale=self.consts[:, 3:4])
        self.g("act", ["nrm"], ["rn"], _sqrt)

        def _rn():
            nc.vector.tensor_scalar(out=self.rn[:, :], in0=self.rn[:, :],
                                    scalar1=float(1e-6), scalar2=None, op0=ALU.add)
            nc.vector.drain()
            return nc.vector.reciprocal(self.rn[:, :], self.rn[:, :])
        self.g("dve", ["rn"], ["rn"], _rn)

        def _norm():
            nc.vector.tensor_tensor(
                out=self.q1pad[:, :].rearrange("p (h c) -> p h c", c=128)[:, :, 0:64],
                in0=self.p_qk[:, 0:256].rearrange("p (a b) -> p a b", a=4),
                in1=self.rn[:, 0:4].unsqueeze(2).broadcast_to([128, 4, 64]), op=ALU.mult)
            return nc.vector.tensor_tensor(
                out=self.k1pad[:, :].rearrange("p (h c) -> p h c", c=128)[:, :, 0:64],
                in0=self.p_qk[:, 256:512].rearrange("p (a b) -> p a b", a=4),
                in1=self.rn[:, 4:8].unsqueeze(2).broadcast_to([128, 4, 64]), op=ALU.mult)
        self.g("dve", ["pqk", "rn"], ["q1pad", "k1pad"], _norm, drain=True)
        self.dump("l1_pqk", self.p_qk[:, :])
        self.dump("l1_rn", self.rn[:, :])
        self.dump("l1_q1pad", self.q1pad[:, :])
        self.dump("l1_k1pad", self.k1pad[:, :])
        self.dump("l1_v1aug", self.v1_aug[:, :])

        def _t():
            for j in range(4):
                nc.tensor.transpose(self.tp_ps[:, j * 128:(j + 1) * 128],
                                    self.q1pad[:, j * 128:(j + 1) * 128], self.idn_bf[:, :])
                last = nc.tensor.transpose(self.tp_ps[:, 512 + j * 128:512 + (j + 1) * 128],
                                           self.k1pad[:, j * 128:(j + 1) * 128],
                                           self.idn_bf[:, :])
            return last
        self.g("pe", ["q1pad", "k1pad"], ["tp"], _t)
        self.g("act", ["tp"], ["qk1T"], lambda: nc.scalar.activation(
            out=self.qk1T[:, :], in_=self.tp_ps[:, :], func=AF.Copy))

        def _p1():
            for h in range(4):
                lhs = self.qk1T[0:64, 512 + h * 128:512 + (h + 1) * 128]
                rhs = self.qk1T[0:64, h * 128:(h + 1) * 128]
                last = self.mm(self.sm_ps[:, h * 128:(h + 1) * 128], lhs, rhs,
                               start=(h == 0), stop=(h == 3))
            return last
        self.g("pe", ["qk1T"], ["sm"], _p1)
        self.g("dve", ["sm"], ["A1T"], lambda: nc.vector.tensor_tensor(
            out=self.A1T[:, :].rearrange("p (h t) -> p h t", h=4),
            in0=self.sm_ps[:, :].rearrange("p (h t) -> p h t", h=4),
            in1=self.maskT[:, :].unsqueeze(1).broadcast_to([128, 4, 128]), op=ALU.mult))
        self.dump("l1_A1T", self.A1T[:, :])

        def _y1():
            for h in range(4):
                yc = h * 128
                self.mm(self.sm_ps[:, yc:yc + 65], self.A1T[:, h * 128:(h + 1) * 128],
                        self.v1_aug[:, h * 65:(h + 1) * 65], start=(h == 0), stop=False)
                last = self.mm(self.sm_ps[:, yc:yc + 65],
                               self.qk1T[0:64, h * 128:(h + 1) * 128],
                               self.S1_bf[0:64, h * 65:(h + 1) * 65],
                               start=False, stop=(h == 3))
            return last
        self.g("pe", ["A1T", "v1aug", "qk1T", "S1bf"], ["sm"], _y1)

        denv = self.sm_ps[:, :].rearrange("p (h c) -> p h c", c=128)[:, :, 64:65]

        def _z1():
            nc.vector.tensor_scalar(out=self.dent4[:, :], in0=denv.squeeze(2),
                                    scalar1=float(1e-6), scalar2=None, op0=ALU.add)
            nc.vector.drain()
            return nc.vector.reciprocal(self.z4[:, :], self.dent4[:, :])
        self.g("dve", ["sm"], ["z4"], _z1)
        self.g("dve", ["sm", "z4"], ["y1"], lambda: nc.vector.tensor_tensor(
            out=self.y1_bf[:, :].rearrange("p (h e) -> p h e", h=4),
            in0=self.sm_ps[:, :].rearrange("p (h c) -> p h c", c=128)[:, :, 0:64],
            in1=self.z4[:, :].unsqueeze(2).broadcast_to([128, 4, 64]), op=ALU.mult),
            drain=True)
        self.dump("l1_y1bf", self.y1_bf[:, :])

        def _y1t():
            for j in range(2):
                last = nc.tensor.transpose(self.tp_ps[:, j * 128:(j + 1) * 128],
                                           self.y1_bf[:, j * 128:(j + 1) * 128],
                                           self.idn_bf[:, :])
            return last
        self.g("pe", ["y1"], ["tp"], _y1t)
        self.g("act", ["tp"], ["y1T"], lambda: nc.scalar.activation(
            out=self.y1T[:, :], in_=self.tp_ps[:, 0:256], func=AF.Copy))

        def _wo1():
            for j in range(2):
                lhs = self.y1T[:, j * 128:(j + 1) * 128]
                self.mm(self.out_ps[:, 0:512], lhs, self.wo1T[:, j * 1024:j * 1024 + 512],
                        start=False, stop=(j == 1))
                last = self.mm(self.out_ps[:, 512:1024], lhs,
                               self.wo1T[:, j * 1024 + 512:(j + 1) * 1024],
                               start=False, stop=(j == 1))
            return last
        self.g("pe", ["y1T"], ["out"], _wo1)

        self.g("dve", ["out", xbuf], [xbuf], lambda: nc.vector.tensor_tensor(
            out=x_ap, in0=x_ap, in1=self.out_ps[:, :], op=ALU.add))
        self.resid_mark = self.n["dve"]

        def _ds1():
            for h in range(4):
                last = self.mm(self.sm_ps[0:64, h * 65:(h + 1) * 65],
                               self.k1pad[:, h * 128:h * 128 + 64],
                               self.v1_aug[:, h * 65:(h + 1) * 65],
                               start=(h == 0), stop=(h == 3))
            return last
        self.g("pe", ["k1pad", "v1aug"], ["sm"], _ds1)
        self.g("dve", ["sm", "S1"], ["S1"], lambda: nc.vector.tensor_tensor(
            out=self.S1[:, :], in0=self.S1[:, :], in1=self.sm_ps[0:64, 0:260],
            op=ALU.add))
        self.g("act", ["S1"], ["S1bf"], lambda: nc.scalar.activation(
            out=self.S1_bf[:, :], in_=self.S1[:, :], func=AF.Copy))
        self.dump("l1_S1", self.S1_bf[:, :])

    def build(self):
        nc = self.nc
        self.alloc()
        self.prep()
        self.bar()
        self.reset_deps()

        for c in range(min(2, self.nchunks)):
            nc.sync.dma_start(self.x_sb[c % 2][:, :],
                              self.i_x[c * C:(c + 1) * C, :]).then_inc(self.sem_ld, 16)
            self.ld += 16

        for c in range(self.nchunks):
            self.cur_chunk = c
            xbuf = "x%d" % (c % 2)
            buf = self.x_sb[c % 2][:, :]
            nc.vector.wait_ge(self.sem_ld, 16 * (c + 1))
            nc.scalar.wait_ge(self.sem_ld, 16 * (c + 1))
            if 0 in self.layers:
                self.layer0(buf, xbuf)
            if 1 in self.layers:
                self.layer1(buf, xbuf)
            else:
                self.resid_mark = self.n["dve"]
            nc.sync.wait_ge(self.sems["dve"], self.resid_mark)
            nc.sync.dma_start(self.o_out[c * C:(c + 1) * C, :], buf).then_inc(
                self.sem_st, 16)
            self.st += 16
            if c + 2 < self.nchunks:
                nc.sync.wait_ge(self.sem_st, 16 * (c + 1))
                nc.sync.dma_start(self.x_sb[c % 2][:, :],
                                  self.i_x[(c + 2) * C:(c + 3) * C, :]).then_inc(
                    self.sem_ld, 16)
                self.ld += 16

        nc.sync.wait_ge(self.sem_st, self.st)
        return nc


def build_program(nchunks=NCH, layers=(0, 1), dbg=False):
    return KB(nchunks=nchunks, layers=layers, dbg=dbg).build()


def make_in_map(inputs, b):
    f32 = np.float32
    return {
        "x": np.ascontiguousarray(np.asarray(inputs["L0_attn_hidden_states"])[b], dtype=f32),
        "wq0": np.ascontiguousarray(np.asarray(inputs["L0_attn_Wq"], dtype=f32)),
        "wk0": np.ascontiguousarray(np.asarray(inputs["L0_attn_Wk"], dtype=f32)),
        "wv0": np.ascontiguousarray(np.asarray(inputs["L0_attn_Wv"], dtype=f32)),
        "wo0": np.ascontiguousarray(np.asarray(inputs["L0_attn_Wo"], dtype=f32)),
        "ker": np.ascontiguousarray(np.asarray(inputs["L0_ffn_kernel"], dtype=f32)),
        "bias": np.ascontiguousarray(np.asarray(inputs["L0_ffn_bias"], dtype=f32)).reshape(1, 1024),
        "la": np.ascontiguousarray(np.asarray(inputs["L0_ffn_a"], dtype=f32)),
        "lb": np.ascontiguousarray(np.asarray(inputs["L0_ffn_b"], dtype=f32)),
        "wq1": np.ascontiguousarray(np.asarray(inputs["L1_attn_Wq"], dtype=f32)),
        "wk1": np.ascontiguousarray(np.asarray(inputs["L1_attn_Wk"], dtype=f32)),
        "wv1": np.ascontiguousarray(np.asarray(inputs["L1_attn_Wv"], dtype=f32)),
        "wo1": np.ascontiguousarray(np.asarray(inputs["L1_attn_Wo"], dtype=f32)),
        "w1": np.ascontiguousarray(np.asarray(inputs["L1_ffn_weight"], dtype=f32)),
        "idn": np.eye(128, dtype=f32),
        "maskT": np.triu(np.ones((128, 128), dtype=f32)),
        "mask05": np.triu(np.full((128, 128), 0.5, dtype=ml_dtypes.bfloat16)),
    }


def kernel(**inputs):
    nc = build_program()
    in_maps = [make_in_map(inputs, b) for b in range(8)]
    res = run_bass_kernel_spmd(nc, in_maps, core_ids=list(range(8)))
    return np.stack([r["out"] for r in res.results], axis=0).astype(np.float32)


# version 04
# speedup vs baseline: 1.4178x; 1.0698x over previous
import contextlib
import numpy as np
import ml_dtypes

import concourse.bass as bass
import concourse.mybir as mybir
from concourse.bass_utils import run_bass_kernel_spmd

F32 = mybir.dt.float32
BF16 = mybir.dt.bfloat16
AF = mybir.ActivationFunctionType
ALU = mybir.AluOpType
AX = mybir.AxisListType
ET = mybir.EngineType

MAGIC = float(1.5 * 2 ** 23)
SQ5 = 0.7071067811865476
QS = 0.35355339059327373

T_SEQ = 4096
D = 1024
C = 128
NCH = T_SEQ // C
H0, F0, V0 = 16, 16, 64
H1, D1 = 4, 64

def colh(h):
    return 65 * (h % 7) + 512 * (h // 7)

S_W = 1536


class KB:

    def __init__(self, nchunks=NCH, layers=(0, 1), dbg=False):
        self.nchunks = nchunks
        self.layers = layers
        self.dbg = dbg
        self.cur_chunk = -1
        self.dbg_tensors = {}
        self.nc = bass.Bass("TRN2", target_bir_lowering=False)
        self.stack = contextlib.ExitStack()
        self.ld = 0
        self.wld = 0
        self.st = 0
        self.n = {"pe": 0, "dve": 0, "act": 0, "pool": 0}
        self.deps = {}
        self.last_wait = {}

    def bar(self, pool=False):
        self.nc.all_engine_barrier()

    def eng_handle(self, eng):
        nc = self.nc
        return {"pe": nc.tensor, "dve": nc.vector, "act": nc.scalar,
                "pool": nc.gpsimd}[eng]

    def reset_deps(self):
        self.deps = {}
        self.last_wait = {}

    def dep_wait(self, eng, reads=(), writes=()):
        h = self.eng_handle(eng)
        need = {}
        for b in reads:
            w = self.deps.get(b, {}).get("w")
            if w and w[0] != eng:
                need[w[0]] = max(need.get(w[0], 0), w[1])
        for b in writes:
            d = self.deps.get(b, {})
            w = d.get("w")
            if w and w[0] != eng:
                need[w[0]] = max(need.get(w[0], 0), w[1])
            for re_, rc in d.get("r", {}).items():
                if re_ != eng:
                    need[re_] = max(need.get(re_, 0), rc)
        for prod, cnt in sorted(need.items()):
            key = (eng, prod)
            if self.last_wait.get(key, 0) < cnt:
                h.wait_ge(self.sems[prod], cnt)
                self.last_wait[key] = cnt

    def dep_done(self, inst, eng, reads=(), writes=()):
        self.n[eng] += 1
        inst.then_inc(self.sems[eng], 1)
        c = self.n[eng]
        for b in writes:
            self.deps[b] = {"w": (eng, c), "r": {}}
        for b in reads:
            d = self.deps.setdefault(b, {"w": None, "r": {}})
            d["r"][eng] = c

    def g(self, eng, reads, writes, emit, drain=False):
        self.dep_wait(eng, reads, writes)
        if drain:
            self.eng_handle(eng).drain()
        last = emit()
        self.dep_done(last, eng, reads, writes)
        return self.n[eng]

    def alloc(self):
        nc = self.nc
        sb = nc.alloc_sbuf_tensor
        ps = nc.alloc_psum_tensor

        self.i_x = nc.dram_tensor("x", [T_SEQ, D], F32, kind="ExternalInput").ap()
        self.i_wq0 = nc.dram_tensor("wq0", [256, 1024], F32, kind="ExternalInput").ap()
        self.i_wk0 = nc.dram_tensor("wk0", [256, 1024], F32, kind="ExternalInput").ap()
        self.i_wv0 = nc.dram_tensor("wv0", [1024, 1024], F32, kind="ExternalInput").ap()
        self.i_wo0 = nc.dram_tensor("wo0", [1024, 1024], F32, kind="ExternalInput").ap()
        self.i_ker = nc.dram_tensor("ker", [1024, 1024], F32, kind="ExternalInput").ap()
        self.i_bias = nc.dram_tensor("bias", [1, 1024], F32, kind="ExternalInput").ap()
        self.i_a = nc.dram_tensor("la", [1024, 16], F32, kind="ExternalInput").ap()
        self.i_b = nc.dram_tensor("lb", [16, 1024], F32, kind="ExternalInput").ap()
        self.i_wq1 = nc.dram_tensor("wq1", [256, 1024], F32, kind="ExternalInput").ap()
        self.i_wk1 = nc.dram_tensor("wk1", [256, 1024], F32, kind="ExternalInput").ap()
        self.i_wv1 = nc.dram_tensor("wv1", [256, 1024], F32, kind="ExternalInput").ap()
        self.i_wo1 = nc.dram_tensor("wo1", [1024, 256], F32, kind="ExternalInput").ap()
        self.i_w1 = nc.dram_tensor("w1", [1024, 1024], F32, kind="ExternalInput").ap()
        self.i_idn = nc.dram_tensor("idn", [128, 128], F32, kind="ExternalInput").ap()
        self.i_mask = nc.dram_tensor("maskT", [128, 128], F32, kind="ExternalInput").ap()
        self.i_mask05 = nc.dram_tensor("mask05", [128, 128], BF16, kind="ExternalInput").ap()
        self.o_out = nc.dram_tensor("out", [T_SEQ, D], F32, kind="ExternalOutput").ap()

        self.wqT = sb("wqT", [128, 2048], BF16)
        self.wkT = sb("wkT", [128, 2048], BF16)
        self.wvT = sb("wvT", [128, 8192], BF16)
        self.woT = sb("woT", [128, 8192], BF16)
        self.kerq = sb("kerq", [128, 8192], BF16)
        self.a_sb = sb("a_sb", [128, 128], BF16)
        self.b_sb = sb("b_sb", [16, 1024], BF16)
        self.bias_bf = sb("bias_bf", [1, 1024], BF16)
        self.wq1T = sb("wq1T", [128, 2048], BF16)
        self.wk1T = sb("wk1T", [128, 2048], BF16)
        self.wv1T = sb("wv1T", [128, 2048], BF16)
        self.wo1T = sb("wo1T", [128, 2048], BF16)
        self.w1T = sb("w1T", [128, 8192], BF16)

        self.idn_f = sb("idn_f", [128, 128], F32)
        self.idn_bf = sb("idn_bf", [128, 128], BF16)
        self.maskT = sb("maskT_sb", [128, 128], F32)
        self.mask05 = sb("mask05_sb", [128, 128], BF16)
        self.ones_bf = sb("ones_bf", [1, 128], BF16)
        self.ones_f = sb("ones_f", [1, 128], F32)
        self.consts = sb("consts", [128, 8], F32)

        self.x_sb = [sb("x_sb0", [128, D], F32), sb("x_sb1", [128, D], F32)]
        self.sq = sb("sq_scr", [128, 2048], F32)
        self.xn = sb("xn_bf", [128, D], BF16)
        self.xnT = sb("xnT_bf", [128, D], BF16)
        self.qpad = sb("qpad", [128, 2048], BF16)
        self.kpad = sb("kpad", [128, 2048], BF16)
        self.q_s = sb("q_s", [128, 256], BF16)
        self.k_s = sb("k_s", [128, 256], BF16)
        self.qkT = sb("qkT", [128, 4096], BF16)
        self.v_aug = sb("v_aug", [128, H0 * 65], BF16)
        self.uT = sb("uT_bf", [16, 128], BF16)
        self.qf = sb("qf", [128, 4096], BF16)
        self.kf = sb("kf", [128, 4096], BF16)
        self.qfT = sb("qfT", [128, 4096], BF16)
        self.AT = sb("AT", [128, 2048], BF16)
        self.y_bf = sb("y_bf", [128, 1024], BF16)
        self.yT = sb("yT", [128, 1024], BF16)
        self.m_col = sb("m_col", [128, 1], F32)
        self.s2_col = sb("s2_col", [128, 1], F32)
        self.t_col = sb("t_col", [128, 1], F32)
        self.v_col = sb("v_col", [128, 1], F32)
        self.r_col = sb("r_col", [128, 1], F32)
        self.dent = sb("dent", [128, 16], F32)
        self.z16 = sb("z16", [128, 16], F32)

        self.S_low = sb("S_low", [17, S_W], F32)
        self.S_low_bf = sb("S_low_bf", [17, S_W], BF16)
        self.S_q = [sb("S_q0", [128, S_W], F32), sb("S_q1", [128, S_W], F32)]
        self.S_q_bf = [sb("S_q0_bf", [128, S_W], BF16), sb("S_q1_bf", [128, S_W], BF16)]

        self.p_qk = sb("p_qk", [128, 512], F32)
        self.nrm = sb("nrm", [128, 8], F32)
        self.rn = sb("rn", [128, 8], F32)
        self.q1pad = sb("q1pad", [128, 512], BF16)
        self.k1pad = sb("k1pad", [128, 512], BF16)
        self.qk1T = sb("qk1T", [128, 1024], BF16)
        self.v1_aug = sb("v1_aug", [128, H1 * 65], BF16)
        self.A1T = sb("A1T", [128, 512], BF16)
        self.y1_bf = sb("y1_bf", [128, 256], BF16)
        self.y1T = sb("y1T", [128, 256], BF16)
        self.dent4 = sb("dent4", [128, 4], F32)
        self.z4 = sb("z4", [128, 4], F32)
        self.S1 = sb("S1", [64, H1 * 65], F32)
        self.S1_bf = sb("S1_bf", [64, H1 * 65], BF16)

        self.stage = self.x_sb[0]
        self.stage_bf = self.y_bf
        self.amax8 = sb("amax8", [128, 8], F32)
        self.red_row = sb("red_row", [1, 128], F32)
        self.scal2 = sb("scal2", [1, 2], F32)
        self.rs_cols = sb("rs_cols", [128, 2], F32)

        self.tp_ps = ps("tp_ps", [128, 1024], BF16)
        self.out_ps = ps("out_ps", [128, 1024], F32)
        self.ws_ps = ps("ws_ps", [128, 2048], F32)
        self.sm_ps = ps("sm_ps", [128, 512], F32)

        self.sem_ld = self.stack.enter_context(nc.semaphore("s_ld"))
        self.sem_wld = self.stack.enter_context(nc.semaphore("s_wld"))
        self.sem_st = self.stack.enter_context(nc.semaphore("s_st"))
        self.sem_rdy = self.stack.enter_context(nc.semaphore("s_rdy"))
        self.sems = {
            "pe": self.stack.enter_context(nc.semaphore("s_pe")),
            "dve": self.stack.enter_context(nc.semaphore("s_dve")),
            "act": self.stack.enter_context(nc.semaphore("s_act")),
            "pool": self.stack.enter_context(nc.semaphore("s_pool")),
        }

    def dump(self, name, ap):
        if not self.dbg or self.cur_chunk > 0:
            return
        nc = self.nc
        if name not in self.dbg_tensors:
            sh = list(ap.shape)
            self.dbg_tensors[name] = nc.dram_tensor(
                "dbg_" + name, sh, F32, kind="ExternalOutput").ap()
        self.bar(pool=True)
        nc.gpsimd.dma_start(self.dbg_tensors[name], ap).then_inc(self.sem_wld, 16)
        self.wld += 16
        nc.gpsimd.wait_ge(self.sem_wld, self.wld)
        self.bar(pool=True)

    def wload(self, dst_ap, src_ap):
        nc = self.nc
        nc.sync.dma_start(dst_ap, src_ap).then_inc(self.sem_wld, 16)
        self.wld += 16
        nc.vector.wait_ge(self.sem_wld, self.wld)
        nc.scalar.wait_ge(self.sem_wld, self.wld)
        self.bar()

    def mm(self, out_ap, lhsT, rhs, start, stop, tile_position=None):
        return self.nc.tensor.matmul(out_ap, lhsT, rhs, start=start, stop=stop,
                                     tile_position=tile_position)

    def transpose_blocks(self, src_bf, blocks, dst_view):
        nc = self.nc
        for (sc, tc) in blocks:
            nc.tensor.transpose(self.tp_ps[:, tc:tc + 128],
                                src_bf[:, sc:sc + 128], self.idn_bf[:, :])
        self.bar()

    def prep_wT(self, w_ap, R, Dc, dst):
        nc = self.nc
        nblk = Dc // 128
        for g in range(R // 128):
            self.wload(self.stage[:, 0:Dc], w_ap[g * 128:(g + 1) * 128, :])
            nc.vector.tensor_copy(self.stage_bf[:, 0:Dc], self.stage[:, 0:Dc])
            self.bar()
            for j in range(nblk):
                nc.tensor.transpose(self.tp_ps[:, j * 128:(j + 1) * 128],
                                    self.stage_bf[:, j * 128:(j + 1) * 128],
                                    self.idn_bf[:, :])
            self.bar()
            src = self.tp_ps[:, 0:nblk * 128].rearrange("p (j r) -> p j r", j=nblk)
            dstv = dst[:, :].rearrange("p (j r) -> p j r", j=nblk)[:, :, g * 128:(g + 1) * 128]
            nc.scalar.activation(out=dstv, in_=src, func=AF.Copy)
            self.bar()

    def quant_scale_pass(self, w_ap, mode):
        nc = self.nc
        for g in range(8):
            self.wload(self.stage[:, :], w_ap[g * 128:(g + 1) * 128, :])
            op = ALU.max if mode == "int4" else ALU.add
            nc.vector.tensor_reduce(out=self.amax8[:, g:g + 1], in_=self.stage[:, :],
                                    axis=AX.X, op=op, apply_absolute_value=True)
            self.bar()
        op = ALU.max if mode == "int4" else ALU.add
        nc.vector.tensor_reduce(out=self.m_col[:, :], in_=self.amax8[:, :],
                                axis=AX.X, op=op)
        self.bar()
        nc.tensor.transpose(self.sm_ps[0:1, 0:128], self.m_col[:, :], self.idn_f[:, :])
        self.bar()
        nc.vector.tensor_copy(self.red_row[:, :], self.sm_ps[0:1, 0:128])
        self.bar()
        nc.vector.tensor_reduce(out=self.scal2[0:1, 0:1], in_=self.red_row[:, :],
                                axis=AX.X, op=op)
        self.bar()
        if mode == "int4":
            nc.vector.tensor_scalar(out=self.scal2[0:1, 0:1], in0=self.scal2[0:1, 0:1],
                                    scalar1=float(1.0 / 7.0), scalar2=None, op0=ALU.mult)
        else:
            nc.vector.tensor_scalar(out=self.scal2[0:1, 0:1], in0=self.scal2[0:1, 0:1],
                                    scalar1=float(1.0 / 1048576.0), scalar2=float(1e-5),
                                    op0=ALU.mult, op1=ALU.max)
        self.bar()
        self.bar()
        nc.vector.reciprocal(self.scal2[0:1, 1:2], self.scal2[0:1, 0:1])
        self.bar()
        self.mm(self.sm_ps[:, 0:2], self.ones_f[:, :], self.scal2[0:1, 0:2],
                start=True, stop=True)
        self.bar()
        nc.vector.tensor_copy(self.rs_cols[:, :], self.sm_ps[:, 0:2])
        self.bar()

    def quant_tile(self, dst_bf_ap, qmin, qmax):
        nc = self.nc
        s = self.stage[:, :]
        nc.vector.tensor_scalar(out=s, in0=s, scalar1=self.rs_cols[:, 1:2],
                                scalar2=None, op0=ALU.mult)
        self.bar()
        nc.vector.tensor_scalar(out=s, in0=s, scalar1=MAGIC, scalar2=MAGIC,
                                op0=ALU.add, op1=ALU.subtract)
        self.bar()
        nc.vector.tensor_scalar(out=s, in0=s, scalar1=float(qmin), scalar2=float(qmax),
                                op0=ALU.max, op1=ALU.min)
        self.bar()
        nc.vector.tensor_scalar(out=dst_bf_ap, in0=s, scalar1=self.rs_cols[:, 0:1],
                                scalar2=None, op0=ALU.mult)
        self.bar()

    def prep(self):
        nc = self.nc
        self.wload(self.idn_f[:, :], self.i_idn)
        self.wload(self.maskT[:, :], self.i_mask)
        self.wload(self.mask05[:, :], self.i_mask05)
        nc.vector.tensor_copy(self.idn_bf[:, :], self.idn_f[:, :])
        nc.vector.memset(self.ones_bf[:, :], 1.0)
        nc.vector.memset(self.ones_f[:, :], 1.0)
        for col, val in ((0, 1e-5), (1, -0.5), (2, SQ5), (3, 0.5), (4, QS),
                         (5, 1.0 / 1024.0)):
            nc.vector.memset(self.consts[:, col:col + 1], float(val))
        nc.vector.memset(self.ws_ps[:, :], 0.0)
        nc.vector.memset(self.out_ps[:, :], 0.0)
        nc.vector.memset(self.sm_ps[:, :], 0.0)
        nc.vector.memset(self.qpad[:, :], 0.0)
        nc.vector.memset(self.kpad[:, :], 0.0)
        nc.vector.memset(self.v_aug[:, :], 0.0)
        nc.vector.memset(self.v1_aug[:, :], 0.0)
        nc.vector.memset(self.q1pad[:, :], 0.0)
        nc.vector.memset(self.k1pad[:, :], 0.0)
        for t in (self.S_low, self.S_q[0], self.S_q[1], self.S1):
            nc.vector.memset(t[:, :], 0.0)
        for t in (self.S_low_bf, self.S_q_bf[0], self.S_q_bf[1], self.S1_bf):
            nc.vector.memset(t[:, :], 0.0)
        self.bar()
        nc.vector.memset(
            self.qpad[:, :].rearrange("p (h c) -> p h c", c=128)[:, :, 16:17], 1.0)
        nc.vector.memset(
            self.kpad[:, :].rearrange("p (h c) -> p h c", c=128)[:, :, 16:17], 1.0)
        nc.vector.memset(
            self.v_aug[:, :].rearrange("p (h c) -> p h c", c=65)[:, :, 64:65], 1.0)
        nc.vector.memset(
            self.v1_aug[:, :].rearrange("p (h c) -> p h c", c=65)[:, :, 64:65], 1.0)
        self.bar()

        self.prep_wT(self.i_wq0, 256, 1024, self.wqT)
        self.prep_wT(self.i_wk0, 256, 1024, self.wkT)
        self.prep_wT(self.i_wv0, 1024, 1024, self.wvT)
        self.prep_wT(self.i_wo0, 1024, 1024, self.woT)
        self.prep_wT(self.i_wq1, 256, 1024, self.wq1T)
        self.prep_wT(self.i_wk1, 256, 1024, self.wk1T)
        self.prep_wT(self.i_wv1, 256, 1024, self.wv1T)
        self.prep_wT(self.i_wo1, 1024, 256, self.wo1T)

        a_src = self.i_a.rearrange("(g p) r -> p g r", p=128)
        self.wload(self.stage[:, 0:128].rearrange("p (g r) -> p g r", g=8), a_src)
        nc.vector.tensor_copy(self.a_sb[:, :], self.stage[:, 0:128])
        self.wload(self.stage[0:16, 0:1024], self.i_b)
        nc.vector.tensor_copy(self.b_sb[:, :], self.stage[0:16, 0:1024])
        self.wload(self.stage[0:1, 0:1024], self.i_bias)
        nc.vector.tensor_copy(self.bias_bf[:, :], self.stage[0:1, 0:1024])
        self.bar()

        self.quant_scale_pass(self.i_ker, "int4")
        for g in range(8):
            self.wload(self.stage[:, :], self.i_ker[g * 128:(g + 1) * 128, :])
            self.quant_tile(self.kerq[:, g * 1024:(g + 1) * 1024], -8.0, 7.0)

        self.quant_scale_pass(self.i_w1, "tern")
        for g in range(8):
            self.wload(self.stage[:, :], self.i_w1[g * 128:(g + 1) * 128, :])
            self.quant_tile(self.stage_bf[:, :], -1.0, 1.0)
            for j in range(8):
                nc.tensor.transpose(self.tp_ps[:, j * 128:(j + 1) * 128],
                                    self.stage_bf[:, j * 128:(j + 1) * 128],
                                    self.idn_bf[:, :])
            self.bar()
            src = self.tp_ps[:, :].rearrange("p (j r) -> p j r", j=8)
            dstv = self.w1T[:, :].rearrange("p (j r) -> p j r", j=8)[:, :, g * 128:(g + 1) * 128]
            nc.scalar.activation(out=dstv, in_=src, func=AF.Copy)
            self.bar()
        self.cur_chunk = 0
        self.dump("kerq0", self.kerq[:, 0:1024])
        self.dump("wqT0", self.wqT[:, 0:256])
        self.dump("wvT0", self.wvT[:, 0:1024])
        self.dump("w1T0", self.w1T[:, 0:1024])
        self.dump("wo1T0", self.wo1T[:, 0:1024])
        self.dump("a_sb", self.a_sb[:, :])
        self.dump("b_sb", self.b_sb[:, :])
        self.dump("bias_bf", self.bias_bf[:, :])
        self.dump("idn_bf", self.idn_bf[:, :])
        self.cur_chunk = -1

    def layernorm(self, x_ap, xbuf):
        nc = self.nc
        C1024 = float(1.0 / 1024.0)
        self.g("dve", [xbuf], ["mcol"], lambda: nc.vector.tensor_reduce(
            out=self.m_col[:, :], in_=x_ap, axis=AX.X, op=ALU.add))
        self.g("act", [xbuf], ["sqA", "s2col"], lambda: nc.scalar.activation(
            out=self.sq[:, 0:1024], in_=x_ap, func=AF.Square,
            accum_out=self.s2_col[:, :]))

        def _v0():
            nc.vector.scalar_tensor_tensor(
                out=self.v_col[:, :], in0=self.m_col[:, :], scalar=-C1024,
                in1=self.m_col[:, :], op0=ALU.mult, op1=ALU.mult)
            nc.vector.drain()
            return nc.vector.tensor_scalar(
                out=self.t_col[:, :], in0=self.m_col[:, :], scalar1=C1024,
                scalar2=None, op0=ALU.mult)
        self.g("dve", ["mcol"], ["mcol"], _v0, drain=True)
        self.g("dve", ["mcol", "s2col"], ["mcol"], lambda: nc.vector.tensor_scalar(
            out=self.v_col[:, :], in0=self.v_col[:, :],
            scalar1=self.s2_col[:, 0:1], scalar2=C1024,
            op0=ALU.add, op1=ALU.mult), drain=True)

        def _rstd():
            nc.scalar.activation(out=self.r_col[:, :], in_=self.v_col[:, :],
                                 func=AF.Ln, bias=self.consts[:, 0:1])
            nc.scalar.drain()
            return nc.scalar.activation(out=self.r_col[:, :], in_=self.r_col[:, :],
                                        func=AF.Exp, scale=self.consts[:, 1:2])
        self.g("act", ["mcol"], ["rcol"], _rstd)
        self.g("dve", [xbuf, "mcol", "rcol"], ["xn"], lambda: nc.vector.tensor_scalar(
            out=self.xn[:, :], in0=x_ap, scalar1=self.t_col[:, 0:1],
            scalar2=self.r_col[:, 0:1], op0=ALU.subtract, op1=ALU.mult),
            drain=True)

    def xn_transpose(self):
        nc = self.nc

        def _t():
            for j in range(8):
                last = nc.tensor.transpose(self.tp_ps[:, j * 128:(j + 1) * 128],
                                           self.xn[:, j * 128:(j + 1) * 128],
                                           self.idn_bf[:, :])
            return last
        self.g("pe", ["xn"], ["tp"], _t)
        self.g("act", ["tp"], ["xnT"], lambda: nc.scalar.activation(
            out=self.xnT[:, :], in_=self.tp_ps[:, :], func=AF.Copy))

    def chunk_body(self, x_ap, xbuf):
        nc = self.nc
        WSA = ["wsA"]
        WSB = ["wsB"]
        WS = ["wsA", "wsB"]

        self.layernorm(x_ap, xbuf)
        self.dump("l0_xn", self.xn[:, :])
        self.xn_transpose()
        self.dump("l0_xnT", self.xnT[:, :])

        def _proj():
            for g_ in range(8):
                lhs = self.xnT[:, g_ * 128:(g_ + 1) * 128]
                self.mm(self.ws_ps[:, 0:256], lhs, self.wqT[:, g_ * 256:(g_ + 1) * 256],
                        start=(g_ == 0), stop=False)
                self.mm(self.ws_ps[:, 256:512], lhs, self.wkT[:, g_ * 256:(g_ + 1) * 256],
                        start=False, stop=(g_ == 7))
                self.mm(self.ws_ps[:, 512:1024], lhs,
                        self.wvT[:, g_ * 1024:g_ * 1024 + 512], start=(g_ == 0), stop=(g_ == 7))
                last = self.mm(self.ws_ps[:, 1024:1536], lhs,
                               self.wvT[:, g_ * 1024 + 512:(g_ + 1) * 1024],
                               start=(g_ == 0), stop=(g_ == 7))
            return last
        self.g("pe", ["xnT"], WS, _proj)

        def _ffn():
            self.mm(self.out_ps[:, 0:512], self.ones_bf[:, :], self.bias_bf[:, 0:512],
                    start=True, stop=False)
            self.mm(self.out_ps[:, 512:1024], self.ones_bf[:, :], self.bias_bf[:, 512:1024],
                    start=True, stop=False)
            for g_ in range(8):
                lhs = self.xnT[:, g_ * 128:(g_ + 1) * 128]
                self.mm(self.out_ps[:, 0:512], lhs, self.kerq[:, g_ * 1024:g_ * 1024 + 512],
                        start=False, stop=False)
                last = self.mm(self.out_ps[:, 512:1024], lhs,
                               self.kerq[:, g_ * 1024 + 512:(g_ + 1) * 1024],
                               start=False, stop=False)
            return last
        self.g("pe", ["xnT"], ["out"], _ffn)

        def _ut():
            for g_ in range(8):
                last = self.mm(self.sm_ps[0:16, 0:128], self.a_sb[:, g_ * 16:(g_ + 1) * 16],
                               self.xnT[:, g_ * 128:(g_ + 1) * 128],
                               start=(g_ == 0), stop=(g_ == 7))
            return last
        self.g("pe", ["xnT"], ["sm"], _ut)

        qv = self.ws_ps[:, 0:256].rearrange("p (h f) -> p h f", h=16)
        kv = self.ws_ps[:, 256:512].rearrange("p (h f) -> p h f", h=16)
        qpadv = self.qpad[:, :].rearrange("p (h c) -> p h c", c=128)[:, :, 0:16]
        kpadv = self.kpad[:, :].rearrange("p (h c) -> p h c", c=128)[:, :, 0:16]

        def _evac():
            nc.scalar.activation(out=qpadv, in_=qv, func=AF.Copy, scale=self.consts[:, 3:4])
            nc.scalar.activation(out=kpadv, in_=kv, func=AF.Copy, scale=self.consts[:, 3:4])
            nc.scalar.activation(out=self.q_s[:, :].rearrange("p (h f) -> p h f", h=16),
                                 in_=qv, func=AF.Copy, scale=self.consts[:, 4:5])
            nc.scalar.activation(out=self.k_s[:, :].rearrange("p (h f) -> p h f", h=16),
                                 in_=kv, func=AF.Copy, scale=self.consts[:, 4:5])
            vaugv = self.v_aug[:, :].rearrange("p (h c) -> p h c", c=65)[:, :, 0:64]
            return nc.scalar.activation(
                out=vaugv,
                in_=self.ws_ps[:, 512:1536].rearrange("p (h e) -> p h e", h=16),
                func=AF.Copy)
        self.g("act", WS, ["qpad", "kpad", "q_s", "k_s", "v_aug"], _evac)
        self.g("dve", ["sm"], ["uT"], lambda: nc.vector.tensor_copy(
            self.uT[:, :], self.sm_ps[0:16, 0:128]))
        self.dump("l0_qpad", self.qpad[:, :])
        self.dump("l0_q_s", self.q_s[:, :])
        self.dump("l0_v_aug", self.v_aug[:, :])
        self.dump("l0_uT", self.uT[:, :])

        def _lora():
            self.mm(self.out_ps[:, 0:512], self.uT[0:16, :], self.b_sb[:, 0:512],
                    start=False, stop=False)
            return self.mm(self.out_ps[:, 512:1024], self.uT[0:16, :], self.b_sb[:, 512:1024],
                           start=False, stop=False)
        self.g("pe", ["uT"], ["out"], _lora)

        qs_v = self.q_s[:, :].rearrange("p (h f) -> p h f", h=16)
        ks_v = self.k_s[:, :].rearrange("p (h f) -> p h f", h=16)
        qf_v = self.qf[:, :].rearrange("p (h i j) -> p h i j", h=16, i=16)
        kf_v = self.kf[:, :].rearrange("p (h i j) -> p h i j", h=16, i=16)

        for r in range(8):
            if r == 4:
                self.g("dve", ["q_s", "qpad"], ["qf"], lambda: nc.vector.tensor_tensor(
                    out=qf_v,
                    in0=qs_v.unsqueeze(3).broadcast_to([128, 16, 16, 16]),
                    in1=qpadv.unsqueeze(2).broadcast_to([128, 16, 16, 16]),
                    op=ALU.mult))
                self.dump("l0_qf", self.qf[:, :])
            if r < 2:
                srct, src_name, sc0 = self.qpad, "qpad", r * 1024
            elif r < 4:
                srct, src_name, sc0 = self.kpad, "kpad", (r - 2) * 1024
            else:
                srct, src_name, sc0 = self.qf, "qf", (r - 4) * 1024
            use_ws = (r % 2 == 1)
            tgt = self.ws_ps[:, 0:512].bitcast(BF16) if use_ws else self.tp_ps
            tname = "wsA" if use_ws else "tp"
            if r < 4:
                dname, dst, dc0 = "qkT", self.qkT, r * 1024
            else:
                dname, dst, dc0 = "qfT", self.qfT, (r - 4) * 1024

            def _t(srct=srct, sc0=sc0, tgt=tgt):
                for j in range(8):
                    last = nc.tensor.transpose(tgt[:, j * 128:(j + 1) * 128],
                                               srct[:, sc0 + j * 128:sc0 + (j + 1) * 128],
                                               self.idn_bf[:, :])
                return last
            self.g("pe", [src_name], [tname], _t)
            eng = "act" if r % 2 == 0 else "dve"
            h = nc.scalar if eng == "act" else nc.vector

            def _e(h=h, dst=dst, dc0=dc0, tgt=tgt):
                if h is nc.vector:
                    return h.tensor_copy(dst[:, dc0:dc0 + 1024], tgt[:, 0:1024])
                return h.activation(out=dst[:, dc0:dc0 + 1024], in_=tgt[:, 0:1024],
                                    func=AF.Copy)
            self.g(eng, [tname], [dname], _e)
        self.dump("l0_qkT", self.qkT[:, :])
        self.dump("l0_qfT", self.qfT[:, :])

        for half in range(2):
            hs = range(8 * half, 8 * half + 8)
            wsn = WSA if half == 0 else WSB

            def _p(hs=hs):
                for h in hs:
                    lhs = self.qkT[0:16, 2048 + h * 128:2048 + (h + 1) * 128]
                    rhs = self.qkT[0:16, h * 128:(h + 1) * 128]
                    last = self.mm(self.ws_ps[:, h * 128:(h + 1) * 128], lhs, rhs,
                                   start=(h % 4 == 0), stop=(h % 4 == 3))
                return last
            self.g("pe", ["qkT"], wsn, _p)
            c0 = half * 1024
            self.g("act", wsn, ["sqA" if half == 0 else "sqB"], lambda c0=c0: nc.scalar.activation(
                out=self.sq[:, c0:c0 + 1024], in_=self.ws_ps[:, c0:c0 + 1024],
                func=AF.Square, scale=self.consts[:, 2:3], bias=self.consts[:, 2:3]))
            self.g("pool", ["sqA" if half == 0 else "sqB"], ["AT%d" % half],
                   lambda c0=c0: nc.gpsimd.tensor_tensor(
                out=self.AT[:, c0:c0 + 1024].rearrange("p (h t) -> p h t", h=8),
                in0=self.sq[:, c0:c0 + 1024].rearrange("p (h t) -> p h t", h=8),
                in1=self.maskT[:, :].unsqueeze(1).broadcast_to([128, 8, 128]),
                op=ALU.mult))

        self.g("pool", ["k_s", "kpad"], ["kf"], lambda: nc.gpsimd.tensor_tensor(
            out=kf_v,
            in0=ks_v.unsqueeze(3).broadcast_to([128, 16, 16, 16]),
            in1=kpadv.unsqueeze(2).broadcast_to([128, 16, 16, 16]), op=ALU.mult))
        self.dump("l0_kf", self.kf[:, :])
        self.dump("l0_AT", self.AT[:, :])

        for half in range(2):
            hs = range(8 * half, 8 * half + 8)
            wsn = WSA if half == 0 else WSB

            def _y(hs=hs):
                for h in hs:
                    yc = h * 128
                    self.mm(self.ws_ps[:, yc:yc + 65],
                            self.AT[:, h * 128:(h + 1) * 128],
                            self.v_aug[:, h * 65:(h + 1) * 65],
                            start=(h % 4 == 0), stop=False)
                    self.mm(self.ws_ps[:, yc:yc + 65],
                            self.mask05[:, :],
                            self.v_aug[:, h * 65:(h + 1) * 65],
                            start=False, stop=False)
                    self.mm(self.ws_ps[:, yc:yc + 65],
                            self.qkT[0:17, h * 128:(h + 1) * 128],
                            self.S_low_bf[0:17, colh(h):colh(h) + 65],
                            start=False, stop=False)
                    for j in range(2):
                        last = self.mm(self.ws_ps[:, yc:yc + 65],
                                       self.qfT[:, (2 * h + j) * 128:(2 * h + j + 1) * 128],
                                       self.S_q_bf[j][:, colh(h):colh(h) + 65],
                                       start=False, stop=(h % 4 == 3 and j == 1))
                return last
            self.g("pe", ["AT%d" % half, "v_aug", "qkT", "qfT",
                          "Slowbf", "Sq0bf", "Sq1bf"], wsn, _y)

            denv = self.ws_ps[:, :].rearrange(
                "p (h c) -> p h c", c=128)[:, 8 * half:8 * half + 8, 64:65]

            def _z(half=half, denv=denv):
                nc.vector.tensor_scalar(out=self.dent[:, 8 * half:8 * half + 8],
                                        in0=denv.squeeze(2),
                                        scalar1=float(1e-12), scalar2=None, op0=ALU.add)
                nc.vector.drain()
                return nc.vector.reciprocal(self.z16[:, 8 * half:8 * half + 8],
                                            self.dent[:, 8 * half:8 * half + 8])
            self.g("dve", wsn, ["z%d" % half], _z)

            def _zy(half=half):
                return nc.vector.tensor_tensor(
                    out=self.y_bf[:, half * 512:(half + 1) * 512].rearrange(
                        "p (h e) -> p h e", h=8),
                    in0=self.ws_ps[:, :].rearrange(
                        "p (h c) -> p h c", c=128)[:, 8 * half:8 * half + 8, 0:64],
                    in1=self.z16[:, 8 * half:8 * half + 8].unsqueeze(2).broadcast_to(
                        [128, 8, 64]), op=ALU.mult)
            self.g("dve", wsn + ["z%d" % half], ["ybf%d" % half], _zy, drain=True)
        self.dump("l0_dent", self.dent[:, :])
        self.dump("l0_ybf", self.y_bf[:, :])

        def _yt():
            for j in range(8):
                last = nc.tensor.transpose(self.tp_ps[:, j * 128:(j + 1) * 128],
                                           self.y_bf[:, j * 128:(j + 1) * 128],
                                           self.idn_bf[:, :])
            return last
        self.g("pe", ["ybf0", "ybf1"], ["tp"], _yt)
        self.g("act", ["tp"], ["yT"], lambda: nc.scalar.activation(
            out=self.yT[:, :], in_=self.tp_ps[:, :], func=AF.Copy))

        def _wo():
            for j in range(8):
                lhs = self.yT[:, j * 128:(j + 1) * 128]
                self.mm(self.out_ps[:, 0:512], lhs, self.woT[:, j * 1024:j * 1024 + 512],
                        start=False, stop=(j == 7))
                last = self.mm(self.out_ps[:, 512:1024], lhs,
                               self.woT[:, j * 1024 + 512:(j + 1) * 1024],
                               start=False, stop=(j == 7))
            return last
        self.g("pe", ["yT"], ["out"], _wo)

        self.g("dve", ["out", xbuf], [xbuf], lambda: nc.vector.tensor_tensor(
            out=x_ap, in0=x_ap, in1=self.out_ps[:, :], op=ALU.add))
        self.dump("l0_x", x_ap)

        def ds_round(rnd):
            def _ds(rnd=rnd):
                for h in range(16):
                    c0 = colh(h)
                    if rnd == 0:
                        lhs = self.kpad[:, 128 * h:128 * h + 17]
                        out = self.ws_ps[0:17, c0:c0 + 65]
                    else:
                        j = rnd - 1
                        lhs = self.kf[:, (2 * h + j) * 128:(2 * h + j + 1) * 128]
                        out = self.ws_ps[:, c0:c0 + 65]
                    last = self.mm(out, lhs, self.v_aug[:, h * 65:(h + 1) * 65],
                                   start=h in (0, 7, 14), stop=h in (6, 13, 15))
                return last
            src_name = "kpad" if rnd == 0 else "kf"
            self.g("pe", [src_name, "v_aug"], WS, _ds)
            S = self.S_low if rnd == 0 else self.S_q[rnd - 1]
            sname = ["Slow", "Sq0", "Sq1"][rnd]
            rows = 17 if rnd == 0 else 128
            self.g("dve", WS + [sname], [sname], lambda S=S, rows=rows:
                   nc.vector.tensor_tensor(out=S[0:rows, :], in0=S[0:rows, :],
                                           in1=self.ws_ps[0:rows, 0:S_W], op=ALU.add))

        ds_round(0)
        self.layernorm(x_ap, xbuf)
        self.dump("l1_xn", self.xn[:, :])
        self.xn_transpose()

        def _ffn1():
            for g_ in range(8):
                lhs = self.xnT[:, g_ * 128:(g_ + 1) * 128]
                self.mm(self.out_ps[:, 0:512], lhs, self.w1T[:, g_ * 1024:g_ * 1024 + 512],
                        start=(g_ == 0), stop=False)
                last = self.mm(self.out_ps[:, 512:1024], lhs,
                               self.w1T[:, g_ * 1024 + 512:(g_ + 1) * 1024],
                               start=(g_ == 0), stop=False)
            return last
        self.g("pe", ["xnT"], ["out"], _ffn1)

        def _proj1():
            for g_ in range(8):
                lhs = self.xnT[:, g_ * 128:(g_ + 1) * 128]
                self.mm(self.ws_ps[:, 0:256], lhs, self.wq1T[:, g_ * 256:(g_ + 1) * 256],
                        start=(g_ == 0), stop=False)
                self.mm(self.ws_ps[:, 256:512], lhs, self.wk1T[:, g_ * 256:(g_ + 1) * 256],
                        start=False, stop=(g_ == 7))
                last = self.mm(self.ws_ps[:, 512:768], lhs,
                               self.wv1T[:, g_ * 256:(g_ + 1) * 256],
                               start=(g_ == 0), stop=(g_ == 7))
            return last
        self.g("pe", ["xnT"], WS, _proj1)

        e_qk = self.sq[:, 1024:1536]
        self.g("dve", WS, ["sqB"], lambda: nc.vector.tensor_scalar(
            out=e_qk, in0=self.ws_ps[:, 0:512], scalar1=0.0, scalar2=None,
            op0=ALU.min))
        self.g("act", ["sqB"], ["sqB"], lambda: nc.scalar.activation(
            out=e_qk, in_=e_qk, func=AF.Exp))
        self.g("dve", WS + ["sqB"], ["pqk"], lambda: nc.vector.scalar_tensor_tensor(
            out=self.p_qk[:, :], in0=self.ws_ps[:, 0:512], scalar=0.0, in1=e_qk,
            op0=ALU.max, op1=ALU.add))
        self.g("act", WS, ["v1aug"], lambda: nc.scalar.activation(
            out=self.v1_aug[:, :].rearrange("p (h c) -> p h c", c=65)[:, :, 0:64],
            in_=self.ws_ps[:, 512:768].rearrange("p (h e) -> p h e", h=4),
            func=AF.Copy))
        self.g("act", ["pqk"], ["sqA"], lambda: nc.scalar.activation(
            out=self.sq[:, 0:512], in_=self.p_qk[:, :], func=AF.Square))
        self.g("dve", ["sqA"], ["nrm"], lambda: nc.vector.tensor_reduce(
            out=self.nrm[:, :].rearrange("p (a b) -> p a b", b=1),
            in_=self.sq[:, 0:512].rearrange("p (a b) -> p a b", a=8),
            axis=AX.X, op=ALU.add))

        def _sqrt():
            nc.scalar.activation(out=self.rn[:, :], in_=self.nrm[:, :], func=AF.Ln)
            nc.scalar.drain()
            return nc.scalar.activation(out=self.rn[:, :], in_=self.rn[:, :],
                                        func=AF.Exp, scale=self.consts[:, 3:4])
        self.g("act", ["nrm"], ["rn"], _sqrt)

        def _rn():
            nc.vector.tensor_scalar(out=self.rn[:, :], in0=self.rn[:, :],
                                    scalar1=float(1e-6), scalar2=None, op0=ALU.add)
            nc.vector.drain()
            return nc.vector.reciprocal(self.rn[:, :], self.rn[:, :])
        self.g("dve", ["rn"], ["rn"], _rn)

        def _norm():
            nc.vector.tensor_tensor(
                out=self.q1pad[:, :].rearrange("p (h c) -> p h c", c=128)[:, :, 0:64],
                in0=self.p_qk[:, 0:256].rearrange("p (a b) -> p a b", a=4),
                in1=self.rn[:, 0:4].unsqueeze(2).broadcast_to([128, 4, 64]), op=ALU.mult)
            return nc.vector.tensor_tensor(
                out=self.k1pad[:, :].rearrange("p (h c) -> p h c", c=128)[:, :, 0:64],
                in0=self.p_qk[:, 256:512].rearrange("p (a b) -> p a b", a=4),
                in1=self.rn[:, 4:8].unsqueeze(2).broadcast_to([128, 4, 64]), op=ALU.mult)
        self.g("dve", ["pqk", "rn"], ["q1pad", "k1pad"], _norm, drain=True)
        self.dump("l1_pqk", self.p_qk[:, :])
        self.dump("l1_rn", self.rn[:, :])
        self.dump("l1_q1pad", self.q1pad[:, :])
        self.dump("l1_k1pad", self.k1pad[:, :])
        self.dump("l1_v1aug", self.v1_aug[:, :])

        ds_round(1)

        def _t1():
            for j in range(4):
                nc.tensor.transpose(self.tp_ps[:, j * 128:(j + 1) * 128],
                                    self.q1pad[:, j * 128:(j + 1) * 128], self.idn_bf[:, :])
                last = nc.tensor.transpose(self.tp_ps[:, 512 + j * 128:512 + (j + 1) * 128],
                                           self.k1pad[:, j * 128:(j + 1) * 128],
                                           self.idn_bf[:, :])
            return last
        self.g("pe", ["q1pad", "k1pad"], ["tp"], _t1)
        self.g("act", ["tp"], ["qk1T"], lambda: nc.scalar.activation(
            out=self.qk1T[:, :], in_=self.tp_ps[:, :], func=AF.Copy))

        def _p1():
            for h in range(4):
                lhs = self.qk1T[0:64, 512 + h * 128:512 + (h + 1) * 128]
                rhs = self.qk1T[0:64, h * 128:(h + 1) * 128]
                last = self.mm(self.sm_ps[:, h * 128:(h + 1) * 128], lhs, rhs,
                               start=(h == 0), stop=(h == 3))
            return last
        self.g("pe", ["qk1T"], ["sm"], _p1)
        self.g("dve", ["sm"], ["A1T"], lambda: nc.vector.tensor_tensor(
            out=self.A1T[:, :].rearrange("p (h t) -> p h t", h=4),
            in0=self.sm_ps[:, :].rearrange("p (h t) -> p h t", h=4),
            in1=self.maskT[:, :].unsqueeze(1).broadcast_to([128, 4, 128]), op=ALU.mult))
        self.dump("l1_A1T", self.A1T[:, :])

        def _y1():
            for h in range(4):
                yc = h * 128
                self.mm(self.sm_ps[:, yc:yc + 65], self.A1T[:, h * 128:(h + 1) * 128],
                        self.v1_aug[:, h * 65:(h + 1) * 65], start=(h == 0), stop=False)
                last = self.mm(self.sm_ps[:, yc:yc + 65],
                               self.qk1T[0:64, h * 128:(h + 1) * 128],
                               self.S1_bf[0:64, h * 65:(h + 1) * 65],
                               start=False, stop=(h == 3))
            return last
        self.g("pe", ["A1T", "v1aug", "qk1T", "S1bf"], ["sm"], _y1)

        denv1 = self.sm_ps[:, :].rearrange("p (h c) -> p h c", c=128)[:, :, 64:65]

        def _z1():
            nc.vector.tensor_scalar(out=self.dent4[:, :], in0=denv1.squeeze(2),
                                    scalar1=float(1e-6), scalar2=None, op0=ALU.add)
            nc.vector.drain()
            return nc.vector.reciprocal(self.z4[:, :], self.dent4[:, :])
        self.g("dve", ["sm"], ["z4"], _z1)
        self.g("dve", ["sm", "z4"], ["y1"], lambda: nc.vector.tensor_tensor(
            out=self.y1_bf[:, :].rearrange("p (h e) -> p h e", h=4),
            in0=self.sm_ps[:, :].rearrange("p (h c) -> p h c", c=128)[:, :, 0:64],
            in1=self.z4[:, :].unsqueeze(2).broadcast_to([128, 4, 64]), op=ALU.mult),
            drain=True)
        self.dump("l1_y1bf", self.y1_bf[:, :])

        def _y1t():
            for j in range(2):
                last = nc.tensor.transpose(self.tp_ps[:, j * 128:(j + 1) * 128],
                                           self.y1_bf[:, j * 128:(j + 1) * 128],
                                           self.idn_bf[:, :])
            return last
        self.g("pe", ["y1"], ["tp"], _y1t)
        self.g("act", ["tp"], ["y1T"], lambda: nc.scalar.activation(
            out=self.y1T[:, :], in_=self.tp_ps[:, 0:256], func=AF.Copy))

        def _wo1():
            for j in range(2):
                lhs = self.y1T[:, j * 128:(j + 1) * 128]
                self.mm(self.out_ps[:, 0:512], lhs, self.wo1T[:, j * 1024:j * 1024 + 512],
                        start=False, stop=(j == 1))
                last = self.mm(self.out_ps[:, 512:1024], lhs,
                               self.wo1T[:, j * 1024 + 512:(j + 1) * 1024],
                               start=False, stop=(j == 1))
            return last
        self.g("pe", ["y1T"], ["out"], _wo1)

        self.g("dve", ["out", xbuf], [xbuf], lambda: nc.vector.tensor_tensor(
            out=x_ap, in0=x_ap, in1=self.out_ps[:, :], op=ALU.add))
        self.resid_mark = self.n["dve"]

        def _ds1():
            for h in range(4):
                last = self.mm(self.sm_ps[0:64, h * 65:(h + 1) * 65],
                               self.k1pad[:, h * 128:h * 128 + 64],
                               self.v1_aug[:, h * 65:(h + 1) * 65],
                               start=(h == 0), stop=(h == 3))
            return last
        self.g("pe", ["k1pad", "v1aug"], ["sm"], _ds1)
        self.g("dve", ["sm", "S1"], ["S1"], lambda: nc.vector.tensor_tensor(
            out=self.S1[:, :], in0=self.S1[:, :], in1=self.sm_ps[0:64, 0:260],
            op=ALU.add))
        ds_round(2)
        self.g("act", ["Slow"], ["Slowbf"], lambda: nc.scalar.activation(
            out=self.S_low_bf[0:17, :], in_=self.S_low[0:17, :], func=AF.Copy))
        self.g("act", ["Sq0"], ["Sq0bf"], lambda: nc.scalar.activation(
            out=self.S_q_bf[0][:, :], in_=self.S_q[0][:, :], func=AF.Copy))
        self.g("act", ["Sq1"], ["Sq1bf"], lambda: nc.scalar.activation(
            out=self.S_q_bf[1][:, :], in_=self.S_q[1][:, :], func=AF.Copy))
        self.g("act", ["S1"], ["S1bf"], lambda: nc.scalar.activation(
            out=self.S1_bf[:, :], in_=self.S1[:, :], func=AF.Copy))
        self.dump("l0_Slow", self.S_low_bf[:, :])
        self.dump("l0_Sq0", self.S_q_bf[0][:, :])
        self.dump("l0_Sq1", self.S_q_bf[1][:, :])
        self.dump("l1_S1", self.S1_bf[:, :])


    def build(self):
        nc = self.nc
        self.alloc()
        self.prep()
        self.bar()
        self.reset_deps()

        for c in range(min(2, self.nchunks)):
            nc.sync.dma_start(self.x_sb[c % 2][:, :],
                              self.i_x[c * C:(c + 1) * C, :]).then_inc(self.sem_ld, 16)
            self.ld += 16

        for c in range(self.nchunks):
            self.cur_chunk = c
            xbuf = "x%d" % (c % 2)
            buf = self.x_sb[c % 2][:, :]
            nc.vector.wait_ge(self.sem_ld, 16 * (c + 1))
            nc.scalar.wait_ge(self.sem_ld, 16 * (c + 1))
            self.chunk_body(buf, xbuf)
            nc.sync.wait_ge(self.sems["dve"], self.resid_mark)
            nc.sync.dma_start(self.o_out[c * C:(c + 1) * C, :], buf).then_inc(
                self.sem_st, 16)
            self.st += 16
            if c + 2 < self.nchunks:
                nc.sync.wait_ge(self.sem_st, 16 * (c + 1))
                nc.sync.dma_start(self.x_sb[c % 2][:, :],
                                  self.i_x[(c + 2) * C:(c + 3) * C, :]).then_inc(
                    self.sem_ld, 16)
                self.ld += 16

        nc.sync.wait_ge(self.sem_st, self.st)
        return nc


def build_program(nchunks=NCH, layers=(0, 1), dbg=False):
    return KB(nchunks=nchunks, layers=layers, dbg=dbg).build()


def make_in_map(inputs, b):
    f32 = np.float32
    return {
        "x": np.ascontiguousarray(np.asarray(inputs["L0_attn_hidden_states"])[b], dtype=f32),
        "wq0": np.ascontiguousarray(np.asarray(inputs["L0_attn_Wq"], dtype=f32)),
        "wk0": np.ascontiguousarray(np.asarray(inputs["L0_attn_Wk"], dtype=f32)),
        "wv0": np.ascontiguousarray(np.asarray(inputs["L0_attn_Wv"], dtype=f32)),
        "wo0": np.ascontiguousarray(np.asarray(inputs["L0_attn_Wo"], dtype=f32)),
        "ker": np.ascontiguousarray(np.asarray(inputs["L0_ffn_kernel"], dtype=f32)),
        "bias": np.ascontiguousarray(np.asarray(inputs["L0_ffn_bias"], dtype=f32)).reshape(1, 1024),
        "la": np.ascontiguousarray(np.asarray(inputs["L0_ffn_a"], dtype=f32)),
        "lb": np.ascontiguousarray(np.asarray(inputs["L0_ffn_b"], dtype=f32)),
        "wq1": np.ascontiguousarray(np.asarray(inputs["L1_attn_Wq"], dtype=f32)),
        "wk1": np.ascontiguousarray(np.asarray(inputs["L1_attn_Wk"], dtype=f32)),
        "wv1": np.ascontiguousarray(np.asarray(inputs["L1_attn_Wv"], dtype=f32)),
        "wo1": np.ascontiguousarray(np.asarray(inputs["L1_attn_Wo"], dtype=f32)),
        "w1": np.ascontiguousarray(np.asarray(inputs["L1_ffn_weight"], dtype=f32)),
        "idn": np.eye(128, dtype=f32),
        "maskT": np.triu(np.ones((128, 128), dtype=f32)),
        "mask05": np.triu(np.full((128, 128), 0.5, dtype=ml_dtypes.bfloat16)),
    }


def kernel(**inputs):
    nc = build_program()
    in_maps = [make_in_map(inputs, b) for b in range(8)]
    res = run_bass_kernel_spmd(nc, in_maps, core_ids=list(range(8)))
    return np.stack([r["out"] for r in res.results], axis=0).astype(np.float32)


# version 05
# speedup vs baseline: 1.6160x; 1.1398x over previous
import contextlib
import numpy as np
import ml_dtypes

import concourse.bass as bass
import concourse.mybir as mybir
from concourse.bass_utils import run_bass_kernel_spmd

F32 = mybir.dt.float32
BF16 = mybir.dt.bfloat16
AF = mybir.ActivationFunctionType
ALU = mybir.AluOpType
AX = mybir.AxisListType
ET = mybir.EngineType

MAGIC = float(1.5 * 2 ** 23)
SQ5 = 0.7071067811865476
QS = 0.35355339059327373

T_SEQ = 4096
D = 1024
C = 128
NCH = T_SEQ // C
H0, F0, V0 = 16, 16, 64
H1, D1 = 4, 64

def colh(h):
    return 65 * (h % 7) + 512 * (h // 7)

S_W = 1536


class KB:

    def __init__(self, nchunks=NCH, layers=(0, 1), dbg=False):
        self.nchunks = nchunks
        self.layers = layers
        self.dbg = dbg
        self.cur_chunk = -1
        self.dbg_tensors = {}
        self.nc = bass.Bass("TRN2", target_bir_lowering=False)
        self.stack = contextlib.ExitStack()
        self.ld = 0
        self.wld = 0
        self.st = 0
        self.n = {"pe": 0, "dve": 0, "act": 0, "pool": 0}
        self.deps = {}
        self.pending = []
        self.last_wait = {}

    def bar(self, pool=False):
        self.nc.all_engine_barrier()

    def eng_handle(self, eng):
        nc = self.nc
        return {"pe": nc.tensor, "dve": nc.vector, "act": nc.scalar,
                "pool": nc.gpsimd, "sp": nc.sync}[eng]

    def reset_deps(self):
        self.deps = {}
        self.last_wait = {}

    def dep_wait(self, eng, reads=(), writes=()):
        h = self.eng_handle(eng)
        need = {}
        for b in reads:
            w = self.deps.get(b, {}).get("w")
            if w and w[0] != eng:
                need[w[0]] = max(need.get(w[0], 0), w[1])
        for b in writes:
            d = self.deps.get(b, {})
            w = d.get("w")
            if w and w[0] != eng:
                need[w[0]] = max(need.get(w[0], 0), w[1])
            for re_, rc in d.get("r", {}).items():
                if re_ != eng:
                    need[re_] = max(need.get(re_, 0), rc)
        for prod, cnt in sorted(need.items()):
            key = (eng, prod)
            if self.last_wait.get(key, 0) < cnt:
                sem = self.sem_wld if prod == "wld" else self.sems[prod]
                h.wait_ge(sem, cnt)
                self.last_wait[key] = cnt

    def dep_done(self, inst, eng, reads=(), writes=()):
        self.n[eng] += 1
        inst.then_inc(self.sems[eng], 1)
        c = self.n[eng]
        for b in writes:
            self.deps[b] = {"w": (eng, c), "r": {}}
        for b in reads:
            d = self.deps.setdefault(b, {"w": None, "r": {}})
            d["r"][eng] = c

    def g(self, eng, reads, writes, emit, drain=False):
        self.dep_wait(eng, reads, writes)
        if drain:
            self.eng_handle(eng).drain()
        last = emit()
        self.dep_done(last, eng, reads, writes)
        return self.n[eng]

    def alloc(self):
        nc = self.nc
        sb = nc.alloc_sbuf_tensor
        ps = nc.alloc_psum_tensor

        self.i_x = nc.dram_tensor("x", [T_SEQ, D], F32, kind="ExternalInput").ap()
        self.i_wq0 = nc.dram_tensor("wq0", [256, 1024], F32, kind="ExternalInput").ap()
        self.i_wk0 = nc.dram_tensor("wk0", [256, 1024], F32, kind="ExternalInput").ap()
        self.i_wv0 = nc.dram_tensor("wv0", [1024, 1024], F32, kind="ExternalInput").ap()
        self.i_wo0 = nc.dram_tensor("wo0", [1024, 1024], F32, kind="ExternalInput").ap()
        self.i_ker = nc.dram_tensor("ker", [1024, 1024], F32, kind="ExternalInput").ap()
        self.i_bias = nc.dram_tensor("bias", [1, 1024], F32, kind="ExternalInput").ap()
        self.i_a = nc.dram_tensor("la", [1024, 16], F32, kind="ExternalInput").ap()
        self.i_b = nc.dram_tensor("lb", [16, 1024], F32, kind="ExternalInput").ap()
        self.i_wq1 = nc.dram_tensor("wq1", [256, 1024], F32, kind="ExternalInput").ap()
        self.i_wk1 = nc.dram_tensor("wk1", [256, 1024], F32, kind="ExternalInput").ap()
        self.i_wv1 = nc.dram_tensor("wv1", [256, 1024], F32, kind="ExternalInput").ap()
        self.i_wo1 = nc.dram_tensor("wo1", [1024, 256], F32, kind="ExternalInput").ap()
        self.i_w1 = nc.dram_tensor("w1", [1024, 1024], F32, kind="ExternalInput").ap()
        self.i_idn = nc.dram_tensor("idn", [128, 128], F32, kind="ExternalInput").ap()
        self.i_mask = nc.dram_tensor("maskT", [128, 128], F32, kind="ExternalInput").ap()
        self.i_mask05 = nc.dram_tensor("mask05", [128, 128], BF16, kind="ExternalInput").ap()
        self.o_out = nc.dram_tensor("out", [T_SEQ, D], F32, kind="ExternalOutput").ap()

        self.wqT = sb("wqT", [128, 2048], BF16)
        self.wkT = sb("wkT", [128, 2048], BF16)
        self.wvT = sb("wvT", [128, 8192], BF16)
        self.woT = sb("woT", [128, 8192], BF16)
        self.kerq = sb("kerq", [128, 8192], BF16)
        self.a_sb = sb("a_sb", [128, 128], BF16)
        self.b_sb = sb("b_sb", [16, 1024], BF16)
        self.bias_bf = sb("bias_bf", [1, 1024], BF16)
        self.wq1T = sb("wq1T", [128, 2048], BF16)
        self.wk1T = sb("wk1T", [128, 2048], BF16)
        self.wv1T = sb("wv1T", [128, 2048], BF16)
        self.wo1T = sb("wo1T", [128, 2048], BF16)
        self.w1T = sb("w1T", [128, 8192], BF16)

        self.idn_f = sb("idn_f", [128, 128], F32)
        self.idn_bf = sb("idn_bf", [128, 128], BF16)
        self.maskT = sb("maskT_sb", [128, 128], F32)
        self.mask05 = sb("mask05_sb", [128, 128], BF16)
        self.ones_bf = sb("ones_bf", [1, 128], BF16)
        self.ones_f = sb("ones_f", [1, 128], F32)
        self.consts = sb("consts", [128, 8], F32)

        self.x_sb = [sb("x_sb0", [128, D], F32), sb("x_sb1", [128, D], F32)]
        self.sq = sb("sq_scr", [128, 2048], F32)
        self.xn = sb("xn_bf", [128, D], BF16)
        self.xnT = sb("xnT_bf", [128, D], BF16)
        self.qpad = sb("qpad", [128, 2048], BF16)
        self.kpad = sb("kpad", [128, 2048], BF16)
        self.q_s = sb("q_s", [128, 256], BF16)
        self.k_s = sb("k_s", [128, 256], BF16)
        self.qkT = sb("qkT", [128, 4096], BF16)
        self.v_aug = sb("v_aug", [128, H0 * 65], BF16)
        self.uT = sb("uT_bf", [16, 128], BF16)
        self.qf = sb("qf", [128, 4096], BF16)
        self.kf = sb("kf", [128, 4096], BF16)
        self.qfT = sb("qfT", [128, 4096], BF16)
        self.AT = sb("AT", [128, 2048], BF16)
        self.y_bf = sb("y_bf", [128, 1024], BF16)
        self.yT = sb("yT", [128, 1024], BF16)
        self.m_col = sb("m_col", [128, 1], F32)
        self.s2_col = sb("s2_col", [128, 1], F32)
        self.t_col = sb("t_col", [128, 1], F32)
        self.v_col = sb("v_col", [128, 1], F32)
        self.r_col = sb("r_col", [128, 1], F32)
        self.dent = sb("dent", [128, 16], F32)
        self.z16 = sb("z16", [128, 16], F32)

        self.S_low = sb("S_low", [17, S_W], F32)
        self.S_low_bf = sb("S_low_bf", [17, S_W], BF16)
        self.S_q = [sb("S_q0", [128, S_W], F32), sb("S_q1", [128, S_W], F32)]
        self.S_q_bf = [sb("S_q0_bf", [128, S_W], BF16), sb("S_q1_bf", [128, S_W], BF16)]

        self.p_qk = sb("p_qk", [128, 512], F32)
        self.nrm = sb("nrm", [128, 8], F32)
        self.rn = sb("rn", [128, 8], F32)
        self.q1pad = sb("q1pad", [128, 512], BF16)
        self.k1pad = sb("k1pad", [128, 512], BF16)
        self.qk1T = sb("qk1T", [128, 1024], BF16)
        self.v1_aug = sb("v1_aug", [128, H1 * 65], BF16)
        self.A1T = sb("A1T", [128, 512], BF16)
        self.y1_bf = sb("y1_bf", [128, 256], BF16)
        self.y1T = sb("y1T", [128, 256], BF16)
        self.dent4 = sb("dent4", [128, 4], F32)
        self.z4 = sb("z4", [128, 4], F32)
        self.S1 = sb("S1", [64, H1 * 65], F32)
        self.S1_bf = sb("S1_bf", [64, H1 * 65], BF16)

        self.stage = self.x_sb[0]
        self.stage_bf = self.y_bf
        self.amax8 = sb("amax8", [128, 8], F32)
        self.red_row = sb("red_row", [1, 128], F32)
        self.scal2 = sb("scal2", [1, 2], F32)
        self.rs_cols = sb("rs_cols", [128, 2], F32)

        self.tp_ps = ps("tp_ps", [128, 1024], BF16)
        self.out_ps = ps("out_ps", [128, 1024], F32)
        self.ws_ps = ps("ws_ps", [128, 2048], F32)
        self.sm_ps = ps("sm_ps", [128, 512], F32)

        self.sem_ld = self.stack.enter_context(nc.semaphore("s_ld"))
        self.sem_wld = self.stack.enter_context(nc.semaphore("s_wld"))
        self.sem_st = self.stack.enter_context(nc.semaphore("s_st"))
        self.sem_rdy = self.stack.enter_context(nc.semaphore("s_rdy"))
        self.sems = {
            "pe": self.stack.enter_context(nc.semaphore("s_pe")),
            "dve": self.stack.enter_context(nc.semaphore("s_dve")),
            "act": self.stack.enter_context(nc.semaphore("s_act")),
            "pool": self.stack.enter_context(nc.semaphore("s_pool")),
        }

    def dump(self, name, ap):
        if not self.dbg or self.cur_chunk > 0:
            return
        nc = self.nc
        if name not in self.dbg_tensors:
            sh = list(ap.shape)
            self.dbg_tensors[name] = nc.dram_tensor(
                "dbg_" + name, sh, F32, kind="ExternalOutput").ap()
        self.bar(pool=True)
        nc.gpsimd.dma_start(self.dbg_tensors[name], ap).then_inc(self.sem_wld, 16)
        self.wld += 16
        nc.gpsimd.wait_ge(self.sem_wld, self.wld)
        self.bar(pool=True)

    def wload(self, dst_ap, src_ap):
        nc = self.nc
        nc.sync.dma_start(dst_ap, src_ap).then_inc(self.sem_wld, 16)
        self.wld += 16
        nc.vector.wait_ge(self.sem_wld, self.wld)
        nc.scalar.wait_ge(self.sem_wld, self.wld)
        self.bar()

    def dma_load(self, dst_ap, src_ap, bufname):
        self.dep_wait("sp", reads=(), writes=(bufname,))
        self.nc.sync.dma_start(dst_ap, src_ap).then_inc(self.sem_wld, 16)
        self.wld += 16
        self.deps[bufname] = {"w": ("wld", self.wld), "r": {}}

    def mm(self, out_ap, lhsT, rhs, start, stop, tile_position=None):
        return self.nc.tensor.matmul(out_ap, lhsT, rhs, start=start, stop=stop,
                                     tile_position=tile_position)

    def stg(self, k):
        names = ["x0", "x1", "stg2", "stg3"]
        bufs = [self.x_sb[0][:, :], self.x_sb[1][:, :],
                self.sq[:, 0:1024], self.sq[:, 1024:2048]]
        return names[k % 4], bufs[k % 4]

    def sbfs(self, k):
        return (("sbf0", self.y_bf) if k % 2 == 0 else ("sbf1", self.yT))

    def prep_wT(self, w_ap, R, Dc, dst, dname):
        nc = self.nc
        nblk = Dc // 128
        for g in range(R // 128):
            sname, st = self.stg(self.stg_i)
            bname, sbf = self.sbfs(self.stg_i)
            self.stg_i += 1
            self.dma_load(st[:, 0:Dc], w_ap[g * 128:(g + 1) * 128, :], sname)
            self.g("dve", [sname], [bname], lambda st=st, sbf=sbf, Dc=Dc:
                   nc.vector.tensor_copy(sbf[:, 0:Dc], st[:, 0:Dc]))
            use_ws = (self.stg_i % 2 == 1)
            tgt = self.ws_ps[:, 0:512].bitcast(BF16) if use_ws else self.tp_ps
            tname = "wsA" if use_ws else "tp"

            def _t(sbf=sbf, tgt=tgt, nblk=nblk):
                for j in range(nblk):
                    last = nc.tensor.transpose(tgt[:, j * 128:(j + 1) * 128],
                                               sbf[:, j * 128:(j + 1) * 128],
                                               self.idn_bf[:, :])
                return last
            self.g("pe", [bname], [tname], _t)
            eng = "act" if self.stg_i % 2 == 0 else "dve"
            h = nc.scalar if eng == "act" else nc.vector

            def _e(h=h, tgt=tgt, dst=dst, nblk=nblk, g=g, R=R):
                srcv = tgt[:, 0:nblk * 128].rearrange("p (j r) -> p j r", j=nblk)
                dstv = dst[:, :].rearrange("p (j r) -> p j r", j=nblk)[:, :, g * 128:(g + 1) * 128]
                if h is nc.vector:
                    return h.tensor_copy(dstv, srcv)
                return h.activation(out=dstv, in_=srcv, func=AF.Copy)
            self.g(eng, [tname], [dname], _e)

    def quant_scale_pass(self, w_ap, mode):
        nc = self.nc
        op = ALU.max if mode == "int4" else ALU.add
        for g in range(8):
            sname, st = self.stg(self.stg_i)
            self.stg_i += 1
            self.dma_load(st[:, :], w_ap[g * 128:(g + 1) * 128, :], sname)
            self.g("dve", [sname], ["amax8"], lambda st=st, g=g:
                   nc.vector.tensor_reduce(out=self.amax8[:, g:g + 1], in_=st[:, :],
                                           axis=AX.X, op=op,
                                           apply_absolute_value=True))
        self.g("dve", ["amax8"], ["mcol"], lambda: nc.vector.tensor_reduce(
            out=self.m_col[:, :], in_=self.amax8[:, :], axis=AX.X, op=op),
            drain=True)
        self.g("pe", ["mcol"], ["sm"], lambda: nc.tensor.transpose(
            self.sm_ps[0:1, 0:128], self.m_col[:, :], self.idn_f[:, :]))
        self.g("dve", ["sm"], ["red_row"], lambda: nc.vector.tensor_copy(
            self.red_row[:, :], self.sm_ps[0:1, 0:128]))
        self.g("dve", ["red_row"], ["scal2"], lambda: nc.vector.tensor_reduce(
            out=self.scal2[0:1, 0:1], in_=self.red_row[:, :], axis=AX.X, op=op),
            drain=True)
        if mode == "int4":
            fn = lambda: nc.vector.tensor_scalar(
                out=self.scal2[0:1, 0:1], in0=self.scal2[0:1, 0:1],
                scalar1=float(1.0 / 7.0), scalar2=None, op0=ALU.mult)
        else:
            fn = lambda: nc.vector.tensor_scalar(
                out=self.scal2[0:1, 0:1], in0=self.scal2[0:1, 0:1],
                scalar1=float(1.0 / 1048576.0), scalar2=float(1e-5),
                op0=ALU.mult, op1=ALU.max)
        self.g("dve", ["scal2"], ["scal2"], fn, drain=True)
        self.g("dve", ["scal2"], ["scal2"], lambda: nc.vector.reciprocal(
            self.scal2[0:1, 1:2], self.scal2[0:1, 0:1]), drain=True)
        self.g("pe", ["scal2"], ["sm"], lambda: self.mm(
            self.sm_ps[:, 0:2], self.ones_f[:, :], self.scal2[0:1, 0:2],
            start=True, stop=True))
        self.g("dve", ["sm"], ["rs"], lambda: nc.vector.tensor_copy(
            self.rs_cols[:, :], self.sm_ps[:, 0:2]))

    def quant_tile(self, st, sname, dst_bf_ap, dname, qmin, qmax):
        nc = self.nc

        def _q():
            s = st
            nc.vector.tensor_scalar(out=s, in0=s, scalar1=self.rs_cols[:, 1:2],
                                    scalar2=None, op0=ALU.mult)
            nc.vector.drain()
            nc.vector.tensor_scalar(out=s, in0=s, scalar1=MAGIC, scalar2=MAGIC,
                                    op0=ALU.add, op1=ALU.subtract)
            nc.vector.drain()
            nc.vector.tensor_scalar(out=s, in0=s, scalar1=float(qmin),
                                    scalar2=float(qmax), op0=ALU.max, op1=ALU.min)
            nc.vector.drain()
            return nc.vector.tensor_scalar(out=dst_bf_ap, in0=s,
                                           scalar1=self.rs_cols[:, 0:1],
                                           scalar2=None, op0=ALU.mult)
        self.g("dve", [sname, "rs"], [sname, dname], _q, drain=True)

    def prep(self):
        nc = self.nc
        self.stg_i = 0
        self.wload(self.idn_f[:, :], self.i_idn)
        self.wload(self.maskT[:, :], self.i_mask)
        self.wload(self.mask05[:, :], self.i_mask05)
        nc.vector.tensor_copy(self.idn_bf[:, :], self.idn_f[:, :])
        nc.vector.memset(self.ones_bf[:, :], 1.0)
        nc.vector.memset(self.ones_f[:, :], 1.0)
        for col, val in ((0, 1e-5), (1, -0.5), (2, SQ5), (3, 0.5), (4, QS),
                         (5, 1.0 / 1024.0)):
            nc.vector.memset(self.consts[:, col:col + 1], float(val))
        nc.vector.memset(self.ws_ps[:, :], 0.0)
        nc.vector.memset(self.out_ps[:, :], 0.0)
        nc.vector.memset(self.sm_ps[:, :], 0.0)
        nc.vector.memset(self.qpad[:, :], 0.0)
        nc.vector.memset(self.kpad[:, :], 0.0)
        nc.vector.memset(self.v_aug[:, :], 0.0)
        nc.vector.memset(self.v1_aug[:, :], 0.0)
        nc.vector.memset(self.q1pad[:, :], 0.0)
        nc.vector.memset(self.k1pad[:, :], 0.0)
        for t in (self.S_low, self.S_q[0], self.S_q[1], self.S1):
            nc.vector.memset(t[:, :], 0.0)
        for t in (self.S_low_bf, self.S_q_bf[0], self.S_q_bf[1], self.S1_bf):
            nc.vector.memset(t[:, :], 0.0)
        self.bar()
        nc.vector.memset(
            self.qpad[:, :].rearrange("p (h c) -> p h c", c=128)[:, :, 16:17], 1.0)
        nc.vector.memset(
            self.kpad[:, :].rearrange("p (h c) -> p h c", c=128)[:, :, 16:17], 1.0)
        nc.vector.memset(
            self.v_aug[:, :].rearrange("p (h c) -> p h c", c=65)[:, :, 64:65], 1.0)
        nc.vector.memset(
            self.v1_aug[:, :].rearrange("p (h c) -> p h c", c=65)[:, :, 64:65], 1.0)
        self.bar()
        self.reset_deps()

        self.prep_wT(self.i_wq0, 256, 1024, self.wqT, "WQT")
        self.prep_wT(self.i_wk0, 256, 1024, self.wkT, "WKT")
        self.prep_wT(self.i_wv0, 1024, 1024, self.wvT, "WVT")
        self.prep_wT(self.i_wo0, 1024, 1024, self.woT, "WOT")
        self.prep_wT(self.i_wq1, 256, 1024, self.wq1T, "WQ1T")
        self.prep_wT(self.i_wk1, 256, 1024, self.wk1T, "WK1T")
        self.prep_wT(self.i_wv1, 256, 1024, self.wv1T, "WV1T")
        self.prep_wT(self.i_wo1, 1024, 256, self.wo1T, "WO1T")

        self.quant_scale_pass(self.i_ker, "int4")
        for g in range(8):
            sname, st = self.stg(self.stg_i)
            self.stg_i += 1
            self.dma_load(st[:, :], self.i_ker[g * 128:(g + 1) * 128, :], sname)
            self.quant_tile(st[:, :], sname,
                            self.kerq[:, g * 1024:(g + 1) * 1024], "KERQ",
                            -8.0, 7.0)

        self.quant_scale_pass(self.i_w1, "tern")
        for g in range(8):
            sname, st = self.stg(self.stg_i)
            bname, sbf = self.sbfs(self.stg_i)
            self.stg_i += 1
            self.dma_load(st[:, :], self.i_w1[g * 128:(g + 1) * 128, :], sname)
            self.quant_tile(st[:, :], sname, sbf[:, :], bname, -1.0, 1.0)
            use_ws = (self.stg_i % 2 == 1)
            tgt = self.ws_ps[:, 0:512].bitcast(BF16) if use_ws else self.tp_ps
            tname = "wsA" if use_ws else "tp"

            def _t(sbf=sbf, tgt=tgt):
                for j in range(8):
                    last = nc.tensor.transpose(tgt[:, j * 128:(j + 1) * 128],
                                               sbf[:, j * 128:(j + 1) * 128],
                                               self.idn_bf[:, :])
                return last
            self.g("pe", [bname], [tname], _t)
            eng = "act" if self.stg_i % 2 == 0 else "dve"
            h = nc.scalar if eng == "act" else nc.vector

            def _e(h=h, tgt=tgt, g=g):
                srcv = tgt[:, :].rearrange("p (j r) -> p j r", j=8)
                dstv = self.w1T[:, :].rearrange("p (j r) -> p j r", j=8)[:, :, g * 128:(g + 1) * 128]
                if h is nc.vector:
                    return h.tensor_copy(dstv, srcv)
                return h.activation(out=dstv, in_=srcv, func=AF.Copy)
            self.g(eng, [tname], ["W1T"], _e)

        self.bar()
        a_src = self.i_a.rearrange("(g p) r -> p g r", p=128)
        self.wload(self.x_sb[0][:, 0:128].rearrange("p (g r) -> p g r", g=8), a_src)
        nc.vector.tensor_copy(self.a_sb[:, :], self.x_sb[0][:, 0:128])
        self.wload(self.x_sb[0][0:16, 0:1024], self.i_b)
        nc.vector.tensor_copy(self.b_sb[:, :], self.x_sb[0][0:16, 0:1024])
        self.wload(self.x_sb[0][0:1, 0:1024], self.i_bias)
        nc.vector.tensor_copy(self.bias_bf[:, :], self.x_sb[0][0:1, 0:1024])
        self.bar()
        self.cur_chunk = 0
        self.dump("kerq0", self.kerq[:, 0:1024])
        self.dump("wqT0", self.wqT[:, 0:256])
        self.dump("wvT0", self.wvT[:, 0:1024])
        self.dump("w1T0", self.w1T[:, 0:1024])
        self.dump("wo1T0", self.wo1T[:, 0:1024])
        self.dump("a_sb", self.a_sb[:, :])
        self.dump("b_sb", self.b_sb[:, :])
        self.dump("bias_bf", self.bias_bf[:, :])
        self.dump("idn_bf", self.idn_bf[:, :])
        self.cur_chunk = -1

    def layernorm(self, x_ap, xbuf):
        nc = self.nc
        C1024 = float(1.0 / 1024.0)
        self.g("dve", [xbuf], ["mcol"], lambda: nc.vector.tensor_reduce(
            out=self.m_col[:, :], in_=x_ap, axis=AX.X, op=ALU.add))
        self.g("act", [xbuf], ["sqA", "s2col"], lambda: nc.scalar.activation(
            out=self.sq[:, 0:1024], in_=x_ap, func=AF.Square,
            accum_out=self.s2_col[:, :]))

        def _v0():
            nc.vector.scalar_tensor_tensor(
                out=self.v_col[:, :], in0=self.m_col[:, :], scalar=-C1024,
                in1=self.m_col[:, :], op0=ALU.mult, op1=ALU.mult)
            nc.vector.drain()
            return nc.vector.tensor_scalar(
                out=self.t_col[:, :], in0=self.m_col[:, :], scalar1=C1024,
                scalar2=None, op0=ALU.mult)
        self.g("dve", ["mcol"], ["mcol"], _v0, drain=True)
        self.g("dve", ["mcol", "s2col"], ["mcol"], lambda: nc.vector.tensor_scalar(
            out=self.v_col[:, :], in0=self.v_col[:, :],
            scalar1=self.s2_col[:, 0:1], scalar2=C1024,
            op0=ALU.add, op1=ALU.mult), drain=True)

        def _rstd():
            nc.scalar.activation(out=self.r_col[:, :], in_=self.v_col[:, :],
                                 func=AF.Ln, bias=self.consts[:, 0:1])
            nc.scalar.drain()
            return nc.scalar.activation(out=self.r_col[:, :], in_=self.r_col[:, :],
                                        func=AF.Exp, scale=self.consts[:, 1:2])
        self.g("act", ["mcol"], ["rcol"], _rstd)
        self.g("dve", [xbuf, "mcol", "rcol"], ["xn"], lambda: nc.vector.tensor_scalar(
            out=self.xn[:, :], in0=x_ap, scalar1=self.t_col[:, 0:1],
            scalar2=self.r_col[:, 0:1], op0=ALU.subtract, op1=ALU.mult),
            drain=True)

    def xn_transpose(self):
        nc = self.nc

        def _t():
            for j in range(8):
                last = nc.tensor.transpose(self.tp_ps[:, j * 128:(j + 1) * 128],
                                           self.xn[:, j * 128:(j + 1) * 128],
                                           self.idn_bf[:, :])
            return last
        self.g("pe", ["xn"], ["tp"], _t)
        self.g("act", ["tp"], ["xnT"], lambda: nc.scalar.activation(
            out=self.xnT[:, :], in_=self.tp_ps[:, :], func=AF.Copy))

    def chunk_body(self, x_ap, xbuf):
        nc = self.nc
        WSA = ["wsA"]
        WSB = ["wsB"]
        WS = ["wsA", "wsB"]

        self.layernorm(x_ap, xbuf)
        self.dump("l0_xn", self.xn[:, :])
        self.xn_transpose()
        self.dump("l0_xnT", self.xnT[:, :])
        pend, self.pending = self.pending, []
        for f in pend:
            f()

        def _proj():
            for g_ in range(8):
                lhs = self.xnT[:, g_ * 128:(g_ + 1) * 128]
                self.mm(self.ws_ps[:, 0:256], lhs, self.wqT[:, g_ * 256:(g_ + 1) * 256],
                        start=(g_ == 0), stop=False)
                self.mm(self.ws_ps[:, 256:512], lhs, self.wkT[:, g_ * 256:(g_ + 1) * 256],
                        start=False, stop=(g_ == 7))
                self.mm(self.ws_ps[:, 512:1024], lhs,
                        self.wvT[:, g_ * 1024:g_ * 1024 + 512], start=(g_ == 0), stop=(g_ == 7))
                last = self.mm(self.ws_ps[:, 1024:1536], lhs,
                               self.wvT[:, g_ * 1024 + 512:(g_ + 1) * 1024],
                               start=(g_ == 0), stop=(g_ == 7))
            return last
        self.g("pe", ["xnT"], WS, _proj)

        def _ffn():
            self.mm(self.out_ps[:, 0:512], self.ones_bf[:, :], self.bias_bf[:, 0:512],
                    start=True, stop=False)
            self.mm(self.out_ps[:, 512:1024], self.ones_bf[:, :], self.bias_bf[:, 512:1024],
                    start=True, stop=False)
            for g_ in range(8):
                lhs = self.xnT[:, g_ * 128:(g_ + 1) * 128]
                self.mm(self.out_ps[:, 0:512], lhs, self.kerq[:, g_ * 1024:g_ * 1024 + 512],
                        start=False, stop=False)
                last = self.mm(self.out_ps[:, 512:1024], lhs,
                               self.kerq[:, g_ * 1024 + 512:(g_ + 1) * 1024],
                               start=False, stop=False)
            return last
        self.g("pe", ["xnT"], ["out"], _ffn)

        def _ut():
            for g_ in range(8):
                last = self.mm(self.sm_ps[0:16, 0:128], self.a_sb[:, g_ * 16:(g_ + 1) * 16],
                               self.xnT[:, g_ * 128:(g_ + 1) * 128],
                               start=(g_ == 0), stop=(g_ == 7))
            return last
        self.g("pe", ["xnT"], ["sm"], _ut)

        qv = self.ws_ps[:, 0:256].rearrange("p (h f) -> p h f", h=16)
        kv = self.ws_ps[:, 256:512].rearrange("p (h f) -> p h f", h=16)
        qpadv = self.qpad[:, :].rearrange("p (h c) -> p h c", c=128)[:, :, 0:16]
        kpadv = self.kpad[:, :].rearrange("p (h c) -> p h c", c=128)[:, :, 0:16]

        def _evac():
            nc.scalar.activation(out=qpadv, in_=qv, func=AF.Copy, scale=self.consts[:, 3:4])
            nc.scalar.activation(out=kpadv, in_=kv, func=AF.Copy, scale=self.consts[:, 3:4])
            nc.scalar.activation(out=self.q_s[:, :].rearrange("p (h f) -> p h f", h=16),
                                 in_=qv, func=AF.Copy, scale=self.consts[:, 4:5])
            nc.scalar.activation(out=self.k_s[:, :].rearrange("p (h f) -> p h f", h=16),
                                 in_=kv, func=AF.Copy, scale=self.consts[:, 4:5])
            vaugv = self.v_aug[:, :].rearrange("p (h c) -> p h c", c=65)[:, :, 0:64]
            return nc.scalar.activation(
                out=vaugv,
                in_=self.ws_ps[:, 512:1536].rearrange("p (h e) -> p h e", h=16),
                func=AF.Copy)
        self.g("act", WS, ["qpad", "kpad", "q_s", "k_s", "v_aug"], _evac)
        self.g("dve", ["sm"], ["uT"], lambda: nc.vector.tensor_copy(
            self.uT[:, :], self.sm_ps[0:16, 0:128]))
        self.dump("l0_qpad", self.qpad[:, :])
        self.dump("l0_q_s", self.q_s[:, :])
        self.dump("l0_v_aug", self.v_aug[:, :])
        self.dump("l0_uT", self.uT[:, :])

        def _lora():
            self.mm(self.out_ps[:, 0:512], self.uT[0:16, :], self.b_sb[:, 0:512],
                    start=False, stop=False)
            return self.mm(self.out_ps[:, 512:1024], self.uT[0:16, :], self.b_sb[:, 512:1024],
                           start=False, stop=False)
        self.g("pe", ["uT"], ["out"], _lora)

        qs_v = self.q_s[:, :].rearrange("p (h f) -> p h f", h=16)
        ks_v = self.k_s[:, :].rearrange("p (h f) -> p h f", h=16)
        qf_v = self.qf[:, :].rearrange("p (h i j) -> p h i j", h=16, i=16)
        kf_v = self.kf[:, :].rearrange("p (h i j) -> p h i j", h=16, i=16)

        for r in range(8):
            if r == 4:
                self.g("dve", ["q_s", "qpad"], ["qf"], lambda: nc.vector.tensor_tensor(
                    out=qf_v,
                    in0=qs_v.unsqueeze(3).broadcast_to([128, 16, 16, 16]),
                    in1=qpadv.unsqueeze(2).broadcast_to([128, 16, 16, 16]),
                    op=ALU.mult))
                self.dump("l0_qf", self.qf[:, :])
            if r < 2:
                srct, src_name, sc0 = self.qpad, "qpad", r * 1024
            elif r < 4:
                srct, src_name, sc0 = self.kpad, "kpad", (r - 2) * 1024
            else:
                srct, src_name, sc0 = self.qf, "qf", (r - 4) * 1024
            use_ws = (r % 2 == 1)
            tgt = self.ws_ps[:, 0:512].bitcast(BF16) if use_ws else self.tp_ps
            tname = "wsA" if use_ws else "tp"
            if r < 4:
                dname, dst, dc0 = "qkT", self.qkT, r * 1024
            else:
                dname, dst, dc0 = "qfT", self.qfT, (r - 4) * 1024

            def _t(srct=srct, sc0=sc0, tgt=tgt):
                for j in range(8):
                    last = nc.tensor.transpose(tgt[:, j * 128:(j + 1) * 128],
                                               srct[:, sc0 + j * 128:sc0 + (j + 1) * 128],
                                               self.idn_bf[:, :])
                return last
            self.g("pe", [src_name], [tname], _t)
            eng = "act" if r % 2 == 0 else "dve"
            h = nc.scalar if eng == "act" else nc.vector

            def _e(h=h, dst=dst, dc0=dc0, tgt=tgt):
                if h is nc.vector:
                    return h.tensor_copy(dst[:, dc0:dc0 + 1024], tgt[:, 0:1024])
                return h.activation(out=dst[:, dc0:dc0 + 1024], in_=tgt[:, 0:1024],
                                    func=AF.Copy)
            self.g(eng, [tname], [dname], _e)
        self.dump("l0_qkT", self.qkT[:, :])
        self.dump("l0_qfT", self.qfT[:, :])

        for half in range(2):
            hs = range(8 * half, 8 * half + 8)
            wsn = WSA if half == 0 else WSB

            def _p(hs=hs):
                for h in hs:
                    lhs = self.qkT[0:16, 2048 + h * 128:2048 + (h + 1) * 128]
                    rhs = self.qkT[0:16, h * 128:(h + 1) * 128]
                    last = self.mm(self.ws_ps[:, h * 128:(h + 1) * 128], lhs, rhs,
                                   start=(h % 4 == 0), stop=(h % 4 == 3))
                return last
            self.g("pe", ["qkT"], wsn, _p)
            c0 = half * 1024
            self.g("act", wsn, ["sqA" if half == 0 else "sqB"], lambda c0=c0: nc.scalar.activation(
                out=self.sq[:, c0:c0 + 1024], in_=self.ws_ps[:, c0:c0 + 1024],
                func=AF.Square, scale=self.consts[:, 2:3], bias=self.consts[:, 2:3]))
            self.g("pool", ["sqA" if half == 0 else "sqB"], ["AT%d" % half],
                   lambda c0=c0: nc.gpsimd.tensor_tensor(
                out=self.AT[:, c0:c0 + 1024].rearrange("p (h t) -> p h t", h=8),
                in0=self.sq[:, c0:c0 + 1024].rearrange("p (h t) -> p h t", h=8),
                in1=self.maskT[:, :].unsqueeze(1).broadcast_to([128, 8, 128]),
                op=ALU.mult))

        self.g("pool", ["k_s", "kpad"], ["kf"], lambda: nc.gpsimd.tensor_tensor(
            out=kf_v,
            in0=ks_v.unsqueeze(3).broadcast_to([128, 16, 16, 16]),
            in1=kpadv.unsqueeze(2).broadcast_to([128, 16, 16, 16]), op=ALU.mult))
        self.dump("l0_kf", self.kf[:, :])
        self.dump("l0_AT", self.AT[:, :])

        for half in range(2):
            hs = range(8 * half, 8 * half + 8)
            wsn = WSA if half == 0 else WSB

            def _y(hs=hs):
                for h in hs:
                    yc = h * 128
                    self.mm(self.ws_ps[:, yc:yc + 65],
                            self.AT[:, h * 128:(h + 1) * 128],
                            self.v_aug[:, h * 65:(h + 1) * 65],
                            start=(h % 4 == 0), stop=False)
                    self.mm(self.ws_ps[:, yc:yc + 65],
                            self.mask05[:, :],
                            self.v_aug[:, h * 65:(h + 1) * 65],
                            start=False, stop=False)
                    self.mm(self.ws_ps[:, yc:yc + 65],
                            self.qkT[0:17, h * 128:(h + 1) * 128],
                            self.S_low_bf[0:17, colh(h):colh(h) + 65],
                            start=False, stop=False)
                    for j in range(2):
                        last = self.mm(self.ws_ps[:, yc:yc + 65],
                                       self.qfT[:, (2 * h + j) * 128:(2 * h + j + 1) * 128],
                                       self.S_q_bf[j][:, colh(h):colh(h) + 65],
                                       start=False, stop=(h % 4 == 3 and j == 1))
                return last
            self.g("pe", ["AT%d" % half, "v_aug", "qkT", "qfT",
                          "Slowbf", "Sq0bf", "Sq1bf"], wsn, _y)

            denv = self.ws_ps[:, :].rearrange(
                "p (h c) -> p h c", c=128)[:, 8 * half:8 * half + 8, 64:65]

            def _z(half=half, denv=denv):
                nc.vector.tensor_scalar(out=self.dent[:, 8 * half:8 * half + 8],
                                        in0=denv.squeeze(2),
                                        scalar1=float(1e-12), scalar2=None, op0=ALU.add)
                nc.vector.drain()
                return nc.vector.reciprocal(self.z16[:, 8 * half:8 * half + 8],
                                            self.dent[:, 8 * half:8 * half + 8])
            self.g("dve", wsn, ["z%d" % half], _z)

            def _zy(half=half):
                return nc.vector.tensor_tensor(
                    out=self.y_bf[:, half * 512:(half + 1) * 512].rearrange(
                        "p (h e) -> p h e", h=8),
                    in0=self.ws_ps[:, :].rearrange(
                        "p (h c) -> p h c", c=128)[:, 8 * half:8 * half + 8, 0:64],
                    in1=self.z16[:, 8 * half:8 * half + 8].unsqueeze(2).broadcast_to(
                        [128, 8, 64]), op=ALU.mult)
            self.g("dve", wsn + ["z%d" % half], ["ybf%d" % half], _zy, drain=True)
        self.dump("l0_dent", self.dent[:, :])
        self.dump("l0_ybf", self.y_bf[:, :])

        def _yt():
            for j in range(8):
                last = nc.tensor.transpose(self.tp_ps[:, j * 128:(j + 1) * 128],
                                           self.y_bf[:, j * 128:(j + 1) * 128],
                                           self.idn_bf[:, :])
            return last
        self.g("pe", ["ybf0", "ybf1"], ["tp"], _yt)
        self.g("act", ["tp"], ["yT"], lambda: nc.scalar.activation(
            out=self.yT[:, :], in_=self.tp_ps[:, :], func=AF.Copy))

        def _wo():
            for j in range(8):
                lhs = self.yT[:, j * 128:(j + 1) * 128]
                self.mm(self.out_ps[:, 0:512], lhs, self.woT[:, j * 1024:j * 1024 + 512],
                        start=False, stop=(j == 7))
                last = self.mm(self.out_ps[:, 512:1024], lhs,
                               self.woT[:, j * 1024 + 512:(j + 1) * 1024],
                               start=False, stop=(j == 7))
            return last
        self.g("pe", ["yT"], ["out"], _wo)

        self.g("dve", ["out", xbuf], [xbuf], lambda: nc.vector.tensor_tensor(
            out=x_ap, in0=x_ap, in1=self.out_ps[:, :], op=ALU.add))
        self.dump("l0_x", x_ap)

        def ds_round(rnd, defer_add=False):
            def _ds(rnd=rnd):
                for h in range(16):
                    c0 = colh(h)
                    if rnd == 0:
                        lhs = self.kpad[:, 128 * h:128 * h + 17]
                        out = self.ws_ps[0:17, c0:c0 + 65]
                    else:
                        j = rnd - 1
                        lhs = self.kf[:, (2 * h + j) * 128:(2 * h + j + 1) * 128]
                        out = self.ws_ps[:, c0:c0 + 65]
                    last = self.mm(out, lhs, self.v_aug[:, h * 65:(h + 1) * 65],
                                   start=h in (0, 7, 14), stop=h in (6, 13, 15))
                return last
            src_name = "kpad" if rnd == 0 else "kf"
            self.g("pe", [src_name, "v_aug"], WS, _ds)
            S = self.S_low if rnd == 0 else self.S_q[rnd - 1]
            sname = ["Slow", "Sq0", "Sq1"][rnd]
            rows = 17 if rnd == 0 else 128

            def _add(S=S, rows=rows, sname=sname):
                self.g("dve", WS + [sname], [sname], lambda:
                       nc.vector.tensor_tensor(out=S[0:rows, :], in0=S[0:rows, :],
                                               in1=self.ws_ps[0:rows, 0:S_W],
                                               op=ALU.add))
            if defer_add:
                self.pending.append(_add)
            else:
                _add()

        ds_round(0)
        self.layernorm(x_ap, xbuf)
        self.dump("l1_xn", self.xn[:, :])
        self.xn_transpose()

        def _ffn1():
            for g_ in range(8):
                lhs = self.xnT[:, g_ * 128:(g_ + 1) * 128]
                self.mm(self.out_ps[:, 0:512], lhs, self.w1T[:, g_ * 1024:g_ * 1024 + 512],
                        start=(g_ == 0), stop=False)
                last = self.mm(self.out_ps[:, 512:1024], lhs,
                               self.w1T[:, g_ * 1024 + 512:(g_ + 1) * 1024],
                               start=(g_ == 0), stop=False)
            return last
        self.g("pe", ["xnT"], ["out"], _ffn1)

        def _proj1():
            for g_ in range(8):
                lhs = self.xnT[:, g_ * 128:(g_ + 1) * 128]
                self.mm(self.ws_ps[:, 0:256], lhs, self.wq1T[:, g_ * 256:(g_ + 1) * 256],
                        start=(g_ == 0), stop=False)
                self.mm(self.ws_ps[:, 256:512], lhs, self.wk1T[:, g_ * 256:(g_ + 1) * 256],
                        start=False, stop=(g_ == 7))
                last = self.mm(self.ws_ps[:, 512:768], lhs,
                               self.wv1T[:, g_ * 256:(g_ + 1) * 256],
                               start=(g_ == 0), stop=(g_ == 7))
            return last
        self.g("pe", ["xnT"], WS, _proj1)

        e_qk = self.sq[:, 1024:1536]
        self.g("dve", WS, ["sqB"], lambda: nc.vector.tensor_scalar(
            out=e_qk, in0=self.ws_ps[:, 0:512], scalar1=0.0, scalar2=None,
            op0=ALU.min))
        self.g("act", ["sqB"], ["sqB"], lambda: nc.scalar.activation(
            out=e_qk, in_=e_qk, func=AF.Exp))
        self.g("dve", WS + ["sqB"], ["pqk"], lambda: nc.vector.scalar_tensor_tensor(
            out=self.p_qk[:, :], in0=self.ws_ps[:, 0:512], scalar=0.0, in1=e_qk,
            op0=ALU.max, op1=ALU.add))
        self.g("act", WS, ["v1aug"], lambda: nc.scalar.activation(
            out=self.v1_aug[:, :].rearrange("p (h c) -> p h c", c=65)[:, :, 0:64],
            in_=self.ws_ps[:, 512:768].rearrange("p (h e) -> p h e", h=4),
            func=AF.Copy))
        self.g("act", ["pqk"], ["sqA"], lambda: nc.scalar.activation(
            out=self.sq[:, 0:512], in_=self.p_qk[:, :], func=AF.Square))
        self.g("dve", ["sqA"], ["nrm"], lambda: nc.vector.tensor_reduce(
            out=self.nrm[:, :].rearrange("p (a b) -> p a b", b=1),
            in_=self.sq[:, 0:512].rearrange("p (a b) -> p a b", a=8),
            axis=AX.X, op=ALU.add))

        def _sqrt():
            nc.scalar.activation(out=self.rn[:, :], in_=self.nrm[:, :], func=AF.Ln)
            nc.scalar.drain()
            return nc.scalar.activation(out=self.rn[:, :], in_=self.rn[:, :],
                                        func=AF.Exp, scale=self.consts[:, 3:4])
        self.g("act", ["nrm"], ["rn"], _sqrt)

        def _rn():
            nc.vector.tensor_scalar(out=self.rn[:, :], in0=self.rn[:, :],
                                    scalar1=float(1e-6), scalar2=None, op0=ALU.add)
            nc.vector.drain()
            return nc.vector.reciprocal(self.rn[:, :], self.rn[:, :])
        self.g("dve", ["rn"], ["rn"], _rn)

        def _norm():
            nc.vector.tensor_tensor(
                out=self.q1pad[:, :].rearrange("p (h c) -> p h c", c=128)[:, :, 0:64],
                in0=self.p_qk[:, 0:256].rearrange("p (a b) -> p a b", a=4),
                in1=self.rn[:, 0:4].unsqueeze(2).broadcast_to([128, 4, 64]), op=ALU.mult)
            return nc.vector.tensor_tensor(
                out=self.k1pad[:, :].rearrange("p (h c) -> p h c", c=128)[:, :, 0:64],
                in0=self.p_qk[:, 256:512].rearrange("p (a b) -> p a b", a=4),
                in1=self.rn[:, 4:8].unsqueeze(2).broadcast_to([128, 4, 64]), op=ALU.mult)
        self.g("dve", ["pqk", "rn"], ["q1pad", "k1pad"], _norm, drain=True)
        self.dump("l1_pqk", self.p_qk[:, :])
        self.dump("l1_rn", self.rn[:, :])
        self.dump("l1_q1pad", self.q1pad[:, :])
        self.dump("l1_k1pad", self.k1pad[:, :])
        self.dump("l1_v1aug", self.v1_aug[:, :])

        ds_round(1)

        def _t1():
            for j in range(4):
                nc.tensor.transpose(self.tp_ps[:, j * 128:(j + 1) * 128],
                                    self.q1pad[:, j * 128:(j + 1) * 128], self.idn_bf[:, :])
                last = nc.tensor.transpose(self.tp_ps[:, 512 + j * 128:512 + (j + 1) * 128],
                                           self.k1pad[:, j * 128:(j + 1) * 128],
                                           self.idn_bf[:, :])
            return last
        self.g("pe", ["q1pad", "k1pad"], ["tp"], _t1)
        self.g("act", ["tp"], ["qk1T"], lambda: nc.scalar.activation(
            out=self.qk1T[:, :], in_=self.tp_ps[:, :], func=AF.Copy))

        def _p1():
            for h in range(4):
                lhs = self.qk1T[0:64, 512 + h * 128:512 + (h + 1) * 128]
                rhs = self.qk1T[0:64, h * 128:(h + 1) * 128]
                last = self.mm(self.sm_ps[:, h * 128:(h + 1) * 128], lhs, rhs,
                               start=(h == 0), stop=(h == 3))
            return last
        self.g("pe", ["qk1T"], ["sm"], _p1)
        self.g("dve", ["sm"], ["A1T"], lambda: nc.vector.tensor_tensor(
            out=self.A1T[:, :].rearrange("p (h t) -> p h t", h=4),
            in0=self.sm_ps[:, :].rearrange("p (h t) -> p h t", h=4),
            in1=self.maskT[:, :].unsqueeze(1).broadcast_to([128, 4, 128]), op=ALU.mult))
        self.dump("l1_A1T", self.A1T[:, :])

        def _y1():
            for h in range(4):
                yc = h * 128
                self.mm(self.sm_ps[:, yc:yc + 65], self.A1T[:, h * 128:(h + 1) * 128],
                        self.v1_aug[:, h * 65:(h + 1) * 65], start=(h == 0), stop=False)
                last = self.mm(self.sm_ps[:, yc:yc + 65],
                               self.qk1T[0:64, h * 128:(h + 1) * 128],
                               self.S1_bf[0:64, h * 65:(h + 1) * 65],
                               start=False, stop=(h == 3))
            return last
        self.g("pe", ["A1T", "v1aug", "qk1T", "S1bf"], ["sm"], _y1)

        denv1 = self.sm_ps[:, :].rearrange("p (h c) -> p h c", c=128)[:, :, 64:65]

        def _z1():
            nc.vector.tensor_scalar(out=self.dent4[:, :], in0=denv1.squeeze(2),
                                    scalar1=float(1e-6), scalar2=None, op0=ALU.add)
            nc.vector.drain()
            return nc.vector.reciprocal(self.z4[:, :], self.dent4[:, :])
        self.g("dve", ["sm"], ["z4"], _z1)
        self.g("dve", ["sm", "z4"], ["y1"], lambda: nc.vector.tensor_tensor(
            out=self.y1_bf[:, :].rearrange("p (h e) -> p h e", h=4),
            in0=self.sm_ps[:, :].rearrange("p (h c) -> p h c", c=128)[:, :, 0:64],
            in1=self.z4[:, :].unsqueeze(2).broadcast_to([128, 4, 64]), op=ALU.mult),
            drain=True)
        self.dump("l1_y1bf", self.y1_bf[:, :])

        def _y1t():
            for j in range(2):
                last = nc.tensor.transpose(self.tp_ps[:, j * 128:(j + 1) * 128],
                                           self.y1_bf[:, j * 128:(j + 1) * 128],
                                           self.idn_bf[:, :])
            return last
        self.g("pe", ["y1"], ["tp"], _y1t)
        self.g("act", ["tp"], ["y1T"], lambda: nc.scalar.activation(
            out=self.y1T[:, :], in_=self.tp_ps[:, 0:256], func=AF.Copy))

        def _wo1():
            for j in range(2):
                lhs = self.y1T[:, j * 128:(j + 1) * 128]
                self.mm(self.out_ps[:, 0:512], lhs, self.wo1T[:, j * 1024:j * 1024 + 512],
                        start=False, stop=(j == 1))
                last = self.mm(self.out_ps[:, 512:1024], lhs,
                               self.wo1T[:, j * 1024 + 512:(j + 1) * 1024],
                               start=False, stop=(j == 1))
            return last
        self.g("pe", ["y1T"], ["out"], _wo1)

        def _resid2():
            nc.vector.tensor_tensor(out=x_ap, in0=x_ap, in1=self.out_ps[:, :],
                                    op=ALU.add)
            return nc.vector.drain()
        self.g("dve", ["out", xbuf], [xbuf], _resid2)
        self.resid_mark = self.n["dve"]

        def _ds1():
            for h in range(4):
                last = self.mm(self.sm_ps[0:64, h * 65:(h + 1) * 65],
                               self.k1pad[:, h * 128:h * 128 + 64],
                               self.v1_aug[:, h * 65:(h + 1) * 65],
                               start=(h == 0), stop=(h == 3))
            return last
        self.g("pe", ["k1pad", "v1aug"], ["sm"], _ds1)
        self.pending.append(lambda: self.g(
            "dve", ["sm", "S1"], ["S1"], lambda: nc.vector.tensor_tensor(
                out=self.S1[:, :], in0=self.S1[:, :], in1=self.sm_ps[0:64, 0:260],
                op=ALU.add)))
        ds_round(2, defer_add=True)
        self.pending.append(lambda: self.g(
            "act", ["Slow"], ["Slowbf"], lambda: nc.scalar.activation(
                out=self.S_low_bf[0:17, :], in_=self.S_low[0:17, :], func=AF.Copy)))
        self.pending.append(lambda: self.g(
            "act", ["Sq0"], ["Sq0bf"], lambda: nc.scalar.activation(
                out=self.S_q_bf[0][:, :], in_=self.S_q[0][:, :], func=AF.Copy)))
        self.pending.append(lambda: self.g(
            "act", ["Sq1"], ["Sq1bf"], lambda: nc.scalar.activation(
                out=self.S_q_bf[1][:, :], in_=self.S_q[1][:, :], func=AF.Copy)))
        self.pending.append(lambda: self.g(
            "act", ["S1"], ["S1bf"], lambda: nc.scalar.activation(
                out=self.S1_bf[:, :], in_=self.S1[:, :], func=AF.Copy)))
        self.dump("l0_Slow", self.S_low_bf[:, :])
        self.dump("l0_Sq0", self.S_q_bf[0][:, :])
        self.dump("l0_Sq1", self.S_q_bf[1][:, :])
        self.dump("l1_S1", self.S1_bf[:, :])


    def build(self):
        nc = self.nc
        self.alloc()
        self.prep()
        self.bar()
        self.reset_deps()

        for c in range(min(2, self.nchunks)):
            nc.sync.dma_start(self.x_sb[c % 2][:, :],
                              self.i_x[c * C:(c + 1) * C, :]).then_inc(self.sem_ld, 16)
            self.ld += 16

        for c in range(self.nchunks):
            self.cur_chunk = c
            xbuf = "x%d" % (c % 2)
            buf = self.x_sb[c % 2][:, :]
            nc.vector.wait_ge(self.sem_ld, 16 * (c + 1))
            nc.scalar.wait_ge(self.sem_ld, 16 * (c + 1))
            self.chunk_body(buf, xbuf)
            nc.sync.wait_ge(self.sems["dve"], self.resid_mark)
            nc.sync.dma_start(self.o_out[c * C:(c + 1) * C, :], buf).then_inc(
                self.sem_st, 16)
            self.st += 16
            if c + 2 < self.nchunks:
                nc.sync.wait_ge(self.sem_st, 16 * (c + 1))
                nc.sync.dma_start(self.x_sb[c % 2][:, :],
                                  self.i_x[(c + 2) * C:(c + 3) * C, :]).then_inc(
                    self.sem_ld, 16)
                self.ld += 16

        nc.sync.wait_ge(self.sem_st, self.st)
        return nc


def build_program(nchunks=NCH, layers=(0, 1), dbg=False):
    return KB(nchunks=nchunks, layers=layers, dbg=dbg).build()


def make_in_map(inputs, b):
    f32 = np.float32
    return {
        "x": np.ascontiguousarray(np.asarray(inputs["L0_attn_hidden_states"])[b], dtype=f32),
        "wq0": np.ascontiguousarray(np.asarray(inputs["L0_attn_Wq"], dtype=f32)),
        "wk0": np.ascontiguousarray(np.asarray(inputs["L0_attn_Wk"], dtype=f32)),
        "wv0": np.ascontiguousarray(np.asarray(inputs["L0_attn_Wv"], dtype=f32)),
        "wo0": np.ascontiguousarray(np.asarray(inputs["L0_attn_Wo"], dtype=f32)),
        "ker": np.ascontiguousarray(np.asarray(inputs["L0_ffn_kernel"], dtype=f32)),
        "bias": np.ascontiguousarray(np.asarray(inputs["L0_ffn_bias"], dtype=f32)).reshape(1, 1024),
        "la": np.ascontiguousarray(np.asarray(inputs["L0_ffn_a"], dtype=f32)),
        "lb": np.ascontiguousarray(np.asarray(inputs["L0_ffn_b"], dtype=f32)),
        "wq1": np.ascontiguousarray(np.asarray(inputs["L1_attn_Wq"], dtype=f32)),
        "wk1": np.ascontiguousarray(np.asarray(inputs["L1_attn_Wk"], dtype=f32)),
        "wv1": np.ascontiguousarray(np.asarray(inputs["L1_attn_Wv"], dtype=f32)),
        "wo1": np.ascontiguousarray(np.asarray(inputs["L1_attn_Wo"], dtype=f32)),
        "w1": np.ascontiguousarray(np.asarray(inputs["L1_ffn_weight"], dtype=f32)),
        "idn": np.eye(128, dtype=f32),
        "maskT": np.triu(np.ones((128, 128), dtype=f32)),
        "mask05": np.triu(np.full((128, 128), 0.5, dtype=ml_dtypes.bfloat16)),
    }


def kernel(**inputs):
    nc = build_program()
    in_maps = [make_in_map(inputs, b) for b in range(8)]
    res = run_bass_kernel_spmd(nc, in_maps, core_ids=list(range(8)))
    return np.stack([r["out"] for r in res.results], axis=0).astype(np.float32)


# version 06
# speedup vs baseline: 1.6179x; 1.0012x over previous
import contextlib
import numpy as np
import ml_dtypes

import concourse.bass as bass
import concourse.mybir as mybir
from concourse.bass_utils import run_bass_kernel_spmd

F32 = mybir.dt.float32
BF16 = mybir.dt.bfloat16
AF = mybir.ActivationFunctionType
ALU = mybir.AluOpType
AX = mybir.AxisListType
ET = mybir.EngineType

MAGIC = float(1.5 * 2 ** 23)
SQ5 = 0.7071067811865476
QS = 0.35355339059327373

T_SEQ = 4096
D = 1024
C = 128
NCH = T_SEQ // C
H0, F0, V0 = 16, 16, 64
H1, D1 = 4, 64

def colh(h):
    return 65 * (h % 7) + 512 * (h // 7)

S_W = 1536


class KB:

    def __init__(self, nchunks=NCH, layers=(0, 1), dbg=False):
        self.nchunks = nchunks
        self.layers = layers
        self.dbg = dbg
        self.cur_chunk = -1
        self.dbg_tensors = {}
        self.nc = bass.Bass("TRN2", target_bir_lowering=False)
        self.stack = contextlib.ExitStack()
        self.ld = 0
        self.wld = 0
        self.st = 0
        self.n = {"pe": 0, "dve": 0, "act": 0, "pool": 0}
        self.deps = {}
        self.pending = []
        self.last_wait = {}

    def bar(self, pool=False):
        self.nc.all_engine_barrier()

    def eng_handle(self, eng):
        nc = self.nc
        return {"pe": nc.tensor, "dve": nc.vector, "act": nc.scalar,
                "pool": nc.gpsimd, "sp": nc.sync}[eng]

    def reset_deps(self):
        self.deps = {}
        self.last_wait = {}

    def dep_wait(self, eng, reads=(), writes=()):
        h = self.eng_handle(eng)
        need = {}
        for b in reads:
            w = self.deps.get(b, {}).get("w")
            if w and w[0] != eng:
                need[w[0]] = max(need.get(w[0], 0), w[1])
        for b in writes:
            d = self.deps.get(b, {})
            w = d.get("w")
            if w and w[0] != eng:
                need[w[0]] = max(need.get(w[0], 0), w[1])
            for re_, rc in d.get("r", {}).items():
                if re_ != eng:
                    need[re_] = max(need.get(re_, 0), rc)
        for prod, cnt in sorted(need.items()):
            key = (eng, prod)
            if self.last_wait.get(key, 0) < cnt:
                sem = self.sem_wld if prod == "wld" else self.sems[prod]
                h.wait_ge(sem, cnt)
                self.last_wait[key] = cnt

    def dep_done(self, inst, eng, reads=(), writes=()):
        self.n[eng] += 1
        inst.then_inc(self.sems[eng], 1)
        c = self.n[eng]
        for b in writes:
            self.deps[b] = {"w": (eng, c), "r": {}}
        for b in reads:
            d = self.deps.setdefault(b, {"w": None, "r": {}})
            d["r"][eng] = c

    def g(self, eng, reads, writes, emit, drain=False):
        self.dep_wait(eng, reads, writes)
        if drain:
            self.eng_handle(eng).drain()
        last = emit()
        self.dep_done(last, eng, reads, writes)
        return self.n[eng]

    def alloc(self):
        nc = self.nc
        sb = nc.alloc_sbuf_tensor
        ps = nc.alloc_psum_tensor

        self.i_x = nc.dram_tensor("x", [T_SEQ, D], F32, kind="ExternalInput").ap()
        self.i_wq0 = nc.dram_tensor("wq0", [256, 1024], F32, kind="ExternalInput").ap()
        self.i_wk0 = nc.dram_tensor("wk0", [256, 1024], F32, kind="ExternalInput").ap()
        self.i_wv0 = nc.dram_tensor("wv0", [1024, 1024], F32, kind="ExternalInput").ap()
        self.i_wo0 = nc.dram_tensor("wo0", [1024, 1024], F32, kind="ExternalInput").ap()
        self.i_ker = nc.dram_tensor("ker", [1024, 1024], F32, kind="ExternalInput").ap()
        self.i_bias = nc.dram_tensor("bias", [1, 1024], F32, kind="ExternalInput").ap()
        self.i_a = nc.dram_tensor("la", [1024, 16], F32, kind="ExternalInput").ap()
        self.i_b = nc.dram_tensor("lb", [16, 1024], F32, kind="ExternalInput").ap()
        self.i_wq1 = nc.dram_tensor("wq1", [256, 1024], F32, kind="ExternalInput").ap()
        self.i_wk1 = nc.dram_tensor("wk1", [256, 1024], F32, kind="ExternalInput").ap()
        self.i_wv1 = nc.dram_tensor("wv1", [256, 1024], F32, kind="ExternalInput").ap()
        self.i_wo1 = nc.dram_tensor("wo1", [1024, 256], F32, kind="ExternalInput").ap()
        self.i_w1 = nc.dram_tensor("w1", [1024, 1024], F32, kind="ExternalInput").ap()
        self.i_idn = nc.dram_tensor("idn", [128, 128], F32, kind="ExternalInput").ap()
        self.i_mask = nc.dram_tensor("maskT", [128, 128], F32, kind="ExternalInput").ap()
        self.i_mask05 = nc.dram_tensor("mask05", [128, 128], BF16, kind="ExternalInput").ap()
        self.o_out = nc.dram_tensor("out", [T_SEQ, D], F32, kind="ExternalOutput").ap()

        self.wqT = sb("wqT", [128, 2048], BF16)
        self.wkT = sb("wkT", [128, 2048], BF16)
        self.wvT = sb("wvT", [128, 8192], BF16)
        self.woT = sb("woT", [128, 8192], BF16)
        self.kerq = sb("kerq", [128, 8192], BF16)
        self.a_sb = sb("a_sb", [128, 128], BF16)
        self.b_sb = sb("b_sb", [16, 1024], BF16)
        self.bias_bf = sb("bias_bf", [1, 1024], BF16)
        self.wq1T = sb("wq1T", [128, 2048], BF16)
        self.wk1T = sb("wk1T", [128, 2048], BF16)
        self.wv1T = sb("wv1T", [128, 2048], BF16)
        self.wo1T = sb("wo1T", [128, 2048], BF16)
        self.w1T = sb("w1T", [128, 8192], BF16)

        self.idn_f = sb("idn_f", [128, 128], F32)
        self.idn_bf = sb("idn_bf", [128, 128], BF16)
        self.maskT = sb("maskT_sb", [128, 128], F32)
        self.mask05 = sb("mask05_sb", [128, 128], BF16)
        self.ones_bf = sb("ones_bf", [1, 128], BF16)
        self.ones_f = sb("ones_f", [1, 128], F32)
        self.consts = sb("consts", [128, 8], F32)

        self.x_sb = [sb("x_sb0", [128, D], F32), sb("x_sb1", [128, D], F32)]
        self.sq = sb("sq_scr", [128, 2048], F32)
        self.xn = sb("xn_bf", [128, D], BF16)
        self.xnT = sb("xnT_bf", [128, D], BF16)
        self.qpad = sb("qpad", [128, 2048], BF16)
        self.kpad = sb("kpad", [128, 2048], BF16)
        self.q_s = sb("q_s", [128, 256], BF16)
        self.k_s = sb("k_s", [128, 256], BF16)
        self.qkT = sb("qkT", [128, 4096], BF16)
        self.v_aug = sb("v_aug", [128, H0 * 65], BF16)
        self.uT = sb("uT_bf", [16, 128], BF16)
        self.qf = sb("qf", [128, 4096], BF16)
        self.kf = sb("kf", [128, 4096], BF16)
        self.qfT = sb("qfT", [128, 4096], BF16)
        self.AT = sb("AT", [128, 2048], BF16)
        self.y_bf = sb("y_bf", [128, 1024], BF16)
        self.yT = sb("yT", [128, 1024], BF16)
        self.m_col = sb("m_col", [128, 1], F32)
        self.s2_col = sb("s2_col", [128, 1], F32)
        self.t_col = sb("t_col", [128, 1], F32)
        self.v_col = sb("v_col", [128, 1], F32)
        self.r_col = sb("r_col", [128, 1], F32)
        self.dent = sb("dent", [128, 16], F32)
        self.z16 = sb("z16", [128, 16], F32)

        self.S_low = sb("S_low", [17, S_W], F32)
        self.S_low_bf = sb("S_low_bf", [17, S_W], BF16)
        self.S_q = [sb("S_q0", [128, S_W], F32), sb("S_q1", [128, S_W], F32)]
        self.S_q_bf = [sb("S_q0_bf", [128, S_W], BF16), sb("S_q1_bf", [128, S_W], BF16)]

        self.p_qk = sb("p_qk", [128, 512], F32)
        self.nrm = sb("nrm", [128, 8], F32)
        self.rn = sb("rn", [128, 8], F32)
        self.q1pad = sb("q1pad", [128, 512], BF16)
        self.k1pad = sb("k1pad", [128, 512], BF16)
        self.qk1T = sb("qk1T", [128, 1024], BF16)
        self.v1_aug = sb("v1_aug", [128, H1 * 65], BF16)
        self.A1T = sb("A1T", [128, 512], BF16)
        self.y1_bf = sb("y1_bf", [128, 256], BF16)
        self.y1T = sb("y1T", [128, 256], BF16)
        self.dent4 = sb("dent4", [128, 4], F32)
        self.z4 = sb("z4", [128, 4], F32)
        self.S1 = sb("S1", [64, H1 * 65], F32)
        self.S1_bf = sb("S1_bf", [64, H1 * 65], BF16)

        self.stage = self.x_sb[0]
        self.stage_bf = self.y_bf
        self.amax8 = sb("amax8", [128, 8], F32)
        self.red_row = sb("red_row", [1, 128], F32)
        self.scal2 = sb("scal2", [1, 2], F32)
        self.rs_cols = sb("rs_cols", [128, 2], F32)

        self.tp_ps = ps("tp_ps", [128, 1024], BF16)
        self.out_ps = ps("out_ps", [128, 1024], F32)
        self.ws_ps = ps("ws_ps", [128, 2048], F32)
        self.sm_ps = ps("sm_ps", [128, 512], F32)

        self.sem_ld = self.stack.enter_context(nc.semaphore("s_ld"))
        self.sem_wld = self.stack.enter_context(nc.semaphore("s_wld"))
        self.sem_st = self.stack.enter_context(nc.semaphore("s_st"))
        self.sem_rdy = self.stack.enter_context(nc.semaphore("s_rdy"))
        self.sems = {
            "pe": self.stack.enter_context(nc.semaphore("s_pe")),
            "dve": self.stack.enter_context(nc.semaphore("s_dve")),
            "act": self.stack.enter_context(nc.semaphore("s_act")),
            "pool": self.stack.enter_context(nc.semaphore("s_pool")),
        }

    def dump(self, name, ap):
        if not self.dbg or self.cur_chunk > 0:
            return
        nc = self.nc
        if name not in self.dbg_tensors:
            sh = list(ap.shape)
            self.dbg_tensors[name] = nc.dram_tensor(
                "dbg_" + name, sh, F32, kind="ExternalOutput").ap()
        self.bar(pool=True)
        nc.gpsimd.dma_start(self.dbg_tensors[name], ap).then_inc(self.sem_wld, 16)
        self.wld += 16
        nc.gpsimd.wait_ge(self.sem_wld, self.wld)
        self.bar(pool=True)

    def wload(self, dst_ap, src_ap):
        nc = self.nc
        nc.sync.dma_start(dst_ap, src_ap).then_inc(self.sem_wld, 16)
        self.wld += 16
        nc.vector.wait_ge(self.sem_wld, self.wld)
        nc.scalar.wait_ge(self.sem_wld, self.wld)
        self.bar()

    def dma_load(self, dst_ap, src_ap, bufname):
        self.dep_wait("sp", reads=(), writes=(bufname,))
        self.nc.sync.dma_start(dst_ap, src_ap).then_inc(self.sem_wld, 16)
        self.wld += 16
        self.deps[bufname] = {"w": ("wld", self.wld), "r": {}}

    def mm(self, out_ap, lhsT, rhs, start, stop, tile_position=None):
        return self.nc.tensor.matmul(out_ap, lhsT, rhs, start=start, stop=stop,
                                     tile_position=tile_position)

    def stg(self, k):
        names = ["x0", "x1", "stg2", "stg3"]
        bufs = [self.x_sb[0][:, :], self.x_sb[1][:, :],
                self.sq[:, 0:1024], self.sq[:, 1024:2048]]
        return names[k % 4], bufs[k % 4]

    def sbfs(self, k):
        return (("sbf0", self.y_bf) if k % 2 == 0 else ("sbf1", self.yT))

    def prep_wT(self, w_ap, R, Dc, dst, dname):
        nc = self.nc
        nblk = Dc // 128
        for g in range(R // 128):
            sname, st = self.stg(self.stg_i)
            bname, sbf = self.sbfs(self.stg_i)
            self.stg_i += 1
            self.dma_load(st[:, 0:Dc], w_ap[g * 128:(g + 1) * 128, :], sname)
            self.g("dve", [sname], [bname], lambda st=st, sbf=sbf, Dc=Dc:
                   nc.vector.tensor_copy(sbf[:, 0:Dc], st[:, 0:Dc]))
            use_ws = (self.stg_i % 2 == 1)
            tgt = self.ws_ps[:, 0:512].bitcast(BF16) if use_ws else self.tp_ps
            tname = "wsA" if use_ws else "tp"

            def _t(sbf=sbf, tgt=tgt, nblk=nblk):
                for j in range(nblk):
                    last = nc.tensor.transpose(tgt[:, j * 128:(j + 1) * 128],
                                               sbf[:, j * 128:(j + 1) * 128],
                                               self.idn_bf[:, :])
                return last
            self.g("pe", [bname], [tname], _t)
            eng = "act" if self.stg_i % 2 == 0 else "dve"
            h = nc.scalar if eng == "act" else nc.vector

            def _e(h=h, tgt=tgt, dst=dst, nblk=nblk, g=g, R=R):
                srcv = tgt[:, 0:nblk * 128].rearrange("p (j r) -> p j r", j=nblk)
                dstv = dst[:, :].rearrange("p (j r) -> p j r", j=nblk)[:, :, g * 128:(g + 1) * 128]
                if h is nc.vector:
                    return h.tensor_copy(dstv, srcv)
                return h.activation(out=dstv, in_=srcv, func=AF.Copy)
            self.g(eng, [tname], [dname], _e)

    def quant_scale_pass(self, w_ap, mode):
        nc = self.nc
        op = ALU.max if mode == "int4" else ALU.add
        for g in range(8):
            sname, st = self.stg(self.stg_i)
            self.stg_i += 1
            self.dma_load(st[:, :], w_ap[g * 128:(g + 1) * 128, :], sname)
            self.g("dve", [sname], ["amax8"], lambda st=st, g=g:
                   nc.vector.tensor_reduce(out=self.amax8[:, g:g + 1], in_=st[:, :],
                                           axis=AX.X, op=op,
                                           apply_absolute_value=True))
        self.g("dve", ["amax8"], ["mcol"], lambda: nc.vector.tensor_reduce(
            out=self.m_col[:, :], in_=self.amax8[:, :], axis=AX.X, op=op),
            drain=True)
        self.g("pe", ["mcol"], ["sm"], lambda: nc.tensor.transpose(
            self.sm_ps[0:1, 0:128], self.m_col[:, :], self.idn_f[:, :]))
        self.g("dve", ["sm"], ["red_row"], lambda: nc.vector.tensor_copy(
            self.red_row[:, :], self.sm_ps[0:1, 0:128]))
        self.g("dve", ["red_row"], ["scal2"], lambda: nc.vector.tensor_reduce(
            out=self.scal2[0:1, 0:1], in_=self.red_row[:, :], axis=AX.X, op=op),
            drain=True)
        if mode == "int4":
            fn = lambda: nc.vector.tensor_scalar(
                out=self.scal2[0:1, 0:1], in0=self.scal2[0:1, 0:1],
                scalar1=float(1.0 / 7.0), scalar2=None, op0=ALU.mult)
        else:
            fn = lambda: nc.vector.tensor_scalar(
                out=self.scal2[0:1, 0:1], in0=self.scal2[0:1, 0:1],
                scalar1=float(1.0 / 1048576.0), scalar2=float(1e-5),
                op0=ALU.mult, op1=ALU.max)
        self.g("dve", ["scal2"], ["scal2"], fn, drain=True)
        self.g("dve", ["scal2"], ["scal2"], lambda: nc.vector.reciprocal(
            self.scal2[0:1, 1:2], self.scal2[0:1, 0:1]), drain=True)
        self.g("pe", ["scal2"], ["sm"], lambda: self.mm(
            self.sm_ps[:, 0:2], self.ones_f[:, :], self.scal2[0:1, 0:2],
            start=True, stop=True))
        self.g("dve", ["sm"], ["rs"], lambda: nc.vector.tensor_copy(
            self.rs_cols[:, :], self.sm_ps[:, 0:2]))

    def quant_tile(self, st, sname, dst_bf_ap, dname, qmin, qmax):
        nc = self.nc

        def _q():
            s = st
            nc.vector.tensor_scalar(out=s, in0=s, scalar1=self.rs_cols[:, 1:2],
                                    scalar2=None, op0=ALU.mult)
            nc.vector.drain()
            nc.vector.tensor_scalar(out=s, in0=s, scalar1=MAGIC, scalar2=MAGIC,
                                    op0=ALU.add, op1=ALU.subtract)
            nc.vector.drain()
            nc.vector.tensor_scalar(out=s, in0=s, scalar1=float(qmin),
                                    scalar2=float(qmax), op0=ALU.max, op1=ALU.min)
            nc.vector.drain()
            return nc.vector.tensor_scalar(out=dst_bf_ap, in0=s,
                                           scalar1=self.rs_cols[:, 0:1],
                                           scalar2=None, op0=ALU.mult)
        self.g("dve", [sname, "rs"], [sname, dname], _q, drain=True)

    def prep(self):
        nc = self.nc
        self.stg_i = 0
        self.wload(self.idn_f[:, :], self.i_idn)
        self.wload(self.maskT[:, :], self.i_mask)
        self.wload(self.mask05[:, :], self.i_mask05)
        nc.vector.tensor_copy(self.idn_bf[:, :], self.idn_f[:, :])
        nc.vector.memset(self.ones_bf[:, :], 1.0)
        nc.vector.memset(self.ones_f[:, :], 1.0)
        for col, val in ((0, 1e-5), (1, -0.5), (2, SQ5), (3, 0.5), (4, QS),
                         (5, 1.0 / 1024.0)):
            nc.vector.memset(self.consts[:, col:col + 1], float(val))
        nc.vector.memset(self.ws_ps[:, :], 0.0)
        nc.vector.memset(self.out_ps[:, :], 0.0)
        nc.vector.memset(self.sm_ps[:, :], 0.0)
        nc.vector.memset(self.qpad[:, :], 0.0)
        nc.vector.memset(self.kpad[:, :], 0.0)
        nc.vector.memset(self.v_aug[:, :], 0.0)
        nc.vector.memset(self.v1_aug[:, :], 0.0)
        nc.vector.memset(self.q1pad[:, :], 0.0)
        nc.vector.memset(self.k1pad[:, :], 0.0)
        for t in (self.S_low, self.S_q[0], self.S_q[1], self.S1):
            nc.vector.memset(t[:, :], 0.0)
        for t in (self.S_low_bf, self.S_q_bf[0], self.S_q_bf[1], self.S1_bf):
            nc.vector.memset(t[:, :], 0.0)
        self.bar()
        nc.vector.memset(
            self.qpad[:, :].rearrange("p (h c) -> p h c", c=128)[:, :, 16:17], 1.0)
        nc.vector.memset(
            self.kpad[:, :].rearrange("p (h c) -> p h c", c=128)[:, :, 16:17], 1.0)
        nc.vector.memset(
            self.v_aug[:, :].rearrange("p (h c) -> p h c", c=65)[:, :, 64:65], 1.0)
        nc.vector.memset(
            self.v1_aug[:, :].rearrange("p (h c) -> p h c", c=65)[:, :, 64:65], 1.0)
        self.bar()
        self.reset_deps()

        self.prep_wT(self.i_wq0, 256, 1024, self.wqT, "WQT")
        self.prep_wT(self.i_wk0, 256, 1024, self.wkT, "WKT")
        self.prep_wT(self.i_wv0, 1024, 1024, self.wvT, "WVT")
        self.prep_wT(self.i_wo0, 1024, 1024, self.woT, "WOT")
        self.prep_wT(self.i_wq1, 256, 1024, self.wq1T, "WQ1T")
        self.prep_wT(self.i_wk1, 256, 1024, self.wk1T, "WK1T")
        self.prep_wT(self.i_wv1, 256, 1024, self.wv1T, "WV1T")
        self.prep_wT(self.i_wo1, 1024, 256, self.wo1T, "WO1T")

        self.quant_scale_pass(self.i_ker, "int4")
        for g in range(8):
            sname, st = self.stg(self.stg_i)
            self.stg_i += 1
            self.dma_load(st[:, :], self.i_ker[g * 128:(g + 1) * 128, :], sname)
            self.quant_tile(st[:, :], sname,
                            self.kerq[:, g * 1024:(g + 1) * 1024], "KERQ",
                            -8.0, 7.0)

        self.quant_scale_pass(self.i_w1, "tern")
        for g in range(8):
            sname, st = self.stg(self.stg_i)
            bname, sbf = self.sbfs(self.stg_i)
            self.stg_i += 1
            self.dma_load(st[:, :], self.i_w1[g * 128:(g + 1) * 128, :], sname)
            self.quant_tile(st[:, :], sname, sbf[:, :], bname, -1.0, 1.0)
            use_ws = (self.stg_i % 2 == 1)
            tgt = self.ws_ps[:, 0:512].bitcast(BF16) if use_ws else self.tp_ps
            tname = "wsA" if use_ws else "tp"

            def _t(sbf=sbf, tgt=tgt):
                for j in range(8):
                    last = nc.tensor.transpose(tgt[:, j * 128:(j + 1) * 128],
                                               sbf[:, j * 128:(j + 1) * 128],
                                               self.idn_bf[:, :])
                return last
            self.g("pe", [bname], [tname], _t)
            eng = "act" if self.stg_i % 2 == 0 else "dve"
            h = nc.scalar if eng == "act" else nc.vector

            def _e(h=h, tgt=tgt, g=g):
                srcv = tgt[:, :].rearrange("p (j r) -> p j r", j=8)
                dstv = self.w1T[:, :].rearrange("p (j r) -> p j r", j=8)[:, :, g * 128:(g + 1) * 128]
                if h is nc.vector:
                    return h.tensor_copy(dstv, srcv)
                return h.activation(out=dstv, in_=srcv, func=AF.Copy)
            self.g(eng, [tname], ["W1T"], _e)

        self.bar()
        a_src = self.i_a.rearrange("(g p) r -> p g r", p=128)
        self.wload(self.x_sb[0][:, 0:128].rearrange("p (g r) -> p g r", g=8), a_src)
        nc.vector.tensor_copy(self.a_sb[:, :], self.x_sb[0][:, 0:128])
        self.wload(self.x_sb[0][0:16, 0:1024], self.i_b)
        nc.vector.tensor_copy(self.b_sb[:, :], self.x_sb[0][0:16, 0:1024])
        self.wload(self.x_sb[0][0:1, 0:1024], self.i_bias)
        nc.vector.tensor_copy(self.bias_bf[:, :], self.x_sb[0][0:1, 0:1024])
        self.bar()
        self.cur_chunk = 0
        self.dump("kerq0", self.kerq[:, 0:1024])
        self.dump("wqT0", self.wqT[:, 0:256])
        self.dump("wvT0", self.wvT[:, 0:1024])
        self.dump("w1T0", self.w1T[:, 0:1024])
        self.dump("wo1T0", self.wo1T[:, 0:1024])
        self.dump("a_sb", self.a_sb[:, :])
        self.dump("b_sb", self.b_sb[:, :])
        self.dump("bias_bf", self.bias_bf[:, :])
        self.dump("idn_bf", self.idn_bf[:, :])
        self.cur_chunk = -1

    def layernorm(self, x_ap, xbuf):
        nc = self.nc
        C1024 = float(1.0 / 1024.0)
        self.g("dve", [xbuf], ["mcol"], lambda: nc.vector.tensor_reduce(
            out=self.m_col[:, :], in_=x_ap, axis=AX.X, op=ALU.add))
        self.g("act", [xbuf], ["sqA", "s2col"], lambda: nc.scalar.activation(
            out=self.sq[:, 0:1024], in_=x_ap, func=AF.Square,
            accum_out=self.s2_col[:, :]))

        def _v0():
            nc.vector.scalar_tensor_tensor(
                out=self.v_col[:, :], in0=self.m_col[:, :], scalar=-C1024,
                in1=self.m_col[:, :], op0=ALU.mult, op1=ALU.mult)
            return nc.vector.tensor_scalar(
                out=self.t_col[:, :], in0=self.m_col[:, :], scalar1=C1024,
                scalar2=None, op0=ALU.mult)
        self.g("dve", ["mcol"], ["mcol"], _v0, drain=True)
        self.g("dve", ["mcol", "s2col"], ["mcol"], lambda: nc.vector.tensor_scalar(
            out=self.v_col[:, :], in0=self.v_col[:, :],
            scalar1=self.s2_col[:, 0:1], scalar2=C1024,
            op0=ALU.add, op1=ALU.mult), drain=True)

        def _rstd():
            nc.scalar.activation(out=self.r_col[:, :], in_=self.v_col[:, :],
                                 func=AF.Ln, bias=self.consts[:, 0:1])
            nc.scalar.drain()
            return nc.scalar.activation(out=self.r_col[:, :], in_=self.r_col[:, :],
                                        func=AF.Exp, scale=self.consts[:, 1:2])
        self.g("act", ["mcol"], ["rcol"], _rstd)
        self.g("dve", [xbuf, "mcol", "rcol"], ["xn"], lambda: nc.vector.tensor_scalar(
            out=self.xn[:, :], in0=x_ap, scalar1=self.t_col[:, 0:1],
            scalar2=self.r_col[:, 0:1], op0=ALU.subtract, op1=ALU.mult),
            drain=True)

    def xn_transpose(self):
        nc = self.nc

        def _t():
            for j in range(8):
                last = nc.tensor.transpose(self.tp_ps[:, j * 128:(j + 1) * 128],
                                           self.xn[:, j * 128:(j + 1) * 128],
                                           self.idn_bf[:, :])
            return last
        self.g("pe", ["xn"], ["tp", "tp0", "tp1"], _t)
        self.g("act", ["tp"], ["xnT"], lambda: nc.scalar.activation(
            out=self.xnT[:, 0:512], in_=self.tp_ps[:, 0:512], func=AF.Copy))
        self.g("act", ["tp"], ["xnTb"], lambda: nc.scalar.activation(
            out=self.xnT[:, 512:1024], in_=self.tp_ps[:, 512:1024], func=AF.Copy))

    def chunk_body(self, x_ap, xbuf):
        nc = self.nc
        WSA = ["wsA"]
        WSB = ["wsB"]
        WS = ["wsA", "wsB"]

        self.layernorm(x_ap, xbuf)
        self.dump("l0_xn", self.xn[:, :])
        self.xn_transpose()
        self.dump("l0_xnT", self.xnT[:, :])
        pend, self.pending = self.pending, []
        for f in pend:
            f()

        def _proj():
            for g_ in range(8):
                lhs = self.xnT[:, g_ * 128:(g_ + 1) * 128]
                self.mm(self.ws_ps[:, 0:256], lhs, self.wqT[:, g_ * 256:(g_ + 1) * 256],
                        start=(g_ == 0), stop=False)
                self.mm(self.ws_ps[:, 256:512], lhs, self.wkT[:, g_ * 256:(g_ + 1) * 256],
                        start=False, stop=(g_ == 7))
                self.mm(self.ws_ps[:, 512:1024], lhs,
                        self.wvT[:, g_ * 1024:g_ * 1024 + 512], start=(g_ == 0), stop=(g_ == 7))
                last = self.mm(self.ws_ps[:, 1024:1536], lhs,
                               self.wvT[:, g_ * 1024 + 512:(g_ + 1) * 1024],
                               start=(g_ == 0), stop=(g_ == 7))
            return last
        self.g("pe", ["xnT", "xnTb"], WS, _proj)

        def _ffn():
            self.mm(self.out_ps[:, 0:512], self.ones_bf[:, :], self.bias_bf[:, 0:512],
                    start=True, stop=False)
            self.mm(self.out_ps[:, 512:1024], self.ones_bf[:, :], self.bias_bf[:, 512:1024],
                    start=True, stop=False)
            for g_ in range(8):
                lhs = self.xnT[:, g_ * 128:(g_ + 1) * 128]
                self.mm(self.out_ps[:, 0:512], lhs, self.kerq[:, g_ * 1024:g_ * 1024 + 512],
                        start=False, stop=False)
                last = self.mm(self.out_ps[:, 512:1024], lhs,
                               self.kerq[:, g_ * 1024 + 512:(g_ + 1) * 1024],
                               start=False, stop=False)
            return last
        self.g("pe", ["xnT", "xnTb"], ["out"], _ffn)

        def _ut():
            for g_ in range(8):
                last = self.mm(self.sm_ps[0:16, 0:128], self.a_sb[:, g_ * 16:(g_ + 1) * 16],
                               self.xnT[:, g_ * 128:(g_ + 1) * 128],
                               start=(g_ == 0), stop=(g_ == 7))
            return last
        self.g("pe", ["xnT", "xnTb"], ["sm"], _ut)

        qv = self.ws_ps[:, 0:256].rearrange("p (h f) -> p h f", h=16)
        kv = self.ws_ps[:, 256:512].rearrange("p (h f) -> p h f", h=16)
        qpadv = self.qpad[:, :].rearrange("p (h c) -> p h c", c=128)[:, :, 0:16]
        kpadv = self.kpad[:, :].rearrange("p (h c) -> p h c", c=128)[:, :, 0:16]

        def _evac():
            nc.scalar.activation(out=qpadv, in_=qv, func=AF.Copy, scale=self.consts[:, 3:4])
            nc.scalar.activation(out=kpadv, in_=kv, func=AF.Copy, scale=self.consts[:, 3:4])
            nc.scalar.activation(out=self.q_s[:, :].rearrange("p (h f) -> p h f", h=16),
                                 in_=qv, func=AF.Copy, scale=self.consts[:, 4:5])
            nc.scalar.activation(out=self.k_s[:, :].rearrange("p (h f) -> p h f", h=16),
                                 in_=kv, func=AF.Copy, scale=self.consts[:, 4:5])
            vaugv = self.v_aug[:, :].rearrange("p (h c) -> p h c", c=65)[:, :, 0:64]
            return nc.scalar.activation(
                out=vaugv,
                in_=self.ws_ps[:, 512:1536].rearrange("p (h e) -> p h e", h=16),
                func=AF.Copy)
        self.g("act", WS, ["qpad", "kpad", "q_s", "k_s", "v_aug"], _evac)
        self.g("dve", ["sm"], ["uT"], lambda: nc.vector.tensor_copy(
            self.uT[:, :], self.sm_ps[0:16, 0:128]))
        self.dump("l0_qpad", self.qpad[:, :])
        self.dump("l0_q_s", self.q_s[:, :])
        self.dump("l0_v_aug", self.v_aug[:, :])
        self.dump("l0_uT", self.uT[:, :])

        def _lora():
            self.mm(self.out_ps[:, 0:512], self.uT[0:16, :], self.b_sb[:, 0:512],
                    start=False, stop=False)
            return self.mm(self.out_ps[:, 512:1024], self.uT[0:16, :], self.b_sb[:, 512:1024],
                           start=False, stop=False)
        self.g("pe", ["uT"], ["out"], _lora)

        qs_v = self.q_s[:, :].rearrange("p (h f) -> p h f", h=16)
        ks_v = self.k_s[:, :].rearrange("p (h f) -> p h f", h=16)
        qf_v = self.qf[:, :].rearrange("p (h i j) -> p h i j", h=16, i=16)
        kf_v = self.kf[:, :].rearrange("p (h i j) -> p h i j", h=16, i=16)

        for r in range(8):
            if r == 4:
                self.g("dve", ["q_s", "qpad"], ["qf"], lambda: nc.vector.tensor_tensor(
                    out=qf_v,
                    in0=qs_v.unsqueeze(3).broadcast_to([128, 16, 16, 16]),
                    in1=qpadv.unsqueeze(2).broadcast_to([128, 16, 16, 16]),
                    op=ALU.mult))
                self.dump("l0_qf", self.qf[:, :])
            if r < 2:
                srct, src_name, sc0 = self.qpad, "qpad", r * 1024
            elif r < 4:
                srct, src_name, sc0 = self.kpad, "kpad", (r - 2) * 1024
            else:
                srct, src_name, sc0 = self.qf, "qf", (r - 4) * 1024
            use_ws = (r % 2 == 1)
            tgt = self.ws_ps[:, 0:512].bitcast(BF16) if use_ws else self.tp_ps
            tname = "wsA" if use_ws else "tp"
            if r < 4:
                dname, dst, dc0 = "qkT", self.qkT, r * 1024
            else:
                dname, dst, dc0 = "qfT", self.qfT, (r - 4) * 1024

            def _t(srct=srct, sc0=sc0, tgt=tgt):
                for j in range(8):
                    last = nc.tensor.transpose(tgt[:, j * 128:(j + 1) * 128],
                                               srct[:, sc0 + j * 128:sc0 + (j + 1) * 128],
                                               self.idn_bf[:, :])
                return last
            wnames = [tname] if tname != "tp" else ["tp", "tp0", "tp1"]
            self.g("pe", [src_name], wnames, _t)
            eng = "act" if r % 2 == 0 else "dve"
            h = nc.scalar if eng == "act" else nc.vector

            def _e(h=h, dst=dst, dc0=dc0, tgt=tgt):
                if h is nc.vector:
                    return h.tensor_copy(dst[:, dc0:dc0 + 1024], tgt[:, 0:1024])
                return h.activation(out=dst[:, dc0:dc0 + 1024], in_=tgt[:, 0:1024],
                                    func=AF.Copy)
            self.g(eng, [tname], [dname], _e)
        self.dump("l0_qkT", self.qkT[:, :])
        self.dump("l0_qfT", self.qfT[:, :])

        for half in range(2):
            hs = range(8 * half, 8 * half + 8)
            wsn = WSA if half == 0 else WSB

            def _p(hs=hs):
                for h in hs:
                    lhs = self.qkT[0:16, 2048 + h * 128:2048 + (h + 1) * 128]
                    rhs = self.qkT[0:16, h * 128:(h + 1) * 128]
                    last = self.mm(self.ws_ps[:, h * 128:(h + 1) * 128], lhs, rhs,
                                   start=(h % 4 == 0), stop=(h % 4 == 3))
                return last
            self.g("pe", ["qkT"], wsn, _p)
            c0 = half * 1024
            self.g("act", wsn, ["sqA" if half == 0 else "sqB"], lambda c0=c0: nc.scalar.activation(
                out=self.sq[:, c0:c0 + 1024], in_=self.ws_ps[:, c0:c0 + 1024],
                func=AF.Square, scale=self.consts[:, 2:3], bias=self.consts[:, 2:3]))
            self.g("pool", ["sqA" if half == 0 else "sqB"], ["AT%d" % half],
                   lambda c0=c0: nc.gpsimd.tensor_tensor(
                out=self.AT[:, c0:c0 + 1024].rearrange("p (h t) -> p h t", h=8),
                in0=self.sq[:, c0:c0 + 1024].rearrange("p (h t) -> p h t", h=8),
                in1=self.maskT[:, :].unsqueeze(1).broadcast_to([128, 8, 128]),
                op=ALU.mult))

        self.g("pool", ["k_s", "kpad"], ["kf"], lambda: nc.gpsimd.tensor_tensor(
            out=kf_v,
            in0=ks_v.unsqueeze(3).broadcast_to([128, 16, 16, 16]),
            in1=kpadv.unsqueeze(2).broadcast_to([128, 16, 16, 16]), op=ALU.mult))
        self.dump("l0_kf", self.kf[:, :])
        self.dump("l0_AT", self.AT[:, :])

        for half in range(2):
            hs = range(8 * half, 8 * half + 8)
            wsn = WSA if half == 0 else WSB

            def _y(hs=hs):
                for h in hs:
                    yc = h * 128
                    self.mm(self.ws_ps[:, yc:yc + 65],
                            self.AT[:, h * 128:(h + 1) * 128],
                            self.v_aug[:, h * 65:(h + 1) * 65],
                            start=(h % 4 == 0), stop=False)
                    self.mm(self.ws_ps[:, yc:yc + 65],
                            self.mask05[:, :],
                            self.v_aug[:, h * 65:(h + 1) * 65],
                            start=False, stop=False)
                    self.mm(self.ws_ps[:, yc:yc + 65],
                            self.qkT[0:17, h * 128:(h + 1) * 128],
                            self.S_low_bf[0:17, colh(h):colh(h) + 65],
                            start=False, stop=False)
                    for j in range(2):
                        last = self.mm(self.ws_ps[:, yc:yc + 65],
                                       self.qfT[:, (2 * h + j) * 128:(2 * h + j + 1) * 128],
                                       self.S_q_bf[j][:, colh(h):colh(h) + 65],
                                       start=False, stop=(h % 4 == 3 and j == 1))
                return last
            self.g("pe", ["AT%d" % half, "v_aug", "qkT", "qfT",
                          "Slowbf", "Sq0bf", "Sq1bf"], wsn, _y)

            denv = self.ws_ps[:, :].rearrange(
                "p (h c) -> p h c", c=128)[:, 8 * half:8 * half + 8, 64:65]

            def _z(half=half, denv=denv):
                nc.vector.tensor_scalar(out=self.dent[:, 8 * half:8 * half + 8],
                                        in0=denv.squeeze(2),
                                        scalar1=float(1e-12), scalar2=None, op0=ALU.add)
                nc.vector.drain()
                return nc.vector.reciprocal(self.z16[:, 8 * half:8 * half + 8],
                                            self.dent[:, 8 * half:8 * half + 8])
            self.g("dve", wsn, ["z%d" % half], _z)

            def _zy(half=half):
                return nc.vector.tensor_tensor(
                    out=self.y_bf[:, half * 512:(half + 1) * 512].rearrange(
                        "p (h e) -> p h e", h=8),
                    in0=self.ws_ps[:, :].rearrange(
                        "p (h c) -> p h c", c=128)[:, 8 * half:8 * half + 8, 0:64],
                    in1=self.z16[:, 8 * half:8 * half + 8].unsqueeze(2).broadcast_to(
                        [128, 8, 64]), op=ALU.mult)
            self.g("dve", wsn + ["z%d" % half], ["ybf%d" % half], _zy, drain=True)
        self.dump("l0_dent", self.dent[:, :])
        self.dump("l0_ybf", self.y_bf[:, :])

        for half in range(2):
            o = half * 512

            def _yt(o=o):
                for j in range(4):
                    last = nc.tensor.transpose(self.tp_ps[:, o + j * 128:o + (j + 1) * 128],
                                               self.y_bf[:, o + j * 128:o + (j + 1) * 128],
                                               self.idn_bf[:, :])
                return last
            self.g("pe", ["ybf%d" % half], ["tp%d" % half], _yt)
            self.g("act", ["tp%d" % half], ["yT%d" % half], lambda o=o: nc.scalar.activation(
                out=self.yT[:, o:o + 512], in_=self.tp_ps[:, o:o + 512], func=AF.Copy))

            def _wo(half=half):
                for j in range(4 * half, 4 * half + 4):
                    lhs = self.yT[:, j * 128:(j + 1) * 128]
                    self.mm(self.out_ps[:, 0:512], lhs, self.woT[:, j * 1024:j * 1024 + 512],
                            start=False, stop=(j == 7))
                    last = self.mm(self.out_ps[:, 512:1024], lhs,
                                   self.woT[:, j * 1024 + 512:(j + 1) * 1024],
                                   start=False, stop=(j == 7))
                return last
            self.g("pe", ["yT%d" % half], ["out"], _wo)

        self.g("dve", ["out", xbuf], [xbuf], lambda: nc.vector.tensor_tensor(
            out=x_ap, in0=x_ap, in1=self.out_ps[:, :], op=ALU.add))
        self.dump("l0_x", x_ap)

        def ds_round(rnd, defer_add=False):
            def _ds(rnd=rnd):
                for h in range(16):
                    c0 = colh(h)
                    if rnd == 0:
                        lhs = self.kpad[:, 128 * h:128 * h + 17]
                        out = self.ws_ps[0:17, c0:c0 + 65]
                    else:
                        j = rnd - 1
                        lhs = self.kf[:, (2 * h + j) * 128:(2 * h + j + 1) * 128]
                        out = self.ws_ps[:, c0:c0 + 65]
                    last = self.mm(out, lhs, self.v_aug[:, h * 65:(h + 1) * 65],
                                   start=h in (0, 7, 14), stop=h in (6, 13, 15))
                return last
            src_name = "kpad" if rnd == 0 else "kf"
            self.g("pe", [src_name, "v_aug"], WS, _ds)
            S = self.S_low if rnd == 0 else self.S_q[rnd - 1]
            sname = ["Slow", "Sq0", "Sq1"][rnd]
            rows = 17 if rnd == 0 else 128

            def _add(S=S, rows=rows, sname=sname):
                self.g("dve", WS + [sname], [sname], lambda:
                       nc.vector.tensor_tensor(out=S[0:rows, :], in0=S[0:rows, :],
                                               in1=self.ws_ps[0:rows, 0:S_W],
                                               op=ALU.add))
            if defer_add:
                self.pending.append(_add)
            else:
                _add()

        ds_round(0)
        self.layernorm(x_ap, xbuf)
        self.dump("l1_xn", self.xn[:, :])
        self.xn_transpose()

        def _ffn1():
            for g_ in range(8):
                lhs = self.xnT[:, g_ * 128:(g_ + 1) * 128]
                self.mm(self.out_ps[:, 0:512], lhs, self.w1T[:, g_ * 1024:g_ * 1024 + 512],
                        start=(g_ == 0), stop=False)
                last = self.mm(self.out_ps[:, 512:1024], lhs,
                               self.w1T[:, g_ * 1024 + 512:(g_ + 1) * 1024],
                               start=(g_ == 0), stop=False)
            return last
        self.g("pe", ["xnT", "xnTb"], ["out"], _ffn1)

        def _proj1():
            for g_ in range(8):
                lhs = self.xnT[:, g_ * 128:(g_ + 1) * 128]
                self.mm(self.ws_ps[:, 0:256], lhs, self.wq1T[:, g_ * 256:(g_ + 1) * 256],
                        start=(g_ == 0), stop=False)
                self.mm(self.ws_ps[:, 256:512], lhs, self.wk1T[:, g_ * 256:(g_ + 1) * 256],
                        start=False, stop=(g_ == 7))
                last = self.mm(self.ws_ps[:, 512:768], lhs,
                               self.wv1T[:, g_ * 256:(g_ + 1) * 256],
                               start=(g_ == 0), stop=(g_ == 7))
            return last
        self.g("pe", ["xnT", "xnTb"], WS, _proj1)

        e_qk = self.sq[:, 1024:1536]
        self.g("dve", WS, ["sqB"], lambda: nc.vector.tensor_scalar(
            out=e_qk, in0=self.ws_ps[:, 0:512], scalar1=0.0, scalar2=None,
            op0=ALU.min))
        self.g("act", ["sqB"], ["sqB"], lambda: nc.scalar.activation(
            out=e_qk, in_=e_qk, func=AF.Exp))
        self.g("dve", WS + ["sqB"], ["pqk"], lambda: nc.vector.scalar_tensor_tensor(
            out=self.p_qk[:, :], in0=self.ws_ps[:, 0:512], scalar=0.0, in1=e_qk,
            op0=ALU.max, op1=ALU.add))
        self.g("act", WS, ["v1aug"], lambda: nc.scalar.activation(
            out=self.v1_aug[:, :].rearrange("p (h c) -> p h c", c=65)[:, :, 0:64],
            in_=self.ws_ps[:, 512:768].rearrange("p (h e) -> p h e", h=4),
            func=AF.Copy))
        self.g("act", ["pqk"], ["sqA"], lambda: nc.scalar.activation(
            out=self.sq[:, 0:512], in_=self.p_qk[:, :], func=AF.Square))
        self.g("dve", ["sqA"], ["nrm"], lambda: nc.vector.tensor_reduce(
            out=self.nrm[:, :].rearrange("p (a b) -> p a b", b=1),
            in_=self.sq[:, 0:512].rearrange("p (a b) -> p a b", a=8),
            axis=AX.X, op=ALU.add))

        def _sqrt():
            nc.scalar.activation(out=self.rn[:, :], in_=self.nrm[:, :], func=AF.Ln)
            nc.scalar.drain()
            return nc.scalar.activation(out=self.rn[:, :], in_=self.rn[:, :],
                                        func=AF.Exp, scale=self.consts[:, 3:4])
        self.g("act", ["nrm"], ["rn"], _sqrt)

        def _rn():
            nc.vector.tensor_scalar(out=self.rn[:, :], in0=self.rn[:, :],
                                    scalar1=float(1e-6), scalar2=None, op0=ALU.add)
            nc.vector.drain()
            return nc.vector.reciprocal(self.rn[:, :], self.rn[:, :])
        self.g("dve", ["rn"], ["rn"], _rn)

        self.g("dve", ["pqk", "rn"], ["q1pad"], lambda: nc.vector.tensor_tensor(
            out=self.q1pad[:, :].rearrange("p (h c) -> p h c", c=128)[:, :, 0:64],
            in0=self.p_qk[:, 0:256].rearrange("p (a b) -> p a b", a=4),
            in1=self.rn[:, 0:4].unsqueeze(2).broadcast_to([128, 4, 64]), op=ALU.mult),
            drain=True)
        self.g("dve", ["pqk", "rn"], ["k1pad"], lambda: nc.vector.tensor_tensor(
            out=self.k1pad[:, :].rearrange("p (h c) -> p h c", c=128)[:, :, 0:64],
            in0=self.p_qk[:, 256:512].rearrange("p (a b) -> p a b", a=4),
            in1=self.rn[:, 4:8].unsqueeze(2).broadcast_to([128, 4, 64]), op=ALU.mult))
        self.dump("l1_pqk", self.p_qk[:, :])
        self.dump("l1_rn", self.rn[:, :])
        self.dump("l1_q1pad", self.q1pad[:, :])
        self.dump("l1_k1pad", self.k1pad[:, :])
        self.dump("l1_v1aug", self.v1_aug[:, :])

        ds_round(1)

        def _t1q():
            for j in range(4):
                last = nc.tensor.transpose(self.tp_ps[:, j * 128:(j + 1) * 128],
                                           self.q1pad[:, j * 128:(j + 1) * 128],
                                           self.idn_bf[:, :])
            return last
        self.g("pe", ["q1pad"], ["tp", "tp0"], _t1q)

        def _t1k():
            for j in range(4):
                last = nc.tensor.transpose(self.tp_ps[:, 512 + j * 128:512 + (j + 1) * 128],
                                           self.k1pad[:, j * 128:(j + 1) * 128],
                                           self.idn_bf[:, :])
            return last
        self.g("pe", ["k1pad"], ["tp", "tp1"], _t1k)
        self.g("act", ["tp", "tp0", "tp1"], ["qk1T"], lambda: nc.scalar.activation(
            out=self.qk1T[:, :], in_=self.tp_ps[:, :], func=AF.Copy))

        def _p1():
            for h in range(4):
                lhs = self.qk1T[0:64, 512 + h * 128:512 + (h + 1) * 128]
                rhs = self.qk1T[0:64, h * 128:(h + 1) * 128]
                last = self.mm(self.sm_ps[:, h * 128:(h + 1) * 128], lhs, rhs,
                               start=(h == 0), stop=(h == 3))
            return last
        self.g("pe", ["qk1T"], ["sm"], _p1)
        self.g("dve", ["sm"], ["A1T"], lambda: nc.vector.tensor_tensor(
            out=self.A1T[:, :].rearrange("p (h t) -> p h t", h=4),
            in0=self.sm_ps[:, :].rearrange("p (h t) -> p h t", h=4),
            in1=self.maskT[:, :].unsqueeze(1).broadcast_to([128, 4, 128]), op=ALU.mult))
        self.dump("l1_A1T", self.A1T[:, :])

        def _y1():
            for h in range(4):
                yc = h * 128
                self.mm(self.sm_ps[:, yc:yc + 65], self.A1T[:, h * 128:(h + 1) * 128],
                        self.v1_aug[:, h * 65:(h + 1) * 65], start=(h == 0), stop=False)
                last = self.mm(self.sm_ps[:, yc:yc + 65],
                               self.qk1T[0:64, h * 128:(h + 1) * 128],
                               self.S1_bf[0:64, h * 65:(h + 1) * 65],
                               start=False, stop=(h == 3))
            return last
        self.g("pe", ["A1T", "v1aug", "qk1T", "S1bf"], ["sm"], _y1)

        denv1 = self.sm_ps[:, :].rearrange("p (h c) -> p h c", c=128)[:, :, 64:65]

        def _z1():
            nc.vector.tensor_scalar(out=self.dent4[:, :], in0=denv1.squeeze(2),
                                    scalar1=float(1e-6), scalar2=None, op0=ALU.add)
            nc.vector.drain()
            return nc.vector.reciprocal(self.z4[:, :], self.dent4[:, :])
        self.g("dve", ["sm"], ["z4"], _z1)
        self.g("dve", ["sm", "z4"], ["y1"], lambda: nc.vector.tensor_tensor(
            out=self.y1_bf[:, :].rearrange("p (h e) -> p h e", h=4),
            in0=self.sm_ps[:, :].rearrange("p (h c) -> p h c", c=128)[:, :, 0:64],
            in1=self.z4[:, :].unsqueeze(2).broadcast_to([128, 4, 64]), op=ALU.mult),
            drain=True)
        self.dump("l1_y1bf", self.y1_bf[:, :])

        def _y1t():
            for j in range(2):
                last = nc.tensor.transpose(self.tp_ps[:, j * 128:(j + 1) * 128],
                                           self.y1_bf[:, j * 128:(j + 1) * 128],
                                           self.idn_bf[:, :])
            return last
        self.g("pe", ["y1"], ["tp", "tp0", "tp1"], _y1t)
        self.g("act", ["tp"], ["y1T"], lambda: nc.scalar.activation(
            out=self.y1T[:, :], in_=self.tp_ps[:, 0:256], func=AF.Copy))

        def _wo1():
            for j in range(2):
                lhs = self.y1T[:, j * 128:(j + 1) * 128]
                self.mm(self.out_ps[:, 0:512], lhs, self.wo1T[:, j * 1024:j * 1024 + 512],
                        start=False, stop=(j == 1))
                last = self.mm(self.out_ps[:, 512:1024], lhs,
                               self.wo1T[:, j * 1024 + 512:(j + 1) * 1024],
                               start=False, stop=(j == 1))
            return last
        self.g("pe", ["y1T"], ["out"], _wo1)

        def _resid2():
            nc.vector.tensor_tensor(out=x_ap, in0=x_ap, in1=self.out_ps[:, :],
                                    op=ALU.add)
            return nc.vector.drain()
        self.g("dve", ["out", xbuf], [xbuf], _resid2)
        self.resid_mark = self.n["dve"]

        def _ds1():
            for h in range(4):
                last = self.mm(self.sm_ps[0:64, h * 65:(h + 1) * 65],
                               self.k1pad[:, h * 128:h * 128 + 64],
                               self.v1_aug[:, h * 65:(h + 1) * 65],
                               start=(h == 0), stop=(h == 3))
            return last
        self.g("pe", ["k1pad", "v1aug"], ["sm"], _ds1)
        self.pending.append(lambda: self.g(
            "dve", ["sm", "S1"], ["S1"], lambda: nc.vector.tensor_tensor(
                out=self.S1[:, :], in0=self.S1[:, :], in1=self.sm_ps[0:64, 0:260],
                op=ALU.add)))
        ds_round(2, defer_add=True)
        self.pending.append(lambda: self.g(
            "act", ["Slow"], ["Slowbf"], lambda: nc.scalar.activation(
                out=self.S_low_bf[0:17, :], in_=self.S_low[0:17, :], func=AF.Copy)))
        self.pending.append(lambda: self.g(
            "act", ["Sq0"], ["Sq0bf"], lambda: nc.scalar.activation(
                out=self.S_q_bf[0][:, :], in_=self.S_q[0][:, :], func=AF.Copy)))
        self.pending.append(lambda: self.g(
            "act", ["Sq1"], ["Sq1bf"], lambda: nc.scalar.activation(
                out=self.S_q_bf[1][:, :], in_=self.S_q[1][:, :], func=AF.Copy)))
        self.pending.append(lambda: self.g(
            "act", ["S1"], ["S1bf"], lambda: nc.scalar.activation(
                out=self.S1_bf[:, :], in_=self.S1[:, :], func=AF.Copy)))
        self.dump("l0_Slow", self.S_low_bf[:, :])
        self.dump("l0_Sq0", self.S_q_bf[0][:, :])
        self.dump("l0_Sq1", self.S_q_bf[1][:, :])
        self.dump("l1_S1", self.S1_bf[:, :])


    def build(self):
        nc = self.nc
        self.alloc()
        self.prep()
        self.bar()
        self.reset_deps()

        for c in range(min(2, self.nchunks)):
            nc.sync.dma_start(self.x_sb[c % 2][:, :],
                              self.i_x[c * C:(c + 1) * C, :]).then_inc(self.sem_ld, 16)
            self.ld += 16

        for c in range(self.nchunks):
            self.cur_chunk = c
            xbuf = "x%d" % (c % 2)
            buf = self.x_sb[c % 2][:, :]
            nc.vector.wait_ge(self.sem_ld, 16 * (c + 1))
            nc.scalar.wait_ge(self.sem_ld, 16 * (c + 1))
            self.chunk_body(buf, xbuf)
            nc.sync.wait_ge(self.sems["dve"], self.resid_mark)
            nc.sync.dma_start(self.o_out[c * C:(c + 1) * C, :], buf).then_inc(
                self.sem_st, 16)
            self.st += 16
            if c + 2 < self.nchunks:
                nc.sync.wait_ge(self.sem_st, 16 * (c + 1))
                nc.sync.dma_start(self.x_sb[c % 2][:, :],
                                  self.i_x[(c + 2) * C:(c + 3) * C, :]).then_inc(
                    self.sem_ld, 16)
                self.ld += 16

        nc.sync.wait_ge(self.sem_st, self.st)
        return nc


def build_program(nchunks=NCH, layers=(0, 1), dbg=False):
    return KB(nchunks=nchunks, layers=layers, dbg=dbg).build()


def make_in_map(inputs, b):
    f32 = np.float32
    return {
        "x": np.ascontiguousarray(np.asarray(inputs["L0_attn_hidden_states"])[b], dtype=f32),
        "wq0": np.ascontiguousarray(np.asarray(inputs["L0_attn_Wq"], dtype=f32)),
        "wk0": np.ascontiguousarray(np.asarray(inputs["L0_attn_Wk"], dtype=f32)),
        "wv0": np.ascontiguousarray(np.asarray(inputs["L0_attn_Wv"], dtype=f32)),
        "wo0": np.ascontiguousarray(np.asarray(inputs["L0_attn_Wo"], dtype=f32)),
        "ker": np.ascontiguousarray(np.asarray(inputs["L0_ffn_kernel"], dtype=f32)),
        "bias": np.ascontiguousarray(np.asarray(inputs["L0_ffn_bias"], dtype=f32)).reshape(1, 1024),
        "la": np.ascontiguousarray(np.asarray(inputs["L0_ffn_a"], dtype=f32)),
        "lb": np.ascontiguousarray(np.asarray(inputs["L0_ffn_b"], dtype=f32)),
        "wq1": np.ascontiguousarray(np.asarray(inputs["L1_attn_Wq"], dtype=f32)),
        "wk1": np.ascontiguousarray(np.asarray(inputs["L1_attn_Wk"], dtype=f32)),
        "wv1": np.ascontiguousarray(np.asarray(inputs["L1_attn_Wv"], dtype=f32)),
        "wo1": np.ascontiguousarray(np.asarray(inputs["L1_attn_Wo"], dtype=f32)),
        "w1": np.ascontiguousarray(np.asarray(inputs["L1_ffn_weight"], dtype=f32)),
        "idn": np.eye(128, dtype=f32),
        "maskT": np.triu(np.ones((128, 128), dtype=f32)),
        "mask05": np.triu(np.full((128, 128), 0.5, dtype=ml_dtypes.bfloat16)),
    }


def kernel(**inputs):
    nc = build_program()
    in_maps = [make_in_map(inputs, b) for b in range(8)]
    res = run_bass_kernel_spmd(nc, in_maps, core_ids=list(range(8)))
    return np.stack([r["out"] for r in res.results], axis=0).astype(np.float32)


# version 07
# speedup vs baseline: 1.7058x; 1.0543x over previous
import contextlib
import numpy as np
import ml_dtypes

import concourse.bass as bass
import concourse.mybir as mybir
from concourse.bass_utils import run_bass_kernel_spmd

F32 = mybir.dt.float32
BF16 = mybir.dt.bfloat16
AF = mybir.ActivationFunctionType
ALU = mybir.AluOpType
AX = mybir.AxisListType
ET = mybir.EngineType

MAGIC = float(1.5 * 2 ** 23)
SQ5 = 0.7071067811865476
QS = 0.35355339059327373

T_SEQ = 4096
D = 1024
C = 128
NCH = T_SEQ // C
H0, F0, V0 = 16, 16, 64
H1, D1 = 4, 64

def colh(h):
    return 65 * (h % 7) + 512 * (h // 7)

S_W = 1536


class KB:

    def __init__(self, nchunks=NCH, layers=(0, 1), dbg=False):
        self.nchunks = nchunks
        self.layers = layers
        self.dbg = dbg
        self.cur_chunk = -1
        self.dbg_tensors = {}
        self.nc = bass.Bass("TRN2", target_bir_lowering=False)
        self.stack = contextlib.ExitStack()
        self.ld = 0
        self.wld = 0
        self.st = 0
        self.n = {"pe": 0, "dve": 0, "act": 0, "pool": 0}
        self.deps = {}
        self.pending = []
        self.last_wait = {}

    def bar(self, pool=False):
        self.nc.all_engine_barrier()

    def eng_handle(self, eng):
        nc = self.nc
        return {"pe": nc.tensor, "dve": nc.vector, "act": nc.scalar,
                "pool": nc.gpsimd, "sp": nc.sync}[eng]

    def reset_deps(self):
        self.deps = {}
        self.last_wait = {}

    def dep_wait(self, eng, reads=(), writes=()):
        h = self.eng_handle(eng)
        need = {}
        for b in reads:
            w = self.deps.get(b, {}).get("w")
            if w and w[0] != eng:
                need[w[0]] = max(need.get(w[0], 0), w[1])
        for b in writes:
            d = self.deps.get(b, {})
            w = d.get("w")
            if w and w[0] != eng:
                need[w[0]] = max(need.get(w[0], 0), w[1])
            for re_, rc in d.get("r", {}).items():
                if re_ != eng:
                    need[re_] = max(need.get(re_, 0), rc)
        for prod, cnt in sorted(need.items()):
            key = (eng, prod)
            if self.last_wait.get(key, 0) < cnt:
                sem = self.sem_wld if prod == "wld" else self.sems[prod]
                h.wait_ge(sem, cnt)
                self.last_wait[key] = cnt

    def dep_done(self, inst, eng, reads=(), writes=()):
        self.n[eng] += 1
        inst.then_inc(self.sems[eng], 1)
        c = self.n[eng]
        for b in writes:
            self.deps[b] = {"w": (eng, c), "r": {}}
        for b in reads:
            d = self.deps.setdefault(b, {"w": None, "r": {}})
            d["r"][eng] = c

    def g(self, eng, reads, writes, emit, drain=False):
        self.dep_wait(eng, reads, writes)
        if drain:
            self.eng_handle(eng).drain()
        last = emit()
        self.dep_done(last, eng, reads, writes)
        return self.n[eng]

    def alloc(self):
        nc = self.nc
        sb = nc.alloc_sbuf_tensor
        ps = nc.alloc_psum_tensor

        self.i_x = nc.dram_tensor("x", [T_SEQ, D], F32, kind="ExternalInput").ap()
        self.i_wq0 = nc.dram_tensor("wq0", [256, 1024], F32, kind="ExternalInput").ap()
        self.i_wk0 = nc.dram_tensor("wk0", [256, 1024], F32, kind="ExternalInput").ap()
        self.i_wv0 = nc.dram_tensor("wv0", [1024, 1024], F32, kind="ExternalInput").ap()
        self.i_wo0 = nc.dram_tensor("wo0", [1024, 1024], F32, kind="ExternalInput").ap()
        self.i_ker = nc.dram_tensor("ker", [1024, 1024], F32, kind="ExternalInput").ap()
        self.i_bias = nc.dram_tensor("bias", [1, 1024], F32, kind="ExternalInput").ap()
        self.i_a = nc.dram_tensor("la", [1024, 16], F32, kind="ExternalInput").ap()
        self.i_b = nc.dram_tensor("lb", [16, 1024], F32, kind="ExternalInput").ap()
        self.i_wq1 = nc.dram_tensor("wq1", [256, 1024], F32, kind="ExternalInput").ap()
        self.i_wk1 = nc.dram_tensor("wk1", [256, 1024], F32, kind="ExternalInput").ap()
        self.i_wv1 = nc.dram_tensor("wv1", [256, 1024], F32, kind="ExternalInput").ap()
        self.i_wo1 = nc.dram_tensor("wo1", [1024, 256], F32, kind="ExternalInput").ap()
        self.i_w1 = nc.dram_tensor("w1", [1024, 1024], F32, kind="ExternalInput").ap()
        self.i_idn = nc.dram_tensor("idn", [128, 128], F32, kind="ExternalInput").ap()
        self.i_mask = nc.dram_tensor("maskT", [128, 128], F32, kind="ExternalInput").ap()
        self.i_mask05 = nc.dram_tensor("mask05", [128, 128], BF16, kind="ExternalInput").ap()
        self.o_out = nc.dram_tensor("out", [T_SEQ, D], F32, kind="ExternalOutput").ap()

        self.wqT = sb("wqT", [128, 2048], BF16)
        self.wkT = sb("wkT", [128, 2048], BF16)
        self.wvT = sb("wvT", [128, 8192], BF16)
        self.woT = sb("woT", [128, 8192], BF16)
        self.kerq = sb("kerq", [128, 8192], BF16)
        self.a_sb = sb("a_sb", [128, 128], BF16)
        self.b_sb = sb("b_sb", [16, 1024], BF16)
        self.bias_bf = sb("bias_bf", [1, 1024], BF16)
        self.wq1T = sb("wq1T", [128, 2048], BF16)
        self.wk1T = sb("wk1T", [128, 2048], BF16)
        self.wv1T = sb("wv1T", [128, 2048], BF16)
        self.wo1T = sb("wo1T", [128, 2048], BF16)
        self.w1T = sb("w1T", [128, 8192], BF16)

        self.idn_f = sb("idn_f", [128, 128], F32)
        self.idn_bf = sb("idn_bf", [128, 128], BF16)
        self.maskT = sb("maskT_sb", [128, 128], F32)
        self.mask05 = sb("mask05_sb", [128, 128], BF16)
        self.ones_bf = sb("ones_bf", [1, 128], BF16)
        self.ones_f = sb("ones_f", [1, 128], F32)
        self.consts = sb("consts", [128, 8], F32)

        self.x_sb = [sb("x_sb0", [128, D], F32), sb("x_sb1", [128, D], F32)]
        self.sq = sb("sq_scr", [128, 2048], F32)
        self.xn = sb("xn_bf", [128, D], BF16)
        self.xnT = sb("xnT_bf", [128, D], BF16)
        self.qpad = sb("qpad", [128, 2048], BF16)
        self.kpad = sb("kpad", [128, 2048], BF16)
        self.q_s = sb("q_s", [128, 256], BF16)
        self.k_s = sb("k_s", [128, 256], BF16)
        self.qkT = sb("qkT", [128, 4096], BF16)
        self.v_aug = sb("v_aug", [128, H0 * 65], BF16)
        self.uT = sb("uT_bf", [16, 128], BF16)
        self.qf = sb("qf", [128, 4096], BF16)
        self.kf = sb("kf", [128, 4096], BF16)
        self.qfT = sb("qfT", [128, 4096], BF16)
        self.AT = sb("AT", [128, 2048], BF16)
        self.y_bf = sb("y_bf", [128, 1024], BF16)
        self.yT = sb("yT", [128, 1024], BF16)
        self.m_col = sb("m_col", [128, 1], F32)
        self.s2_col = sb("s2_col", [128, 1], F32)
        self.t_col = sb("t_col", [128, 1], F32)
        self.v_col = sb("v_col", [128, 1], F32)
        self.r_col = sb("r_col", [128, 1], F32)
        self.dent = sb("dent", [128, 16], F32)
        self.z16 = sb("z16", [128, 16], F32)

        self.S_low = sb("S_low", [17, S_W], F32)
        self.S_low_bf = sb("S_low_bf", [17, S_W], BF16)
        self.S_q = [sb("S_q0", [128, S_W], F32), sb("S_q1", [128, S_W], F32)]
        self.S_q_bf = [sb("S_q0_bf", [128, S_W], BF16), sb("S_q1_bf", [128, S_W], BF16)]

        self.p_qk = sb("p_qk", [128, 512], F32)
        self.nrm = sb("nrm", [128, 8], F32)
        self.rn = sb("rn", [128, 8], F32)
        self.q1pad = sb("q1pad", [128, 512], BF16)
        self.k1pad = sb("k1pad", [128, 512], BF16)
        self.qk1T = sb("qk1T", [128, 1024], BF16)
        self.v1_aug = sb("v1_aug", [128, H1 * 65], BF16)
        self.A1T = sb("A1T", [128, 512], BF16)
        self.y1_bf = sb("y1_bf", [128, 256], BF16)
        self.y1T = sb("y1T", [128, 256], BF16)
        self.dent4 = sb("dent4", [128, 4], F32)
        self.z4 = sb("z4", [128, 4], F32)
        self.S1 = sb("S1", [64, H1 * 65], F32)
        self.S1_bf = sb("S1_bf", [64, H1 * 65], BF16)

        self.stage = self.x_sb[0]
        self.stage_bf = self.y_bf
        self.amax8 = sb("amax8", [128, 8], F32)
        self.red_row = sb("red_row", [1, 128], F32)
        self.scal2 = sb("scal2", [1, 2], F32)
        self.rs_cols = sb("rs_cols", [128, 2], F32)

        self.tp_ps = ps("tp_ps", [128, 1024], BF16)
        self.out_ps = ps("out_ps", [128, 1024], F32)
        self.ws_ps = ps("ws_ps", [128, 2048], F32)
        self.sm_ps = ps("sm_ps", [128, 512], F32)

        self.sem_ld = self.stack.enter_context(nc.semaphore("s_ld"))
        self.sem_wld = self.stack.enter_context(nc.semaphore("s_wld"))
        self.sem_st = self.stack.enter_context(nc.semaphore("s_st"))
        self.sem_rdy = self.stack.enter_context(nc.semaphore("s_rdy"))
        self.sems = {
            "pe": self.stack.enter_context(nc.semaphore("s_pe")),
            "dve": self.stack.enter_context(nc.semaphore("s_dve")),
            "act": self.stack.enter_context(nc.semaphore("s_act")),
            "pool": self.stack.enter_context(nc.semaphore("s_pool")),
        }

    def dump(self, name, ap):
        if not self.dbg or self.cur_chunk > 0:
            return
        nc = self.nc
        if name not in self.dbg_tensors:
            sh = list(ap.shape)
            self.dbg_tensors[name] = nc.dram_tensor(
                "dbg_" + name, sh, F32, kind="ExternalOutput").ap()
        self.bar(pool=True)
        nc.gpsimd.dma_start(self.dbg_tensors[name], ap).then_inc(self.sem_wld, 16)
        self.wld += 16
        nc.gpsimd.wait_ge(self.sem_wld, self.wld)
        self.bar(pool=True)

    def wload(self, dst_ap, src_ap):
        nc = self.nc
        nc.sync.dma_start(dst_ap, src_ap).then_inc(self.sem_wld, 16)
        self.wld += 16
        nc.vector.wait_ge(self.sem_wld, self.wld)
        nc.scalar.wait_ge(self.sem_wld, self.wld)
        self.bar()

    def dma_load(self, dst_ap, src_ap, bufname):
        self.dep_wait("sp", reads=(), writes=(bufname,))
        self.nc.sync.dma_start(dst_ap, src_ap).then_inc(self.sem_wld, 16)
        self.wld += 16
        self.deps[bufname] = {"w": ("wld", self.wld), "r": {}}

    def mm(self, out_ap, lhsT, rhs, start, stop, tile_position=None):
        return self.nc.tensor.matmul(out_ap, lhsT, rhs, start=start, stop=stop,
                                     tile_position=tile_position)

    def stg(self, k):
        names = ["x0", "x1", "stg2", "stg3"]
        bufs = [self.x_sb[0][:, :], self.x_sb[1][:, :],
                self.sq[:, 0:1024], self.sq[:, 1024:2048]]
        return names[k % 4], bufs[k % 4]

    def sbfs(self, k):
        return (("sbf0", self.y_bf) if k % 2 == 0 else ("sbf1", self.yT))

    def prep_wT(self, w_ap, R, Dc, dst, dname):
        nc = self.nc
        nblk = Dc // 128
        for g in range(R // 128):
            sname, st = self.stg(self.stg_i)
            bname, sbf = self.sbfs(self.stg_i)
            self.stg_i += 1
            self.dma_load(st[:, 0:Dc], w_ap[g * 128:(g + 1) * 128, :], sname)
            self.g("dve", [sname], [bname], lambda st=st, sbf=sbf, Dc=Dc:
                   nc.vector.tensor_copy(sbf[:, 0:Dc], st[:, 0:Dc]))
            use_ws = (self.stg_i % 2 == 1)
            tgt = self.ws_ps[:, 0:512].bitcast(BF16) if use_ws else self.tp_ps
            tname = "wsA" if use_ws else "tp"

            def _t(sbf=sbf, tgt=tgt, nblk=nblk):
                for j in range(nblk):
                    last = nc.tensor.transpose(tgt[:, j * 128:(j + 1) * 128],
                                               sbf[:, j * 128:(j + 1) * 128],
                                               self.idn_bf[:, :])
                return last
            self.g("pe", [bname], [tname], _t)
            eng = "act" if self.stg_i % 2 == 0 else "dve"
            h = nc.scalar if eng == "act" else nc.vector

            def _e(h=h, tgt=tgt, dst=dst, nblk=nblk, g=g, R=R):
                srcv = tgt[:, 0:nblk * 128].rearrange("p (j r) -> p j r", j=nblk)
                dstv = dst[:, :].rearrange("p (j r) -> p j r", j=nblk)[:, :, g * 128:(g + 1) * 128]
                if h is nc.vector:
                    return h.tensor_copy(dstv, srcv)
                return h.activation(out=dstv, in_=srcv, func=AF.Copy)
            self.g(eng, [tname], [dname], _e)

    def quant_scale_pass(self, w_ap, mode):
        nc = self.nc
        op = ALU.max if mode == "int4" else ALU.add
        for g in range(8):
            sname, st = self.stg(self.stg_i)
            self.stg_i += 1
            self.dma_load(st[:, :], w_ap[g * 128:(g + 1) * 128, :], sname)
            self.g("dve", [sname], ["amax8"], lambda st=st, g=g:
                   nc.vector.tensor_reduce(out=self.amax8[:, g:g + 1], in_=st[:, :],
                                           axis=AX.X, op=op,
                                           apply_absolute_value=True))
        self.g("dve", ["amax8"], ["mcol"], lambda: nc.vector.tensor_reduce(
            out=self.m_col[:, :], in_=self.amax8[:, :], axis=AX.X, op=op),
            drain=True)
        self.g("pe", ["mcol"], ["sm"], lambda: nc.tensor.transpose(
            self.sm_ps[0:1, 0:128], self.m_col[:, :], self.idn_f[:, :]))
        self.g("dve", ["sm"], ["red_row"], lambda: nc.vector.tensor_copy(
            self.red_row[:, :], self.sm_ps[0:1, 0:128]))
        self.g("dve", ["red_row"], ["scal2"], lambda: nc.vector.tensor_reduce(
            out=self.scal2[0:1, 0:1], in_=self.red_row[:, :], axis=AX.X, op=op),
            drain=True)
        if mode == "int4":
            fn = lambda: nc.vector.tensor_scalar(
                out=self.scal2[0:1, 0:1], in0=self.scal2[0:1, 0:1],
                scalar1=float(1.0 / 7.0), scalar2=None, op0=ALU.mult)
        else:
            fn = lambda: nc.vector.tensor_scalar(
                out=self.scal2[0:1, 0:1], in0=self.scal2[0:1, 0:1],
                scalar1=float(1.0 / 1048576.0), scalar2=float(1e-5),
                op0=ALU.mult, op1=ALU.max)
        self.g("dve", ["scal2"], ["scal2"], fn, drain=True)
        self.g("dve", ["scal2"], ["scal2"], lambda: nc.vector.reciprocal(
            self.scal2[0:1, 1:2], self.scal2[0:1, 0:1]), drain=True)
        self.g("pe", ["scal2"], ["sm"], lambda: self.mm(
            self.sm_ps[:, 0:2], self.ones_f[:, :], self.scal2[0:1, 0:2],
            start=True, stop=True))
        self.g("dve", ["sm"], ["rs"], lambda: nc.vector.tensor_copy(
            self.rs_cols[:, :], self.sm_ps[:, 0:2]))

    def quant_tile(self, st, sname, dst_bf_ap, dname, qmin, qmax):
        nc = self.nc

        def _q():
            s = st
            nc.vector.tensor_scalar(out=s, in0=s, scalar1=self.rs_cols[:, 1:2],
                                    scalar2=None, op0=ALU.mult)
            nc.vector.drain()
            nc.vector.tensor_scalar(out=s, in0=s, scalar1=MAGIC, scalar2=MAGIC,
                                    op0=ALU.add, op1=ALU.subtract)
            nc.vector.drain()
            nc.vector.tensor_scalar(out=s, in0=s, scalar1=float(qmin),
                                    scalar2=float(qmax), op0=ALU.max, op1=ALU.min)
            nc.vector.drain()
            return nc.vector.tensor_scalar(out=dst_bf_ap, in0=s,
                                           scalar1=self.rs_cols[:, 0:1],
                                           scalar2=None, op0=ALU.mult)
        self.g("dve", [sname, "rs"], [sname, dname], _q, drain=True)

    def prep(self):
        nc = self.nc
        self.stg_i = 0
        self.wload(self.idn_f[:, :], self.i_idn)
        self.wload(self.maskT[:, :], self.i_mask)
        self.wload(self.mask05[:, :], self.i_mask05)
        nc.vector.tensor_copy(self.idn_bf[:, :], self.idn_f[:, :])
        nc.vector.memset(self.ones_bf[:, :], 1.0)
        nc.vector.memset(self.ones_f[:, :], 1.0)
        for col, val in ((0, 1e-5), (1, -0.5), (2, SQ5), (3, 0.5), (4, QS),
                         (5, 1.0 / 1024.0)):
            nc.vector.memset(self.consts[:, col:col + 1], float(val))
        nc.vector.memset(self.ws_ps[:, :], 0.0)
        nc.vector.memset(self.out_ps[:, :], 0.0)
        nc.vector.memset(self.sm_ps[:, :], 0.0)
        nc.vector.memset(self.qpad[:, :], 0.0)
        nc.vector.memset(self.kpad[:, :], 0.0)
        nc.vector.memset(self.v_aug[:, :], 0.0)
        nc.vector.memset(self.v1_aug[:, :], 0.0)
        nc.vector.memset(self.q1pad[:, :], 0.0)
        nc.vector.memset(self.k1pad[:, :], 0.0)
        for t in (self.S_low, self.S_q[0], self.S_q[1], self.S1):
            nc.vector.memset(t[:, :], 0.0)
        for t in (self.S_low_bf, self.S_q_bf[0], self.S_q_bf[1], self.S1_bf):
            nc.vector.memset(t[:, :], 0.0)
        self.bar()
        nc.vector.memset(
            self.qpad[:, :].rearrange("p (h c) -> p h c", c=128)[:, :, 16:17], 1.0)
        nc.vector.memset(
            self.kpad[:, :].rearrange("p (h c) -> p h c", c=128)[:, :, 16:17], 1.0)
        nc.vector.memset(
            self.v_aug[:, :].rearrange("p (h c) -> p h c", c=65)[:, :, 64:65], 1.0)
        nc.vector.memset(
            self.v1_aug[:, :].rearrange("p (h c) -> p h c", c=65)[:, :, 64:65], 1.0)
        self.bar()
        self.reset_deps()

        self.prep_wT(self.i_wq0, 256, 1024, self.wqT, "WQT")
        self.prep_wT(self.i_wk0, 256, 1024, self.wkT, "WKT")
        self.prep_wT(self.i_wv0, 1024, 1024, self.wvT, "WVT")
        self.prep_wT(self.i_wo0, 1024, 1024, self.woT, "WOT")
        self.prep_wT(self.i_wq1, 256, 1024, self.wq1T, "WQ1T")
        self.prep_wT(self.i_wk1, 256, 1024, self.wk1T, "WK1T")
        self.prep_wT(self.i_wv1, 256, 1024, self.wv1T, "WV1T")
        self.prep_wT(self.i_wo1, 1024, 256, self.wo1T, "WO1T")

        self.quant_scale_pass(self.i_ker, "int4")
        for g in range(8):
            sname, st = self.stg(self.stg_i)
            self.stg_i += 1
            self.dma_load(st[:, :], self.i_ker[g * 128:(g + 1) * 128, :], sname)
            self.quant_tile(st[:, :], sname,
                            self.kerq[:, g * 1024:(g + 1) * 1024], "KERQ",
                            -8.0, 7.0)

        self.quant_scale_pass(self.i_w1, "tern")
        for g in range(8):
            sname, st = self.stg(self.stg_i)
            bname, sbf = self.sbfs(self.stg_i)
            self.stg_i += 1
            self.dma_load(st[:, :], self.i_w1[g * 128:(g + 1) * 128, :], sname)
            self.quant_tile(st[:, :], sname, sbf[:, :], bname, -1.0, 1.0)
            use_ws = (self.stg_i % 2 == 1)
            tgt = self.ws_ps[:, 0:512].bitcast(BF16) if use_ws else self.tp_ps
            tname = "wsA" if use_ws else "tp"

            def _t(sbf=sbf, tgt=tgt):
                for j in range(8):
                    last = nc.tensor.transpose(tgt[:, j * 128:(j + 1) * 128],
                                               sbf[:, j * 128:(j + 1) * 128],
                                               self.idn_bf[:, :])
                return last
            self.g("pe", [bname], [tname], _t)
            eng = "act" if self.stg_i % 2 == 0 else "dve"
            h = nc.scalar if eng == "act" else nc.vector

            def _e(h=h, tgt=tgt, g=g):
                srcv = tgt[:, :].rearrange("p (j r) -> p j r", j=8)
                dstv = self.w1T[:, :].rearrange("p (j r) -> p j r", j=8)[:, :, g * 128:(g + 1) * 128]
                if h is nc.vector:
                    return h.tensor_copy(dstv, srcv)
                return h.activation(out=dstv, in_=srcv, func=AF.Copy)
            self.g(eng, [tname], ["W1T"], _e)

        self.bar()
        a_src = self.i_a.rearrange("(g p) r -> p g r", p=128)
        self.wload(self.x_sb[0][:, 0:128].rearrange("p (g r) -> p g r", g=8), a_src)
        nc.vector.tensor_copy(self.a_sb[:, :], self.x_sb[0][:, 0:128])
        self.wload(self.x_sb[0][0:16, 0:1024], self.i_b)
        nc.vector.tensor_copy(self.b_sb[:, :], self.x_sb[0][0:16, 0:1024])
        self.wload(self.x_sb[0][0:1, 0:1024], self.i_bias)
        nc.vector.tensor_copy(self.bias_bf[:, :], self.x_sb[0][0:1, 0:1024])
        self.bar()
        self.cur_chunk = 0
        self.dump("kerq0", self.kerq[:, 0:1024])
        self.dump("wqT0", self.wqT[:, 0:256])
        self.dump("wvT0", self.wvT[:, 0:1024])
        self.dump("w1T0", self.w1T[:, 0:1024])
        self.dump("wo1T0", self.wo1T[:, 0:1024])
        self.dump("a_sb", self.a_sb[:, :])
        self.dump("b_sb", self.b_sb[:, :])
        self.dump("bias_bf", self.bias_bf[:, :])
        self.dump("idn_bf", self.idn_bf[:, :])
        self.cur_chunk = -1

    def layernorm(self, x_ap, xbuf):
        nc = self.nc
        C1024 = float(1.0 / 1024.0)
        if getattr(self, "have_msum", False):
            self.have_msum = False
        else:
            self.g("dve", [xbuf], ["mcol"], lambda: nc.vector.tensor_reduce(
                out=self.m_col[:, :], in_=x_ap, axis=AX.X, op=ALU.add))
        self.g("act", [xbuf], ["sqA", "s2col"], lambda: nc.scalar.activation(
            out=self.sq[:, 0:1024], in_=x_ap, func=AF.Square,
            accum_out=self.s2_col[:, :]))

        def _v0():
            nc.vector.scalar_tensor_tensor(
                out=self.v_col[:, :], in0=self.m_col[:, :], scalar=-C1024,
                in1=self.m_col[:, :], op0=ALU.mult, op1=ALU.mult)
            return nc.vector.tensor_scalar(
                out=self.t_col[:, :], in0=self.m_col[:, :], scalar1=C1024,
                scalar2=None, op0=ALU.mult)
        self.g("dve", ["mcol"], ["mcol"], _v0, drain=True)
        self.g("dve", ["mcol", "s2col"], ["mcol"], lambda: nc.vector.tensor_scalar(
            out=self.v_col[:, :], in0=self.v_col[:, :],
            scalar1=self.s2_col[:, 0:1], scalar2=C1024,
            op0=ALU.add, op1=ALU.mult), drain=True)

        def _rstd():
            nc.scalar.activation(out=self.r_col[:, :], in_=self.v_col[:, :],
                                 func=AF.Ln, bias=self.consts[:, 0:1])
            nc.scalar.drain()
            return nc.scalar.activation(out=self.r_col[:, :], in_=self.r_col[:, :],
                                        func=AF.Exp, scale=self.consts[:, 1:2])
        self.g("act", ["mcol"], ["rcol"], _rstd)
        self.g("dve", [xbuf, "mcol", "rcol"], ["xn"], lambda: nc.vector.tensor_scalar(
            out=self.xn[:, :], in0=x_ap, scalar1=self.t_col[:, 0:1],
            scalar2=self.r_col[:, 0:1], op0=ALU.subtract, op1=ALU.mult),
            drain=True)

    def xn_transpose(self):
        nc = self.nc

        def _t():
            for j in range(8):
                last = nc.tensor.transpose(self.tp_ps[:, j * 128:(j + 1) * 128],
                                           self.xn[:, j * 128:(j + 1) * 128],
                                           self.idn_bf[:, :])
            return last
        self.g("pe", ["xn"], ["tp", "tp0", "tp1"], _t)
        self.g("act", ["tp"], ["xnT"], lambda: nc.scalar.activation(
            out=self.xnT[:, 0:512], in_=self.tp_ps[:, 0:512], func=AF.Copy))
        self.g("act", ["tp"], ["xnTb"], lambda: nc.scalar.activation(
            out=self.xnT[:, 512:1024], in_=self.tp_ps[:, 512:1024], func=AF.Copy))

    def chunk_body(self, x_ap, xbuf):
        nc = self.nc
        WSA = ["wsA"]
        WSB = ["wsB"]
        WS = ["wsA", "wsB"]

        self.layernorm(x_ap, xbuf)
        self.dump("l0_xn", self.xn[:, :])
        self.xn_transpose()
        self.dump("l0_xnT", self.xnT[:, :])
        pend, self.pending = self.pending, []
        for f in pend:
            f()

        def _proj():
            for g_ in range(8):
                lhs = self.xnT[:, g_ * 128:(g_ + 1) * 128]
                self.mm(self.ws_ps[:, 0:256], lhs, self.wqT[:, g_ * 256:(g_ + 1) * 256],
                        start=(g_ == 0), stop=False)
                self.mm(self.ws_ps[:, 256:512], lhs, self.wkT[:, g_ * 256:(g_ + 1) * 256],
                        start=False, stop=(g_ == 7))
                self.mm(self.ws_ps[:, 512:1024], lhs,
                        self.wvT[:, g_ * 1024:g_ * 1024 + 512], start=(g_ == 0), stop=(g_ == 7))
                last = self.mm(self.ws_ps[:, 1024:1536], lhs,
                               self.wvT[:, g_ * 1024 + 512:(g_ + 1) * 1024],
                               start=(g_ == 0), stop=(g_ == 7))
            return last
        self.g("pe", ["xnT", "xnTb"], WS, _proj)

        def _ffn():
            self.mm(self.out_ps[:, 0:512], self.ones_bf[:, :], self.bias_bf[:, 0:512],
                    start=True, stop=False)
            self.mm(self.out_ps[:, 512:1024], self.ones_bf[:, :], self.bias_bf[:, 512:1024],
                    start=True, stop=False)
            for g_ in range(8):
                lhs = self.xnT[:, g_ * 128:(g_ + 1) * 128]
                self.mm(self.out_ps[:, 0:512], lhs, self.kerq[:, g_ * 1024:g_ * 1024 + 512],
                        start=False, stop=False)
                last = self.mm(self.out_ps[:, 512:1024], lhs,
                               self.kerq[:, g_ * 1024 + 512:(g_ + 1) * 1024],
                               start=False, stop=False)
            return last
        self.g("pe", ["xnT", "xnTb"], ["out"], _ffn)

        def _ut():
            for g_ in range(8):
                last = self.mm(self.sm_ps[0:16, 0:128], self.a_sb[:, g_ * 16:(g_ + 1) * 16],
                               self.xnT[:, g_ * 128:(g_ + 1) * 128],
                               start=(g_ == 0), stop=(g_ == 7))
            return last
        self.g("pe", ["xnT", "xnTb"], ["sm"], _ut)

        qv = self.ws_ps[:, 0:256].rearrange("p (h f) -> p h f", h=16)
        kv = self.ws_ps[:, 256:512].rearrange("p (h f) -> p h f", h=16)
        qpadv = self.qpad[:, :].rearrange("p (h c) -> p h c", c=128)[:, :, 0:16]
        kpadv = self.kpad[:, :].rearrange("p (h c) -> p h c", c=128)[:, :, 0:16]

        def _evac():
            nc.scalar.activation(out=qpadv, in_=qv, func=AF.Copy, scale=self.consts[:, 3:4])
            nc.scalar.activation(out=kpadv, in_=kv, func=AF.Copy, scale=self.consts[:, 3:4])
            nc.scalar.activation(out=self.q_s[:, :].rearrange("p (h f) -> p h f", h=16),
                                 in_=qv, func=AF.Copy, scale=self.consts[:, 4:5])
            nc.scalar.activation(out=self.k_s[:, :].rearrange("p (h f) -> p h f", h=16),
                                 in_=kv, func=AF.Copy, scale=self.consts[:, 4:5])
            vaugv = self.v_aug[:, :].rearrange("p (h c) -> p h c", c=65)[:, :, 0:64]
            return nc.scalar.activation(
                out=vaugv,
                in_=self.ws_ps[:, 512:1536].rearrange("p (h e) -> p h e", h=16),
                func=AF.Copy)
        self.g("act", WS, ["qpad", "kpad", "q_s", "k_s", "v_aug"], _evac)
        self.g("dve", ["sm"], ["uT"], lambda: nc.vector.tensor_copy(
            self.uT[:, :], self.sm_ps[0:16, 0:128]))
        self.dump("l0_qpad", self.qpad[:, :])
        self.dump("l0_q_s", self.q_s[:, :])
        self.dump("l0_v_aug", self.v_aug[:, :])
        self.dump("l0_uT", self.uT[:, :])

        def _lora():
            self.mm(self.out_ps[:, 0:512], self.uT[0:16, :], self.b_sb[:, 0:512],
                    start=False, stop=False)
            return self.mm(self.out_ps[:, 512:1024], self.uT[0:16, :], self.b_sb[:, 512:1024],
                           start=False, stop=False)
        self.g("pe", ["uT"], ["out"], _lora)

        qs_v = self.q_s[:, :].rearrange("p (h f) -> p h f", h=16)
        ks_v = self.k_s[:, :].rearrange("p (h f) -> p h f", h=16)
        qf_v = self.qf[:, :].rearrange("p (h i j) -> p h i j", h=16, i=16)
        kf_v = self.kf[:, :].rearrange("p (h i j) -> p h i j", h=16, i=16)

        for r in range(8):
            if r == 2:
                self.g("dve", ["q_s", "qpad"], ["qfa"], lambda: nc.vector.tensor_tensor(
                    out=qf_v[:, 0:8],
                    in0=qs_v[:, 0:8].unsqueeze(3).broadcast_to([128, 8, 16, 16]),
                    in1=qpadv[:, 0:8].unsqueeze(2).broadcast_to([128, 8, 16, 16]),
                    op=ALU.mult))
            if r == 4:
                self.g("dve", ["q_s", "qpad"], ["qfb"], lambda: nc.vector.tensor_tensor(
                    out=qf_v[:, 8:16],
                    in0=qs_v[:, 8:16].unsqueeze(3).broadcast_to([128, 8, 16, 16]),
                    in1=qpadv[:, 8:16].unsqueeze(2).broadcast_to([128, 8, 16, 16]),
                    op=ALU.mult))
                self.dump("l0_qf", self.qf[:, :])
            if r < 2:
                srct, src_name, sc0 = self.qpad, "qpad", r * 1024
            elif r < 4:
                srct, src_name, sc0 = self.kpad, "kpad", (r - 2) * 1024
            else:
                srct, src_name, sc0 = self.qf, ("qfa" if r < 6 else "qfb"), (r - 4) * 1024
            use_ws = (r % 2 == 1)
            tgt = self.ws_ps[:, 0:512].bitcast(BF16) if use_ws else self.tp_ps
            tname = "wsA" if use_ws else "tp"
            if r < 4:
                dname, dst, dc0 = "qkT", self.qkT, r * 1024
            else:
                dname, dst, dc0 = "qfT", self.qfT, (r - 4) * 1024

            def _t(srct=srct, sc0=sc0, tgt=tgt):
                for j in range(8):
                    last = nc.tensor.transpose(tgt[:, j * 128:(j + 1) * 128],
                                               srct[:, sc0 + j * 128:sc0 + (j + 1) * 128],
                                               self.idn_bf[:, :])
                return last
            wnames = [tname] if tname != "tp" else ["tp", "tp0", "tp1"]
            self.g("pe", [src_name], wnames, _t)
            eng = "act" if r % 2 == 0 else "dve"
            h = nc.scalar if eng == "act" else nc.vector

            def _e(h=h, dst=dst, dc0=dc0, tgt=tgt):
                if h is nc.vector:
                    return h.tensor_copy(dst[:, dc0:dc0 + 1024], tgt[:, 0:1024])
                return h.activation(out=dst[:, dc0:dc0 + 1024], in_=tgt[:, 0:1024],
                                    func=AF.Copy)
            self.g(eng, [tname], [dname], _e)
        self.dump("l0_qkT", self.qkT[:, :])
        self.dump("l0_qfT", self.qfT[:, :])

        for half in range(2):
            hs = range(8 * half, 8 * half + 8)
            wsn = WSA if half == 0 else WSB

            def _p(hs=hs):
                for h in hs:
                    lhs = self.qkT[0:16, 2048 + h * 128:2048 + (h + 1) * 128]
                    rhs = self.qkT[0:16, h * 128:(h + 1) * 128]
                    last = self.mm(self.ws_ps[:, h * 128:(h + 1) * 128], lhs, rhs,
                                   start=(h % 4 == 0), stop=(h % 4 == 3))
                return last
            self.g("pe", ["qkT"], wsn, _p)
            c0 = half * 1024
            self.g("act", wsn, ["sqA" if half == 0 else "sqB"], lambda c0=c0: nc.scalar.activation(
                out=self.sq[:, c0:c0 + 1024], in_=self.ws_ps[:, c0:c0 + 1024],
                func=AF.Square, scale=self.consts[:, 2:3], bias=self.consts[:, 2:3]))
            for q in range(2):
                qc = c0 + q * 512
                self.g("pool", ["sqA" if half == 0 else "sqB"], ["AT%d" % (2 * half + q)],
                       lambda qc=qc: nc.gpsimd.tensor_tensor(
                    out=self.AT[:, qc:qc + 512].rearrange("p (h t) -> p h t", h=4),
                    in0=self.sq[:, qc:qc + 512].rearrange("p (h t) -> p h t", h=4),
                    in1=self.maskT[:, :].unsqueeze(1).broadcast_to([128, 4, 128]),
                    op=ALU.mult))

        self.g("pool", ["k_s", "kpad"], ["kf"], lambda: nc.gpsimd.tensor_tensor(
            out=kf_v,
            in0=ks_v.unsqueeze(3).broadcast_to([128, 16, 16, 16]),
            in1=kpadv.unsqueeze(2).broadcast_to([128, 16, 16, 16]), op=ALU.mult))
        self.dump("l0_kf", self.kf[:, :])
        self.dump("l0_AT", self.AT[:, :])

        for half in range(2):
            hs = range(8 * half, 8 * half + 8)
            wsn = WSA if half == 0 else WSB

            def _y(hs=hs):
                for h in hs:
                    yc = h * 128
                    self.mm(self.ws_ps[:, yc:yc + 65],
                            self.AT[:, h * 128:(h + 1) * 128],
                            self.v_aug[:, h * 65:(h + 1) * 65],
                            start=(h % 4 == 0), stop=False)
                    self.mm(self.ws_ps[:, yc:yc + 65],
                            self.mask05[:, :],
                            self.v_aug[:, h * 65:(h + 1) * 65],
                            start=False, stop=False)
                    self.mm(self.ws_ps[:, yc:yc + 65],
                            self.qkT[0:17, h * 128:(h + 1) * 128],
                            self.S_low_bf[0:17, colh(h):colh(h) + 65],
                            start=False, stop=False)
                    for j in range(2):
                        last = self.mm(self.ws_ps[:, yc:yc + 65],
                                       self.qfT[:, (2 * h + j) * 128:(2 * h + j + 1) * 128],
                                       self.S_q_bf[j][:, colh(h):colh(h) + 65],
                                       start=False, stop=(h % 4 == 3 and j == 1))
                return last
            self.g("pe", ["AT%d" % (2 * half), "AT%d" % (2 * half + 1), "v_aug",
                          "qkT", "qfT", "Slowbf", "Sq0bf", "Sq1bf"], wsn, _y)

            denv = self.ws_ps[:, :].rearrange(
                "p (h c) -> p h c", c=128)[:, 8 * half:8 * half + 8, 64:65]

            def _z(half=half, denv=denv):
                nc.vector.tensor_scalar(out=self.dent[:, 8 * half:8 * half + 8],
                                        in0=denv.squeeze(2),
                                        scalar1=float(1e-12), scalar2=None, op0=ALU.add)
                nc.vector.drain()
                return nc.vector.reciprocal(self.z16[:, 8 * half:8 * half + 8],
                                            self.dent[:, 8 * half:8 * half + 8])
            self.g("dve", wsn, ["z%d" % half], _z)

            def _zy(half=half):
                return nc.vector.tensor_tensor(
                    out=self.y_bf[:, half * 512:(half + 1) * 512].rearrange(
                        "p (h e) -> p h e", h=8),
                    in0=self.ws_ps[:, :].rearrange(
                        "p (h c) -> p h c", c=128)[:, 8 * half:8 * half + 8, 0:64],
                    in1=self.z16[:, 8 * half:8 * half + 8].unsqueeze(2).broadcast_to(
                        [128, 8, 64]), op=ALU.mult)
            self.g("dve", wsn + ["z%d" % half], ["ybf%d" % half], _zy, drain=True)
        self.dump("l0_dent", self.dent[:, :])
        self.dump("l0_ybf", self.y_bf[:, :])

        for half in range(2):
            o = half * 512

            def _yt(o=o):
                for j in range(4):
                    last = nc.tensor.transpose(self.tp_ps[:, o + j * 128:o + (j + 1) * 128],
                                               self.y_bf[:, o + j * 128:o + (j + 1) * 128],
                                               self.idn_bf[:, :])
                return last
            self.g("pe", ["ybf%d" % half], ["tp%d" % half], _yt)
            self.g("act", ["tp%d" % half], ["yT%d" % half], lambda o=o: nc.scalar.activation(
                out=self.yT[:, o:o + 512], in_=self.tp_ps[:, o:o + 512], func=AF.Copy))

            def _wo(half=half):
                for j in range(4 * half, 4 * half + 4):
                    lhs = self.yT[:, j * 128:(j + 1) * 128]
                    self.mm(self.out_ps[:, 0:512], lhs, self.woT[:, j * 1024:j * 1024 + 512],
                            start=False, stop=(j == 7))
                    last = self.mm(self.out_ps[:, 512:1024], lhs,
                                   self.woT[:, j * 1024 + 512:(j + 1) * 1024],
                                   start=False, stop=(j == 7))
                return last
            self.g("pe", ["yT%d" % half], ["out"], _wo)

        self.g("dve", ["out", xbuf], [xbuf, "mcol"],
               lambda: nc.vector.scalar_tensor_tensor(
                   out=x_ap, in0=self.out_ps[:, :], scalar=0.0, in1=x_ap,
                   op0=ALU.add, op1=ALU.add, accum_out=self.m_col[:, :]))
        self.dump("l0_x", x_ap)
        self.have_msum = True

        def ds_round(rnd, defer_add=False):
            def _ds(rnd=rnd):
                for h in range(16):
                    c0 = colh(h)
                    if rnd == 0:
                        lhs = self.kpad[:, 128 * h:128 * h + 17]
                        out = self.ws_ps[0:17, c0:c0 + 65]
                    else:
                        j = rnd - 1
                        lhs = self.kf[:, (2 * h + j) * 128:(2 * h + j + 1) * 128]
                        out = self.ws_ps[:, c0:c0 + 65]
                    last = self.mm(out, lhs, self.v_aug[:, h * 65:(h + 1) * 65],
                                   start=h in (0, 7, 14), stop=h in (6, 13, 15))
                return last
            src_name = "kpad" if rnd == 0 else "kf"
            self.g("pe", [src_name, "v_aug"], WS, _ds)
            S = self.S_low if rnd == 0 else self.S_q[rnd - 1]
            sname = ["Slow", "Sq0", "Sq1"][rnd]
            rows = 17 if rnd == 0 else 128

            def _add(S=S, rows=rows, sname=sname):
                self.g("dve", WS + [sname], [sname], lambda:
                       nc.vector.tensor_tensor(out=S[0:rows, :], in0=S[0:rows, :],
                                               in1=self.ws_ps[0:rows, 0:S_W],
                                               op=ALU.add))
            if defer_add:
                self.pending.append(_add)
            else:
                _add()

        ds_round(0)
        self.layernorm(x_ap, xbuf)
        self.dump("l1_xn", self.xn[:, :])
        self.xn_transpose()

        def _ffn1():
            for g_ in range(8):
                lhs = self.xnT[:, g_ * 128:(g_ + 1) * 128]
                self.mm(self.out_ps[:, 0:512], lhs, self.w1T[:, g_ * 1024:g_ * 1024 + 512],
                        start=(g_ == 0), stop=False)
                last = self.mm(self.out_ps[:, 512:1024], lhs,
                               self.w1T[:, g_ * 1024 + 512:(g_ + 1) * 1024],
                               start=(g_ == 0), stop=False)
            return last
        self.g("pe", ["xnT", "xnTb"], ["out"], _ffn1)

        def _proj1():
            for g_ in range(8):
                lhs = self.xnT[:, g_ * 128:(g_ + 1) * 128]
                self.mm(self.ws_ps[:, 0:256], lhs, self.wq1T[:, g_ * 256:(g_ + 1) * 256],
                        start=(g_ == 0), stop=False)
                self.mm(self.ws_ps[:, 256:512], lhs, self.wk1T[:, g_ * 256:(g_ + 1) * 256],
                        start=False, stop=(g_ == 7))
                last = self.mm(self.ws_ps[:, 512:768], lhs,
                               self.wv1T[:, g_ * 256:(g_ + 1) * 256],
                               start=(g_ == 0), stop=(g_ == 7))
            return last
        self.g("pe", ["xnT", "xnTb"], WS, _proj1)

        e_qk = self.sq[:, 1024:1536]
        self.g("dve", WS, ["sqB"], lambda: nc.vector.tensor_scalar(
            out=e_qk, in0=self.ws_ps[:, 0:512], scalar1=0.0, scalar2=None,
            op0=ALU.min))
        self.g("act", ["sqB"], ["sqB"], lambda: nc.scalar.activation(
            out=e_qk, in_=e_qk, func=AF.Exp))
        self.g("dve", WS + ["sqB"], ["pqk"], lambda: nc.vector.scalar_tensor_tensor(
            out=self.p_qk[:, :], in0=self.ws_ps[:, 0:512], scalar=0.0, in1=e_qk,
            op0=ALU.max, op1=ALU.add))
        self.g("act", WS, ["v1aug"], lambda: nc.scalar.activation(
            out=self.v1_aug[:, :].rearrange("p (h c) -> p h c", c=65)[:, :, 0:64],
            in_=self.ws_ps[:, 512:768].rearrange("p (h e) -> p h e", h=4),
            func=AF.Copy))
        self.g("act", ["pqk"], ["sqA"], lambda: nc.scalar.activation(
            out=self.sq[:, 0:512], in_=self.p_qk[:, :], func=AF.Square))
        self.g("dve", ["sqA"], ["nrm"], lambda: nc.vector.tensor_reduce(
            out=self.nrm[:, :].rearrange("p (a b) -> p a b", b=1),
            in_=self.sq[:, 0:512].rearrange("p (a b) -> p a b", a=8),
            axis=AX.X, op=ALU.add))

        def _sqrt():
            nc.scalar.activation(out=self.rn[:, :], in_=self.nrm[:, :], func=AF.Ln)
            nc.scalar.drain()
            return nc.scalar.activation(out=self.rn[:, :], in_=self.rn[:, :],
                                        func=AF.Exp, scale=self.consts[:, 3:4])
        self.g("act", ["nrm"], ["rn"], _sqrt)

        def _rn():
            nc.vector.tensor_scalar(out=self.rn[:, :], in0=self.rn[:, :],
                                    scalar1=float(1e-6), scalar2=None, op0=ALU.add)
            nc.vector.drain()
            return nc.vector.reciprocal(self.rn[:, :], self.rn[:, :])
        self.g("dve", ["rn"], ["rn"], _rn)

        self.g("dve", ["pqk", "rn"], ["q1pad"], lambda: nc.vector.tensor_tensor(
            out=self.q1pad[:, :].rearrange("p (h c) -> p h c", c=128)[:, :, 0:64],
            in0=self.p_qk[:, 0:256].rearrange("p (a b) -> p a b", a=4),
            in1=self.rn[:, 0:4].unsqueeze(2).broadcast_to([128, 4, 64]), op=ALU.mult),
            drain=True)
        self.g("dve", ["pqk", "rn"], ["k1pad"], lambda: nc.vector.tensor_tensor(
            out=self.k1pad[:, :].rearrange("p (h c) -> p h c", c=128)[:, :, 0:64],
            in0=self.p_qk[:, 256:512].rearrange("p (a b) -> p a b", a=4),
            in1=self.rn[:, 4:8].unsqueeze(2).broadcast_to([128, 4, 64]), op=ALU.mult))
        self.dump("l1_pqk", self.p_qk[:, :])
        self.dump("l1_rn", self.rn[:, :])
        self.dump("l1_q1pad", self.q1pad[:, :])
        self.dump("l1_k1pad", self.k1pad[:, :])
        self.dump("l1_v1aug", self.v1_aug[:, :])

        ds_round(1)

        def _t1q():
            for j in range(4):
                last = nc.tensor.transpose(self.tp_ps[:, j * 128:(j + 1) * 128],
                                           self.q1pad[:, j * 128:(j + 1) * 128],
                                           self.idn_bf[:, :])
            return last
        self.g("pe", ["q1pad"], ["tp", "tp0"], _t1q)

        def _t1k():
            for j in range(4):
                last = nc.tensor.transpose(self.tp_ps[:, 512 + j * 128:512 + (j + 1) * 128],
                                           self.k1pad[:, j * 128:(j + 1) * 128],
                                           self.idn_bf[:, :])
            return last
        self.g("pe", ["k1pad"], ["tp", "tp1"], _t1k)
        self.g("act", ["tp", "tp0", "tp1"], ["qk1T"], lambda: nc.scalar.activation(
            out=self.qk1T[:, :], in_=self.tp_ps[:, :], func=AF.Copy))

        def _p1():
            for h in range(4):
                lhs = self.qk1T[0:64, 512 + h * 128:512 + (h + 1) * 128]
                rhs = self.qk1T[0:64, h * 128:(h + 1) * 128]
                last = self.mm(self.sm_ps[:, h * 128:(h + 1) * 128], lhs, rhs,
                               start=(h == 0), stop=(h == 3))
            return last
        self.g("pe", ["qk1T"], ["sm"], _p1)
        self.g("dve", ["sm"], ["A1T"], lambda: nc.vector.tensor_tensor(
            out=self.A1T[:, :].rearrange("p (h t) -> p h t", h=4),
            in0=self.sm_ps[:, :].rearrange("p (h t) -> p h t", h=4),
            in1=self.maskT[:, :].unsqueeze(1).broadcast_to([128, 4, 128]), op=ALU.mult))
        self.dump("l1_A1T", self.A1T[:, :])

        def _y1():
            for h in range(4):
                yc = h * 128
                self.mm(self.sm_ps[:, yc:yc + 65], self.A1T[:, h * 128:(h + 1) * 128],
                        self.v1_aug[:, h * 65:(h + 1) * 65], start=(h == 0), stop=False)
                last = self.mm(self.sm_ps[:, yc:yc + 65],
                               self.qk1T[0:64, h * 128:(h + 1) * 128],
                               self.S1_bf[0:64, h * 65:(h + 1) * 65],
                               start=False, stop=(h == 3))
            return last
        self.g("pe", ["A1T", "v1aug", "qk1T", "S1bf"], ["sm"], _y1)

        denv1 = self.sm_ps[:, :].rearrange("p (h c) -> p h c", c=128)[:, :, 64:65]

        def _z1():
            nc.vector.tensor_scalar(out=self.dent4[:, :], in0=denv1.squeeze(2),
                                    scalar1=float(1e-6), scalar2=None, op0=ALU.add)
            nc.vector.drain()
            return nc.vector.reciprocal(self.z4[:, :], self.dent4[:, :])
        self.g("dve", ["sm"], ["z4"], _z1)
        self.g("dve", ["sm", "z4"], ["y1"], lambda: nc.vector.tensor_tensor(
            out=self.y1_bf[:, :].rearrange("p (h e) -> p h e", h=4),
            in0=self.sm_ps[:, :].rearrange("p (h c) -> p h c", c=128)[:, :, 0:64],
            in1=self.z4[:, :].unsqueeze(2).broadcast_to([128, 4, 64]), op=ALU.mult),
            drain=True)
        self.dump("l1_y1bf", self.y1_bf[:, :])

        def _y1t():
            for j in range(2):
                last = nc.tensor.transpose(self.tp_ps[:, j * 128:(j + 1) * 128],
                                           self.y1_bf[:, j * 128:(j + 1) * 128],
                                           self.idn_bf[:, :])
            return last
        self.g("pe", ["y1"], ["tp", "tp0", "tp1"], _y1t)
        self.g("act", ["tp"], ["y1T"], lambda: nc.scalar.activation(
            out=self.y1T[:, :], in_=self.tp_ps[:, 0:256], func=AF.Copy))

        def _wo1():
            for j in range(2):
                lhs = self.y1T[:, j * 128:(j + 1) * 128]
                self.mm(self.out_ps[:, 0:512], lhs, self.wo1T[:, j * 1024:j * 1024 + 512],
                        start=False, stop=(j == 1))
                last = self.mm(self.out_ps[:, 512:1024], lhs,
                               self.wo1T[:, j * 1024 + 512:(j + 1) * 1024],
                               start=False, stop=(j == 1))
            return last
        self.g("pe", ["y1T"], ["out"], _wo1)

        def _resid2():
            nc.vector.tensor_tensor(out=x_ap, in0=x_ap, in1=self.out_ps[:, :],
                                    op=ALU.add)
            return nc.vector.drain()
        self.g("dve", ["out", xbuf], [xbuf], _resid2)
        self.resid_mark = self.n["dve"]

        def _ds1():
            for h in range(4):
                last = self.mm(self.sm_ps[0:64, h * 65:(h + 1) * 65],
                               self.k1pad[:, h * 128:h * 128 + 64],
                               self.v1_aug[:, h * 65:(h + 1) * 65],
                               start=(h == 0), stop=(h == 3))
            return last
        self.g("pe", ["k1pad", "v1aug"], ["sm"], _ds1)
        self.pending.append(lambda: self.g(
            "dve", ["sm", "S1"], ["S1"], lambda: nc.vector.tensor_tensor(
                out=self.S1[:, :], in0=self.S1[:, :], in1=self.sm_ps[0:64, 0:260],
                op=ALU.add)))
        ds_round(2, defer_add=True)
        self.pending.append(lambda: self.g(
            "act", ["Slow"], ["Slowbf"], lambda: nc.scalar.activation(
                out=self.S_low_bf[0:17, :], in_=self.S_low[0:17, :], func=AF.Copy)))
        self.pending.append(lambda: self.g(
            "act", ["Sq0"], ["Sq0bf"], lambda: nc.scalar.activation(
                out=self.S_q_bf[0][:, :], in_=self.S_q[0][:, :], func=AF.Copy)))
        self.pending.append(lambda: self.g(
            "act", ["Sq1"], ["Sq1bf"], lambda: nc.scalar.activation(
                out=self.S_q_bf[1][:, :], in_=self.S_q[1][:, :], func=AF.Copy)))
        self.pending.append(lambda: self.g(
            "act", ["S1"], ["S1bf"], lambda: nc.scalar.activation(
                out=self.S1_bf[:, :], in_=self.S1[:, :], func=AF.Copy)))
        self.dump("l0_Slow", self.S_low_bf[:, :])
        self.dump("l0_Sq0", self.S_q_bf[0][:, :])
        self.dump("l0_Sq1", self.S_q_bf[1][:, :])
        self.dump("l1_S1", self.S1_bf[:, :])


    def build(self):
        nc = self.nc
        self.alloc()
        self.prep()
        self.bar()
        self.reset_deps()

        for c in range(min(2, self.nchunks)):
            nc.sync.dma_start(self.x_sb[c % 2][:, :],
                              self.i_x[c * C:(c + 1) * C, :]).then_inc(self.sem_ld, 16)
            self.ld += 16

        for c in range(self.nchunks):
            self.cur_chunk = c
            xbuf = "x%d" % (c % 2)
            buf = self.x_sb[c % 2][:, :]
            nc.vector.wait_ge(self.sem_ld, 16 * (c + 1))
            nc.scalar.wait_ge(self.sem_ld, 16 * (c + 1))
            self.chunk_body(buf, xbuf)
            nc.sync.wait_ge(self.sems["dve"], self.resid_mark)
            nc.sync.dma_start(self.o_out[c * C:(c + 1) * C, :], buf).then_inc(
                self.sem_st, 16)
            self.st += 16
            if c + 2 < self.nchunks:
                nc.sync.wait_ge(self.sem_st, 16 * (c + 1))
                nc.sync.dma_start(self.x_sb[c % 2][:, :],
                                  self.i_x[(c + 2) * C:(c + 3) * C, :]).then_inc(
                    self.sem_ld, 16)
                self.ld += 16

        nc.sync.wait_ge(self.sem_st, self.st)
        return nc


def build_program(nchunks=NCH, layers=(0, 1), dbg=False):
    return KB(nchunks=nchunks, layers=layers, dbg=dbg).build()


def make_in_map(inputs, b):
    f32 = np.float32
    return {
        "x": np.ascontiguousarray(np.asarray(inputs["L0_attn_hidden_states"])[b], dtype=f32),
        "wq0": np.ascontiguousarray(np.asarray(inputs["L0_attn_Wq"], dtype=f32)),
        "wk0": np.ascontiguousarray(np.asarray(inputs["L0_attn_Wk"], dtype=f32)),
        "wv0": np.ascontiguousarray(np.asarray(inputs["L0_attn_Wv"], dtype=f32)),
        "wo0": np.ascontiguousarray(np.asarray(inputs["L0_attn_Wo"], dtype=f32)),
        "ker": np.ascontiguousarray(np.asarray(inputs["L0_ffn_kernel"], dtype=f32)),
        "bias": np.ascontiguousarray(np.asarray(inputs["L0_ffn_bias"], dtype=f32)).reshape(1, 1024),
        "la": np.ascontiguousarray(np.asarray(inputs["L0_ffn_a"], dtype=f32)),
        "lb": np.ascontiguousarray(np.asarray(inputs["L0_ffn_b"], dtype=f32)),
        "wq1": np.ascontiguousarray(np.asarray(inputs["L1_attn_Wq"], dtype=f32)),
        "wk1": np.ascontiguousarray(np.asarray(inputs["L1_attn_Wk"], dtype=f32)),
        "wv1": np.ascontiguousarray(np.asarray(inputs["L1_attn_Wv"], dtype=f32)),
        "wo1": np.ascontiguousarray(np.asarray(inputs["L1_attn_Wo"], dtype=f32)),
        "w1": np.ascontiguousarray(np.asarray(inputs["L1_ffn_weight"], dtype=f32)),
        "idn": np.eye(128, dtype=f32),
        "maskT": np.triu(np.ones((128, 128), dtype=f32)),
        "mask05": np.triu(np.full((128, 128), 0.5, dtype=ml_dtypes.bfloat16)),
    }


def kernel(**inputs):
    nc = build_program()
    in_maps = [make_in_map(inputs, b) for b in range(8)]
    res = run_bass_kernel_spmd(nc, in_maps, core_ids=list(range(8)))
    return np.stack([r["out"] for r in res.results], axis=0).astype(np.float32)


# version 08
# speedup vs baseline: 1.7355x; 1.0174x over previous
import contextlib
import numpy as np
import ml_dtypes

import concourse.bass as bass
import concourse.mybir as mybir
from concourse.bass_utils import run_bass_kernel_spmd

F32 = mybir.dt.float32
BF16 = mybir.dt.bfloat16
AF = mybir.ActivationFunctionType
ALU = mybir.AluOpType
AX = mybir.AxisListType
ET = mybir.EngineType

MAGIC = float(1.5 * 2 ** 23)
SQ5 = 0.7071067811865476
QS = 0.35355339059327373

T_SEQ = 4096
D = 1024
C = 128
NCH = T_SEQ // C
H0, F0, V0 = 16, 16, 64
H1, D1 = 4, 64

def colh(h):
    return 65 * (h % 7) + 512 * (h // 7)

S_W = 1536


class KB:

    def __init__(self, nchunks=NCH, layers=(0, 1), dbg=False):
        self.nchunks = nchunks
        self.layers = layers
        self.dbg = dbg
        self.cur_chunk = -1
        self.dbg_tensors = {}
        self.nc = bass.Bass("TRN2", target_bir_lowering=False)
        self.stack = contextlib.ExitStack()
        self.ld = 0
        self.wld = 0
        self.st = 0
        self.n = {"pe": 0, "dve": 0, "act": 0, "pool": 0}
        self.deps = {}
        self.pending = []
        self.last_wait = {}

    def bar(self, pool=False):
        self.nc.all_engine_barrier()

    def eng_handle(self, eng):
        nc = self.nc
        return {"pe": nc.tensor, "dve": nc.vector, "act": nc.scalar,
                "pool": nc.gpsimd, "sp": nc.sync}[eng]

    def reset_deps(self):
        self.deps = {}
        self.last_wait = {}

    def dep_wait(self, eng, reads=(), writes=()):
        h = self.eng_handle(eng)
        need = {}
        for b in reads:
            w = self.deps.get(b, {}).get("w")
            if w and w[0] != eng:
                need[w[0]] = max(need.get(w[0], 0), w[1])
        for b in writes:
            d = self.deps.get(b, {})
            w = d.get("w")
            if w and w[0] != eng:
                need[w[0]] = max(need.get(w[0], 0), w[1])
            for re_, rc in d.get("r", {}).items():
                if re_ != eng:
                    need[re_] = max(need.get(re_, 0), rc)
        for prod, cnt in sorted(need.items()):
            key = (eng, prod)
            if self.last_wait.get(key, 0) < cnt:
                sem = self.sem_wld if prod == "wld" else self.sems[prod]
                h.wait_ge(sem, cnt)
                self.last_wait[key] = cnt

    def dep_done(self, inst, eng, reads=(), writes=()):
        self.n[eng] += 1
        inst.then_inc(self.sems[eng], 1)
        c = self.n[eng]
        for b in writes:
            self.deps[b] = {"w": (eng, c), "r": {}}
        for b in reads:
            d = self.deps.setdefault(b, {"w": None, "r": {}})
            d["r"][eng] = c

    def g(self, eng, reads, writes, emit, drain=False, post_drain=False):
        self.dep_wait(eng, reads, writes)
        if drain:
            self.eng_handle(eng).drain()
        last = emit()
        if post_drain:
            last = self.eng_handle(eng).drain()
        self.dep_done(last, eng, reads, writes)
        return self.n[eng]

    def alloc(self):
        nc = self.nc
        sb = nc.alloc_sbuf_tensor
        ps = nc.alloc_psum_tensor

        self.i_x = nc.dram_tensor("x", [T_SEQ, D], F32, kind="ExternalInput").ap()
        self.i_wq0 = nc.dram_tensor("wq0", [256, 1024], F32, kind="ExternalInput").ap()
        self.i_wk0 = nc.dram_tensor("wk0", [256, 1024], F32, kind="ExternalInput").ap()
        self.i_wv0 = nc.dram_tensor("wv0", [1024, 1024], F32, kind="ExternalInput").ap()
        self.i_wo0 = nc.dram_tensor("wo0", [1024, 1024], F32, kind="ExternalInput").ap()
        self.i_ker = nc.dram_tensor("ker", [1024, 1024], F32, kind="ExternalInput").ap()
        self.i_bias = nc.dram_tensor("bias", [1, 1024], F32, kind="ExternalInput").ap()
        self.i_a = nc.dram_tensor("la", [1024, 16], F32, kind="ExternalInput").ap()
        self.i_b = nc.dram_tensor("lb", [16, 1024], F32, kind="ExternalInput").ap()
        self.i_wq1 = nc.dram_tensor("wq1", [256, 1024], F32, kind="ExternalInput").ap()
        self.i_wk1 = nc.dram_tensor("wk1", [256, 1024], F32, kind="ExternalInput").ap()
        self.i_wv1 = nc.dram_tensor("wv1", [256, 1024], F32, kind="ExternalInput").ap()
        self.i_wo1 = nc.dram_tensor("wo1", [1024, 256], F32, kind="ExternalInput").ap()
        self.i_w1 = nc.dram_tensor("w1", [1024, 1024], F32, kind="ExternalInput").ap()
        self.i_idn = nc.dram_tensor("idn", [128, 128], F32, kind="ExternalInput").ap()
        self.i_mask = nc.dram_tensor("maskT", [128, 128], F32, kind="ExternalInput").ap()
        self.i_mask05 = nc.dram_tensor("mask05", [128, 128], BF16, kind="ExternalInput").ap()
        self.o_out = nc.dram_tensor("out", [T_SEQ, D], F32, kind="ExternalOutput").ap()

        self.wqT = sb("wqT", [128, 2048], BF16)
        self.wkT = sb("wkT", [128, 2048], BF16)
        self.wvT = sb("wvT", [128, 8192], BF16)
        self.woT = sb("woT", [128, 8192], BF16)
        self.kerq = sb("kerq", [128, 8192], BF16)
        self.a_sb = sb("a_sb", [128, 128], BF16)
        self.b_sb = sb("b_sb", [16, 1024], BF16)
        self.bias_bf = sb("bias_bf", [1, 1024], BF16)
        self.wq1T = sb("wq1T", [128, 2048], BF16)
        self.wk1T = sb("wk1T", [128, 2048], BF16)
        self.wv1T = sb("wv1T", [128, 2048], BF16)
        self.wo1T = sb("wo1T", [128, 2048], BF16)
        self.w1T = sb("w1T", [128, 8192], BF16)

        self.idn_f = sb("idn_f", [128, 128], F32)
        self.idn_bf = sb("idn_bf", [128, 128], BF16)
        self.maskT = sb("maskT_sb", [128, 128], F32)
        self.mask05 = sb("mask05_sb", [128, 128], BF16)
        self.ones_bf = sb("ones_bf", [1, 128], BF16)
        self.ones_f = sb("ones_f", [1, 128], F32)
        self.consts = sb("consts", [128, 8], F32)

        self.x_sb = [sb("x_sb0", [128, D], F32), sb("x_sb1", [128, D], F32)]
        self.sq = sb("sq_scr", [128, 2048], F32)
        self.xn = sb("xn_bf", [128, D], BF16)
        self.xnT = sb("xnT_bf", [128, D], BF16)
        self.qpad = sb("qpad", [128, 2048], BF16)
        self.kpad = sb("kpad", [128, 2048], BF16)
        self.q_s = sb("q_s", [128, 256], BF16)
        self.k_s = sb("k_s", [128, 256], BF16)
        self.qkT = sb("qkT", [128, 4096], BF16)
        self.v_aug = sb("v_aug", [128, H0 * 65], BF16)
        self.uT = sb("uT_bf", [16, 128], BF16)
        self.qf = sb("qf", [128, 4096], BF16)
        self.kf = sb("kf", [128, 4096], BF16)
        self.qfT = sb("qfT", [128, 4096], BF16)
        self.AT = sb("AT", [128, 2048], BF16)
        self.y_bf = sb("y_bf", [128, 1024], BF16)
        self.yT = sb("yT", [128, 1024], BF16)
        self.m_col = sb("m_col", [128, 1], F32)
        self.s2_col = sb("s2_col", [128, 1], F32)
        self.t_col = sb("t_col", [128, 1], F32)
        self.v_col = sb("v_col", [128, 1], F32)
        self.r_col = sb("r_col", [128, 1], F32)
        self.m_colb = sb("m_colb", [128, 1], F32)
        self.s2_colb = sb("s2_colb", [128, 1], F32)
        self.t_colb = sb("t_colb", [128, 1], F32)
        self.v_colb = sb("v_colb", [128, 1], F32)
        self.r_colb = sb("r_colb", [128, 1], F32)
        self.xn_l0 = sb("xn_l0", [128, D], BF16)
        self.dent = sb("dent", [128, 16], F32)
        self.z16 = sb("z16", [128, 16], F32)

        self.S_low = sb("S_low", [17, S_W], F32)
        self.S_low_bf = sb("S_low_bf", [17, S_W], BF16)
        self.S_q = [sb("S_q0", [128, S_W], F32), sb("S_q1", [128, S_W], F32)]
        self.S_q_bf = [sb("S_q0_bf", [128, S_W], BF16), sb("S_q1_bf", [128, S_W], BF16)]

        self.p_qk = sb("p_qk", [128, 512], F32)
        self.nrm = sb("nrm", [128, 8], F32)
        self.rn = sb("rn", [128, 8], F32)
        self.q1pad = sb("q1pad", [128, 512], BF16)
        self.k1pad = sb("k1pad", [128, 512], BF16)
        self.qk1T = sb("qk1T", [128, 1024], BF16)
        self.v1_aug = sb("v1_aug", [128, H1 * 65], BF16)
        self.A1T = sb("A1T", [128, 512], BF16)
        self.y1_bf = sb("y1_bf", [128, 256], BF16)
        self.y1T = sb("y1T", [128, 256], BF16)
        self.dent4 = sb("dent4", [128, 4], F32)
        self.z4 = sb("z4", [128, 4], F32)
        self.S1 = sb("S1", [64, H1 * 65], F32)
        self.S1_bf = sb("S1_bf", [64, H1 * 65], BF16)

        self.stage = self.x_sb[0]
        self.stage_bf = self.y_bf
        self.amax8 = sb("amax8", [128, 8], F32)
        self.red_row = sb("red_row", [1, 128], F32)
        self.scal2 = sb("scal2", [1, 2], F32)
        self.rs_cols = sb("rs_cols", [128, 2], F32)

        self.tp_ps = ps("tp_ps", [128, 1024], BF16)
        self.out_ps = ps("out_ps", [128, 1024], F32)
        self.ws_ps = ps("ws_ps", [128, 2048], F32)
        self.sm_ps = ps("sm_ps", [128, 512], F32)

        self.sem_ld = self.stack.enter_context(nc.semaphore("s_ld"))
        self.sem_wld = self.stack.enter_context(nc.semaphore("s_wld"))
        self.sem_st = self.stack.enter_context(nc.semaphore("s_st"))
        self.sem_rdy = self.stack.enter_context(nc.semaphore("s_rdy"))
        self.sems = {
            "pe": self.stack.enter_context(nc.semaphore("s_pe")),
            "dve": self.stack.enter_context(nc.semaphore("s_dve")),
            "act": self.stack.enter_context(nc.semaphore("s_act")),
            "pool": self.stack.enter_context(nc.semaphore("s_pool")),
        }

    def dump(self, name, ap):
        if not self.dbg or self.cur_chunk > 0:
            return
        nc = self.nc
        if name not in self.dbg_tensors:
            sh = list(ap.shape)
            self.dbg_tensors[name] = nc.dram_tensor(
                "dbg_" + name, sh, F32, kind="ExternalOutput").ap()
        self.bar(pool=True)
        nc.gpsimd.dma_start(self.dbg_tensors[name], ap).then_inc(self.sem_wld, 16)
        self.wld += 16
        nc.gpsimd.wait_ge(self.sem_wld, self.wld)
        self.bar(pool=True)

    def wload(self, dst_ap, src_ap):
        nc = self.nc
        nc.sync.dma_start(dst_ap, src_ap).then_inc(self.sem_wld, 16)
        self.wld += 16
        nc.vector.wait_ge(self.sem_wld, self.wld)
        nc.scalar.wait_ge(self.sem_wld, self.wld)
        self.bar()

    def dma_load(self, dst_ap, src_ap, bufname):
        self.dep_wait("sp", reads=(), writes=(bufname,))
        self.nc.sync.dma_start(dst_ap, src_ap).then_inc(self.sem_wld, 16)
        self.wld += 16
        self.deps[bufname] = {"w": ("wld", self.wld), "r": {}}

    def mm(self, out_ap, lhsT, rhs, start, stop, tile_position=None):
        return self.nc.tensor.matmul(out_ap, lhsT, rhs, start=start, stop=stop,
                                     tile_position=tile_position)

    def stg(self, k):
        names = ["x0", "x1", "stg2", "stg3"]
        bufs = [self.x_sb[0][:, :], self.x_sb[1][:, :],
                self.sq[:, 0:1024], self.sq[:, 1024:2048]]
        return names[k % 4], bufs[k % 4]

    def sbfs(self, k):
        return (("sbf0", self.y_bf) if k % 2 == 0 else ("sbf1", self.yT))

    def prep_wT(self, w_ap, R, Dc, dst, dname):
        nc = self.nc
        nblk = Dc // 128
        for g in range(R // 128):
            sname, st = self.stg(self.stg_i)
            bname, sbf = self.sbfs(self.stg_i)
            self.stg_i += 1
            self.dma_load(st[:, 0:Dc], w_ap[g * 128:(g + 1) * 128, :], sname)
            self.g("dve", [sname], [bname], lambda st=st, sbf=sbf, Dc=Dc:
                   nc.vector.tensor_copy(sbf[:, 0:Dc], st[:, 0:Dc]))
            use_ws = (self.stg_i % 2 == 1)
            tgt = self.ws_ps[:, 0:512].bitcast(BF16) if use_ws else self.tp_ps
            tname = "wsA" if use_ws else "tp"

            def _t(sbf=sbf, tgt=tgt, nblk=nblk):
                for j in range(nblk):
                    last = nc.tensor.transpose(tgt[:, j * 128:(j + 1) * 128],
                                               sbf[:, j * 128:(j + 1) * 128],
                                               self.idn_bf[:, :])
                return last
            self.g("pe", [bname], [tname], _t)
            eng = "act" if self.stg_i % 2 == 0 else "dve"
            h = nc.scalar if eng == "act" else nc.vector

            def _e(h=h, tgt=tgt, dst=dst, nblk=nblk, g=g, R=R):
                srcv = tgt[:, 0:nblk * 128].rearrange("p (j r) -> p j r", j=nblk)
                dstv = dst[:, :].rearrange("p (j r) -> p j r", j=nblk)[:, :, g * 128:(g + 1) * 128]
                if h is nc.vector:
                    return h.tensor_copy(dstv, srcv)
                return h.activation(out=dstv, in_=srcv, func=AF.Copy)
            self.g(eng, [tname], [dname], _e)

    def quant_scale_pass(self, w_ap, mode):
        nc = self.nc
        op = ALU.max if mode == "int4" else ALU.add
        for g in range(8):
            sname, st = self.stg(self.stg_i)
            self.stg_i += 1
            self.dma_load(st[:, :], w_ap[g * 128:(g + 1) * 128, :], sname)
            self.g("dve", [sname], ["amax8"], lambda st=st, g=g:
                   nc.vector.tensor_reduce(out=self.amax8[:, g:g + 1], in_=st[:, :],
                                           axis=AX.X, op=op,
                                           apply_absolute_value=True))
        self.g("dve", ["amax8"], ["mcol"], lambda: nc.vector.tensor_reduce(
            out=self.m_col[:, :], in_=self.amax8[:, :], axis=AX.X, op=op),
            drain=True)
        self.g("pe", ["mcol"], ["sm"], lambda: nc.tensor.transpose(
            self.sm_ps[0:1, 0:128], self.m_col[:, :], self.idn_f[:, :]))
        self.g("dve", ["sm"], ["red_row"], lambda: nc.vector.tensor_copy(
            self.red_row[:, :], self.sm_ps[0:1, 0:128]))
        self.g("dve", ["red_row"], ["scal2"], lambda: nc.vector.tensor_reduce(
            out=self.scal2[0:1, 0:1], in_=self.red_row[:, :], axis=AX.X, op=op),
            drain=True)
        if mode == "int4":
            fn = lambda: nc.vector.tensor_scalar(
                out=self.scal2[0:1, 0:1], in0=self.scal2[0:1, 0:1],
                scalar1=float(1.0 / 7.0), scalar2=None, op0=ALU.mult)
        else:
            fn = lambda: nc.vector.tensor_scalar(
                out=self.scal2[0:1, 0:1], in0=self.scal2[0:1, 0:1],
                scalar1=float(1.0 / 1048576.0), scalar2=float(1e-5),
                op0=ALU.mult, op1=ALU.max)
        self.g("dve", ["scal2"], ["scal2"], fn, drain=True)
        self.g("dve", ["scal2"], ["scal2"], lambda: nc.vector.reciprocal(
            self.scal2[0:1, 1:2], self.scal2[0:1, 0:1]), drain=True)
        self.g("pe", ["scal2"], ["sm"], lambda: self.mm(
            self.sm_ps[:, 0:2], self.ones_f[:, :], self.scal2[0:1, 0:2],
            start=True, stop=True))
        self.g("dve", ["sm"], ["rs"], lambda: nc.vector.tensor_copy(
            self.rs_cols[:, :], self.sm_ps[:, 0:2]))

    def quant_tile(self, st, sname, dst_bf_ap, dname, qmin, qmax):
        nc = self.nc

        def _q():
            s = st
            nc.vector.tensor_scalar(out=s, in0=s, scalar1=self.rs_cols[:, 1:2],
                                    scalar2=None, op0=ALU.mult)
            nc.vector.drain()
            nc.vector.tensor_scalar(out=s, in0=s, scalar1=MAGIC, scalar2=MAGIC,
                                    op0=ALU.add, op1=ALU.subtract)
            nc.vector.drain()
            nc.vector.tensor_scalar(out=s, in0=s, scalar1=float(qmin),
                                    scalar2=float(qmax), op0=ALU.max, op1=ALU.min)
            nc.vector.drain()
            return nc.vector.tensor_scalar(out=dst_bf_ap, in0=s,
                                           scalar1=self.rs_cols[:, 0:1],
                                           scalar2=None, op0=ALU.mult)
        self.g("dve", [sname, "rs"], [sname, dname], _q, drain=True)

    def prep(self):
        nc = self.nc
        self.stg_i = 0
        self.wload(self.idn_f[:, :], self.i_idn)
        self.wload(self.maskT[:, :], self.i_mask)
        self.wload(self.mask05[:, :], self.i_mask05)
        nc.vector.tensor_copy(self.idn_bf[:, :], self.idn_f[:, :])
        nc.vector.memset(self.ones_bf[:, :], 1.0)
        nc.vector.memset(self.ones_f[:, :], 1.0)
        for col, val in ((0, 1e-5), (1, -0.5), (2, SQ5), (3, 0.5), (4, QS),
                         (5, 1.0 / 1024.0)):
            nc.vector.memset(self.consts[:, col:col + 1], float(val))
        nc.vector.memset(self.ws_ps[:, :], 0.0)
        nc.vector.memset(self.out_ps[:, :], 0.0)
        nc.vector.memset(self.sm_ps[:, :], 0.0)
        nc.vector.memset(self.qpad[:, :], 0.0)
        nc.vector.memset(self.kpad[:, :], 0.0)
        nc.vector.memset(self.v_aug[:, :], 0.0)
        nc.vector.memset(self.v1_aug[:, :], 0.0)
        nc.vector.memset(self.q1pad[:, :], 0.0)
        nc.vector.memset(self.k1pad[:, :], 0.0)
        for t in (self.S_low, self.S_q[0], self.S_q[1], self.S1):
            nc.vector.memset(t[:, :], 0.0)
        for t in (self.S_low_bf, self.S_q_bf[0], self.S_q_bf[1], self.S1_bf):
            nc.vector.memset(t[:, :], 0.0)
        self.bar()
        nc.vector.memset(
            self.qpad[:, :].rearrange("p (h c) -> p h c", c=128)[:, :, 16:17], 1.0)
        nc.vector.memset(
            self.kpad[:, :].rearrange("p (h c) -> p h c", c=128)[:, :, 16:17], 1.0)
        nc.vector.memset(
            self.v_aug[:, :].rearrange("p (h c) -> p h c", c=65)[:, :, 64:65], 1.0)
        nc.vector.memset(
            self.v1_aug[:, :].rearrange("p (h c) -> p h c", c=65)[:, :, 64:65], 1.0)
        self.bar()
        self.reset_deps()

        self.prep_wT(self.i_wq0, 256, 1024, self.wqT, "WQT")
        self.prep_wT(self.i_wk0, 256, 1024, self.wkT, "WKT")
        self.prep_wT(self.i_wv0, 1024, 1024, self.wvT, "WVT")
        self.prep_wT(self.i_wo0, 1024, 1024, self.woT, "WOT")
        self.prep_wT(self.i_wq1, 256, 1024, self.wq1T, "WQ1T")
        self.prep_wT(self.i_wk1, 256, 1024, self.wk1T, "WK1T")
        self.prep_wT(self.i_wv1, 256, 1024, self.wv1T, "WV1T")
        self.prep_wT(self.i_wo1, 1024, 256, self.wo1T, "WO1T")

        self.quant_scale_pass(self.i_ker, "int4")
        for g in range(8):
            sname, st = self.stg(self.stg_i)
            self.stg_i += 1
            self.dma_load(st[:, :], self.i_ker[g * 128:(g + 1) * 128, :], sname)
            self.quant_tile(st[:, :], sname,
                            self.kerq[:, g * 1024:(g + 1) * 1024], "KERQ",
                            -8.0, 7.0)

        self.quant_scale_pass(self.i_w1, "tern")
        for g in range(8):
            sname, st = self.stg(self.stg_i)
            bname, sbf = self.sbfs(self.stg_i)
            self.stg_i += 1
            self.dma_load(st[:, :], self.i_w1[g * 128:(g + 1) * 128, :], sname)
            self.quant_tile(st[:, :], sname, sbf[:, :], bname, -1.0, 1.0)
            use_ws = (self.stg_i % 2 == 1)
            tgt = self.ws_ps[:, 0:512].bitcast(BF16) if use_ws else self.tp_ps
            tname = "wsA" if use_ws else "tp"

            def _t(sbf=sbf, tgt=tgt):
                for j in range(8):
                    last = nc.tensor.transpose(tgt[:, j * 128:(j + 1) * 128],
                                               sbf[:, j * 128:(j + 1) * 128],
                                               self.idn_bf[:, :])
                return last
            self.g("pe", [bname], [tname], _t)
            eng = "act" if self.stg_i % 2 == 0 else "dve"
            h = nc.scalar if eng == "act" else nc.vector

            def _e(h=h, tgt=tgt, g=g):
                srcv = tgt[:, :].rearrange("p (j r) -> p j r", j=8)
                dstv = self.w1T[:, :].rearrange("p (j r) -> p j r", j=8)[:, :, g * 128:(g + 1) * 128]
                if h is nc.vector:
                    return h.tensor_copy(dstv, srcv)
                return h.activation(out=dstv, in_=srcv, func=AF.Copy)
            self.g(eng, [tname], ["W1T"], _e)

        self.bar()
        a_src = self.i_a.rearrange("(g p) r -> p g r", p=128)
        self.wload(self.x_sb[0][:, 0:128].rearrange("p (g r) -> p g r", g=8), a_src)
        nc.vector.tensor_copy(self.a_sb[:, :], self.x_sb[0][:, 0:128])
        self.wload(self.x_sb[0][0:16, 0:1024], self.i_b)
        nc.vector.tensor_copy(self.b_sb[:, :], self.x_sb[0][0:16, 0:1024])
        self.wload(self.x_sb[0][0:1, 0:1024], self.i_bias)
        nc.vector.tensor_copy(self.bias_bf[:, :], self.x_sb[0][0:1, 0:1024])
        self.bar()
        self.cur_chunk = 0
        self.dump("kerq0", self.kerq[:, 0:1024])
        self.dump("wqT0", self.wqT[:, 0:256])
        self.dump("wvT0", self.wvT[:, 0:1024])
        self.dump("w1T0", self.w1T[:, 0:1024])
        self.dump("wo1T0", self.wo1T[:, 0:1024])
        self.dump("a_sb", self.a_sb[:, :])
        self.dump("b_sb", self.b_sb[:, :])
        self.dump("bias_bf", self.bias_bf[:, :])
        self.dump("idn_bf", self.idn_bf[:, :])
        self.cur_chunk = -1

    def layernorm(self, x_ap, xbuf, set_b=False):
        nc = self.nc
        C1024 = float(1.0 / 1024.0)
        if set_b:
            m_col, s2_col, t_col, v_col, r_col = (self.m_colb, self.s2_colb,
                                                  self.t_colb, self.v_colb,
                                                  self.r_colb)
            xn = self.xn_l0
            nm, ns, nr, nxn, nsq = "mcolb", "s2colb", "rcolb", "xnl0", "sqA"
        else:
            m_col, s2_col, t_col, v_col, r_col = (self.m_col, self.s2_col,
                                                  self.t_col, self.v_col,
                                                  self.r_col)
            xn = self.xn
            nm, ns, nr, nxn, nsq = "mcol", "s2col", "rcol", "xn", "sqA"
        if getattr(self, "have_msum", False) and not set_b:
            self.have_msum = False
        else:
            self.g("dve", [xbuf], [nm], lambda: nc.vector.tensor_reduce(
                out=m_col[:, :], in_=x_ap, axis=AX.X, op=ALU.add))
        self.g("act", [xbuf], [nsq, ns], lambda: nc.scalar.activation(
            out=self.sq[:, 0:1024], in_=x_ap, func=AF.Square,
            accum_out=s2_col[:, :]))

        def _v0():
            nc.vector.scalar_tensor_tensor(
                out=v_col[:, :], in0=m_col[:, :], scalar=-C1024,
                in1=m_col[:, :], op0=ALU.mult, op1=ALU.mult)
            return nc.vector.tensor_scalar(
                out=t_col[:, :], in0=m_col[:, :], scalar1=C1024,
                scalar2=None, op0=ALU.mult)
        self.g("dve", [nm], [nm], _v0, drain=True)
        self.g("dve", [nm, ns], [nm], lambda: nc.vector.tensor_scalar(
            out=v_col[:, :], in0=v_col[:, :],
            scalar1=s2_col[:, 0:1], scalar2=C1024,
            op0=ALU.add, op1=ALU.mult), drain=True)

        def _rstd():
            nc.scalar.activation(out=r_col[:, :], in_=v_col[:, :], func=AF.Ln,
                                 bias=self.consts[:, 0:1])
            nc.scalar.drain()
            return nc.scalar.activation(out=r_col[:, :], in_=r_col[:, :],
                                        func=AF.Exp, scale=self.consts[:, 1:2])
        self.g("act", [nm], [nr], _rstd)
        self.g("dve", [xbuf, nm, nr], [nxn], lambda: nc.vector.tensor_scalar(
            out=xn[:, :], in0=x_ap, scalar1=t_col[:, 0:1],
            scalar2=r_col[:, 0:1], op0=ALU.subtract, op1=ALU.mult),
            drain=True, post_drain=True)

    def xn_transpose(self, l0=False):
        nc = self.nc
        srct = self.xn_l0 if l0 else self.xn
        sname = "xnl0" if l0 else "xn"

        def _t(srct=srct):
            for j in range(8):
                last = nc.tensor.transpose(self.tp_ps[:, j * 128:(j + 1) * 128],
                                           srct[:, j * 128:(j + 1) * 128],
                                           self.idn_bf[:, :])
            return last
        self.g("pe", [sname], ["tp", "tp0", "tp1"], _t)
        self.g("act", ["tp"], ["xnT"], lambda: nc.scalar.activation(
            out=self.xnT[:, 0:512], in_=self.tp_ps[:, 0:512], func=AF.Copy),
            post_drain=True)
        self.g("act", ["tp"], ["xnTb"], lambda: nc.scalar.activation(
            out=self.xnT[:, 512:1024], in_=self.tp_ps[:, 512:1024], func=AF.Copy),
            post_drain=True)

    def chunk_body(self, x_ap, xbuf, c):
        nc = self.nc
        WSA = ["wsA"]
        WSB = ["wsB"]
        WS = ["wsA", "wsB"]

        if c == 0:
            self.layernorm(x_ap, xbuf, set_b=True)
            self.dump("l0_xn", self.xn_l0[:, :])
            self.xn_transpose(l0=True)
        self.dump("l0_xnT", self.xnT[:, :])
        pend, self.pending = self.pending, []
        for f in pend:
            f()

        def _proj():
            for g_ in range(8):
                lhs = self.xnT[:, g_ * 128:(g_ + 1) * 128]
                self.mm(self.ws_ps[:, 0:256], lhs, self.wqT[:, g_ * 256:(g_ + 1) * 256],
                        start=(g_ == 0), stop=False)
                self.mm(self.ws_ps[:, 256:512], lhs, self.wkT[:, g_ * 256:(g_ + 1) * 256],
                        start=False, stop=(g_ == 7))
                self.mm(self.ws_ps[:, 512:1024], lhs,
                        self.wvT[:, g_ * 1024:g_ * 1024 + 512], start=(g_ == 0), stop=(g_ == 7))
                last = self.mm(self.ws_ps[:, 1024:1536], lhs,
                               self.wvT[:, g_ * 1024 + 512:(g_ + 1) * 1024],
                               start=(g_ == 0), stop=(g_ == 7))
            return last
        self.g("pe", ["xnT", "xnTb"], WS, _proj)

        def _ffn():
            self.mm(self.out_ps[:, 0:512], self.ones_bf[:, :], self.bias_bf[:, 0:512],
                    start=True, stop=False)
            self.mm(self.out_ps[:, 512:1024], self.ones_bf[:, :], self.bias_bf[:, 512:1024],
                    start=True, stop=False)
            for g_ in range(8):
                lhs = self.xnT[:, g_ * 128:(g_ + 1) * 128]
                self.mm(self.out_ps[:, 0:512], lhs, self.kerq[:, g_ * 1024:g_ * 1024 + 512],
                        start=False, stop=False)
                last = self.mm(self.out_ps[:, 512:1024], lhs,
                               self.kerq[:, g_ * 1024 + 512:(g_ + 1) * 1024],
                               start=False, stop=False)
            return last
        self.g("pe", ["xnT", "xnTb"], ["out"], _ffn)

        def _ut():
            for g_ in range(8):
                last = self.mm(self.sm_ps[0:16, 0:128], self.a_sb[:, g_ * 16:(g_ + 1) * 16],
                               self.xnT[:, g_ * 128:(g_ + 1) * 128],
                               start=(g_ == 0), stop=(g_ == 7))
            return last
        self.g("pe", ["xnT", "xnTb"], ["sm"], _ut)

        qv = self.ws_ps[:, 0:256].rearrange("p (h f) -> p h f", h=16)
        kv = self.ws_ps[:, 256:512].rearrange("p (h f) -> p h f", h=16)
        qpadv = self.qpad[:, :].rearrange("p (h c) -> p h c", c=128)[:, :, 0:16]
        kpadv = self.kpad[:, :].rearrange("p (h c) -> p h c", c=128)[:, :, 0:16]

        def _evac():
            nc.scalar.activation(out=qpadv, in_=qv, func=AF.Copy, scale=self.consts[:, 3:4])
            nc.scalar.activation(out=kpadv, in_=kv, func=AF.Copy, scale=self.consts[:, 3:4])
            nc.scalar.activation(out=self.q_s[:, :].rearrange("p (h f) -> p h f", h=16),
                                 in_=qv, func=AF.Copy, scale=self.consts[:, 4:5])
            nc.scalar.activation(out=self.k_s[:, :].rearrange("p (h f) -> p h f", h=16),
                                 in_=kv, func=AF.Copy, scale=self.consts[:, 4:5])
            vaugv = self.v_aug[:, :].rearrange("p (h c) -> p h c", c=65)[:, :, 0:64]
            return nc.scalar.activation(
                out=vaugv,
                in_=self.ws_ps[:, 512:1536].rearrange("p (h e) -> p h e", h=16),
                func=AF.Copy)
        self.g("act", WS, ["qpad", "kpad", "q_s", "k_s", "v_aug"], _evac,
               post_drain=True)
        self.g("dve", ["sm"], ["uT"], lambda: nc.vector.tensor_copy(
            self.uT[:, :], self.sm_ps[0:16, 0:128]), post_drain=True)
        self.dump("l0_qpad", self.qpad[:, :])
        self.dump("l0_q_s", self.q_s[:, :])
        self.dump("l0_v_aug", self.v_aug[:, :])
        self.dump("l0_uT", self.uT[:, :])

        def _lora():
            self.mm(self.out_ps[:, 0:512], self.uT[0:16, :], self.b_sb[:, 0:512],
                    start=False, stop=False)
            return self.mm(self.out_ps[:, 512:1024], self.uT[0:16, :], self.b_sb[:, 512:1024],
                           start=False, stop=False)
        self.g("pe", ["uT"], ["out"], _lora)

        qs_v = self.q_s[:, :].rearrange("p (h f) -> p h f", h=16)
        ks_v = self.k_s[:, :].rearrange("p (h f) -> p h f", h=16)
        qf_v = self.qf[:, :].rearrange("p (h i j) -> p h i j", h=16, i=16)
        kf_v = self.kf[:, :].rearrange("p (h i j) -> p h i j", h=16, i=16)

        for r in range(8):
            if r == 2:
                self.g("dve", ["q_s", "qpad"], ["qfa"], lambda: nc.vector.tensor_tensor(
                    out=qf_v[:, 0:8],
                    in0=qs_v[:, 0:8].unsqueeze(3).broadcast_to([128, 8, 16, 16]),
                    in1=qpadv[:, 0:8].unsqueeze(2).broadcast_to([128, 8, 16, 16]),
                    op=ALU.mult), post_drain=True)
            if r == 4:
                self.g("dve", ["q_s", "qpad"], ["qfb"], lambda: nc.vector.tensor_tensor(
                    out=qf_v[:, 8:16],
                    in0=qs_v[:, 8:16].unsqueeze(3).broadcast_to([128, 8, 16, 16]),
                    in1=qpadv[:, 8:16].unsqueeze(2).broadcast_to([128, 8, 16, 16]),
                    op=ALU.mult), post_drain=True)
                self.dump("l0_qf", self.qf[:, :])
            if r < 2:
                srct, src_name, sc0 = self.qpad, "qpad", r * 1024
            elif r < 4:
                srct, src_name, sc0 = self.kpad, "kpad", (r - 2) * 1024
            else:
                srct, src_name, sc0 = self.qf, ("qfa" if r < 6 else "qfb"), (r - 4) * 1024
            use_ws = (r % 2 == 1)
            tgt = self.ws_ps[:, 0:512].bitcast(BF16) if use_ws else self.tp_ps
            tname = "wsA" if use_ws else "tp"
            if r < 4:
                dname, dst, dc0 = "qkT", self.qkT, r * 1024
            else:
                dname, dst, dc0 = "qfT", self.qfT, (r - 4) * 1024

            def _t(srct=srct, sc0=sc0, tgt=tgt):
                for j in range(8):
                    last = nc.tensor.transpose(tgt[:, j * 128:(j + 1) * 128],
                                               srct[:, sc0 + j * 128:sc0 + (j + 1) * 128],
                                               self.idn_bf[:, :])
                return last
            wnames = [tname] if tname != "tp" else ["tp", "tp0", "tp1"]
            self.g("pe", [src_name], wnames, _t)
            eng = "act" if r % 2 == 0 else "dve"
            h = nc.scalar if eng == "act" else nc.vector

            def _e(h=h, dst=dst, dc0=dc0, tgt=tgt):
                if h is nc.vector:
                    return h.tensor_copy(dst[:, dc0:dc0 + 1024], tgt[:, 0:1024])
                return h.activation(out=dst[:, dc0:dc0 + 1024], in_=tgt[:, 0:1024],
                                    func=AF.Copy)
            self.g(eng, [tname], [dname], _e, post_drain=True)
        self.dump("l0_qkT", self.qkT[:, :])
        self.dump("l0_qfT", self.qfT[:, :])

        for half in range(2):
            hs = range(8 * half, 8 * half + 8)
            wsn = WSA if half == 0 else WSB

            def _p(hs=hs):
                for h in hs:
                    lhs = self.qkT[0:16, 2048 + h * 128:2048 + (h + 1) * 128]
                    rhs = self.qkT[0:16, h * 128:(h + 1) * 128]
                    last = self.mm(self.ws_ps[:, h * 128:(h + 1) * 128], lhs, rhs,
                                   start=(h % 4 == 0), stop=(h % 4 == 3))
                return last
            self.g("pe", ["qkT"], wsn, _p)
            c0 = half * 1024
            self.g("act", wsn, ["sqA" if half == 0 else "sqB"], lambda c0=c0: nc.scalar.activation(
                out=self.sq[:, c0:c0 + 1024], in_=self.ws_ps[:, c0:c0 + 1024],
                func=AF.Square, scale=self.consts[:, 2:3], bias=self.consts[:, 2:3]))
            for q in range(2):
                qc = c0 + q * 512
                self.g("pool", ["sqA" if half == 0 else "sqB"], ["AT%d" % (2 * half + q)],
                       lambda qc=qc: nc.gpsimd.tensor_tensor(
                    out=self.AT[:, qc:qc + 512].rearrange("p (h t) -> p h t", h=4),
                    in0=self.sq[:, qc:qc + 512].rearrange("p (h t) -> p h t", h=4),
                    in1=self.maskT[:, :].unsqueeze(1).broadcast_to([128, 4, 128]),
                    op=ALU.mult), post_drain=True)

        self.g("pool", ["k_s", "kpad"], ["kf"], lambda: nc.gpsimd.tensor_tensor(
            out=kf_v,
            in0=ks_v.unsqueeze(3).broadcast_to([128, 16, 16, 16]),
            in1=kpadv.unsqueeze(2).broadcast_to([128, 16, 16, 16]), op=ALU.mult),
            post_drain=True)
        self.dump("l0_kf", self.kf[:, :])
        self.dump("l0_AT", self.AT[:, :])

        for half in range(2):
            hs = range(8 * half, 8 * half + 8)
            wsn = WSA if half == 0 else WSB

            def _y(hs=hs):
                for h in hs:
                    yc = h * 128
                    self.mm(self.ws_ps[:, yc:yc + 65],
                            self.AT[:, h * 128:(h + 1) * 128],
                            self.v_aug[:, h * 65:(h + 1) * 65],
                            start=(h % 4 == 0), stop=False)
                    self.mm(self.ws_ps[:, yc:yc + 65],
                            self.mask05[:, :],
                            self.v_aug[:, h * 65:(h + 1) * 65],
                            start=False, stop=False)
                    self.mm(self.ws_ps[:, yc:yc + 65],
                            self.qkT[0:17, h * 128:(h + 1) * 128],
                            self.S_low_bf[0:17, colh(h):colh(h) + 65],
                            start=False, stop=False)
                    for j in range(2):
                        last = self.mm(self.ws_ps[:, yc:yc + 65],
                                       self.qfT[:, (2 * h + j) * 128:(2 * h + j + 1) * 128],
                                       self.S_q_bf[j][:, colh(h):colh(h) + 65],
                                       start=False, stop=(h % 4 == 3 and j == 1))
                return last
            self.g("pe", ["AT%d" % (2 * half), "AT%d" % (2 * half + 1), "v_aug",
                          "qkT", "qfT", "Slowbf", "Sq0bf", "Sq1bf"], wsn, _y)

            denv = self.ws_ps[:, :].rearrange(
                "p (h c) -> p h c", c=128)[:, 8 * half:8 * half + 8, 64:65]

            def _z(half=half, denv=denv):
                nc.vector.tensor_scalar(out=self.dent[:, 8 * half:8 * half + 8],
                                        in0=denv.squeeze(2),
                                        scalar1=float(1e-12), scalar2=None, op0=ALU.add)
                nc.vector.drain()
                return nc.vector.reciprocal(self.z16[:, 8 * half:8 * half + 8],
                                            self.dent[:, 8 * half:8 * half + 8])
            self.g("dve", wsn, ["z%d" % half], _z)

            def _zy(half=half):
                return nc.vector.tensor_tensor(
                    out=self.y_bf[:, half * 512:(half + 1) * 512].rearrange(
                        "p (h e) -> p h e", h=8),
                    in0=self.ws_ps[:, :].rearrange(
                        "p (h c) -> p h c", c=128)[:, 8 * half:8 * half + 8, 0:64],
                    in1=self.z16[:, 8 * half:8 * half + 8].unsqueeze(2).broadcast_to(
                        [128, 8, 64]), op=ALU.mult)
            self.g("dve", wsn + ["z%d" % half], ["ybf%d" % half], _zy, drain=True,
                   post_drain=True)
        self.dump("l0_dent", self.dent[:, :])
        self.dump("l0_ybf", self.y_bf[:, :])

        for half in range(2):
            o = half * 512

            def _yt(o=o):
                for j in range(4):
                    last = nc.tensor.transpose(self.tp_ps[:, o + j * 128:o + (j + 1) * 128],
                                               self.y_bf[:, o + j * 128:o + (j + 1) * 128],
                                               self.idn_bf[:, :])
                return last
            self.g("pe", ["ybf%d" % half], ["tp%d" % half], _yt)
            self.g("act", ["tp%d" % half], ["yT%d" % half], lambda o=o: nc.scalar.activation(
                out=self.yT[:, o:o + 512], in_=self.tp_ps[:, o:o + 512], func=AF.Copy),
                post_drain=True)

            def _wo(half=half):
                for j in range(4 * half, 4 * half + 4):
                    lhs = self.yT[:, j * 128:(j + 1) * 128]
                    self.mm(self.out_ps[:, 0:512], lhs, self.woT[:, j * 1024:j * 1024 + 512],
                            start=False, stop=(j == 7))
                    last = self.mm(self.out_ps[:, 512:1024], lhs,
                                   self.woT[:, j * 1024 + 512:(j + 1) * 1024],
                                   start=False, stop=(j == 7))
                return last
            self.g("pe", ["yT%d" % half], ["out"], _wo)

        self.g("dve", ["out", xbuf], [xbuf, "mcol"],
               lambda: nc.vector.scalar_tensor_tensor(
                   out=x_ap, in0=self.out_ps[:, :], scalar=0.0, in1=x_ap,
                   op0=ALU.add, op1=ALU.add, accum_out=self.m_col[:, :]),
               post_drain=True)
        self.dump("l0_x", x_ap)
        self.have_msum = True

        def ds_round(rnd, defer_add=False):
            def _ds(rnd=rnd):
                for h in range(16):
                    c0 = colh(h)
                    if rnd == 0:
                        lhs = self.kpad[:, 128 * h:128 * h + 17]
                        out = self.ws_ps[0:17, c0:c0 + 65]
                    else:
                        j = rnd - 1
                        lhs = self.kf[:, (2 * h + j) * 128:(2 * h + j + 1) * 128]
                        out = self.ws_ps[:, c0:c0 + 65]
                    last = self.mm(out, lhs, self.v_aug[:, h * 65:(h + 1) * 65],
                                   start=h in (0, 7, 14), stop=h in (6, 13, 15))
                return last
            src_name = "kpad" if rnd == 0 else "kf"
            self.g("pe", [src_name, "v_aug"], WS, _ds)
            S = self.S_low if rnd == 0 else self.S_q[rnd - 1]
            sname = ["Slow", "Sq0", "Sq1"][rnd]
            rows = 17 if rnd == 0 else 128

            def _add(S=S, rows=rows, sname=sname):
                self.g("dve", WS + [sname], [sname], lambda:
                       nc.vector.tensor_tensor(out=S[0:rows, :], in0=S[0:rows, :],
                                               in1=self.ws_ps[0:rows, 0:S_W],
                                               op=ALU.add))
            if defer_add:
                self.pending.append(_add)
            else:
                _add()

        ds_round(0)
        self.layernorm(x_ap, xbuf)
        self.dump("l1_xn", self.xn[:, :])
        self.xn_transpose()
        if c + 1 < self.nchunks:
            nxbuf = "x%d" % ((c + 1) % 2)
            nx_ap = self.x_sb[(c + 1) % 2][:, :]
            nc.vector.wait_ge(self.sem_ld, 16 * (c + 2))
            nc.scalar.wait_ge(self.sem_ld, 16 * (c + 2))
            self.layernorm(nx_ap, nxbuf, set_b=True)

        def _ffn1():
            for g_ in range(8):
                lhs = self.xnT[:, g_ * 128:(g_ + 1) * 128]
                self.mm(self.out_ps[:, 0:512], lhs, self.w1T[:, g_ * 1024:g_ * 1024 + 512],
                        start=(g_ == 0), stop=False)
                last = self.mm(self.out_ps[:, 512:1024], lhs,
                               self.w1T[:, g_ * 1024 + 512:(g_ + 1) * 1024],
                               start=(g_ == 0), stop=False)
            return last
        self.g("pe", ["xnT", "xnTb"], ["out"], _ffn1)

        def _proj1():
            for g_ in range(8):
                lhs = self.xnT[:, g_ * 128:(g_ + 1) * 128]
                self.mm(self.ws_ps[:, 0:256], lhs, self.wq1T[:, g_ * 256:(g_ + 1) * 256],
                        start=(g_ == 0), stop=False)
                self.mm(self.ws_ps[:, 256:512], lhs, self.wk1T[:, g_ * 256:(g_ + 1) * 256],
                        start=False, stop=(g_ == 7))
                last = self.mm(self.ws_ps[:, 512:768], lhs,
                               self.wv1T[:, g_ * 256:(g_ + 1) * 256],
                               start=(g_ == 0), stop=(g_ == 7))
            return last
        self.g("pe", ["xnT", "xnTb"], WS, _proj1)

        e_qk = self.sq[:, 1024:1536]
        self.g("dve", WS, ["sqB"], lambda: nc.vector.tensor_scalar(
            out=e_qk, in0=self.ws_ps[:, 0:512], scalar1=0.0, scalar2=None,
            op0=ALU.min))
        self.g("act", ["sqB"], ["sqB"], lambda: nc.scalar.activation(
            out=e_qk, in_=e_qk, func=AF.Exp))
        self.g("dve", WS + ["sqB"], ["pqk"], lambda: nc.vector.scalar_tensor_tensor(
            out=self.p_qk[:, :], in0=self.ws_ps[:, 0:512], scalar=0.0, in1=e_qk,
            op0=ALU.max, op1=ALU.add))
        self.g("act", WS, ["v1aug"], lambda: nc.scalar.activation(
            out=self.v1_aug[:, :].rearrange("p (h c) -> p h c", c=65)[:, :, 0:64],
            in_=self.ws_ps[:, 512:768].rearrange("p (h e) -> p h e", h=4),
            func=AF.Copy))
        self.g("dve", ["pqk"], ["sqA"], lambda: nc.vector.tensor_tensor(
            out=self.sq[:, 0:512], in0=self.p_qk[:, :], in1=self.p_qk[:, :],
            op=ALU.mult))
        self.g("dve", ["sqA"], ["nrm"], lambda: nc.vector.tensor_reduce(
            out=self.nrm[:, :].rearrange("p (a b) -> p a b", b=1),
            in_=self.sq[:, 0:512].rearrange("p (a b) -> p a b", a=8),
            axis=AX.X, op=ALU.add))

        def _sqrt():
            nc.scalar.activation(out=self.rn[:, :], in_=self.nrm[:, :], func=AF.Ln)
            nc.scalar.drain()
            return nc.scalar.activation(out=self.rn[:, :], in_=self.rn[:, :],
                                        func=AF.Exp, scale=self.consts[:, 3:4])
        self.g("act", ["nrm"], ["rn"], _sqrt)

        def _rn():
            nc.vector.tensor_scalar(out=self.rn[:, :], in0=self.rn[:, :],
                                    scalar1=float(1e-6), scalar2=None, op0=ALU.add)
            nc.vector.drain()
            return nc.vector.reciprocal(self.rn[:, :], self.rn[:, :])
        self.g("dve", ["rn"], ["rn"], _rn)

        self.g("dve", ["pqk", "rn"], ["q1pad"], lambda: nc.vector.tensor_tensor(
            out=self.q1pad[:, :].rearrange("p (h c) -> p h c", c=128)[:, :, 0:64],
            in0=self.p_qk[:, 0:256].rearrange("p (a b) -> p a b", a=4),
            in1=self.rn[:, 0:4].unsqueeze(2).broadcast_to([128, 4, 64]), op=ALU.mult),
            drain=True, post_drain=True)
        self.g("dve", ["pqk", "rn"], ["k1pad"], lambda: nc.vector.tensor_tensor(
            out=self.k1pad[:, :].rearrange("p (h c) -> p h c", c=128)[:, :, 0:64],
            in0=self.p_qk[:, 256:512].rearrange("p (a b) -> p a b", a=4),
            in1=self.rn[:, 4:8].unsqueeze(2).broadcast_to([128, 4, 64]), op=ALU.mult),
            post_drain=True)
        self.dump("l1_pqk", self.p_qk[:, :])
        self.dump("l1_rn", self.rn[:, :])
        self.dump("l1_q1pad", self.q1pad[:, :])
        self.dump("l1_k1pad", self.k1pad[:, :])
        self.dump("l1_v1aug", self.v1_aug[:, :])

        ds_round(1)

        def _t1q():
            for j in range(4):
                last = nc.tensor.transpose(self.tp_ps[:, j * 128:(j + 1) * 128],
                                           self.q1pad[:, j * 128:(j + 1) * 128],
                                           self.idn_bf[:, :])
            return last
        self.g("pe", ["q1pad"], ["tp", "tp0"], _t1q)

        def _t1k():
            for j in range(4):
                last = nc.tensor.transpose(self.tp_ps[:, 512 + j * 128:512 + (j + 1) * 128],
                                           self.k1pad[:, j * 128:(j + 1) * 128],
                                           self.idn_bf[:, :])
            return last
        self.g("pe", ["k1pad"], ["tp", "tp1"], _t1k)
        self.g("act", ["tp", "tp0", "tp1"], ["qk1T"], lambda: nc.scalar.activation(
            out=self.qk1T[:, :], in_=self.tp_ps[:, :], func=AF.Copy), post_drain=True)

        def _p1():
            for h in range(4):
                lhs = self.qk1T[0:64, 512 + h * 128:512 + (h + 1) * 128]
                rhs = self.qk1T[0:64, h * 128:(h + 1) * 128]
                last = self.mm(self.sm_ps[:, h * 128:(h + 1) * 128], lhs, rhs,
                               start=(h == 0), stop=(h == 3))
            return last
        self.g("pe", ["qk1T"], ["sm"], _p1)
        self.g("dve", ["sm"], ["A1T"], lambda: nc.vector.tensor_tensor(
            out=self.A1T[:, :].rearrange("p (h t) -> p h t", h=4),
            in0=self.sm_ps[:, :].rearrange("p (h t) -> p h t", h=4),
            in1=self.maskT[:, :].unsqueeze(1).broadcast_to([128, 4, 128]), op=ALU.mult),
            post_drain=True)
        self.dump("l1_A1T", self.A1T[:, :])

        def _y1():
            for h in range(4):
                yc = h * 128
                self.mm(self.sm_ps[:, yc:yc + 65], self.A1T[:, h * 128:(h + 1) * 128],
                        self.v1_aug[:, h * 65:(h + 1) * 65], start=(h == 0), stop=False)
                last = self.mm(self.sm_ps[:, yc:yc + 65],
                               self.qk1T[0:64, h * 128:(h + 1) * 128],
                               self.S1_bf[0:64, h * 65:(h + 1) * 65],
                               start=False, stop=(h == 3))
            return last
        self.g("pe", ["A1T", "v1aug", "qk1T", "S1bf"], ["sm"], _y1)

        denv1 = self.sm_ps[:, :].rearrange("p (h c) -> p h c", c=128)[:, :, 64:65]

        def _z1():
            nc.vector.tensor_scalar(out=self.dent4[:, :], in0=denv1.squeeze(2),
                                    scalar1=float(1e-6), scalar2=None, op0=ALU.add)
            nc.vector.drain()
            return nc.vector.reciprocal(self.z4[:, :], self.dent4[:, :])
        self.g("dve", ["sm"], ["z4"], _z1)
        self.g("dve", ["sm", "z4"], ["y1"], lambda: nc.vector.tensor_tensor(
            out=self.y1_bf[:, :].rearrange("p (h e) -> p h e", h=4),
            in0=self.sm_ps[:, :].rearrange("p (h c) -> p h c", c=128)[:, :, 0:64],
            in1=self.z4[:, :].unsqueeze(2).broadcast_to([128, 4, 64]), op=ALU.mult),
            drain=True, post_drain=True)
        self.dump("l1_y1bf", self.y1_bf[:, :])

        def _y1t():
            for j in range(2):
                last = nc.tensor.transpose(self.tp_ps[:, j * 128:(j + 1) * 128],
                                           self.y1_bf[:, j * 128:(j + 1) * 128],
                                           self.idn_bf[:, :])
            return last
        self.g("pe", ["y1"], ["tp", "tp0", "tp1"], _y1t)
        self.g("act", ["tp"], ["y1T"], lambda: nc.scalar.activation(
            out=self.y1T[:, :], in_=self.tp_ps[:, 0:256], func=AF.Copy), post_drain=True)

        def _wo1():
            for j in range(2):
                lhs = self.y1T[:, j * 128:(j + 1) * 128]
                self.mm(self.out_ps[:, 0:512], lhs, self.wo1T[:, j * 1024:j * 1024 + 512],
                        start=False, stop=(j == 1))
                last = self.mm(self.out_ps[:, 512:1024], lhs,
                               self.wo1T[:, j * 1024 + 512:(j + 1) * 1024],
                               start=False, stop=(j == 1))
            return last
        self.g("pe", ["y1T"], ["out"], _wo1)

        def _resid2():
            nc.vector.tensor_tensor(out=x_ap, in0=x_ap, in1=self.out_ps[:, :],
                                    op=ALU.add)
            return nc.vector.drain()
        self.g("dve", ["out", xbuf], [xbuf], _resid2)
        self.resid_mark = self.n["dve"]

        def _ds1():
            for h in range(4):
                last = self.mm(self.sm_ps[0:64, h * 65:(h + 1) * 65],
                               self.k1pad[:, h * 128:h * 128 + 64],
                               self.v1_aug[:, h * 65:(h + 1) * 65],
                               start=(h == 0), stop=(h == 3))
            return last
        self.g("pe", ["k1pad", "v1aug"], ["sm"], _ds1)
        self.pending.append(lambda: self.g(
            "dve", ["sm", "S1"], ["S1"], lambda: nc.vector.tensor_tensor(
                out=self.S1[:, :], in0=self.S1[:, :], in1=self.sm_ps[0:64, 0:260],
                op=ALU.add)))
        ds_round(2, defer_add=True)
        self.pending.append(lambda: self.g(
            "act", ["Slow"], ["Slowbf"], lambda: nc.scalar.activation(
                out=self.S_low_bf[0:17, :], in_=self.S_low[0:17, :], func=AF.Copy),
            post_drain=True))
        self.pending.append(lambda: self.g(
            "act", ["Sq0"], ["Sq0bf"], lambda: nc.scalar.activation(
                out=self.S_q_bf[0][:, :], in_=self.S_q[0][:, :], func=AF.Copy),
            post_drain=True))
        self.pending.append(lambda: self.g(
            "act", ["Sq1"], ["Sq1bf"], lambda: nc.scalar.activation(
                out=self.S_q_bf[1][:, :], in_=self.S_q[1][:, :], func=AF.Copy),
            post_drain=True))
        self.pending.append(lambda: self.g(
            "act", ["S1"], ["S1bf"], lambda: nc.scalar.activation(
                out=self.S1_bf[:, :], in_=self.S1[:, :], func=AF.Copy),
            post_drain=True))
        if c + 1 < self.nchunks:
            self.xn_transpose(l0=True)
        self.dump("l0_Slow", self.S_low_bf[:, :])
        self.dump("l0_Sq0", self.S_q_bf[0][:, :])
        self.dump("l0_Sq1", self.S_q_bf[1][:, :])
        self.dump("l1_S1", self.S1_bf[:, :])


    def build(self):
        nc = self.nc
        self.alloc()
        self.prep()
        self.bar()
        self.reset_deps()

        for c in range(min(2, self.nchunks)):
            nc.sync.dma_start(self.x_sb[c % 2][:, :],
                              self.i_x[c * C:(c + 1) * C, :]).then_inc(self.sem_ld, 16)
            self.ld += 16

        for c in range(self.nchunks):
            self.cur_chunk = c
            xbuf = "x%d" % (c % 2)
            buf = self.x_sb[c % 2][:, :]
            nc.vector.wait_ge(self.sem_ld, 16 * (c + 1))
            nc.scalar.wait_ge(self.sem_ld, 16 * (c + 1))
            self.chunk_body(buf, xbuf, c)
            nc.sync.wait_ge(self.sems["dve"], self.resid_mark)
            nc.sync.dma_start(self.o_out[c * C:(c + 1) * C, :], buf).then_inc(
                self.sem_st, 16)
            self.st += 16
            if c + 2 < self.nchunks:
                nc.sync.wait_ge(self.sem_st, 16 * (c + 1))
                nc.sync.dma_start(self.x_sb[c % 2][:, :],
                                  self.i_x[(c + 2) * C:(c + 3) * C, :]).then_inc(
                    self.sem_ld, 16)
                self.ld += 16

        nc.sync.wait_ge(self.sem_st, self.st)
        return nc


def build_program(nchunks=NCH, layers=(0, 1), dbg=False):
    return KB(nchunks=nchunks, layers=layers, dbg=dbg).build()


def make_in_map(inputs, b):
    f32 = np.float32
    return {
        "x": np.ascontiguousarray(np.asarray(inputs["L0_attn_hidden_states"])[b], dtype=f32),
        "wq0": np.ascontiguousarray(np.asarray(inputs["L0_attn_Wq"], dtype=f32)),
        "wk0": np.ascontiguousarray(np.asarray(inputs["L0_attn_Wk"], dtype=f32)),
        "wv0": np.ascontiguousarray(np.asarray(inputs["L0_attn_Wv"], dtype=f32)),
        "wo0": np.ascontiguousarray(np.asarray(inputs["L0_attn_Wo"], dtype=f32)),
        "ker": np.ascontiguousarray(np.asarray(inputs["L0_ffn_kernel"], dtype=f32)),
        "bias": np.ascontiguousarray(np.asarray(inputs["L0_ffn_bias"], dtype=f32)).reshape(1, 1024),
        "la": np.ascontiguousarray(np.asarray(inputs["L0_ffn_a"], dtype=f32)),
        "lb": np.ascontiguousarray(np.asarray(inputs["L0_ffn_b"], dtype=f32)),
        "wq1": np.ascontiguousarray(np.asarray(inputs["L1_attn_Wq"], dtype=f32)),
        "wk1": np.ascontiguousarray(np.asarray(inputs["L1_attn_Wk"], dtype=f32)),
        "wv1": np.ascontiguousarray(np.asarray(inputs["L1_attn_Wv"], dtype=f32)),
        "wo1": np.ascontiguousarray(np.asarray(inputs["L1_attn_Wo"], dtype=f32)),
        "w1": np.ascontiguousarray(np.asarray(inputs["L1_ffn_weight"], dtype=f32)),
        "idn": np.eye(128, dtype=f32),
        "maskT": np.triu(np.ones((128, 128), dtype=f32)),
        "mask05": np.triu(np.full((128, 128), 0.5, dtype=ml_dtypes.bfloat16)),
    }


def kernel(**inputs):
    nc = build_program()
    in_maps = [make_in_map(inputs, b) for b in range(8)]
    res = run_bass_kernel_spmd(nc, in_maps, core_ids=list(range(8)))
    return np.stack([r["out"] for r in res.results], axis=0).astype(np.float32)


# version 11
# speedup vs baseline: 1.7846x; 1.0283x over previous
import contextlib
import numpy as np
import ml_dtypes

import concourse.bass as bass
import concourse.mybir as mybir
from concourse.bass_utils import run_bass_kernel_spmd

F32 = mybir.dt.float32
BF16 = mybir.dt.bfloat16
AF = mybir.ActivationFunctionType
ALU = mybir.AluOpType
AX = mybir.AxisListType
ET = mybir.EngineType

MAGIC = float(1.5 * 2 ** 23)
SQ5 = 0.7071067811865476
QS = 0.35355339059327373

T_SEQ = 4096
D = 1024
C = 128
NCH = T_SEQ // C
H0, F0, V0 = 16, 16, 64
H1, D1 = 4, 64

def colh(h):
    return 65 * (h % 7) + 512 * (h // 7)

S_W = 1536


class KB:

    def __init__(self, nchunks=NCH, layers=(0, 1), dbg=False):
        self.nchunks = nchunks
        self.layers = layers
        self.dbg = dbg
        self.cur_chunk = -1
        self.dbg_tensors = {}
        self.nc = bass.Bass("TRN2", target_bir_lowering=False)
        self.stack = contextlib.ExitStack()
        self.ld = 0
        self.wld = 0
        self.st = 0
        self.n = {"pe": 0, "dve": 0, "act": 0, "pool": 0}
        self.deps = {}
        self.pending = []
        self.last_wait = {}

    def bar(self, pool=False):
        self.nc.all_engine_barrier()

    def eng_handle(self, eng):
        nc = self.nc
        return {"pe": nc.tensor, "dve": nc.vector, "act": nc.scalar,
                "pool": nc.gpsimd, "sp": nc.sync}[eng]

    def reset_deps(self):
        self.deps = {}
        self.last_wait = {}

    def dep_wait(self, eng, reads=(), writes=()):
        h = self.eng_handle(eng)
        need = {}
        for b in reads:
            w = self.deps.get(b, {}).get("w")
            if w and w[0] != eng:
                need[w[0]] = max(need.get(w[0], 0), w[1])
        for b in writes:
            d = self.deps.get(b, {})
            w = d.get("w")
            if w and w[0] != eng:
                need[w[0]] = max(need.get(w[0], 0), w[1])
            for re_, rc in d.get("r", {}).items():
                if re_ != eng:
                    need[re_] = max(need.get(re_, 0), rc)
        for prod, cnt in sorted(need.items()):
            key = (eng, prod)
            if self.last_wait.get(key, 0) < cnt:
                sem = self.sem_wld if prod == "wld" else self.sems[prod]
                h.wait_ge(sem, cnt)
                self.last_wait[key] = cnt

    def dep_done(self, inst, eng, reads=(), writes=()):
        self.n[eng] += 1
        inst.then_inc(self.sems[eng], 1)
        c = self.n[eng]
        for b in writes:
            self.deps[b] = {"w": (eng, c), "r": {}}
        for b in reads:
            d = self.deps.setdefault(b, {"w": None, "r": {}})
            d["r"][eng] = c

    def g(self, eng, reads, writes, emit, drain=False, post_drain=False):
        self.dep_wait(eng, reads, writes)
        if drain:
            self.eng_handle(eng).drain()
        last = emit()
        if post_drain:
            last = self.eng_handle(eng).drain()
        self.dep_done(last, eng, reads, writes)
        return self.n[eng]

    def alloc(self):
        nc = self.nc
        sb = nc.alloc_sbuf_tensor
        ps = nc.alloc_psum_tensor

        self.i_x = nc.dram_tensor("x", [T_SEQ, D], F32, kind="ExternalInput").ap()
        self.i_wq0 = nc.dram_tensor("wq0", [256, 1024], F32, kind="ExternalInput").ap()
        self.i_wk0 = nc.dram_tensor("wk0", [256, 1024], F32, kind="ExternalInput").ap()
        self.i_wv0 = nc.dram_tensor("wv0", [1024, 1024], F32, kind="ExternalInput").ap()
        self.i_wo0 = nc.dram_tensor("wo0", [1024, 1024], F32, kind="ExternalInput").ap()
        self.i_ker = nc.dram_tensor("ker", [1024, 1024], F32, kind="ExternalInput").ap()
        self.i_bias = nc.dram_tensor("bias", [1, 1024], F32, kind="ExternalInput").ap()
        self.i_a = nc.dram_tensor("la", [1024, 16], F32, kind="ExternalInput").ap()
        self.i_b = nc.dram_tensor("lb", [16, 1024], F32, kind="ExternalInput").ap()
        self.i_wq1 = nc.dram_tensor("wq1", [256, 1024], F32, kind="ExternalInput").ap()
        self.i_wk1 = nc.dram_tensor("wk1", [256, 1024], F32, kind="ExternalInput").ap()
        self.i_wv1 = nc.dram_tensor("wv1", [256, 1024], F32, kind="ExternalInput").ap()
        self.i_wo1 = nc.dram_tensor("wo1", [1024, 256], F32, kind="ExternalInput").ap()
        self.i_w1 = nc.dram_tensor("w1", [1024, 1024], F32, kind="ExternalInput").ap()
        self.i_idn = nc.dram_tensor("idn", [128, 128], F32, kind="ExternalInput").ap()
        self.i_mask = nc.dram_tensor("maskT", [128, 128], F32, kind="ExternalInput").ap()
        self.i_mask05 = nc.dram_tensor("mask05", [128, 128], BF16, kind="ExternalInput").ap()
        self.o_out = nc.dram_tensor("out", [T_SEQ, D], F32, kind="ExternalOutput").ap()

        self.wqT = sb("wqT", [128, 2048], BF16)
        self.wkT = sb("wkT", [128, 2048], BF16)
        self.wvT = sb("wvT", [128, 8192], BF16)
        self.woT = sb("woT", [128, 8192], BF16)
        self.kerq = sb("kerq", [128, 8192], BF16)
        self.a_sb = sb("a_sb", [128, 128], BF16)
        self.b_sb = sb("b_sb", [16, 1024], BF16)
        self.bias_bf = sb("bias_bf", [1, 1024], BF16)
        self.wq1T = sb("wq1T", [128, 2048], BF16)
        self.wk1T = sb("wk1T", [128, 2048], BF16)
        self.wv1T = sb("wv1T", [128, 2048], BF16)
        self.wo1T = sb("wo1T", [128, 2048], BF16)
        self.w1T = sb("w1T", [128, 8192], BF16)

        self.idn_f = sb("idn_f", [128, 128], F32)
        self.idn_bf = sb("idn_bf", [128, 128], BF16)
        self.maskT = sb("maskT_sb", [128, 128], F32)
        self.mask05 = sb("mask05_sb", [128, 128], BF16)
        self.ones_bf = sb("ones_bf", [1, 128], BF16)
        self.ones_f = sb("ones_f", [1, 128], F32)
        self.consts = sb("consts", [128, 8], F32)

        self.x_sb = [sb("x_sb0", [128, D], F32), sb("x_sb1", [128, D], F32)]
        self.sq = sb("sq_scr", [128, 2048], F32)
        self.xn = sb("xn_bf", [128, D], BF16)
        self.xnT = sb("xnT_bf", [128, D], BF16)
        self.qpad = sb("qpad", [128, 512], BF16)
        self.kpad = sb("kpad", [128, 512], BF16)
        self.q_s = sb("q_s", [128, 256], BF16)
        self.k_s = sb("k_s", [128, 256], BF16)
        self.qkT = sb("qkT", [128, 1024], BF16)
        self.v_aug = sb("v_aug", [128, H0 * 65], BF16)
        self.uT = sb("uT_bf", [16, 128], BF16)
        self.qf = sb("qf", [128, 4096], BF16)
        self.kf = sb("kf", [128, 4096], BF16)
        self.qfT = sb("qfT", [128, 4096], BF16)
        self.AT = sb("AT", [128, 2048], BF16)
        self.y_bf = sb("y_bf", [128, 1024], BF16)
        self.yT = sb("yT", [128, 1024], BF16)
        self.m_col = sb("m_col", [128, 1], F32)
        self.s2_col = sb("s2_col", [128, 1], F32)
        self.t_col = sb("t_col", [128, 1], F32)
        self.v_col = sb("v_col", [128, 1], F32)
        self.r_col = sb("r_col", [128, 1], F32)
        self.m_colb = sb("m_colb", [128, 1], F32)
        self.s2_colb = sb("s2_colb", [128, 1], F32)
        self.t_colb = sb("t_colb", [128, 1], F32)
        self.v_colb = sb("v_colb", [128, 1], F32)
        self.r_colb = sb("r_colb", [128, 1], F32)
        self.xn_l0 = sb("xn_l0", [128, D], BF16)
        self.dent = sb("dent", [128, 16], F32)
        self.z16 = sb("z16", [128, 16], F32)

        self.S_low = sb("S_low", [17, S_W], F32)
        self.S_low_bf = sb("S_low_bf", [128, S_W], BF16)
        self.S_q = [sb("S_q0", [128, S_W], F32), sb("S_q1", [128, S_W], F32)]
        self.S_q_bf = [sb("S_q0_bf", [128, S_W], BF16), sb("S_q1_bf", [128, S_W], BF16)]

        self.p_qk = sb("p_qk", [128, 512], F32)
        self.nrm = sb("nrm", [128, 8], F32)
        self.rn = sb("rn", [128, 8], F32)
        self.q1pad = sb("q1pad", [128, 256], BF16)
        self.k1pad = sb("k1pad", [128, 256], BF16)
        self.qk1T = sb("qk1T", [128, 512], BF16)
        self.v1_aug = sb("v1_aug", [128, H1 * 65], BF16)
        self.A1T = sb("A1T", [128, 512], BF16)
        self.y1_bf = sb("y1_bf", [128, 256], BF16)
        self.y1T = sb("y1T", [128, 256], BF16)
        self.dent4 = sb("dent4", [128, 4], F32)
        self.z4 = sb("z4", [128, 4], F32)
        self.S1 = sb("S1", [64, H1 * 65], F32)
        self.S1_bf = sb("S1_bf", [128, H1 * 65], BF16)

        self.stage = self.x_sb[0]
        self.stage_bf = self.y_bf
        self.amax8 = sb("amax8", [128, 8], F32)
        self.red_row = sb("red_row", [1, 128], F32)
        self.scal2 = sb("scal2", [1, 2], F32)
        self.rs_cols = sb("rs_cols", [128, 2], F32)

        self.tp_ps = ps("tp_ps", [128, 1024], BF16)
        self.out_ps = ps("out_ps", [128, 1024], F32)
        self.ws_ps = ps("ws_ps", [128, 2048], F32)
        self.sm_ps = ps("sm_ps", [128, 512], F32)

        self.sem_ld = self.stack.enter_context(nc.semaphore("s_ld"))
        self.sem_wld = self.stack.enter_context(nc.semaphore("s_wld"))
        self.sem_st = self.stack.enter_context(nc.semaphore("s_st"))
        self.sem_rdy = self.stack.enter_context(nc.semaphore("s_rdy"))
        self.sems = {
            "pe": self.stack.enter_context(nc.semaphore("s_pe")),
            "dve": self.stack.enter_context(nc.semaphore("s_dve")),
            "act": self.stack.enter_context(nc.semaphore("s_act")),
            "pool": self.stack.enter_context(nc.semaphore("s_pool")),
        }

    def dump(self, name, ap):
        if not self.dbg or self.cur_chunk > 0:
            return
        nc = self.nc
        if name not in self.dbg_tensors:
            sh = list(ap.shape)
            self.dbg_tensors[name] = nc.dram_tensor(
                "dbg_" + name, sh, F32, kind="ExternalOutput").ap()
        self.bar(pool=True)
        nc.gpsimd.dma_start(self.dbg_tensors[name], ap).then_inc(self.sem_wld, 16)
        self.wld += 16
        nc.gpsimd.wait_ge(self.sem_wld, self.wld)
        self.bar(pool=True)

    def wload(self, dst_ap, src_ap):
        nc = self.nc
        nc.sync.dma_start(dst_ap, src_ap).then_inc(self.sem_wld, 16)
        self.wld += 16
        nc.vector.wait_ge(self.sem_wld, self.wld)
        nc.scalar.wait_ge(self.sem_wld, self.wld)
        self.bar()

    def dma_load(self, dst_ap, src_ap, bufname):
        self.dep_wait("sp", reads=(), writes=(bufname,))
        self.nc.sync.dma_start(dst_ap, src_ap).then_inc(self.sem_wld, 16)
        self.wld += 16
        self.deps[bufname] = {"w": ("wld", self.wld), "r": {}}

    def mm(self, out_ap, lhsT, rhs, start, stop, tile_position=None):
        return self.nc.tensor.matmul(out_ap, lhsT, rhs, start=start, stop=stop,
                                     tile_position=tile_position)

    def stg(self, k):
        names = ["x0", "x1", "stg2", "stg3"]
        bufs = [self.x_sb[0][:, :], self.x_sb[1][:, :],
                self.sq[:, 0:1024], self.sq[:, 1024:2048]]
        return names[k % 4], bufs[k % 4]

    def sbfs(self, k):
        return (("sbf0", self.y_bf) if k % 2 == 0 else ("sbf1", self.yT))

    def prep_wT(self, w_ap, R, Dc, dst, dname):
        nc = self.nc
        nblk = Dc // 128
        for g in range(R // 128):
            sname, st = self.stg(self.stg_i)
            bname, sbf = self.sbfs(self.stg_i)
            self.stg_i += 1
            self.dma_load(st[:, 0:Dc], w_ap[g * 128:(g + 1) * 128, :], sname)
            self.g("dve", [sname], [bname], lambda st=st, sbf=sbf, Dc=Dc:
                   nc.vector.tensor_copy(sbf[:, 0:Dc], st[:, 0:Dc]))
            use_ws = (self.stg_i % 2 == 1)
            tgt = self.ws_ps[:, 0:512].bitcast(BF16) if use_ws else self.tp_ps
            tname = "wsA" if use_ws else "tp"

            def _t(sbf=sbf, tgt=tgt, nblk=nblk):
                for j in range(nblk):
                    last = nc.tensor.transpose(tgt[:, j * 128:(j + 1) * 128],
                                               sbf[:, j * 128:(j + 1) * 128],
                                               self.idn_bf[:, :])
                return last
            self.g("pe", [bname], [tname], _t)
            eng = "act" if self.stg_i % 2 == 0 else "dve"
            h = nc.scalar if eng == "act" else nc.vector

            def _e(h=h, tgt=tgt, dst=dst, nblk=nblk, g=g, R=R):
                srcv = tgt[:, 0:nblk * 128].rearrange("p (j r) -> p j r", j=nblk)
                dstv = dst[:, :].rearrange("p (j r) -> p j r", j=nblk)[:, :, g * 128:(g + 1) * 128]
                if h is nc.vector:
                    return h.tensor_copy(dstv, srcv)
                return h.activation(out=dstv, in_=srcv, func=AF.Copy)
            self.g(eng, [tname], [dname], _e)

    def quant_scale_pass(self, w_ap, mode):
        nc = self.nc
        op = ALU.max if mode == "int4" else ALU.add
        for g in range(8):
            sname, st = self.stg(self.stg_i)
            self.stg_i += 1
            self.dma_load(st[:, :], w_ap[g * 128:(g + 1) * 128, :], sname)
            self.g("dve", [sname], ["amax8"], lambda st=st, g=g:
                   nc.vector.tensor_reduce(out=self.amax8[:, g:g + 1], in_=st[:, :],
                                           axis=AX.X, op=op,
                                           apply_absolute_value=True))
        self.g("dve", ["amax8"], ["mcol"], lambda: nc.vector.tensor_reduce(
            out=self.m_col[:, :], in_=self.amax8[:, :], axis=AX.X, op=op),
            drain=True)
        self.g("pe", ["mcol"], ["sm"], lambda: nc.tensor.transpose(
            self.sm_ps[0:1, 0:128], self.m_col[:, :], self.idn_f[:, :]))
        self.g("dve", ["sm"], ["red_row"], lambda: nc.vector.tensor_copy(
            self.red_row[:, :], self.sm_ps[0:1, 0:128]))
        self.g("dve", ["red_row"], ["scal2"], lambda: nc.vector.tensor_reduce(
            out=self.scal2[0:1, 0:1], in_=self.red_row[:, :], axis=AX.X, op=op),
            drain=True)
        if mode == "int4":
            fn = lambda: nc.vector.tensor_scalar(
                out=self.scal2[0:1, 0:1], in0=self.scal2[0:1, 0:1],
                scalar1=float(1.0 / 7.0), scalar2=None, op0=ALU.mult)
        else:
            fn = lambda: nc.vector.tensor_scalar(
                out=self.scal2[0:1, 0:1], in0=self.scal2[0:1, 0:1],
                scalar1=float(1.0 / 1048576.0), scalar2=float(1e-5),
                op0=ALU.mult, op1=ALU.max)
        self.g("dve", ["scal2"], ["scal2"], fn, drain=True)
        self.g("dve", ["scal2"], ["scal2"], lambda: nc.vector.reciprocal(
            self.scal2[0:1, 1:2], self.scal2[0:1, 0:1]), drain=True)
        self.g("pe", ["scal2"], ["sm"], lambda: self.mm(
            self.sm_ps[:, 0:2], self.ones_f[:, :], self.scal2[0:1, 0:2],
            start=True, stop=True))
        self.g("dve", ["sm"], ["rs"], lambda: nc.vector.tensor_copy(
            self.rs_cols[:, :], self.sm_ps[:, 0:2]))

    def quant_tile(self, st, sname, dst_bf_ap, dname, qmin, qmax):
        nc = self.nc

        def _q():
            s = st
            nc.vector.tensor_scalar(out=s, in0=s, scalar1=self.rs_cols[:, 1:2],
                                    scalar2=None, op0=ALU.mult)
            nc.vector.drain()
            nc.vector.tensor_scalar(out=s, in0=s, scalar1=MAGIC, scalar2=MAGIC,
                                    op0=ALU.add, op1=ALU.subtract)
            nc.vector.drain()
            nc.vector.tensor_scalar(out=s, in0=s, scalar1=float(qmin),
                                    scalar2=float(qmax), op0=ALU.max, op1=ALU.min)
            nc.vector.drain()
            return nc.vector.tensor_scalar(out=dst_bf_ap, in0=s,
                                           scalar1=self.rs_cols[:, 0:1],
                                           scalar2=None, op0=ALU.mult)
        self.g("dve", [sname, "rs"], [sname, dname], _q, drain=True)

    def prep(self):
        nc = self.nc
        self.stg_i = 0
        self.wload(self.idn_f[:, :], self.i_idn)
        self.wload(self.maskT[:, :], self.i_mask)
        self.wload(self.mask05[:, :], self.i_mask05)
        nc.vector.tensor_copy(self.idn_bf[:, :], self.idn_f[:, :])
        nc.vector.memset(self.ones_bf[:, :], 1.0)
        nc.vector.memset(self.ones_f[:, :], 1.0)
        for col, val in ((0, 1e-5), (1, -0.5), (2, SQ5), (3, 0.5), (4, QS),
                         (5, 1.0 / 1024.0)):
            nc.vector.memset(self.consts[:, col:col + 1], float(val))
        nc.vector.memset(self.ws_ps[:, :], 0.0)
        nc.vector.memset(self.out_ps[:, :], 0.0)
        nc.vector.memset(self.sm_ps[:, :], 0.0)
        nc.vector.memset(self.qpad[:, :], 0.0)
        nc.vector.memset(self.kpad[:, :], 0.0)
        nc.vector.memset(self.v_aug[:, :], 0.0)
        nc.vector.memset(self.v1_aug[:, :], 0.0)
        nc.vector.memset(self.q1pad[:, :], 0.0)
        nc.vector.memset(self.k1pad[:, :], 0.0)
        for t in (self.S_low, self.S_q[0], self.S_q[1], self.S1):
            nc.vector.memset(t[:, :], 0.0)
        for t in (self.S_low_bf, self.S_q_bf[0], self.S_q_bf[1], self.S1_bf):
            nc.vector.memset(t[:, :], 0.0)
        self.bar()
        nc.vector.memset(
            self.qpad[:, :].rearrange("p (h c) -> p h c", c=32)[:, :, 16:17], 1.0)
        nc.vector.memset(
            self.kpad[:, :].rearrange("p (h c) -> p h c", c=32)[:, :, 16:17], 1.0)
        nc.vector.memset(
            self.v_aug[:, :].rearrange("p (h c) -> p h c", c=65)[:, :, 64:65], 1.0)
        nc.vector.memset(
            self.v1_aug[:, :].rearrange("p (h c) -> p h c", c=65)[:, :, 64:65], 1.0)
        self.bar()
        self.reset_deps()

        self.prep_wT(self.i_wq0, 256, 1024, self.wqT, "WQT")
        self.prep_wT(self.i_wk0, 256, 1024, self.wkT, "WKT")
        self.prep_wT(self.i_wv0, 1024, 1024, self.wvT, "WVT")
        self.prep_wT(self.i_wo0, 1024, 1024, self.woT, "WOT")
        self.prep_wT(self.i_wq1, 256, 1024, self.wq1T, "WQ1T")
        self.prep_wT(self.i_wk1, 256, 1024, self.wk1T, "WK1T")
        self.prep_wT(self.i_wv1, 256, 1024, self.wv1T, "WV1T")
        self.prep_wT(self.i_wo1, 1024, 256, self.wo1T, "WO1T")

        self.quant_scale_pass(self.i_ker, "int4")
        for g in range(8):
            sname, st = self.stg(self.stg_i)
            self.stg_i += 1
            self.dma_load(st[:, :], self.i_ker[g * 128:(g + 1) * 128, :], sname)
            self.quant_tile(st[:, :], sname,
                            self.kerq[:, g * 1024:(g + 1) * 1024], "KERQ",
                            -8.0, 7.0)

        self.quant_scale_pass(self.i_w1, "tern")
        for g in range(8):
            sname, st = self.stg(self.stg_i)
            bname, sbf = self.sbfs(self.stg_i)
            self.stg_i += 1
            self.dma_load(st[:, :], self.i_w1[g * 128:(g + 1) * 128, :], sname)
            self.quant_tile(st[:, :], sname, sbf[:, :], bname, -1.0, 1.0)
            use_ws = (self.stg_i % 2 == 1)
            tgt = self.ws_ps[:, 0:512].bitcast(BF16) if use_ws else self.tp_ps
            tname = "wsA" if use_ws else "tp"

            def _t(sbf=sbf, tgt=tgt):
                for j in range(8):
                    last = nc.tensor.transpose(tgt[:, j * 128:(j + 1) * 128],
                                               sbf[:, j * 128:(j + 1) * 128],
                                               self.idn_bf[:, :])
                return last
            self.g("pe", [bname], [tname], _t)
            eng = "act" if self.stg_i % 2 == 0 else "dve"
            h = nc.scalar if eng == "act" else nc.vector

            def _e(h=h, tgt=tgt, g=g):
                srcv = tgt[:, :].rearrange("p (j r) -> p j r", j=8)
                dstv = self.w1T[:, :].rearrange("p (j r) -> p j r", j=8)[:, :, g * 128:(g + 1) * 128]
                if h is nc.vector:
                    return h.tensor_copy(dstv, srcv)
                return h.activation(out=dstv, in_=srcv, func=AF.Copy)
            self.g(eng, [tname], ["W1T"], _e)

        self.bar()
        a_src = self.i_a.rearrange("(g p) r -> p g r", p=128)
        self.wload(self.x_sb[0][:, 0:128].rearrange("p (g r) -> p g r", g=8), a_src)
        nc.vector.tensor_copy(self.a_sb[:, :], self.x_sb[0][:, 0:128])
        self.wload(self.x_sb[0][0:16, 0:1024], self.i_b)
        nc.vector.tensor_copy(self.b_sb[:, :], self.x_sb[0][0:16, 0:1024])
        self.wload(self.x_sb[0][0:1, 0:1024], self.i_bias)
        nc.vector.tensor_copy(self.bias_bf[:, :], self.x_sb[0][0:1, 0:1024])
        self.bar()
        self.cur_chunk = 0
        self.dump("kerq0", self.kerq[:, 0:1024])
        self.dump("wqT0", self.wqT[:, 0:256])
        self.dump("wvT0", self.wvT[:, 0:1024])
        self.dump("w1T0", self.w1T[:, 0:1024])
        self.dump("wo1T0", self.wo1T[:, 0:1024])
        self.dump("a_sb", self.a_sb[:, :])
        self.dump("b_sb", self.b_sb[:, :])
        self.dump("bias_bf", self.bias_bf[:, :])
        self.dump("idn_bf", self.idn_bf[:, :])
        self.cur_chunk = -1

    def layernorm(self, x_ap, xbuf, set_b=False):
        nc = self.nc
        C1024 = float(1.0 / 1024.0)
        if set_b:
            m_col, s2_col, t_col, v_col, r_col = (self.m_colb, self.s2_colb,
                                                  self.t_colb, self.v_colb,
                                                  self.r_colb)
            xn = self.xn_l0
            nm, ns, nr, nxn, nsq = "mcolb", "s2colb", "rcolb", "xnl0", "sqA"
        else:
            m_col, s2_col, t_col, v_col, r_col = (self.m_col, self.s2_col,
                                                  self.t_col, self.v_col,
                                                  self.r_col)
            xn = self.xn
            nm, ns, nr, nxn, nsq = "mcol", "s2col", "rcol", "xn", "sqA"
        if getattr(self, "have_msum", False) and not set_b:
            self.have_msum = False
        else:
            self.g("dve", [xbuf], [nm], lambda: nc.vector.tensor_reduce(
                out=m_col[:, :], in_=x_ap, axis=AX.X, op=ALU.add))
        self.g("act", [xbuf], [nsq, ns], lambda: nc.scalar.activation(
            out=self.sq[:, 0:1024], in_=x_ap, func=AF.Square,
            accum_out=s2_col[:, :]))

        def _v0():
            nc.vector.scalar_tensor_tensor(
                out=v_col[:, :], in0=m_col[:, :], scalar=-C1024,
                in1=m_col[:, :], op0=ALU.mult, op1=ALU.mult)
            return nc.vector.tensor_scalar(
                out=t_col[:, :], in0=m_col[:, :], scalar1=C1024,
                scalar2=None, op0=ALU.mult)
        self.g("dve", [nm], [nm], _v0, drain=True)
        self.g("dve", [nm, ns], [nm], lambda: nc.vector.tensor_scalar(
            out=v_col[:, :], in0=v_col[:, :],
            scalar1=s2_col[:, 0:1], scalar2=C1024,
            op0=ALU.add, op1=ALU.mult), drain=True)

        def _rstd():
            nc.scalar.activation(out=r_col[:, :], in_=v_col[:, :], func=AF.Ln,
                                 bias=self.consts[:, 0:1])
            nc.scalar.drain()
            return nc.scalar.activation(out=r_col[:, :], in_=r_col[:, :],
                                        func=AF.Exp, scale=self.consts[:, 1:2])
        self.g("act", [nm], [nr], _rstd)
        self.g("dve", [xbuf, nm, nr], [nxn], lambda: nc.vector.tensor_scalar(
            out=xn[:, :], in0=x_ap, scalar1=t_col[:, 0:1],
            scalar2=r_col[:, 0:1], op0=ALU.subtract, op1=ALU.mult),
            drain=True)

    def xn_transpose(self, l0=False):
        nc = self.nc
        srct = self.xn_l0 if l0 else self.xn
        sname = "xnl0" if l0 else "xn"

        def _t(srct=srct):
            for j in range(8):
                last = nc.tensor.transpose(self.tp_ps[:, j * 128:(j + 1) * 128],
                                           srct[:, j * 128:(j + 1) * 128],
                                           self.idn_bf[:, :])
            return last
        self.g("pe", [sname], ["tp", "tp0", "tp1"], _t)
        self.g("act", ["tp"], ["xnT"], lambda: nc.scalar.activation(
            out=self.xnT[:, 0:512], in_=self.tp_ps[:, 0:512], func=AF.Copy))
        self.g("act", ["tp"], ["xnTb"], lambda: nc.scalar.activation(
            out=self.xnT[:, 512:1024], in_=self.tp_ps[:, 512:1024], func=AF.Copy))

    def chunk_body(self, x_ap, xbuf, c):
        nc = self.nc
        WSA = ["wsA"]
        WSB = ["wsB"]
        WS = ["wsA", "wsB"]

        if c == 0:
            self.layernorm(x_ap, xbuf, set_b=True)
            self.dump("l0_xn", self.xn_l0[:, :])
            self.xn_transpose(l0=True)
        self.dump("l0_xnT", self.xnT[:, :])
        pend, self.pending = self.pending, []
        for f in pend:
            f()

        def _proj():
            for g_ in range(8):
                lhs = self.xnT[:, g_ * 128:(g_ + 1) * 128]
                self.mm(self.ws_ps[:, 0:256], lhs, self.wqT[:, g_ * 256:(g_ + 1) * 256],
                        start=(g_ == 0), stop=False)
                self.mm(self.ws_ps[:, 256:512], lhs, self.wkT[:, g_ * 256:(g_ + 1) * 256],
                        start=False, stop=(g_ == 7))
                self.mm(self.ws_ps[:, 512:1024], lhs,
                        self.wvT[:, g_ * 1024:g_ * 1024 + 512], start=(g_ == 0), stop=(g_ == 7))
                last = self.mm(self.ws_ps[:, 1024:1536], lhs,
                               self.wvT[:, g_ * 1024 + 512:(g_ + 1) * 1024],
                               start=(g_ == 0), stop=(g_ == 7))
            return last
        self.g("pe", ["xnT", "xnTb"], WS, _proj)

        def _ffn():
            self.mm(self.out_ps[:, 0:512], self.ones_bf[:, :], self.bias_bf[:, 0:512],
                    start=True, stop=False)
            self.mm(self.out_ps[:, 512:1024], self.ones_bf[:, :], self.bias_bf[:, 512:1024],
                    start=True, stop=False)
            for g_ in range(8):
                lhs = self.xnT[:, g_ * 128:(g_ + 1) * 128]
                self.mm(self.out_ps[:, 0:512], lhs, self.kerq[:, g_ * 1024:g_ * 1024 + 512],
                        start=False, stop=False)
                last = self.mm(self.out_ps[:, 512:1024], lhs,
                               self.kerq[:, g_ * 1024 + 512:(g_ + 1) * 1024],
                               start=False, stop=False)
            return last
        self.g("pe", ["xnT", "xnTb"], ["out"], _ffn)

        def _ut():
            for g_ in range(8):
                last = self.mm(self.sm_ps[0:16, 0:128], self.a_sb[:, g_ * 16:(g_ + 1) * 16],
                               self.xnT[:, g_ * 128:(g_ + 1) * 128],
                               start=(g_ == 0), stop=(g_ == 7))
            return last
        self.g("pe", ["xnT", "xnTb"], ["sm"], _ut)

        qv = self.ws_ps[:, 0:256].rearrange("p (h f) -> p h f", h=16)
        kv = self.ws_ps[:, 256:512].rearrange("p (h f) -> p h f", h=16)
        qpadv = self.qpad[:, :].rearrange("p (h c) -> p h c", c=32)[:, :, 0:16]
        kpadv = self.kpad[:, :].rearrange("p (h c) -> p h c", c=32)[:, :, 0:16]

        def _evac():
            nc.scalar.activation(out=qpadv, in_=qv, func=AF.Copy, scale=self.consts[:, 3:4])
            nc.scalar.activation(out=kpadv, in_=kv, func=AF.Copy, scale=self.consts[:, 3:4])
            nc.scalar.activation(out=self.q_s[:, :].rearrange("p (h f) -> p h f", h=16),
                                 in_=qv, func=AF.Copy, scale=self.consts[:, 4:5])
            nc.scalar.activation(out=self.k_s[:, :].rearrange("p (h f) -> p h f", h=16),
                                 in_=kv, func=AF.Copy, scale=self.consts[:, 4:5])
            vaugv = self.v_aug[:, :].rearrange("p (h c) -> p h c", c=65)[:, :, 0:64]
            return nc.scalar.activation(
                out=vaugv,
                in_=self.ws_ps[:, 512:1536].rearrange("p (h e) -> p h e", h=16),
                func=AF.Copy)
        self.g("act", WS, ["qpad", "kpad", "q_s", "k_s", "v_aug"], _evac)
        self.g("dve", ["sm"], ["uT"], lambda: nc.vector.tensor_copy(
            self.uT[:, :], self.sm_ps[0:16, 0:128]))
        self.dump("l0_qpad", self.qpad[:, :])
        self.dump("l0_q_s", self.q_s[:, :])
        self.dump("l0_v_aug", self.v_aug[:, :])
        self.dump("l0_uT", self.uT[:, :])

        def _lora():
            self.mm(self.out_ps[:, 0:512], self.uT[0:16, :], self.b_sb[:, 0:512],
                    start=False, stop=False)
            return self.mm(self.out_ps[:, 512:1024], self.uT[0:16, :], self.b_sb[:, 512:1024],
                           start=False, stop=False)
        self.g("pe", ["uT"], ["out"], _lora)

        qs_v = self.q_s[:, :].rearrange("p (h f) -> p h f", h=16)
        ks_v = self.k_s[:, :].rearrange("p (h f) -> p h f", h=16)
        qf_v = self.qf[:, :].rearrange("p (h i j) -> p h i j", h=16, i=16)
        kf_v = self.kf[:, :].rearrange("p (h i j) -> p h i j", h=16, i=16)

        for r in range(5):
            use_ws = (r % 2 == 1)
            tgt = self.ws_ps[:, 0:512].bitcast(BF16) if use_ws else self.tp_ps
            tname = "wsA" if use_ws else "tp"
            if r == 1:
                self.g("dve", ["q_s", "qpad"], ["qfa"], lambda: nc.vector.tensor_tensor(
                    out=qf_v[:, 0:8],
                    in0=qs_v[:, 0:8].unsqueeze(3).broadcast_to([128, 8, 16, 16]),
                    in1=qpadv[:, 0:8].unsqueeze(2).broadcast_to([128, 8, 16, 16]),
                    op=ALU.mult), post_drain=True)
            if r == 3:
                self.g("dve", ["q_s", "qpad"], ["qfb"], lambda: nc.vector.tensor_tensor(
                    out=qf_v[:, 8:16],
                    in0=qs_v[:, 8:16].unsqueeze(3).broadcast_to([128, 8, 16, 16]),
                    in1=qpadv[:, 8:16].unsqueeze(2).broadcast_to([128, 8, 16, 16]),
                    op=ALU.mult), post_drain=True)
                self.dump("l0_qf", self.qf[:, :])

            if r == 0:
                def _t(tgt=tgt):
                    for j in range(4):
                        nc.tensor.transpose(tgt[:, j * 128:(j + 1) * 128],
                                            self.qpad[:, j * 128:(j + 1) * 128],
                                            self.idn_bf[:, :])
                        last = nc.tensor.transpose(tgt[:, 512 + j * 128:512 + (j + 1) * 128],
                                                   self.kpad[:, j * 128:(j + 1) * 128],
                                                   self.idn_bf[:, :])
                    return last
                src_names = ["qpad", "kpad"]
                dname, dst, dc0 = "qkT", self.qkT, 0
            else:
                sc0 = (r - 1) * 1024

                def _t(tgt=tgt, sc0=sc0):
                    for j in range(8):
                        last = nc.tensor.transpose(tgt[:, j * 128:(j + 1) * 128],
                                                   self.qf[:, sc0 + j * 128:sc0 + (j + 1) * 128],
                                                   self.idn_bf[:, :])
                    return last
                src_names = ["qfa" if r < 3 else "qfb"]
                dname, dst, dc0 = "qfT", self.qfT, (r - 1) * 1024
            wnames = [tname] if tname != "tp" else ["tp", "tp0", "tp1"]
            self.g("pe", src_names, wnames, _t)
            eng = "act" if r % 2 == 0 else "dve"
            h = nc.scalar if eng == "act" else nc.vector

            def _e(h=h, dst=dst, dc0=dc0, tgt=tgt):
                if h is nc.vector:
                    return h.tensor_copy(dst[:, dc0:dc0 + 1024], tgt[:, 0:1024])
                return h.activation(out=dst[:, dc0:dc0 + 1024], in_=tgt[:, 0:1024],
                                    func=AF.Copy)
            self.g(eng, [tname], [dname], _e, post_drain=(eng == "dve"))
        self.dump("l0_qkT", self.qkT[:, :])
        self.dump("l0_qfT", self.qfT[:, :])

        for half in range(2):
            hs = range(8 * half, 8 * half + 8)
            wsn = WSA if half == 0 else WSB

            def _p(hs=hs):
                nc.tensor.drain()
                first = True
                for h in hs:
                    a, b = h % 4, h // 4
                    if not first:
                        nc.tensor.drain()
                    first = False
                    lhs = self.qkT[32 * a:32 * a + 16, 512 + b * 128:512 + (b + 1) * 128]
                    rhs = self.qkT[32 * a:32 * a + 16, b * 128:(b + 1) * 128]
                    last = self.mm(self.ws_ps[:, h * 128:(h + 1) * 128], lhs, rhs,
                                   start=(h % 4 == 0), stop=(h % 4 == 3),
                                   tile_position=(32 * a, 0))
                return last
            self.g("pe", ["qkT"], wsn, _p)
            c0 = half * 1024
            self.g("act", wsn, ["sqA" if half == 0 else "sqB"], lambda c0=c0: nc.scalar.activation(
                out=self.sq[:, c0:c0 + 1024], in_=self.ws_ps[:, c0:c0 + 1024],
                func=AF.Square, scale=self.consts[:, 2:3], bias=self.consts[:, 2:3]))
            for q in range(2):
                qc = c0 + q * 512
                self.g("pool", ["sqA" if half == 0 else "sqB"], ["AT%d" % (2 * half + q)],
                       lambda qc=qc: nc.gpsimd.tensor_tensor(
                    out=self.AT[:, qc:qc + 512].rearrange("p (h t) -> p h t", h=4),
                    in0=self.sq[:, qc:qc + 512].rearrange("p (h t) -> p h t", h=4),
                    in1=self.maskT[:, :].unsqueeze(1).broadcast_to([128, 4, 128]),
                    op=ALU.mult), post_drain=True)

        self.g("pool", ["k_s", "kpad"], ["kf"], lambda: nc.gpsimd.tensor_tensor(
            out=kf_v,
            in0=ks_v.unsqueeze(3).broadcast_to([128, 16, 16, 16]),
            in1=kpadv.unsqueeze(2).broadcast_to([128, 16, 16, 16]), op=ALU.mult),
            post_drain=True)
        self.dump("l0_kf", self.kf[:, :])
        self.dump("l0_AT", self.AT[:, :])

        def _ypre():
            for h in range(16):
                yc = h * 128
                self.mm(self.ws_ps[:, yc:yc + 65],
                        self.mask05[:, :],
                        self.v_aug[:, h * 65:(h + 1) * 65],
                        start=(h % 4 == 0), stop=False)
                a, b = h % 4, h // 4
                self.mm(self.ws_ps[:, yc:yc + 65],
                        self.qkT[32 * a:32 * a + 17, b * 128:(b + 1) * 128],
                        self.S_low_bf[32 * a:32 * a + 17, colh(h):colh(h) + 65],
                        start=False, stop=False, tile_position=(32 * a, 0))
                for j in range(2):
                    last = self.mm(self.ws_ps[:, yc:yc + 65],
                                   self.qfT[:, (2 * h + j) * 128:(2 * h + j + 1) * 128],
                                   self.S_q_bf[j][:, colh(h):colh(h) + 65],
                                   start=False, stop=False)
            return last
        self.g("pe", ["v_aug", "qkT", "qfT", "Slowbf", "Sq0bf", "Sq1bf"],
               WS, _ypre)

        for quarter in range(4):
            half = quarter // 2
            wsn = WSA if half == 0 else WSB

            def _yintra(quarter=quarter):
                for h in range(4 * quarter, 4 * quarter + 4):
                    yc = h * 128
                    last = self.mm(self.ws_ps[:, yc:yc + 65],
                                   self.AT[:, h * 128:(h + 1) * 128],
                                   self.v_aug[:, h * 65:(h + 1) * 65],
                                   start=False, stop=(h % 4 == 3))
                return last
            self.g("pe", ["AT%d" % quarter, "v_aug"], wsn, _yintra)
            if quarter % 2 == 0:
                continue

            denv = self.ws_ps[:, :].rearrange(
                "p (h c) -> p h c", c=128)[:, 8 * half:8 * half + 8, 64:65]

            def _z(half=half, denv=denv):
                nc.vector.tensor_scalar(out=self.dent[:, 8 * half:8 * half + 8],
                                        in0=denv.squeeze(2),
                                        scalar1=float(1e-12), scalar2=None, op0=ALU.add)
                nc.vector.drain()
                return nc.vector.reciprocal(self.z16[:, 8 * half:8 * half + 8],
                                            self.dent[:, 8 * half:8 * half + 8])
            self.g("dve", wsn, ["z%d" % half], _z)

            def _zy(half=half):
                return nc.vector.tensor_tensor(
                    out=self.y_bf[:, half * 512:(half + 1) * 512].rearrange(
                        "p (h e) -> p h e", h=8),
                    in0=self.ws_ps[:, :].rearrange(
                        "p (h c) -> p h c", c=128)[:, 8 * half:8 * half + 8, 0:64],
                    in1=self.z16[:, 8 * half:8 * half + 8].unsqueeze(2).broadcast_to(
                        [128, 8, 64]), op=ALU.mult)
            self.g("dve", wsn + ["z%d" % half], ["ybf%d" % half], _zy, drain=True,
                   post_drain=True)
        self.dump("l0_dent", self.dent[:, :])
        self.dump("l0_ybf", self.y_bf[:, :])

        for half in range(2):
            o = half * 512

            def _yt(o=o):
                for j in range(4):
                    last = nc.tensor.transpose(self.tp_ps[:, o + j * 128:o + (j + 1) * 128],
                                               self.y_bf[:, o + j * 128:o + (j + 1) * 128],
                                               self.idn_bf[:, :])
                return last
            self.g("pe", ["ybf%d" % half], ["tp%d" % half], _yt)
            self.g("act", ["tp%d" % half], ["yT%d" % half], lambda o=o: nc.scalar.activation(
                out=self.yT[:, o:o + 512], in_=self.tp_ps[:, o:o + 512], func=AF.Copy))

            def _wo(half=half):
                for j in range(4 * half, 4 * half + 4):
                    lhs = self.yT[:, j * 128:(j + 1) * 128]
                    self.mm(self.out_ps[:, 0:512], lhs, self.woT[:, j * 1024:j * 1024 + 512],
                            start=False, stop=(j == 7))
                    last = self.mm(self.out_ps[:, 512:1024], lhs,
                                   self.woT[:, j * 1024 + 512:(j + 1) * 1024],
                                   start=False, stop=(j == 7))
                return last
            self.g("pe", ["yT%d" % half], ["out"], _wo)

        self.g("dve", ["out", xbuf], [xbuf, "mcol"],
               lambda: nc.vector.scalar_tensor_tensor(
                   out=x_ap, in0=self.out_ps[:, :], scalar=0.0, in1=x_ap,
                   op0=ALU.add, op1=ALU.add, accum_out=self.m_col[:, :]),
               post_drain=True)
        self.dump("l0_x", x_ap)
        self.have_msum = True

        def ds_round(rnd, defer_add=False):
            def _ds(rnd=rnd):
                for h in range(16):
                    c0 = colh(h)
                    if rnd == 0:
                        lhs = self.kpad[:, 32 * h:32 * h + 17]
                        out = self.ws_ps[0:17, c0:c0 + 65]
                    else:
                        j = rnd - 1
                        lhs = self.kf[:, (2 * h + j) * 128:(2 * h + j + 1) * 128]
                        out = self.ws_ps[:, c0:c0 + 65]
                    last = self.mm(out, lhs, self.v_aug[:, h * 65:(h + 1) * 65],
                                   start=h in (0, 7, 14), stop=h in (6, 13, 15))
                return last
            src_name = "kpad" if rnd == 0 else "kf"
            self.g("pe", [src_name, "v_aug"], WS, _ds)
            S = self.S_low if rnd == 0 else self.S_q[rnd - 1]
            sname = ["Slow", "Sq0", "Sq1"][rnd]
            rows = 17 if rnd == 0 else 128

            def _add(S=S, rows=rows, sname=sname):
                self.g("dve", WS + [sname], [sname], lambda:
                       nc.vector.tensor_tensor(out=S[0:rows, :], in0=S[0:rows, :],
                                               in1=self.ws_ps[0:rows, 0:S_W],
                                               op=ALU.add))
            if defer_add == "manual":
                return _add
            if defer_add:
                self.pending.append(_add)
            else:
                _add()

        add0 = ds_round(0, defer_add="manual")
        self.layernorm(x_ap, xbuf)
        self.dump("l1_xn", self.xn[:, :])
        add0()
        self.xn_transpose()
        if c + 1 < self.nchunks:
            nxbuf = "x%d" % ((c + 1) % 2)
            nx_ap = self.x_sb[(c + 1) % 2][:, :]
            nc.vector.wait_ge(self.sem_ld, 16 * (c + 2))
            nc.scalar.wait_ge(self.sem_ld, 16 * (c + 2))
            self.layernorm(nx_ap, nxbuf, set_b=True)

        def _ffn1():
            for g_ in range(8):
                lhs = self.xnT[:, g_ * 128:(g_ + 1) * 128]
                self.mm(self.out_ps[:, 0:512], lhs, self.w1T[:, g_ * 1024:g_ * 1024 + 512],
                        start=(g_ == 0), stop=False)
                last = self.mm(self.out_ps[:, 512:1024], lhs,
                               self.w1T[:, g_ * 1024 + 512:(g_ + 1) * 1024],
                               start=(g_ == 0), stop=False)
            return last
        self.g("pe", ["xnT", "xnTb"], ["out"], _ffn1)

        def _proj1():
            for g_ in range(8):
                lhs = self.xnT[:, g_ * 128:(g_ + 1) * 128]
                self.mm(self.ws_ps[:, 0:256], lhs, self.wq1T[:, g_ * 256:(g_ + 1) * 256],
                        start=(g_ == 0), stop=False)
                self.mm(self.ws_ps[:, 256:512], lhs, self.wk1T[:, g_ * 256:(g_ + 1) * 256],
                        start=False, stop=(g_ == 7))
                last = self.mm(self.ws_ps[:, 512:768], lhs,
                               self.wv1T[:, g_ * 256:(g_ + 1) * 256],
                               start=(g_ == 0), stop=(g_ == 7))
            return last
        self.g("pe", ["xnT", "xnTb"], WS, _proj1)

        e_qk = self.sq[:, 1024:1536]
        self.g("dve", WS, ["sqB"], lambda: nc.vector.tensor_scalar(
            out=e_qk, in0=self.ws_ps[:, 0:512], scalar1=0.0, scalar2=None,
            op0=ALU.min))
        self.g("act", ["sqB"], ["sqB"], lambda: nc.scalar.activation(
            out=e_qk, in_=e_qk, func=AF.Exp))
        self.g("dve", WS + ["sqB"], ["pqk"], lambda: nc.vector.scalar_tensor_tensor(
            out=self.p_qk[:, :], in0=self.ws_ps[:, 0:512], scalar=0.0, in1=e_qk,
            op0=ALU.max, op1=ALU.add))
        self.g("act", WS, ["v1aug"], lambda: nc.scalar.activation(
            out=self.v1_aug[:, :].rearrange("p (h c) -> p h c", c=65)[:, :, 0:64],
            in_=self.ws_ps[:, 512:768].rearrange("p (h e) -> p h e", h=4),
            func=AF.Copy))
        self.g("dve", ["pqk"], ["sqA"], lambda: nc.vector.tensor_tensor(
            out=self.sq[:, 0:512], in0=self.p_qk[:, :], in1=self.p_qk[:, :],
            op=ALU.mult))
        self.g("dve", ["sqA"], ["nrm"], lambda: nc.vector.tensor_reduce(
            out=self.nrm[:, :].rearrange("p (a b) -> p a b", b=1),
            in_=self.sq[:, 0:512].rearrange("p (a b) -> p a b", a=8),
            axis=AX.X, op=ALU.add))

        def _sqrt():
            nc.scalar.activation(out=self.rn[:, :], in_=self.nrm[:, :], func=AF.Ln)
            nc.scalar.drain()
            return nc.scalar.activation(out=self.rn[:, :], in_=self.rn[:, :],
                                        func=AF.Exp, scale=self.consts[:, 3:4])
        self.g("act", ["nrm"], ["rn"], _sqrt)

        def _rn():
            nc.vector.tensor_scalar(out=self.rn[:, :], in0=self.rn[:, :],
                                    scalar1=float(1e-6), scalar2=None, op0=ALU.add)
            nc.vector.drain()
            return nc.vector.reciprocal(self.rn[:, :], self.rn[:, :])
        self.g("dve", ["rn"], ["rn"], _rn)

        self.g("dve", ["pqk", "rn"], ["q1pad"], lambda: nc.vector.tensor_tensor(
            out=self.q1pad[:, :].rearrange("p (h c) -> p h c", c=64),
            in0=self.p_qk[:, 0:256].rearrange("p (a b) -> p a b", a=4),
            in1=self.rn[:, 0:4].unsqueeze(2).broadcast_to([128, 4, 64]), op=ALU.mult),
            drain=True, post_drain=True)
        self.g("dve", ["pqk", "rn"], ["k1pad"], lambda: nc.vector.tensor_tensor(
            out=self.k1pad[:, :].rearrange("p (h c) -> p h c", c=64),
            in0=self.p_qk[:, 256:512].rearrange("p (a b) -> p a b", a=4),
            in1=self.rn[:, 4:8].unsqueeze(2).broadcast_to([128, 4, 64]), op=ALU.mult),
            post_drain=True)
        self.dump("l1_pqk", self.p_qk[:, :])
        self.dump("l1_rn", self.rn[:, :])
        self.dump("l1_q1pad", self.q1pad[:, :])
        self.dump("l1_k1pad", self.k1pad[:, :])
        self.dump("l1_v1aug", self.v1_aug[:, :])

        ds_round(1)

        def _t1q():
            for j in range(2):
                last = nc.tensor.transpose(self.tp_ps[:, j * 128:(j + 1) * 128],
                                           self.q1pad[:, j * 128:(j + 1) * 128],
                                           self.idn_bf[:, :])
            return last
        self.g("pe", ["q1pad"], ["tp", "tp0"], _t1q)

        def _t1k():
            for j in range(2):
                last = nc.tensor.transpose(self.tp_ps[:, 256 + j * 128:256 + (j + 1) * 128],
                                           self.k1pad[:, j * 128:(j + 1) * 128],
                                           self.idn_bf[:, :])
            return last
        self.g("pe", ["k1pad"], ["tp", "tp1"], _t1k)
        self.g("act", ["tp", "tp0", "tp1"], ["qk1T"], lambda: nc.scalar.activation(
            out=self.qk1T[:, :], in_=self.tp_ps[:, 0:512], func=AF.Copy))

        def _p1():
            for h in range(4):
                a, b = h % 2, h // 2
                if h > 0:
                    nc.tensor.drain()
                lhs = self.qk1T[64 * a:64 * a + 64, 256 + b * 128:256 + (b + 1) * 128]
                rhs = self.qk1T[64 * a:64 * a + 64, b * 128:(b + 1) * 128]
                last = self.mm(self.sm_ps[:, h * 128:(h + 1) * 128], lhs, rhs,
                               start=(h == 0), stop=(h == 3),
                               tile_position=(64 * a, 0))
            return last
        self.g("pe", ["qk1T"], ["sm"], _p1)
        self.g("dve", ["sm"], ["A1T"], lambda: nc.vector.tensor_tensor(
            out=self.A1T[:, :].rearrange("p (h t) -> p h t", h=4),
            in0=self.sm_ps[:, :].rearrange("p (h t) -> p h t", h=4),
            in1=self.maskT[:, :].unsqueeze(1).broadcast_to([128, 4, 128]), op=ALU.mult),
            post_drain=True)
        self.dump("l1_A1T", self.A1T[:, :])

        def _y1():
            for h in range(4):
                yc = h * 128
                self.mm(self.sm_ps[:, yc:yc + 65], self.A1T[:, h * 128:(h + 1) * 128],
                        self.v1_aug[:, h * 65:(h + 1) * 65], start=(h == 0), stop=False)
                a, b = h % 2, h // 2
                last = self.mm(self.sm_ps[:, yc:yc + 65],
                               self.qk1T[64 * a:64 * a + 64, b * 128:(b + 1) * 128],
                               self.S1_bf[64 * a:64 * a + 64, h * 65:(h + 1) * 65],
                               start=False, stop=(h == 3),
                               tile_position=(64 * a, 0))
            return last
        self.g("pe", ["A1T", "v1aug", "qk1T", "S1bf"], ["sm"], _y1)

        denv1 = self.sm_ps[:, :].rearrange("p (h c) -> p h c", c=128)[:, :, 64:65]

        def _z1():
            nc.vector.tensor_scalar(out=self.dent4[:, :], in0=denv1.squeeze(2),
                                    scalar1=float(1e-6), scalar2=None, op0=ALU.add)
            nc.vector.drain()
            return nc.vector.reciprocal(self.z4[:, :], self.dent4[:, :])
        self.g("dve", ["sm"], ["z4"], _z1)
        self.g("dve", ["sm", "z4"], ["y1"], lambda: nc.vector.tensor_tensor(
            out=self.y1_bf[:, :].rearrange("p (h e) -> p h e", h=4),
            in0=self.sm_ps[:, :].rearrange("p (h c) -> p h c", c=128)[:, :, 0:64],
            in1=self.z4[:, :].unsqueeze(2).broadcast_to([128, 4, 64]), op=ALU.mult),
            drain=True, post_drain=True)
        self.dump("l1_y1bf", self.y1_bf[:, :])

        def _y1t():
            for j in range(2):
                last = nc.tensor.transpose(self.tp_ps[:, j * 128:(j + 1) * 128],
                                           self.y1_bf[:, j * 128:(j + 1) * 128],
                                           self.idn_bf[:, :])
            return last
        self.g("pe", ["y1"], ["tp", "tp0", "tp1"], _y1t)
        self.g("act", ["tp"], ["y1T"], lambda: nc.scalar.activation(
            out=self.y1T[:, :], in_=self.tp_ps[:, 0:256], func=AF.Copy))

        def _wo1():
            for j in range(2):
                lhs = self.y1T[:, j * 128:(j + 1) * 128]
                self.mm(self.out_ps[:, 0:512], lhs, self.wo1T[:, j * 1024:j * 1024 + 512],
                        start=False, stop=(j == 1))
                last = self.mm(self.out_ps[:, 512:1024], lhs,
                               self.wo1T[:, j * 1024 + 512:(j + 1) * 1024],
                               start=False, stop=(j == 1))
            return last
        self.g("pe", ["y1T"], ["out"], _wo1)

        def _resid2():
            nc.vector.tensor_tensor(out=x_ap, in0=x_ap, in1=self.out_ps[:, :],
                                    op=ALU.add)
            return nc.vector.drain()
        self.g("dve", ["out", xbuf], [xbuf], _resid2)
        self.resid_mark = self.n["dve"]

        def _ds1():
            for h in range(4):
                last = self.mm(self.sm_ps[0:64, h * 65:(h + 1) * 65],
                               self.k1pad[:, h * 64:(h + 1) * 64],
                               self.v1_aug[:, h * 65:(h + 1) * 65],
                               start=(h == 0), stop=(h == 3))
            return last
        self.g("pe", ["k1pad", "v1aug"], ["sm"], _ds1)
        self.pending.append(lambda: self.g(
            "dve", ["sm", "S1"], ["S1"], lambda: nc.vector.tensor_tensor(
                out=self.S1[:, :], in0=self.S1[:, :], in1=self.sm_ps[0:64, 0:260],
                op=ALU.add)))
        ds_round(2, defer_add=True)
        def _cast_slow():
            for a in (0, 32, 64, 96):
                last = nc.scalar.activation(out=self.S_low_bf[a:a + 17, :],
                                            in_=self.S_low[0:17, :], func=AF.Copy)
            return last
        self.pending.append(lambda: self.g("act", ["Slow"], ["Slowbf"], _cast_slow))
        self.pending.append(lambda: self.g(
            "act", ["Sq0"], ["Sq0bf"], lambda: nc.scalar.activation(
                out=self.S_q_bf[0][:, :], in_=self.S_q[0][:, :], func=AF.Copy)))
        self.pending.append(lambda: self.g(
            "act", ["Sq1"], ["Sq1bf"], lambda: nc.scalar.activation(
                out=self.S_q_bf[1][:, :], in_=self.S_q[1][:, :], func=AF.Copy)))
        def _cast_s1():
            for a in (0, 64):
                last = nc.scalar.activation(out=self.S1_bf[a:a + 64, :],
                                            in_=self.S1[0:64, :], func=AF.Copy)
            return last
        self.pending.append(lambda: self.g("act", ["S1"], ["S1bf"], _cast_s1))
        if c + 1 < self.nchunks:
            self.xn_transpose(l0=True)
        self.dump("l0_Slow", self.S_low_bf[:, :])
        self.dump("l0_Sq0", self.S_q_bf[0][:, :])
        self.dump("l0_Sq1", self.S_q_bf[1][:, :])
        self.dump("l1_S1", self.S1_bf[:, :])


    def build(self):
        nc = self.nc
        self.alloc()
        self.prep()
        self.bar()
        self.reset_deps()

        for c in range(min(2, self.nchunks)):
            nc.sync.dma_start(self.x_sb[c % 2][:, :],
                              self.i_x[c * C:(c + 1) * C, :]).then_inc(self.sem_ld, 16)
            self.ld += 16

        for c in range(self.nchunks):
            self.cur_chunk = c
            xbuf = "x%d" % (c % 2)
            buf = self.x_sb[c % 2][:, :]
            nc.vector.wait_ge(self.sem_ld, 16 * (c + 1))
            nc.scalar.wait_ge(self.sem_ld, 16 * (c + 1))
            self.chunk_body(buf, xbuf, c)
            nc.sync.wait_ge(self.sems["dve"], self.resid_mark)
            nc.sync.dma_start(self.o_out[c * C:(c + 1) * C, :], buf).then_inc(
                self.sem_st, 16)
            self.st += 16
            if c + 2 < self.nchunks:
                nc.sync.wait_ge(self.sem_st, 16 * (c + 1))
                nc.sync.dma_start(self.x_sb[c % 2][:, :],
                                  self.i_x[(c + 2) * C:(c + 3) * C, :]).then_inc(
                    self.sem_ld, 16)
                self.ld += 16

        nc.sync.wait_ge(self.sem_st, self.st)
        return nc


def build_program(nchunks=NCH, layers=(0, 1), dbg=False):
    return KB(nchunks=nchunks, layers=layers, dbg=dbg).build()


def make_in_map(inputs, b):
    f32 = np.float32
    return {
        "x": np.ascontiguousarray(np.asarray(inputs["L0_attn_hidden_states"])[b], dtype=f32),
        "wq0": np.ascontiguousarray(np.asarray(inputs["L0_attn_Wq"], dtype=f32)),
        "wk0": np.ascontiguousarray(np.asarray(inputs["L0_attn_Wk"], dtype=f32)),
        "wv0": np.ascontiguousarray(np.asarray(inputs["L0_attn_Wv"], dtype=f32)),
        "wo0": np.ascontiguousarray(np.asarray(inputs["L0_attn_Wo"], dtype=f32)),
        "ker": np.ascontiguousarray(np.asarray(inputs["L0_ffn_kernel"], dtype=f32)),
        "bias": np.ascontiguousarray(np.asarray(inputs["L0_ffn_bias"], dtype=f32)).reshape(1, 1024),
        "la": np.ascontiguousarray(np.asarray(inputs["L0_ffn_a"], dtype=f32)),
        "lb": np.ascontiguousarray(np.asarray(inputs["L0_ffn_b"], dtype=f32)),
        "wq1": np.ascontiguousarray(np.asarray(inputs["L1_attn_Wq"], dtype=f32)),
        "wk1": np.ascontiguousarray(np.asarray(inputs["L1_attn_Wk"], dtype=f32)),
        "wv1": np.ascontiguousarray(np.asarray(inputs["L1_attn_Wv"], dtype=f32)),
        "wo1": np.ascontiguousarray(np.asarray(inputs["L1_attn_Wo"], dtype=f32)),
        "w1": np.ascontiguousarray(np.asarray(inputs["L1_ffn_weight"], dtype=f32)),
        "idn": np.eye(128, dtype=f32),
        "maskT": np.triu(np.ones((128, 128), dtype=f32)),
        "mask05": np.triu(np.full((128, 128), 0.5, dtype=ml_dtypes.bfloat16)),
    }


def kernel(**inputs):
    nc = build_program()
    in_maps = [make_in_map(inputs, b) for b in range(8)]
    res = run_bass_kernel_spmd(nc, in_maps, core_ids=list(range(8)))
    return np.stack([r["out"] for r in res.results], axis=0).astype(np.float32)
